# version 01
# baseline (speedup 1.0000x reference)
import math
import numpy as np
import ml_dtypes
import concourse.bass as bass
import concourse.mybir as mybir
from concourse.bass_utils import run_bass_kernel_spmd

F32 = mybir.dt.float32
BF16 = mybir.dt.bfloat16
AF = mybir.ActivationFunctionType
ALU = mybir.AluOpType
AX = mybir.AxisListType

NCORES = 8
SEQ = 2048
NSEQ = 2
T = NSEQ * SEQ
NT = T // 128
NT_SEQ = SEQ // 128
D = 1024
NCH = D // 128
NTG = T // 512
H = 16
HD = 64
LN_EPS = 1e-5



DMA_SEMS = ("qr0", "qr1", "qi0", "qi1", "qo0", "qo1", "qp0", "qp1",
            "qn0", "qn1", "qv0", "qv1", "qf0", "qf1", "qm0", "qm1",
            "qx0", "qx1", "qy0", "qy1", "qz0", "qz1", "qd0", "qd1",
            "qe0", "qe1", "qu0", "qu1", "qA", "qB", "qw", "qc")


def phase_fence(ctx):
    return {k: ctx.cnt[k] for k in DMA_SEMS + ("ve", "se", "pe", "gp")
            if k in ctx.cnt}


def fence_wait(ctx, eng, pv):
    for k, val in pv.items():
        if val:
            ctx.w(eng, k, val)


class Ctx:

    def __init__(self, nc):
        self.nc = nc
        self.sem = {}
        self.cnt = {}

    def add_sem(self, es, name):
        self.sem[name] = es.enter_context(self.nc.semaphore(name))
        self.cnt[name] = 0

    def inc(self, instr, name, n=1):
        instr.then_inc(self.sem[name], n)
        self.cnt[name] += n
        return self.cnt[name]

    def w(self, eng, name, value):
        eng.wait_ge(self.sem[name], value)

    def wall(self, eng, name):
        eng.wait_ge(self.sem[name], self.cnt[name])

    def slot_inc(self, instr, name, slot):
        return self.inc(instr, f"{name}{slot}", 16)

    def slot_w(self, eng, name, slot, k):
        eng.wait_ge(self.sem[f"{name}{slot}"], 16 * k)


def emit_ln_transpose(nc, ctx, blk, src_dram, xnT, ln, ident_bf, pT,
                      dest_of_tile=None, dest_war=None, pqo="qo"):
    b = {k: ctx.cnt[k] for k in ("ve", "se", "pe")}
    b["qr0"] = ctx.cnt["qr0"]; b["qr1"] = ctx.cnt["qr1"]
    qo_prev = (ctx.cnt[pqo + "0"], ctx.cnt[pqo + "1"])

    def f_sync(sync):
        ctx.w(sync, pqo + "0", qo_prev[0])
        ctx.w(sync, pqo + "1", qo_prev[1])
        for j in range(NT):
            if j >= 2:
                ctx.w(sync, "ve", b["ve"] + (j - 2) * 5 + 5)
            ctx.slot_inc(sync.dma_start(out=ln[f"rt{j % 2}"][:, :],
                                        in_=src_dram[j * 128:(j + 1) * 128, :]),
                         "qr", j % 2)
    blk.sync(f_sync)

    def f_vec(v):
        for j in range(NT):
            rt = ln[f"rt{j % 2}"]
            xn = ln[f"xn{j % 2}"]
            ctx.slot_w(v, "qr", j % 2, b[f"qr{j % 2}"] // 16 + j // 2 + 1)
            ctx.inc(v.reduce_sum(ln["s"][:, :], rt[:, :], axis=AX.X), "ve")
            ctx.wall(v, "ve")
            ctx.inc(v.tensor_scalar(ln["negm"][:, :], ln["s"][:, :], -1.0 / D, None,
                                    ALU.mult), "ve")
            ctx.w(v, "se", b["se"] + (j + 1) * 2)
            ctx.inc(v.reciprocal(ln["rstd"][:, :], ln["sd"][:, :]), "ve")
            ctx.wall(v, "ve")
            if j >= 2:
                ctx.w(v, "pe", b["pe"] + j - 1)
            ctx.inc(v.tensor_scalar(xn[:, :], rt[:, :], ln["negm"][:, :],
                                    ln["rstd"][:, :], ALU.add, ALU.mult), "ve")
            ctx.w(v, "pe", b["pe"] + j + 1)
            if dest_war is not None:
                dest_war(v, j)
            dst = (xnT.rearrange("p (c t) -> p c t", c=NCH)[:, :,
                                 j * 128:(j + 1) * 128]
                   if dest_of_tile is None else dest_of_tile(j))
            ctx.inc(v.tensor_copy(
                dst, pT[j % 2].rearrange("p (c t) -> p c t", c=NCH)[:, :, :]),
                "ve")
    blk.vector(f_vec)

    def f_scal(sc):
        ctx.w(sc, "gp", ctx.setup["gp"])
        for j in range(NT):
            rt = ln[f"rt{j % 2}"]
            ctx.w(sc, "ve", b["ve"] + j * 5 + 2)
            ctx.inc(sc.activation(ln[f"xn{j % 2}"][:, :], rt[:, :], AF.Square,
                                  bias=ln["negm"][:, :], scale=1.0,
                                  accum_out=ln["ssq"][:, :]), "se")
            ctx.wall(sc, "se")
            ctx.inc(sc.activation(ln["sd"][:, :], ln["ssq"][:, :], AF.Sqrt,
                                  bias=ln["eps"][:, :], scale=1.0 / D), "se")
    blk.scalar(f_scal)

    def f_pe(t):
        for j in range(NT):
            xn = ln[f"xn{j % 2}"]
            ctx.w(t, "ve", b["ve"] + j * 5 + 4)
            if j >= 2:
                ctx.w(t, "ve", b["ve"] + (j - 2) * 5 + 5)
            for c in range(NCH):
                i = t.transpose(pT[j % 2][:, c * 128:(c + 1) * 128],
                                xn[:, c * 128:(c + 1) * 128], ident_bf[:, :])
            ctx.inc(i, "pe")
    blk.tensor(f_pe)


def emit_l0ffn(nc, ctx, blk, es, R_in, R_out, ln, ident_bf, pT, wdr, tag):
    E = es.enter_context
    xnT = E(nc.sbuf_tensor(f"f_xnT{tag}", [128, NCH * T], BF16))
    emit_ln_transpose(nc, ctx, blk, R_in, xnT, ln, ident_bf, pT)
    wr = E(nc.sbuf_tensor(f"f_wr{tag}", [128, NCH * D], BF16))
    wk = E(nc.sbuf_tensor(f"f_wk{tag}", [128, NCH * D], BF16))
    wv = E(nc.sbuf_tensor(f"f_wv{tag}", [128, NCH * D], BF16))
    mur = E(nc.sbuf_tensor(f"f_mur{tag}", [128, NCH], F32))
    muk = E(nc.sbuf_tensor(f"f_muk{tag}", [128, NCH], F32))
    omur = E(nc.sbuf_tensor(f"f_omur{tag}", [128, NCH], F32))
    omuk = E(nc.sbuf_tensor(f"f_omuk{tag}", [128, NCH], F32))
    xrg = [E(nc.sbuf_tensor(f"f_xrg{i}{tag}", [128, NCH * 512], BF16)) for i in range(2)]
    xkg = [E(nc.sbuf_tensor(f"f_xkg{i}{tag}", [128, NCH * 512], BF16)) for i in range(2)]
    kag = [E(nc.sbuf_tensor(f"f_kag{i}{tag}", [128, NCH * 512], BF16)) for i in range(2)]
    tmix = E(nc.sbuf_tensor(f"f_tmix{tag}", [128, 512], F32))
    kscr = [E(nc.sbuf_tensor(f"f_kscr{i}{tag}", [128, 512], BF16)) for i in range(2)]
    rsig = [E(nc.sbuf_tensor(f"f_rsig{i}{tag}", [128, D], BF16)) for i in range(2)]
    rvt = [E(nc.sbuf_tensor(f"f_rvt{i}{tag}", [128, 512], F32)) for i in range(2)]
    rin = [E(nc.sbuf_tensor(f"f_rin{i}{tag}", [128, D], F32)) for i in range(2)]
    pr = [E(nc.psum_tensor(f"f_pr{i}{tag}", [128, 512], F32)) for i in range(4)]
    pk = [E(nc.psum_tensor(f"f_pk{i}{tag}", [128, 512], F32)) for i in range(2)]

    def f_gw(g):
        ctx.wall(g, "qw")
        for (wsb, wd) in ((wr, wdr["wr_t"]), (wk, wdr["wk_t"]), (wv, wdr["wv_t"])):
            for c in range(NCH):
                ctx.inc(g.dma_start(out=wsb[:, c * D:(c + 1) * D],
                                    in_=wd[c * 128:(c + 1) * 128, :]), "qw", 16)
        ctx.inc(g.dma_start(out=mur[:, :], in_=wdr["mu_r2"]), "qw", 16)
        ctx.inc(g.dma_start(out=muk[:, :], in_=wdr["mu_k2"]), "qw", 16)
    blk.gpsimd(f_gw)
    w_loaded = ctx.cnt["qw"]

    pe0 = ctx.cnt["pe"]
    qi0 = (ctx.cnt["qi0"], ctx.cnt["qi1"])
    qo0 = (ctx.cnt["qo0"], ctx.cnt["qo1"])

    R_POS = [0, 1, 3, 5]
    V_POS = [2, 4, 6, 7]

    def pe_r_done(tg, jj):
        return pe0 + tg * 16 + 8 + R_POS[jj] + 1

    def pe_v_done(tg, jj):
        return pe0 + tg * 16 + 8 + V_POS[jj] + 1

    mix_done = {}
    comb_done = {}

    def f_vec(v):
        ctx.wall(v, "ve")
        ctx.w(v, "qw", w_loaded)
        ctx.inc(v.tensor_scalar(omur[:, :], mur[:, :], -1.0, 1.0, ALU.mult, ALU.add), "ve")
        ctx.inc(v.tensor_scalar(omuk[:, :], muk[:, :], -1.0, 1.0, ALU.mult, ALU.add), "ve")
        ctx.wall(v, "ve")

        def emit_mix(tg):
            if tg >= 2:
                ctx.w(v, "pe", pe0 + (tg - 2) * 16 + 16)
            for (dst, mu, omu) in ((xkg[tg % 2], muk, omuk), (xrg[tg % 2], mur, omur)):
                for c in range(NCH):
                    src = xnT[:, c * T + tg * 512: c * T + tg * 512 + 512]
                    d = dst[:, c * 512:(c + 1) * 512]
                    if tg % (SEQ // 512) == 0:
                        ctx.inc(v.tensor_scalar(
                            tmix[:, 1:], xnT[:, c * T + tg * 512: c * T + tg * 512 + 511],
                            omu[:, c:c + 1], None, ALU.mult), "ve")
                        ctx.inc(v.memset(tmix[:, 0:1], 0.0), "ve")
                    else:
                        ctx.inc(v.tensor_scalar(
                            tmix[:, :], xnT[:, c * T + tg * 512 - 1: c * T + tg * 512 + 511],
                            omu[:, c:c + 1], None, ALU.mult), "ve")
                    ctx.wall(v, "ve")
                    ctx.inc(v.scalar_tensor_tensor(d, src, mu[:, c:c + 1], tmix[:, :],
                                                   ALU.mult, ALU.add), "ve")
                    ctx.wall(v, "ve")
            mix_done[tg] = ctx.cnt["ve"]

        def emit_comb(tg):
            for jj in range(4):
                j = tg * 4 + jj
                jm = j % 2
                ctx.w(v, "pe", pe_v_done(tg, jj))
                ctx.slot_w(v, "qi", jm, qi0[jm] // 16 + j // 2 + 1)
                for half in range(2):
                    ctx.inc(v.tensor_tensor(rvt[half][:, :],
                                            rsig[jm][:, half * 512:half * 512 + 512],
                                            pr[2 * jm + half][:, :], ALU.mult), "ve")
                ctx.wall(v, "ve")
                for half in range(2):
                    ctx.inc(v.scalar_tensor_tensor(
                        rin[jm][:, half * 512:half * 512 + 512], rvt[half][:, :], 0.5,
                        rin[jm][:, half * 512:half * 512 + 512], ALU.mult, ALU.add), "ve")
                ctx.wall(v, "ve")
                comb_done[j] = ctx.cnt["ve"]

        emit_mix(0)
        emit_mix(1)
        for tg in range(NTG):
            emit_comb(tg)
            if tg + 2 < NTG:
                emit_mix(tg + 2)
    blk.vector(f_vec)

    se0 = ctx.cnt["se"]

    def se_ka(tg, dn):
        return se0 + tg * 24 + (dn + 1) * 2

    def se_sig(tg, jj):
        return se0 + tg * 24 + 16 + (jj + 1) * 2

    def f_pe(t):
        for tg in range(NTG):
            ctx.w(t, "ve", mix_done[tg])
            for dn in range(NCH):
                gi = tg * NCH + dn
                bnk = pk[gi % 2]
                if gi >= 2:
                    tg2, dn2 = divmod(gi - 2, NCH)
                    ctx.w(t, "se", se_ka(tg2, dn2))
                for c in range(NCH):
                    i = t.matmul(bnk[:, :],
                                 wk[:, c * D + dn * 128: c * D + dn * 128 + 128],
                                 xkg[tg % 2][:, c * 512: c * 512 + 512],
                                 start=(c == 0), stop=(c == NCH - 1))
                ctx.inc(i, "pe")
            def emit_rmm(jj):
                j = tg * 4 + jj
                jm = j % 2
                if j >= 2:
                    ctx.w(t, "ve", comb_done[j - 2])
                for half in range(2):
                    for c in range(NCH):
                        i = t.matmul(pr[2 * jm + half][:, :],
                                     xrg[tg % 2][:, c * 512 + jj * 128: c * 512 + jj * 128 + 128],
                                     wr[:, c * D + half * 512: c * D + half * 512 + 512],
                                     start=(c == 0), stop=(c == NCH - 1))
                ctx.inc(i, "pe")

            def emit_vmm(jj):
                j = tg * 4 + jj
                jm = j % 2
                ctx.w(t, "se", se_sig(tg, jj))
                for half in range(2):
                    for dn in range(NCH):
                        i = t.matmul(pr[2 * jm + half][:, :],
                                     kag[tg % 2][:, dn * 512 + jj * 128: dn * 512 + jj * 128 + 128],
                                     wv[:, dn * D + half * 512: dn * D + half * 512 + 512],
                                     start=(dn == 0), stop=(dn == NCH - 1))
                ctx.inc(i, "pe")

            emit_rmm(0); emit_rmm(1); emit_vmm(0); emit_rmm(2)
            emit_vmm(1); emit_rmm(3); emit_vmm(2); emit_vmm(3)
    blk.tensor(f_pe)

    def f_scal(sc):
        for tg in range(NTG):
            for dn in range(NCH):
                gi = tg * NCH + dn
                bnk = pk[gi % 2]
                ctx.w(sc, "pe", pe0 + tg * 16 + dn + 1)
                if gi >= 2:
                    ctx.w(sc, "se", se0 + ((gi - 2) // NCH) * 24 + ((gi - 2) % NCH + 1) * 2)
                ctx.inc(sc.activation(kscr[gi % 2][:, :], bnk[:, :], AF.Relu), "se")
                ctx.wall(sc, "se")
                if tg >= 2 and dn == 0:
                    ctx.w(sc, "pe", pe0 + (tg - 2) * 16 + 16)
                ctx.inc(sc.activation(kag[tg % 2][:, dn * 512:(dn + 1) * 512],
                                      kscr[gi % 2][:, :], AF.Square), "se")
            for jj in range(4):
                j = tg * 4 + jj
                jm = j % 2
                ctx.w(sc, "pe", pe_r_done(tg, jj))
                for half in range(2):
                    ctx.inc(sc.activation(rsig[jm][:, half * 512:half * 512 + 512],
                                          pr[2 * jm + half][:, :], AF.Sigmoid), "se")
    blk.scalar(f_scal)

    def f_rload(sync):
        for j in range(NT):
            jm = j % 2
            if j < 2:
                ctx.slot_w(sync, "qo", jm, qo0[jm] // 16)
            else:
                ctx.slot_w(sync, "qo", jm, qo0[jm] // 16 + (j - 2) // 2 + 1)
            ctx.slot_inc(sync.dma_start(out=rin[jm][:, :],
                                        in_=R_in[j * 128:(j + 1) * 128, :]),
                         "qi", jm)
    blk.sync(f_rload)

    def f_rstore(g):
        for j in range(NT):
            ctx.w(g, "ve", comb_done[j])
            ctx.slot_inc(g.dma_start(out=R_out[j * 128:(j + 1) * 128, :],
                                     in_=rin[j % 2][:, :]), "qo", j % 2)
    blk.gpsimd(f_rstore)


def emit_l0attn(nc, ctx, blk, es, R_in, R_out, ln, ident_bf, onesb, pT,
                wdr, dscr, tag):
    E = es.enter_context
    xnT = E(nc.sbuf_tensor(f"a_xnT{tag}", [128, NCH * T], BF16))
    emit_ln_transpose(nc, ctx, blk, R_in, xnT, ln, ident_bf, pT)
    wqkv = [E(nc.sbuf_tensor(f"a_wqkv{i}{tag}", [128, NCH * 384], BF16))
            for i in range(2)]
    wpj = E(nc.sbuf_tensor(f"a_wpj{tag}", [128, NCH * D], BF16))
    v2 = E(nc.sbuf_tensor(f"a_v2{tag}", [128, NT * 130], BF16))
    q65 = [E(nc.sbuf_tensor(f"a_q65{i}{tag}", [65, SEQ], BF16)) for i in range(2)]
    k65 = [E(nc.sbuf_tensor(f"a_k65{i}{tag}", [65, SEQ], BF16)) for i in range(2)]
    pbuf = [E(nc.sbuf_tensor(f"a_pb{i}{tag}", [128, 512], BF16)) for i in range(3)]
    pmask = E(nc.sbuf_tensor(f"a_pm{tag}", [128, 512], BF16))
    att = [E(nc.sbuf_tensor(f"a_att{i}{tag}", [64, SEQ], BF16)) for i in range(2)]
    negc = E(nc.sbuf_tensor(f"a_negc{tag}", [128, 16], F32))
    negcb = E(nc.sbuf_tensor(f"a_negcb{tag}", [128, 16], BF16))
    negcT = [E(nc.sbuf_tensor(f"a_negcT{i}{tag}", [16, 128], BF16)) for i in range(2)]
    scr65 = E(nc.sbuf_tensor(f"a_scr65{tag}", [65, 1024], F32))
    r65 = E(nc.sbuf_tensor(f"a_r65{tag}", [65, 1216], BF16))
    vs1 = E(nc.sbuf_tensor(f"a_vs1{tag}", [1, 130], F32))
    vscol = E(nc.sbuf_tensor(f"a_vscol{tag}", [64, 1], F32))
    s64 = E(nc.sbuf_tensor(f"a_s64{tag}", [64, 512], F32))
    corr2 = E(nc.sbuf_tensor(f"a_corr2{tag}", [64, 512], F32))
    arep = E(nc.sbuf_tensor(f"a_arep{tag}", [64, 512], F32))
    rin2 = [E(nc.sbuf_tensor(f"a_rin{i}{tag}", [128, D], F32)) for i in range(2)]
    pA = [E(nc.psum_tensor(f"a_pA{i}{tag}", [128, 512], F32)) for i in range(4)]
    ppv = E(nc.psum_tensor(f"a_ppv{tag}", [65, 512], F32))
    pqk = E(nc.psum_tensor(f"a_pqk{tag}", [64, 512], F32))

    attn_park = dscr["attn_park"]
    negc_d = dscr["negc_d"]
    vsum_d = dscr["vsum_d"]

    GAM, ALP, VSU, ONE = 0, 512, 1024, 1089

    pe_pos = {}

    def plan_pe():
        n = ctx.cnt["pe"]
        for hp in range(NCH):
            for tj in range(NT):
                n += 1; pe_pos[("vproj", hp, tj)] = n
            for sh in range(4):
                for tg in range(4):
                    n += 1; pe_pos[("qproj", hp, sh, tg)] = n
                for tg in range(4):
                    n += 1; pe_pos[("kproj", hp, sh, tg)] = n
                if sh % 2 == 0:
                    n += 1; pe_pos[("vsum", hp, sh)] = n
                for j in range(16):
                    n += 1; pe_pos[("sub", hp, sh, j)] = n
                n += 1; pe_pos[("negcT", hp, sh)] = n
                for qg in range(4):
                    for kc in range(4 * qg + 4):
                        n += 1; pe_pos[("S", hp, sh, qg, kc)] = n
                        n += 1; pe_pos[("PV", hp, sh, qg, kc)] = n
                    n += 1; pe_pos[("bcast_a", hp, sh, qg)] = n
                    n += 1; pe_pos[("bcast_g", hp, sh, qg)] = n
        for j in range(NT):
            n += 1; pe_pos[("proj", j)] = n
        return n

    plan_pe()
    last_kproj = pe_pos[("kproj", NCH - 1, 3, 3)]
    last_bcast = pe_pos[("bcast_g", NCH - 1, 3, 3)]

    gp_pos = {}

    def plan_gp():
        n = ctx.cnt["gp"]
        for hp in range(NCH):
            for sh in range(4):
                for qg in range(4):
                    for kc in range(4 * qg, 4 * qg + 4):
                        n += 1; gp_pos[(hp, sh, qg, kc)] = n
    plan_gp()

    def prev_shk(hp, sh, k=1):
        i = 4 * hp + sh - k
        return (i // 4, i % 4) if i >= 0 else None

    ve_m, se_m, dm_m = {}, {}, {}
    qo_prev = (ctx.cnt["qo0"], ctx.cnt["qo1"])
    qi0 = (ctx.cnt["qi0"], ctx.cnt["qi1"])
    qo0 = (ctx.cnt["qo0"], ctx.cnt["qo1"])
    qp0 = (ctx.cnt["qp0"], ctx.cnt["qp1"])
    qw_base = ctx.cnt["qw"]

    def qp_parks_before(hp, sh):
        return 2 * hp + sh // 2

    def f_vector(v):
        ctx.inc(v.memset(r65[:, ONE:ONE + 64], 1.0), "ve")
        ctx.wall(v, "ve")
        for hp in range(NCH):
            for tj in range(NT):
                ctx.w(v, "pe", pe_pos[("vproj", hp, tj)])
                ctx.inc(v.tensor_copy(v2[:, tj * 130: tj * 130 + 64],
                                      pA[2 + tj % 2][:, 0:64]), "ve")
                ctx.inc(v.tensor_copy(v2[:, tj * 130 + 65: tj * 130 + 129],
                                      pA[2 + tj % 2][:, 64:128]), "ve")
                ve_m[("v2", hp, tj)] = ctx.cnt["ve"]
            ctx.inc(v.memset(
                v2.rearrange("p (t c) -> p t c", c=130)[:, :, 64:65], 1.0), "ve")
            ctx.inc(v.memset(
                v2.rearrange("p (t c) -> p t c", c=130)[:, :, 129:130], 1.0), "ve")
            ve_m[("v2done", hp)] = ctx.cnt["ve"]
            for sh in range(4):
                su = sh % 2
                for tg in range(4):
                    ctx.w(v, "pe", pe_pos[("qproj", hp, sh, tg)])
                    ctx.inc(v.tensor_copy(q65[su][0:64, tg * 512:(tg + 1) * 512],
                                          pqk[:, :]), "ve")
                    ve_m[("qev", hp, sh, tg)] = ctx.cnt["ve"]
                for tg in range(4):
                    ctx.w(v, "pe", pe_pos[("kproj", hp, sh, tg)])
                    ctx.inc(v.tensor_copy(k65[su][0:64, tg * 512:(tg + 1) * 512],
                                          pqk[:, :]), "ve")
                    ve_m[("kev", hp, sh, tg)] = ctx.cnt["ve"]
                ctx.inc(v.memset(k65[su][64:65, :], 1.0), "ve")
                ve_m[("qk", hp, sh)] = ctx.cnt["ve"]
                if sh % 2 == 0:
                    ctx.w(v, "pe", pe_pos[("vsum", hp, sh)])
                    ctx.inc(v.tensor_copy(vs1[:, :], pA[2][0:1, 0:130]), "ve")
                    ve_m[("vs1", hp, sh)] = ctx.cnt["ve"]
                for j in range(16):
                    ctx.w(v, "pe", pe_pos[("sub", hp, sh, j)])
                    ctx.inc(v.reduce_max(negc[:, j:j + 1],
                                         pA[3 - j % 2][:, 0:8 * j + 1], axis=AX.X,
                                         negate=True), "ve")
                    ve_m[("sub", hp, sh, j)] = ctx.cnt["ve"]
                ctx.wall(v, "ve")
                p = prev_shk(hp, sh)
                if p is not None:
                    ctx.w(v, "pe", pe_pos[("negcT", p[0], p[1])])
                ctx.inc(v.tensor_copy(negcb[:, :], negc[:, :]), "ve")
                ve_m[("negcb", hp, sh)] = ctx.cnt["ve"]
                ctx.w(v, "pe", pe_pos[("negcT", hp, sh)])
                ctx.inc(v.tensor_copy(negcT[su][:, :], pT[0][0:16, 0:128]), "ve")
                ve_m[("negcT", hp, sh)] = ctx.cnt["ve"]
                for qg in range(4):
                    nk = 4 * qg + 4
                    ctx.w(v, "pe", pe_pos[("PV", hp, sh, qg, nk - 1)])
                    ctx.inc(v.tensor_scalar(scr65[64:65, 0:512], ppv[64:65, :],
                                            1.0 + 2048e-8, 2.048e-11,
                                            ALU.mult, ALU.add), "ve")
                    ctx.wall(v, "ve")
                    ctx.inc(v.reciprocal(scr65[64:65, 0:512],
                                         scr65[64:65, 0:512]), "ve")
                    ctx.wall(v, "ve")
                    ctx.inc(v.tensor_copy(r65[64:65, ALP:ALP + 512],
                                          scr65[64:65, 0:512]), "ve")
                    ctx.inc(v.tensor_scalar(scr65[64:65, 512:1024], ppv[64:65, :],
                                            1e-8, 1e-14, ALU.mult, ALU.add), "ve")
                    ctx.wall(v, "ve")
                    ctx.inc(v.tensor_tensor(r65[64:65, GAM:GAM + 512],
                                            scr65[64:65, 512:1024],
                                            scr65[64:65, 0:512], ALU.mult), "ve")
                    ve_m[("ag", hp, sh, qg)] = ctx.cnt["ve"]
                    ctx.w(v, "pe", pe_pos[("bcast_g", hp, sh, qg)])
                    ctx.inc(v.tensor_copy(arep[:, :], pA[2][0:64, :]), "ve")
                    ctx.wall(v, "ve")
                    ctx.wall(v, "ve")
                    ctx.inc(v.tensor_scalar(corr2[:, :], pA[3][0:64, 0:512],
                                            vscol[:, :], None, ALU.mult), "ve")
                    ctx.wall(v, "ve")
                    ctx.inc(v.tensor_tensor(s64[:, :], ppv[0:64, :],
                                            arep[:, :], ALU.mult), "ve")
                    ctx.wall(v, "ve")
                    if qg == 0:
                        ctx.slot_w(v, "qp", su,
                                   qp0[su] // 16 + qp_parks_before(hp, sh))
                    ctx.inc(v.tensor_tensor(att[su][:, qg * 512:(qg + 1) * 512],
                                            s64[:, :], corr2[:, :], ALU.add),
                            "ve")
                    ve_m[("att", hp, sh, qg)] = ctx.cnt["ve"]
        for j in range(NT):
            jm = j % 2
            ctx.w(v, "pe", pe_pos[("proj", j)])
            ctx.slot_w(v, "qi", jm, qi0[jm] // 16 + j // 2 + 1)
            for half in range(2):
                ctx.inc(v.scalar_tensor_tensor(
                    rin2[jm][:, half * 512:half * 512 + 512],
                    pA[2 * jm + half][:, :], 1.0,
                    rin2[jm][:, half * 512:half * 512 + 512], ALU.mult, ALU.add),
                    "ve")
            ctx.wall(v, "ve")
            ve_m[("comb", j)] = ctx.cnt["ve"]
    blk.vector(f_vector)

    def f_scalar(sc):
        for hp in range(NCH):
            for sh in range(4):
                for qg in range(4):
                    for kc in range(4 * qg + 4):
                        ctx.w(sc, "pe", pe_pos[("S", hp, sh, qg, kc)])
                        if kc >= 3:
                            k3 = kc - 3
                            if k3 >= 4 * qg:
                                ctx.w(sc, "gp", gp_pos[(hp, sh, qg, k3)])
                            else:
                                ctx.w(sc, "pe", pe_pos[("PV", hp, sh, qg, k3)])
                        ctx.inc(sc.activation(pbuf[kc % 3][:, :],
                                              pA[kc % 2][:, :],
                                              AF.Exp, scale=0.125), "se")
                        se_m[("exp", hp, sh, qg, kc)] = ctx.cnt["se"]
    blk.scalar(f_scalar)

    def f_gp(g):
        ctx.w(g, "qw", qw_base)
        for c in range(NCH):
            ctx.inc(g.dma_start(out=wpj[:, c * D:(c + 1) * D],
                                in_=wdr["wproj_t"][c * 128:(c + 1) * 128, :]),
                    "qw", 16)

        def load_hp(hp):
            for c in range(NCH):
                for (slot, off) in ((0, 0), (1, D), (2, 2 * D)):
                    ctx.inc(g.dma_start(
                        out=wqkv[hp % 2][:, c * 384 + slot * 128:
                                         c * 384 + slot * 128 + 128],
                        in_=wdr["wqkv_t"][c * 128:(c + 1) * 128,
                                          off + hp * 128: off + hp * 128 + 128]),
                        "qw", 16)
            dm_m[("w", hp)] = ctx.cnt["qw"]

        load_hp(0)
        load_hp(1)
        last_diag = None
        for hp in range(NCH):
            for sh in range(4):
                su = sh % 2
                for qg in range(4):
                    for kc in range(4 * qg, 4 * qg + 4):
                        ctx.w(g, "se", se_m[("exp", hp, sh, qg, kc)])
                        if last_diag is not None:
                            ctx.w(g, "pe", pe_pos[("PV", *last_diag)])
                        ctx.inc(g.affine_select(
                            pmask[:, :], pbuf[kc % 3][:, :], pattern=[[1, 512]],
                            compare_op=ALU.is_ge, fill=0.0,
                            base=qg * 512 - kc * 128,
                            channel_multiplier=-1), "gp")
                        last_diag = (hp, sh, qg, kc)
                ctx.w(g, "ve", ve_m[("att", hp, sh, 3)])
                ctx.slot_inc(g.dma_start(
                    out=attn_park[hp, (sh % 2) * 64:(sh % 2) * 64 + 64,
                                  (sh // 2) * SEQ:(sh // 2 + 1) * SEQ],
                    in_=att[su][:, :]), "qp", su)
            if hp + 2 < NCH:
                ctx.w(g, "pe", pe_pos[("kproj", hp, 3, 3)])
                ctx.w(g, "qw", dm_m[("w", hp + 1)])
                load_hp(hp + 2)
        ctx.w(g, "pe", last_kproj)
        ctx.slot_w(g, "qp", 0, ctx.cnt["qp0"] // 16)
        ctx.slot_w(g, "qp", 1, ctx.cnt["qp1"] // 16)
        for c in range(NCH):
            ctx.inc(g.dma_start(out=xnT[:, c * T:(c + 1) * T],
                                in_=attn_park[c, :, :]), "qA", 16)
        dm_m[("qA",)] = ctx.cnt["qA"]
        for j in range(NT):
            ctx.w(g, "ve", ve_m[("comb", j)])
            ctx.slot_inc(g.dma_start(out=R_out[j * 128:(j + 1) * 128, :],
                                     in_=rin2[j % 2][:, :]), "qo", j % 2)
    blk.gpsimd(f_gp)

    def f_sync(sync):
        for hp in range(NCH):
            for sh in range(4):
                su = sh % 2
                s_ = sh // 2
                h01 = sh % 2
                if h01 == 0:
                    ctx.w(sync, "ve", ve_m[("vs1", hp, sh)])
                    ctx.slot_inc(sync.dma_start(out=vsum_d[s_], in_=vs1[:, :]),
                                 "qv", s_)
                ctx.slot_w(sync, "qv", s_, ctx.cnt[f"qv{s_}"] // 16)
                p = prev_shk(hp, sh)
                if p is not None:
                    ctx.w(sync, "ve", ve_m[("att", p[0], p[1], 3)])
                ctx.slot_inc(sync.dma_start(
                    out=vscol[:, :],
                    in_=vsum_d[s_][:, h01 * 65: h01 * 65 + 64].rearrange(
                        "o n -> (o n)")[:, None]), "qv", s_)
                dm_m[("vsrow", hp, sh)] = ctx.cnt[f"qv{s_}"]
                ctx.w(sync, "ve", ve_m[("negcT", hp, sh)])
                ctx.slot_inc(sync.dma_start(out=negc_d[su], in_=negcT[su][:, :]),
                             "qn", su)
                ctx.slot_w(sync, "qn", su, ctx.cnt[f"qn{su}"] // 16)
                p2 = prev_shk(hp, sh, 2)
                if p2 is not None:
                    ctx.w(sync, "pe", pe_pos[("S", p2[0], p2[1], 3, 15)])
                ctx.slot_inc(sync.dma_start(
                    out=q65[su][64:65, :],
                    in_=negc_d[su].rearrange("a b -> (a b)")[None, :]), "qn", su)
                dm_m[("negcrow", hp, sh)] = ctx.cnt[f"qn{su}"]
        for j in range(NT):
            jm = j % 2
            if j < 2:
                ctx.slot_w(sync, "qo", jm, qo_prev[jm] // 16)
            else:
                ctx.slot_w(sync, "qo", jm, qo0[jm] // 16 + (j - 2) // 2 + 1)
            ctx.slot_inc(sync.dma_start(out=rin2[jm][:, :],
                                        in_=R_in[j * 128:(j + 1) * 128, :]),
                         "qi", jm)
    blk.sync(f_sync)

    def f_pe(t):
        for hp in range(NCH):
            ctx.w(t, "qw", dm_m[("w", max(hp, 1))])
            if hp > 0:
                ctx.w(t, "ve", ve_m[("att", hp - 1, 3, 3)])
            for tj in range(NT):
                if tj >= 2:
                    ctx.w(t, "ve", ve_m[("v2", hp, tj - 2)])
                for c in range(NCH):
                    i = t.matmul(pA[2 + tj % 2][:, 0:128],
                                 xnT[:, c * T + tj * 128: c * T + tj * 128 + 128],
                                 wqkv[hp % 2][:, c * 384 + 256: c * 384 + 384],
                                 start=(c == 0), stop=(c == NCH - 1))
                ctx.inc(i, "pe")
            for sh in range(4):
                su = sh % 2
                s_ = sh // 2
                h01 = sh % 2
                scol = s_ * SEQ
                p2 = prev_shk(hp, sh, 2)
                if p2 is not None:
                    ctx.w(t, "pe", pe_pos[("S", p2[0], p2[1], 3, 15)])
                for tg in range(4):
                    if tg > 0:
                        ctx.w(t, "ve", ve_m[("qev", hp, sh, tg - 1)])
                    elif (hp, sh) != (0, 0):
                        p = prev_shk(hp, sh)
                        ctx.w(t, "ve", ve_m[("kev", p[0], p[1], 3)])
                    for c in range(NCH):
                        i = t.matmul(pqk[:, :],
                                     wqkv[hp % 2][:, c * 384 + h01 * 64:
                                                  c * 384 + h01 * 64 + 64],
                                     xnT[:, c * T + scol + tg * 512:
                                         c * T + scol + tg * 512 + 512],
                                     start=(c == 0), stop=(c == NCH - 1))
                    ctx.inc(i, "pe")
                for tg in range(4):
                    ctx.w(t, "ve", ve_m[("qev", hp, sh, 3)] if tg == 0 else
                          ve_m[("kev", hp, sh, tg - 1)])
                    for c in range(NCH):
                        i = t.matmul(pqk[:, :],
                                     wqkv[hp % 2][:, c * 384 + 128 + h01 * 64:
                                                  c * 384 + 128 + h01 * 64 + 64],
                                     xnT[:, c * T + scol + tg * 512:
                                         c * T + scol + tg * 512 + 512],
                                     start=(c == 0), stop=(c == NCH - 1))
                    ctx.inc(i, "pe")
                if h01 == 0:
                    ctx.w(t, "ve", ve_m[("v2done", hp)])
                    if (hp, sh) != (0, 0):
                        p = prev_shk(hp, sh)
                        ctx.w(t, "ve", ve_m[("att", p[0], p[1], 3)])
                    for kt in range(16):
                        i = t.matmul(pA[2][0:1, 0:130], onesb[:, :],
                                     v2[:, (s_ * 16 + kt) * 130:
                                        (s_ * 16 + kt) * 130 + 130],
                                     start=(kt == 0), stop=(kt == 15))
                    ctx.inc(i, "pe")
                ctx.w(t, "ve", ve_m[("qk", hp, sh)])
                if h01 == 1 and (hp, sh) != (0, 1):
                    p = prev_shk(hp, sh)
                    ctx.w(t, "ve", ve_m[("att", p[0], p[1], 3)])
                for j in range(16):
                    if j == 1:
                        ctx.w(t, "ve", ve_m.get(("vs1", hp, sh - h01), 0))
                    if j >= 2:
                        ctx.w(t, "ve", ve_m[("sub", hp, sh, j - 2)])
                    rhs = k65[su][0:64, :].rearrange(
                        "p (a b) -> p a b", b=16)[:, 0:8 * j + 1, 0:1]
                    i = t.matmul(pA[3 - j % 2][:, 0:8 * j + 1],
                                 q65[su][0:64, j * 128:(j + 1) * 128],
                                 rhs, start=True, stop=True)
                    ctx.inc(i, "pe")
                ctx.w(t, "ve", ve_m[("negcb", hp, sh)])
                ctx.inc(t.transpose(pT[0][0:16, 0:128], negcb[:, :],
                                    ident_bf[:, :]), "pe")
                ctx.slot_w(t, "qn", su, dm_m[("negcrow", hp, sh)] // 16)
                ctx.slot_w(t, "qv", s_, dm_m[("vsrow", hp, sh)] // 16)
                for qg in range(4):
                    nk = 4 * qg + 4
                    for kc in range(nk):
                        if kc >= 2:
                            ctx.w(t, "se", se_m[("exp", hp, sh, qg, kc - 2)])
                        elif qg == 0 and kc == 0:
                            pass
                        i = t.matmul(pA[kc % 2][:, :],
                                     k65[su][:, kc * 128:(kc + 1) * 128],
                                     q65[su][:, qg * 512:(qg + 1) * 512],
                                     start=True, stop=True)
                        ctx.inc(i, "pe")
                        if kc == 0 and qg > 0:
                            ctx.w(t, "ve", ve_m[("att", hp, sh, qg - 1)])
                        if kc >= 4 * qg:
                            ctx.w(t, "gp", gp_pos[(hp, sh, qg, kc)])
                            pv_rhs = pmask[:, :]
                        else:
                            ctx.w(t, "se", se_m[("exp", hp, sh, qg, kc)])
                            pv_rhs = pbuf[kc % 3][:, :]
                        i = t.matmul(ppv[:, :],
                                     v2[:, (s_ * 16 + kc) * 130 + h01 * 65:
                                        (s_ * 16 + kc) * 130 + h01 * 65 + 65],
                                     pv_rhs, start=(kc == 0), stop=(kc == nk - 1))
                        ctx.inc(i, "pe")
                    ctx.w(t, "ve", ve_m[("ag", hp, sh, qg)])
                    ctx.inc(t.matmul(pA[2][0:64, :], r65[64:65, ONE:ONE + 64],
                                     r65[64:65, ALP:ALP + 512],
                                     start=True, stop=True), "pe")
                    ctx.inc(t.matmul(pA[3][0:64, 0:512],
                                     r65[64:65, ONE:ONE + 64],
                                     r65[64:65, GAM:GAM + 512],
                                     start=True, stop=True), "pe")
        ctx.w(t, "qA", dm_m[("qA",)])
        ctx.w(t, "ve", ve_m[("att", NCH - 1, 3, 3)])
        for j in range(NT):
            jm = j % 2
            if j >= 2:
                ctx.w(t, "ve", ve_m[("comb", j - 2)])
            for half in range(2):
                for c in range(NCH):
                    i = t.matmul(pA[2 * jm + half][:, :],
                                 xnT[:, c * T + j * 128: c * T + j * 128 + 128],
                                 wpj[:, c * D + half * 512: c * D + half * 512 + 512],
                                 start=(c == 0), stop=(c == NCH - 1))
            ctx.inc(i, "pe")
    blk.tensor(f_pe)


def emit_l1ffn(nc, ctx, blk, es, R_in, R_out, ident_bf, ln, pT, wdr, dscr, tag,
               pqo="qo"):
    phv = phase_fence(ctx)
    from contextlib import ExitStack
    E = es.enter_context
    HT = T // 2
    NST = HT // 512
    gallT = E(nc.sbuf_tensor(f"m_gallT{tag}", [8, T], BF16))
    gb = E(nc.sbuf_tensor(f"m_gb{tag}", [128, 8], BF16))
    wrsb = E(nc.sbuf_tensor(f"m_wr{tag}", [128, NCH * 8], BF16))
    brrep = E(nc.sbuf_tensor(f"m_brrep{tag}", [128, 8], F32))
    brsb = E(nc.sbuf_tensor(f"m_brs{tag}", [1, 8], BF16))
    onesr = E(nc.sbuf_tensor(f"m_onesr{tag}", [1, 128], BF16))
    b1sb = E(nc.sbuf_tensor(f"m_b1{tag}", [128, 8 * 16], F32))
    b2sb = E(nc.sbuf_tensor(f"m_b2{tag}", [8, D], BF16))
    yaccT = E(nc.sbuf_tensor(f"m_yacc{tag}", [128, NCH * HT], BF16))
    lg = E(nc.sbuf_tensor(f"m_lg{tag}", [128, 8], F32))
    gs1 = E(nc.sbuf_tensor(f"m_gs1{tag}", [128, 8], F32))
    gs2 = E(nc.sbuf_tensor(f"m_gs2{tag}", [128, 8], F32))
    gm = E(nc.sbuf_tensor(f"m_gm{tag}", [128, 1], F32))
    gm2 = E(nc.sbuf_tensor(f"m_gm2{tag}", [128, 1], F32))
    gnm = E(nc.sbuf_tensor(f"m_gnm{tag}", [128, 1], F32))
    ged = E(nc.sbuf_tensor(f"m_ged{tag}", [128, 1], F32))
    grd = E(nc.sbuf_tensor(f"m_grd{tag}", [128, 1], F32))
    w2b = E(nc.sbuf_tensor(f"m_w2{tag}", [128, 16 * D], BF16))
    hT = E(nc.sbuf_tensor(f"m_hT{tag}", [128, 16 * 512], BF16))
    xst = [E(nc.sbuf_tensor(f"m_xst{i}{tag}", [128, NCH * 512], BF16))
           for i in range(2)]
    gerow = E(nc.sbuf_tensor(f"m_gerow{tag}", [1, HT], BF16))
    grep = E(nc.sbuf_tensor(f"m_grep{tag}", [128, 512], BF16))
    ytmp = E(nc.sbuf_tensor(f"m_ytmp{tag}", [128, 512], F32))
    ytile = E(nc.sbuf_tensor(f"m_ytile{tag}", [128, D], BF16))
    rin3 = [ln["rt0"], ln["rt1"]]
    ph = [E(nc.psum_tensor(f"m_ph{i}{tag}", [128, 512], F32)) for i in range(2)]
    py = [E(nc.psum_tensor(f"m_py{i}{tag}", [128, 512], F32)) for i in range(4)]

    xnpark = dscr["xnpark"]
    g_d = dscr["g_d"]

    xtile = [E(nc.sbuf_tensor(f"m_xt{i}{tag}", [128, NCH * 128], BF16))
             for i in range(2)]
    xg = [E(nc.sbuf_tensor(f"m_xg{i}{tag}", [128, NCH * 128], BF16))
          for i in range(2)]
    w1b = [E(nc.sbuf_tensor(f"m_w1{i}{tag}", [128, NCH * 2048], BF16))
           for i in range(2)]

    def eidx(hf, e):
        return hf * 8 + e

    def sidx(hf, e, st):
        return (hf * 8 + e) * NST + st

    lnve0 = ctx.cnt["ve"]
    qz_base = (ctx.cnt["qz0"], ctx.cnt["qz1"])

    def ln_dest(j):
        return xtile[j % 2].rearrange("p (c t) -> p c t", c=NCH)[:, :, :]

    def ln_war(v, j):
        if j >= 2:
            ctx.slot_w(v, "qz", j % 2, qz_base[j % 2] // 16 + j // 2)

    emit_ln_transpose(nc, ctx, blk, R_in, None, ln, ident_bf, pT, pqo=pqo,
                      dest_of_tile=ln_dest, dest_war=ln_war)

    def ln_evac(j):
        return lnve0 + 5 * j + 5

    pe_pos = {}

    def plan_pe():
        n = ctx.cnt["pe"]
        n += 1; pe_pos[("brbc",)] = n
        for tj in range(NT):
            n += 1; pe_pos[("lg", tj)] = n
            n += 1; pe_pos[("gT", tj)] = n
        for hf in range(2):
            for dn in range(NCH):
                for tg in range(NST):
                    n += 1; pe_pos[("b2g", hf, dn, tg)] = n
            for e in range(8):
                for st in range(NST):
                    n += 1; pe_pos[("grep", hf, e, st)] = n
                    for fn in range(16):
                        n += 1; pe_pos[("h", hf, e, st, fn)] = n
                        n += 1; pe_pos[("yA", hf, e, st, fn)] = n
                    for fn in range(16):
                        n += 1; pe_pos[("yB", hf, e, st, fn)] = n
            for j in range(16):
                n += 1; pe_pos[("yt", hf, j)] = n
        return n
    plan_pe()

    ve_m, se_m, dm_m = {}, {}, {}
    qi0 = (ctx.cnt["qi0"], ctx.cnt["qi1"])
    qo0 = (ctx.cnt["qu0"], ctx.cnt["qu1"])
    qo_prev = (ctx.cnt[pqo + "0"], ctx.cnt[pqo + "1"])
    qw_base = ctx.cnt["qw"]
    se_gate0 = ctx.cnt["se"]
    assert se_gate0 >= 2 * NT

    def f_vector(v):
        ctx.inc(v.memset(onesr[:, :], 1.0), "ve")
        ve_m[("onesr",)] = ctx.cnt["ve"]
        ctx.wall(v, "ve")
        ctx.w(v, "pe", pe_pos[("brbc",)])
        ctx.inc(v.tensor_copy(brrep[:, :], ph[1][:, 0:8]), "ve")
        ve_m[("brrep",)] = ctx.cnt["ve"]
        ctx.wall(v, "ve")
        for tj in range(NT):
            ctx.w(v, "pe", pe_pos[("lg", tj)])
            ctx.inc(v.tensor_tensor(lg[:, :], ph[0][:, 0:8], brrep[:, :],
                                    ALU.add), "ve")
            ve_m[("glg", tj)] = ctx.cnt["ve"]
            ctx.wall(v, "ve")
            ctx.inc(v.reduce_max(gm[:, :], lg[:, :], axis=AX.X), "ve")
            ctx.wall(v, "ve")
            ctx.inc(v.tensor_scalar(gs1[:, :], lg[:, :], gm[:, :], None,
                                    ALU.is_lt), "ve")
            ctx.inc(v.tensor_scalar(gnm[:, :], gm[:, :], -1.0, None,
                                    ALU.mult), "ve")
            ctx.wall(v, "ve")
            ctx.inc(v.tensor_tensor(gs2[:, :], lg[:, :], gs1[:, :],
                                    ALU.mult), "ve")
            ctx.wall(v, "ve")
            ctx.inc(v.tensor_scalar(gs1[:, :], gs1[:, :], 1e30, 1e30,
                                    ALU.mult, ALU.subtract), "ve")
            ctx.wall(v, "ve")
            ctx.inc(v.tensor_tensor(gs2[:, :], gs2[:, :], gs1[:, :],
                                    ALU.add), "ve")
            ctx.wall(v, "ve")
            ctx.inc(v.reduce_max(gm2[:, :], gs2[:, :], axis=AX.X), "ve")
            ve_m[("gm2", tj)] = ctx.cnt["ve"]
            ctx.w(v, "se", 2 * (tj + 1) + se_gate0)
            ctx.inc(v.tensor_scalar(grd[:, :], ged[:, :], 1.0, None,
                                    ALU.add), "ve")
            ctx.wall(v, "ve")
            ctx.inc(v.reciprocal(grd[:, :], grd[:, :]), "ve")
            ctx.wall(v, "ve")
            ctx.inc(v.tensor_scalar(gs2[:, :], lg[:, :], gm2[:, :], None,
                                    ALU.is_ge), "ve")
            ctx.wall(v, "ve")
            ctx.inc(v.tensor_tensor(gs2[:, :], gs2[:, :], gs1[:, :],
                                    ALU.mult), "ve")
            ctx.wall(v, "ve")
            ctx.inc(v.tensor_scalar(gb[:, :], gs2[:, :],
                                    grd[:, :], 0.5, ALU.mult, ALU.mult), "ve")
            ve_m[("gate", tj)] = ctx.cnt["ve"]
            ctx.w(v, "pe", pe_pos[("gT", tj)])
            ctx.inc(v.tensor_copy(gallT[:, tj * 128:(tj + 1) * 128],
                                  pT[0][0:8, 0:128]), "ve")
            ve_m[("gT", tj)] = ctx.cnt["ve"]
        for hf in range(2):
            for dn in range(NCH):
                for tg in range(NST):
                    ctx.w(v, "pe", pe_pos[("b2g", hf, dn, tg)])
                    ctx.inc(v.tensor_copy(
                        yaccT[:, dn * HT + tg * 512: dn * HT + tg * 512 + 512],
                        py[tg % 4][:, :]), "ve")
                    ve_m[("b2g", hf, dn, tg)] = ctx.cnt["ve"]
            for e in range(8):
                for st in range(NST):
                    ctx.w(v, "pe", pe_pos[("grep", hf, e, st)])
                    ctx.inc(v.tensor_copy(grep[:, :], ph[0][:, :]), "ve")
                    ve_m[("grep", hf, e, st)] = ctx.cnt["ve"]
                    for pss, key in ((0, "yA"), (1, "yB")):
                        ctx.w(v, "pe", pe_pos[(key, hf, e, st, 15)])
                        for dn4 in range(4):
                            dn = pss * 4 + dn4
                            ctx.inc(v.tensor_tensor(ytmp[:, :], py[dn4][:, :],
                                                    grep[:, :], ALU.mult), "ve")
                            ctx.wall(v, "ve")
                            sl = yaccT[:, dn * HT + st * 512:
                                       dn * HT + st * 512 + 512]
                            ctx.inc(v.tensor_tensor(sl, sl, ytmp[:, :],
                                                    ALU.add), "ve")
                            ve_m[("yev", hf, e, st, dn)] = ctx.cnt["ve"]
                            ctx.wall(v, "ve")
            for j in range(16):
                jg = hf * 16 + j
                jm = jg % 2
                ctx.w(v, "pe", pe_pos[("yt", hf, j)])
                ctx.inc(v.tensor_copy(
                    ytile.rearrange("p (c n) -> p c n", c=NCH)[:, :, :],
                    pT[j % 2].rearrange("p (c n) -> p c n", c=NCH)[:, :, :]),
                    "ve")
                ctx.wall(v, "ve")
                ctx.slot_w(v, "qi", jm, qi0[jm] // 16 + jg // 2 + 1)
                for half in range(2):
                    ctx.inc(v.tensor_tensor(
                        rin3[jm][:, half * 512:half * 512 + 512],
                        rin3[jm][:, half * 512:half * 512 + 512],
                        ytile[:, half * 512:half * 512 + 512], ALU.add), "ve")
                ctx.wall(v, "ve")
                ve_m[("comb", jg)] = ctx.cnt["ve"]

    def f_scalar(sc):
        for tj in range(NT):
            ctx.w(sc, "ve", ve_m[("gm2", tj)])
            ctx.inc(sc.activation(gs1[:, :], lg[:, :], AF.Exp,
                                  bias=gnm[:, :]), "se")
            ctx.wall(sc, "se")
            ctx.inc(sc.activation(ged[:, :], gm2[:, :], AF.Exp,
                                  bias=gnm[:, :]), "se")
            se_m[("ged", tj)] = ctx.cnt["se"]
        for hf in range(2):
            for e in range(8):
                for st in range(NST):
                    for fn in range(16):
                        ctx.w(sc, "pe", pe_pos[("h", hf, e, st, fn)])
                        if fn == 0 and sidx(hf, e, st) > 0:
                            i2 = sidx(hf, e, st) - 1
                            h2, r2 = divmod(i2, 8 * NST)
                            e2, s2 = divmod(r2, NST)
                            ctx.w(sc, "pe", pe_pos[("yB", h2, e2, s2, 15)])
                        ctx.inc(sc.activation(
                            hT[:, fn * 512:(fn + 1) * 512],
                            ph[fn % 2][:, :], AF.Relu,
                            bias=b1sb[:, e * 16 + fn: e * 16 + fn + 1]), "se")
                        se_m[("relu", hf, e, st, fn)] = ctx.cnt["se"]

    def f_gp(g):
        fence_wait(ctx, g, phv)
        ctx.w(g, "qw", qw_base)
        for c in range(NCH):
            ctx.inc(g.dma_start(out=wrsb[:, c * 8:(c + 1) * 8],
                                in_=wdr["m_wr"][c * 128:(c + 1) * 128, :]),
                    "qw", 16)
        ctx.inc(g.dma_start(out=brsb[:, :], in_=wdr["m_br"][None, :]), "qw", 16)
        ctx.inc(g.dma_start(out=b1sb[:, :], in_=wdr["m_b1"]), "qw", 16)
        ctx.inc(g.dma_start(out=b2sb[:, :], in_=wdr["m_b2"]), "qw", 16)
        dm_m[("wconst",)] = ctx.cnt["qw"]

        def load_w1(hf, e, slot):
            ctx.slot_w(g, "qf", slot, ctx.cnt[f"qf{slot}"] // 16)
            for c in range(NCH):
                ctx.slot_inc(g.dma_start(
                    out=w1b[slot][:, c * 2048:(c + 1) * 2048],
                    in_=wdr["m_w1"][e, c * 128:(c + 1) * 128, :]), "qf", slot)
            dm_m[("w1", hf, e)] = ctx.cnt[f"qf{slot}"]

        def load_w2(hf, e):
            ctx.slot_w(g, "qm", 0, ctx.cnt["qm0"] // 16)
            for fn in range(16):
                ctx.slot_inc(g.dma_start(
                    out=w2b[:, fn * D:(fn + 1) * D],
                    in_=wdr["m_w2"][e, fn * 128:(fn + 1) * 128, :]), "qm", 0)
            dm_m[("w2", hf, e)] = ctx.cnt["qm0"]

        load_w1(0, 0, 0)
        load_w2(0, 0)
        for hf in range(2):
            for e in range(8):
                i = eidx(hf, e)
                if i + 1 < 16:
                    nhf, ne = divmod(i + 1, 8)
                    if i >= 1:
                        p = i - 1
                        ctx.w(g, "pe", pe_pos[("yA", p // 8, p % 8,
                                               NST - 1, 15)])
                    load_w1(nhf, ne, (i + 1) % 2)
                    ctx.w(g, "pe", pe_pos[("yB", hf, e, NST - 1, 15)])
                    load_w2(nhf, ne)
    xpark_cnt = [0]

    def f_park(g):
        for j in range(NT):
            ctx.w(g, "ve", ln_evac(j))
            ctx.slot_inc(g.dma_start(
                out=xnpark[:, :, j * 128:(j + 1) * 128].rearrange(
                    "c p t -> p c t"),
                in_=xtile[j % 2].rearrange("p (c t) -> p c t", c=NCH)),
                "qz", j % 2)
        xpark_cnt[0] = (ctx.cnt["qz0"], ctx.cnt["qz1"])

    def f_sync(sync):
        for j in range(NT):
            jm = j % 2
            if j >= 2:
                ctx.w(sync, "pe", pe_pos[("lg", j - 2)])
            ctx.slot_w(sync, "qz", jm, qz_base[jm] // 16 + j // 2 + 1)
            ctx.slot_inc(sync.dma_start(
                out=xg[jm].rearrange("p (c t) -> p c t", c=NCH),
                in_=xnpark[:, :, j * 128:(j + 1) * 128].rearrange(
                    "c p t -> p c t")), "qy", jm)
            dm_m[("xg", j)] = ctx.cnt[f"qy{jm}"]
        ctx.w(sync, "ve", ve_m[("gT", NT - 1)])
        ctx.inc(sync.dma_start(out=g_d[:, :], in_=gallT[:, :]), "qB", 16)
        dm_m[("gd",)] = ctx.cnt["qB"]
        for hf in range(2):
            for e in range(8):
                i = eidx(hf, e)
                if i >= 1:
                    p = i - 1
                    ctx.w(sync, "pe", pe_pos[("grep", p // 8, p % 8, NST - 1)])
                ctx.slot_w(sync, "qm", 1, ctx.cnt["qm1"] // 16)
                ctx.w(sync, "qB", dm_m[("gd",)])
                ctx.slot_inc(sync.dma_start(
                    out=gerow[:, :],
                    in_=g_d[e:e + 1, hf * HT:(hf + 1) * HT]), "qm", 1)
                dm_m[("ge", hf, e)] = ctx.cnt["qm1"]
                for st in range(NST):
                    sl = st % 2
                    if sidx(hf, e, st) >= 2:
                        i2 = sidx(hf, e, st) - 2
                        h2, r2 = divmod(i2, 8 * NST)
                        e2, s2 = divmod(r2, NST)
                        ctx.w(sync, "pe", pe_pos[("h", h2, e2, s2, 15)])
                    else:
                        ctx.slot_w(sync, "qz", 0, xpark_cnt[0][0] // 16)
                        ctx.slot_w(sync, "qz", 1, xpark_cnt[0][1] // 16)
                    for c in range(NCH):
                        ctx.slot_inc(sync.dma_start(
                            out=xst[sl][:, c * 512:(c + 1) * 512],
                            in_=xnpark[c, :, hf * HT + st * 512:
                                       hf * HT + st * 512 + 512]), "qx", sl)
                    dm_m[("xst", hf, e, st)] = ctx.cnt[f"qx{sl}"]
            for j in range(16):
                jg = hf * 16 + j
                jm = jg % 2
                if jg >= 2:
                    ctx.w(sync, "ve", ve_m[("comb", jg - 2)])
                    ctx.slot_inc(sync.dma_start(
                        out=R_out[(jg - 2) * 128:(jg - 1) * 128, :],
                        in_=rin3[jm][:, :]), "qu", jm)
                    ctx.slot_w(sync, "qu", jm,
                               qo0[jm] // 16 + (jg - 2) // 2 + 1)
                else:
                    ctx.slot_w(sync, pqo, jm, qo_prev[jm] // 16)
                ctx.slot_inc(sync.dma_start(
                    out=rin3[jm][:, :],
                    in_=R_in[jg * 128:(jg + 1) * 128, :]), "qi", jm)
        for jg in (NT - 2, NT - 1):
            jm = jg % 2
            ctx.w(sync, "ve", ve_m[("comb", jg)])
            ctx.slot_inc(sync.dma_start(
                out=R_out[jg * 128:(jg + 1) * 128, :],
                in_=rin3[jm][:, :]), "qu", jm)

    def f_pe(t):
        ctx.w(t, "qw", dm_m[("wconst",)])
        ctx.w(t, "ve", ve_m[("onesr",)])
        ctx.inc(t.matmul(ph[1][:, 0:8], onesr[:, :], brsb[:, :],
                         start=True, stop=True), "pe")
        for tj in range(NT):
            if tj > 0:
                ctx.w(t, "ve", ve_m[("glg", tj - 1)])
            ctx.slot_w(t, "qy", tj % 2, dm_m[("xg", tj)] // 16)
            for c in range(NCH):
                i = t.matmul(ph[0][:, 0:8],
                             xg[tj % 2][:, c * 128:(c + 1) * 128],
                             wrsb[:, c * 8:(c + 1) * 8],
                             start=(c == 0), stop=(c == NCH - 1))
            ctx.inc(i, "pe")
            ctx.w(t, "ve", ve_m[("gate", tj)])
            if tj >= 1:
                ctx.w(t, "ve", ve_m[("gT", tj - 1)])
            ctx.inc(t.transpose(pT[0][0:8, 0:128], gb[:, :], ident_bf[:, :]),
                    "pe")
        for hf in range(2):
            ctx.w(t, "ve", ve_m[("gT", NT - 1)])
            for dn in range(NCH):
                for tg in range(NST):
                    g4 = dn * NST + tg - 4
                    if g4 >= 0:
                        ctx.w(t, "ve", ve_m[("b2g", hf, g4 // NST, g4 % NST)])
                    elif hf == 1:
                        ctx.w(t, "ve", ve_m[("yev", 0, 7, NST - 1,
                                             4 + (dn * NST + tg) % 4)])
                    i = t.matmul(py[(dn * NST + tg) % 4][:, :],
                                 b2sb[:, dn * 128:(dn + 1) * 128],
                                 gallT[:, hf * HT + tg * 512:
                                       hf * HT + tg * 512 + 512],
                                 start=True, stop=True)
                    ctx.inc(i, "pe")
            for e in range(8):
                i_e = eidx(hf, e)
                ctx.slot_w(t, "qf", i_e % 2, dm_m[("w1", hf, e)] // 16)
                ctx.slot_w(t, "qm", 0, dm_m[("w2", hf, e)] // 16)
                for st in range(NST):
                    sl = st % 2
                    ctx.slot_w(t, "qx", sl, dm_m[("xst", hf, e, st)] // 16)
                    ctx.slot_w(t, "qm", 1, dm_m[("ge", hf, e)] // 16)
                    if sidx(hf, e, st) == 0:
                        ctx.w(t, "ve", ve_m[("glg", NT - 1)])
                        ctx.w(t, "ve", ve_m[("brrep",)])
                    else:
                        i2 = sidx(hf, e, st) - 1
                        h2, r2 = divmod(i2, 8 * NST)
                        e2, s2 = divmod(r2, NST)
                        ctx.w(t, "ve", ve_m[("grep", h2, e2, s2)])
                    ctx.inc(t.matmul(ph[0][:, :], onesr[:, :],
                                     gerow[:, st * 512:(st + 1) * 512],
                                     start=True, stop=True), "pe")

                    w1s = w1b[i_e % 2]

                    def hmm(fn):
                        if fn >= 2:
                            ctx.w(t, "se", se_m[("relu", hf, e, st, fn - 2)])
                        if fn == 0:
                            ctx.w(t, "ve", ve_m[("grep", hf, e, st)])
                        if fn == 1 and sidx(hf, e, st) == 0:
                            ctx.w(t, "ve", ve_m[("brrep",)])
                        for c in range(NCH):
                            i = t.matmul(ph[fn % 2][:, :],
                                         w1s[:, c * 2048 + fn * 128:
                                             c * 2048 + fn * 128 + 128],
                                         xst[sl][:, c * 512:(c + 1) * 512],
                                         start=(c == 0), stop=(c == NCH - 1))
                        ctx.inc(i, "pe")

                    def ymm(fn, pss):
                        ctx.w(t, "se", se_m[("relu", hf, e, st, fn)])
                        if fn == 0:
                            if pss == 0:
                                i2 = sidx(hf, e, st) - 1
                                if i2 >= 0:
                                    h2, r2 = divmod(i2, 8 * NST)
                                    e2, s2 = divmod(r2, NST)
                                    ctx.w(t, "ve",
                                          ve_m[("yev", h2, e2, s2, 7)])
                                else:
                                    ctx.w(t, "ve",
                                          ve_m[("b2g", hf, NCH - 1, NST - 1)])
                            else:
                                ctx.w(t, "ve", ve_m[("yev", hf, e, st, 3)])
                        for dn4 in range(4):
                            dn = pss * 4 + dn4
                            i = t.matmul(py[dn4][:, :],
                                         w2b[:, fn * D + dn * 128:
                                             fn * D + dn * 128 + 128],
                                         hT[:, fn * 512:(fn + 1) * 512],
                                         start=(fn == 0), stop=(fn == 15))
                        ctx.inc(i, "pe")

                    hmm(0)
                    for fn in range(1, 16):
                        hmm(fn)
                        ymm(fn - 1, 0)
                    ymm(15, 0)
                    for fn in range(16):
                        ymm(fn, 1)
            for j in range(16):
                ctx.w(t, "ve", ve_m[("yev", hf, 7, NST - 1, 7)])
                if j >= 2:
                    ctx.w(t, "ve", ve_m[("comb", hf * 16 + j - 2)])
                for c in range(NCH):
                    i = t.transpose(pT[j % 2][:, c * 128:(c + 1) * 128],
                                    yaccT[:, c * HT + j * 128:
                                          c * HT + j * 128 + 128],
                                    ident_bf[:, :])
                ctx.inc(i, "pe")

    blk.vector(f_vector)
    blk.scalar(f_scalar)
    blk.gpsimd(f_park)
    blk.sync(f_sync)
    blk.gpsimd(f_gp)
    blk.tensor(f_pe)


def emit_l1attn(nc, ctx, blk, es, R_in, R_out, ln, ident_bf, onesb, pT, pqo,
                wdr, dscr, tag):
    phv = phase_fence(ctx)
    E = es.enter_context
    xnT = E(nc.sbuf_tensor(f"t_xnT{tag}", [128, NCH * T], BF16))
    emit_ln_transpose(nc, ctx, blk, R_in, xnT, ln, ident_bf, pT, pqo=pqo)

    NSH = 32
    INV_CNT = 1.0 / 33554432.0
    SC = 64.0 ** (-0.25)

    wq = E(nc.sbuf_tensor(f"t_wq{tag}", [128, NCH * D], BF16))
    wk = E(nc.sbuf_tensor(f"t_wk{tag}", [128, NCH * D], BF16))
    wv = E(nc.sbuf_tensor(f"t_wv{tag}", [128, NCH * D], BF16))
    wo = wq
    q2s = E(nc.sbuf_tensor(f"t_q2s{tag}", [128, T], BF16))
    k2s = E(nc.sbuf_tensor(f"t_k2s{tag}", [128, T], BF16))
    v2s = E(nc.sbuf_tensor(f"t_v2s{tag}", [128, T], BF16))
    lqT2 = E(nc.sbuf_tensor(f"t_lq{tag}", [128, 64], BF16))
    lkT2 = E(nc.sbuf_tensor(f"t_lk{tag}", [128, 64], BF16))
    lvT2 = E(nc.sbuf_tensor(f"t_lv{tag}", [128, 64], BF16))
    onesr = E(nc.sbuf_tensor(f"t_onesr{tag}", [1, 128], BF16))
    iota2 = E(nc.sbuf_tensor(f"t_iota{tag}", [128, 16], F32))
    lnc64 = E(nc.sbuf_tensor(f"t_lnc64{tag}", [128, 64], F32))
    lnd64 = E(nc.sbuf_tensor(f"t_lnd64{tag}", [128, 64], F32))
    lqb64 = E(nc.sbuf_tensor(f"t_lqb64{tag}", [128, 64], F32))
    lkb64 = E(nc.sbuf_tensor(f"t_lkb64{tag}", [128, 64], F32))
    lvb64 = E(nc.sbuf_tensor(f"t_lvb64{tag}", [128, 64], F32))
    borep = E(nc.sbuf_tensor(f"t_borep{tag}", [128, D], F32))
    lna2 = E(nc.sbuf_tensor(f"t_lna2{tag}", [128, NCH], F32))
    lnb2 = E(nc.sbuf_tensor(f"t_lnb2{tag}", [128, NCH], F32))
    bq2 = E(nc.sbuf_tensor(f"t_bq2{tag}", [128, NCH], F32))
    bv2 = E(nc.sbuf_tensor(f"t_bv2{tag}", [128, NCH], F32))
    scalin = E(nc.sbuf_tensor(f"t_scalin{tag}", [1, 4], F32))
    lncts = E(nc.sbuf_tensor(f"t_lncts{tag}", [128, 64], F32))
    negmn_all = E(nc.sbuf_tensor(f"t_negmn{tag}", [128, NSH], F32))
    diffcols = E(nc.sbuf_tensor(f"t_dcols{tag}", [64, NSH], F32))
    dcol1 = E(nc.sbuf_tensor(f"t_dcol1{tag}", [64, 1], F32))
    drow = E(nc.sbuf_tensor(f"t_drow{tag}", [1, 64], F32))
    sca = E(nc.sbuf_tensor(f"t_sca{tag}", [1, 32], F32))
    scab = E(nc.sbuf_tensor(f"t_scab{tag}", [1, 8], BF16))
    scb_ = E(nc.sbuf_tensor(f"t_scb{tag}", [1, 8], F32))
    dres = E(nc.sbuf_tensor(f"t_dres{tag}", [1, 8], F32))
    kmask = E(nc.sbuf_tensor(f"t_kmask{tag}", [128, 16], F32))
    kmb = E(nc.sbuf_tensor(f"t_kmb{tag}", [128, 16], BF16))
    qmT = E(nc.sbuf_tensor(f"t_qmT{tag}", [16, 128], BF16))
    qmrow65 = E(nc.sbuf_tensor(f"t_qmrow{tag}", [65, SEQ], BF16))
    tsrep = E(nc.sbuf_tensor(f"t_tsrep{tag}", [128, 1], F32))
    cwrep = E(nc.sbuf_tensor(f"t_cwrep{tag}", [128, 1], F32))
    updrep = E(nc.sbuf_tensor(f"t_updrep{tag}", [64, 1], F32))
    c1rep = E(nc.sbuf_tensor(f"t_c1rep{tag}", [64, 1], F32))
    qcb = [E(nc.sbuf_tensor(f"t_qc{i}{tag}", [64, SEQ], BF16)) for i in range(2)]
    skb = [E(nc.sbuf_tensor(f"t_sk{i}{tag}", [65, SEQ], BF16)) for i in range(2)]
    vpb = [E(nc.sbuf_tensor(f"t_vp{i}{tag}", [128, 16 * 65], BF16))
           for i in range(2)]
    iob = [E(nc.sbuf_tensor(f"t_io{i}{tag}", [64, SEQ], BF16)) for i in range(2)]
    ionew = v2s[0:64, 0:SEQ]
    sqT = [k2s[0:65, 0:SEQ], k2s[0:65, SEQ:2 * SEQ]]
    qpal = E(nc.sbuf_tensor(f"t_qpal{tag}", [128, 1024], F32))
    scrap = E(nc.sbuf_tensor(f"t_scrap{tag}", [128, 1024], F32))
    scrb = v2s[0:64, SEQ:2 * SEQ]
    sqb = E(nc.sbuf_tensor(f"t_sqb{tag}", [128, 1024], BF16))
    st16 = E(nc.sbuf_tensor(f"t_st16{tag}", [128, 16], F32))
    st16b = E(nc.sbuf_tensor(f"t_st16c{tag}", [128, 16], F32))
    stm = E(nc.sbuf_tensor(f"t_stm{tag}", [128, 16], F32))
    strd = E(nc.sbuf_tensor(f"t_strd{tag}", [128, 16], F32))
    nsub = E(nc.sbuf_tensor(f"t_nsub{tag}", [128, 16], F32))
    ncb = E(nc.sbuf_tensor(f"t_ncb{tag}", [128, 16], BF16))
    negcT = [E(nc.sbuf_tensor(f"t_ncT{i}{tag}", [16, 128], BF16))
             for i in range(2)]
    pbuf = [E(nc.sbuf_tensor(f"t_pb{i}{tag}", [128, 512], BF16))
            for i in range(3)]
    arep = E(nc.sbuf_tensor(f"t_arep{tag}", [64, 512], F32))
    r65 = E(nc.sbuf_tensor(f"t_r65{tag}", [65, 640], BF16))
    RAL, RON = 0, 512
    scr65 = E(nc.sbuf_tensor(f"t_scr65{tag}", [65, 512], F32))
    rin4 = [ln["rt0"], ln["rt1"]]
    pS = [E(nc.psum_tensor(f"t_pS{i}{tag}", [128, 512], F32)) for i in range(2)]
    pF = [E(nc.psum_tensor(f"t_pF{i}{tag}", [128, 512], F32)) for i in range(2)]
    ppv = E(nc.psum_tensor(f"t_ppv{tag}", [65, 512], F32))
    pbc = E(nc.psum_tensor(f"t_pbc{tag}", [128, 512], F32))

    qcur_d = dscr["qcur_d"]; sk_d = dscr["sk_d"]; vp_d = dscr["vp_d"]
    io_d = dscr["io_d"]; io1_d = dscr["io1_d"]
    attn_park = dscr["attn_park"]; negc_d = dscr["negc_d"]
    din_d = dscr["din_d"]; dout_d = dscr["dout_d"]; qm_d = dscr["qm_d"]
    dcol_d = dscr["dcol_d"]

    def d_idx(hp, sh4):
        return (sh4 // 2) * 16 + hp * 2 + (sh4 % 2)

    def prevk(it, sh, k):
        i = it * NSH + sh - k
        return (i // NSH, i % NSH) if i >= 0 else None

    pe_pos, se_pos = {}, {}
    cc_base = ctx.cnt["cc"]

    def plan_pe_se():
        n = ctx.cnt["pe"]
        for hp in range(NCH):
            for kind in ("q2", "k2", "v2"):
                for tg in range(NTG):
                    n += 1; pe_pos[(kind, hp, tg)] = n
            for sh4 in range(4):
                n += 1; pe_pos[("kp", hp, sh4)] = n
                n += 1; pe_pos[("skT", hp, sh4)] = n
                n += 1; pe_pos[("knm", hp, sh4)] = n
                n += 1; pe_pos[("mnb", hp, sh4)] = n
                n += 1; pe_pos[("vp", hp, sh4)] = n
        for it in range(3):
            n += 1; pe_pos[("tsb", it)] = n
            if it >= 1:
                n += 1; pe_pos[("cwb", it)] = n
                n += 1; pe_pos[("qmT", it)] = n
            if it == 2:
                n += 1; pe_pos[("upb", it)] = n
            for sh in range(NSH):
                n += 1; pe_pos[("qp", it, sh)] = n
                n += 1; pe_pos[("sqT", it, sh)] = n
                for j in range(16):
                    n += 1; pe_pos[("sub", it, sh, j)] = n
                n += 1; pe_pos[("ncT", it, sh)] = n
                for qg in range(4):
                    for kc in range(16):
                        n += 1; pe_pos[("S", it, sh, qg, kc)] = n
                        n += 1; pe_pos[("PV", it, sh, qg, kc)] = n
                    n += 1; pe_pos[("bca", it, sh, qg)] = n
        for j in range(NT):
            n += 1; pe_pos[("proj", j)] = n

        m = ctx.cnt["se"]
        for hp in range(NCH):
            for sh4 in range(4):
                m += 1; se_pos[("lnk", hp, sh4)] = m
                m += 1; se_pos[("mnsq", hp, sh4)] = m
        for it in range(3):
            m += 1; se_pos[("tsq", it)] = m
            for sh in range(NSH):
                m += 1; se_pos[("lnq", it, sh)] = m
                m += 1; se_pos[("nsq", it, sh)] = m
                for qg in range(4):
                    for kc in range(16):
                        m += 1; se_pos[("exp", it, sh, qg, kc)] = m
    plan_pe_se()

    pl = {}
    qa_pos = {}

    def plan_sync():
        c = {k: ctx.cnt[k] for k in ("qd0", "qd1", "qe0", "qe1", "qn0", "qn1",
                                     "qv0", "qv1", "qB")}
        for hp in range(NCH):
            for sh4 in range(4):
                su = sh4 % 2
                c[f"qe{su}"] += 48
                pl[("dwr", hp, sh4)] = c[f"qe{su}"]
        for it in range(3):
            if it >= 1:
                c["qv1"] += 32
                pl[("qm", it)] = c["qv1"]
            for sh in range(NSH):
                su = sh % 2
                c[f"qd{su}"] += 48 if it == 0 else 64
                pl[("ld", it, sh)] = c[f"qd{su}"]
                c[f"qn{su}"] += 32
                pl[("ncrow", it, sh)] = c[f"qn{su}"]
                if it == 0:
                    c[f"qe{su}"] += 32
                elif it == 1:
                    c[f"qe{su}"] += 16
                pl[("iw", it, sh)] = (c["qe0"], c["qe1"])
            if it < 2:
                c["qv0"] += 32
                pl[("drow", it)] = c["qv0"]
                c["qB"] += 16
                qa_pos[("din", it)] = c["qB"]
                c["qB"] += 16
                qa_pos[("dout", it)] = c["qB"]
        c["qB"] += NCH * 16
        qa_pos[("stagef",)] = c["qB"]
    plan_sync()

    qw_pos = ctx.cnt["qw"] + 16 * (3 * NCH + 6 + 12)
    wo_loaded = [0]
    ve_m, dm_m = {}, {}
    qi0 = (ctx.cnt["qi0"], ctx.cnt["qi1"])
    qo0 = (ctx.cnt["qu0"], ctx.cnt["qu1"])
    qo_prev = (ctx.cnt[pqo + "0"], ctx.cnt[pqo + "1"])
    qw_base = ctx.cnt["qw"]
    qp_base = (ctx.cnt["qp0"], ctx.cnt["qp1"])
    last_v2 = pe_pos[("v2", NCH - 1, NTG - 1)]
    prev_sqrt = [None]

    def ln64(v, x16, wrep, outb, key):
        x3 = x16.rearrange("p (a b) -> p a b", b=64)
        if prev_sqrt[0] is not None:
            ctx.w(v, "se", prev_sqrt[0])
        ctx.inc(v.reduce_sum(st16[:, :], x3, axis=AX.X), "ve")
        ctx.wall(v, "ve")
        ctx.inc(v.tensor_scalar(st16[:, :], st16[:, :], 1.0 / 64, None,
                                ALU.mult), "ve")
        ctx.inc(v.tensor_tensor(scrap[:, :], x16, x16, ALU.mult), "ve")
        ctx.wall(v, "ve")
        ctx.inc(v.reduce_sum(st16b[:, :],
                             scrap.rearrange("p (a b) -> p a b", b=64),
                             axis=AX.X), "ve")
        ctx.inc(v.tensor_tensor(stm[:, :], st16[:, :], st16[:, :],
                                ALU.mult), "ve")
        ctx.wall(v, "ve")
        ctx.inc(v.tensor_scalar(st16b[:, :], st16b[:, :], 1.0 / 64, None,
                                ALU.mult), "ve")
        ctx.wall(v, "ve")
        ctx.inc(v.tensor_tensor(st16b[:, :], st16b[:, :], stm[:, :],
                                ALU.subtract), "ve")
        ve_m[("var",) + key] = ctx.cnt["ve"]
        ctx.wall(v, "ve")
        ctx.w(v, "se", se_pos[key])
        ctx.inc(v.reciprocal(strd[:, :], stm[:, :]), "ve")
        ctx.wall(v, "ve")
        m_bc = st16[:, :].unsqueeze(2).broadcast_to([128, 16, 64])
        r_bc = strd[:, :].unsqueeze(2).broadcast_to([128, 16, 64])
        w_bc = wrep[:, :].unsqueeze(1).broadcast_to([128, 16, 64])
        s3 = scrap.rearrange("p (a b) -> p a b", b=64)
        ctx.inc(v.tensor_tensor(s3, x3, m_bc, ALU.subtract), "ve")
        ctx.wall(v, "ve")
        ctx.inc(v.tensor_tensor(s3, s3, r_bc, ALU.mult), "ve")
        ctx.wall(v, "ve")
        ctx.inc(v.tensor_tensor(outb.rearrange("p (a b) -> p a b", b=64),
                                s3, w_bc, ALU.mult), "ve")
        ctx.wall(v, "ve")
        ctx.inc(v.tensor_tensor(scrap[:, :], outb, outb, ALU.mult), "ve")
        ctx.wall(v, "ve")
        ctx.inc(v.reduce_sum(st16[:, :],
                             scrap.rearrange("p (a b) -> p a b", b=64),
                             axis=AX.X), "ve")
        ve_m[("sq2",) + key] = ctx.cnt["ve"]
        ctx.wall(v, "ve")

    def emit_boundary(v, it):
        if it == 0:
            ctx.inc(v.tensor_copy(sca[:, 13:16], scalin[:, 0:3]), "ve")
            ctx.wall(v, "ve")
            ctx.inc(v.tensor_copy(sca[:, 3:4], sca[:, 14:15]), "ve")
            ctx.wall(v, "ve")
        else:
            ctx.w(v, "qB", qa_pos[("dout", it - 1)])
            ctx.inc(v.tensor_scalar(sca[:, 0:1], dres[:, 0:1], INV_CNT, None,
                                    ALU.mult), "ve")
            ctx.wall(v, "ve")
            if it == 1:
                ctx.inc(v.tensor_scalar(sca[:, 1:2], sca[:, 0:1], 0.001, None,
                                        ALU.is_lt), "ve")
                ctx.wall(v, "ve")
                ctx.inc(v.tensor_scalar(sca[:, 2:3], sca[:, 1:2], -205.0,
                                        2048.0, ALU.mult, ALU.add), "ve")
                ctx.inc(v.tensor_scalar(sca[:, 3:4], sca[:, 3:4], 1.0, 0.005,
                                        ALU.mult, ALU.add), "ve")
                ctx.wall(v, "ve")
            else:
                ctx.inc(v.tensor_tensor(sca[:, 7:8], sca[:, 0:1],
                                        sca[:, 15:16], ALU.mult), "ve")
                ctx.wall(v, "ve")
                ctx.inc(v.tensor_tensor(sca[:, 7:8], sca[:, 7:8],
                                        sca[:, 13:14], ALU.add), "ve")
                ctx.wall(v, "ve")
                ctx.inc(v.tensor_tensor(sca[:, 5:6], sca[:, 0:1], sca[:, 7:8],
                                        ALU.is_lt), "ve")
                ctx.wall(v, "ve")
                ctx.inc(v.tensor_scalar(sca[:, 6:7], sca[:, 5:6], -1.0, 1.0,
                                        ALU.mult, ALU.add), "ve")
                ctx.inc(v.tensor_scalar(sca[:, 8:9], sca[:, 0:1], 0.05, None,
                                        ALU.is_gt), "ve")
                ctx.inc(v.tensor_scalar(sca[:, 9:10], sca[:, 0:1], 0.001, None,
                                        ALU.is_lt), "ve")
                ctx.inc(v.tensor_scalar(sca[:, 10:11], sca[:, 1:2], -21.0,
                                        2048.0, ALU.mult, ALU.add), "ve")
                ctx.inc(v.tensor_scalar(sca[:, 11:12], sca[:, 1:2], -185.0,
                                        1843.0, ALU.mult, ALU.add), "ve")
                ctx.wall(v, "ve")
                ctx.inc(v.tensor_tensor(sca[:, 12:13], sca[:, 9:10],
                                        sca[:, 11:12], ALU.mult), "ve")
                ctx.inc(v.tensor_scalar(sca[:, 9:10], sca[:, 9:10], -1.0, 1.0,
                                        ALU.mult, ALU.add), "ve")
                ctx.wall(v, "ve")
                ctx.inc(v.tensor_tensor(sca[:, 9:10], sca[:, 9:10],
                                        sca[:, 2:3], ALU.mult), "ve")
                ctx.wall(v, "ve")
                ctx.inc(v.tensor_tensor(sca[:, 12:13], sca[:, 12:13],
                                        sca[:, 9:10], ALU.add), "ve")
                ctx.wall(v, "ve")
                ctx.inc(v.tensor_tensor(sca[:, 10:11], sca[:, 8:9],
                                        sca[:, 10:11], ALU.mult), "ve")
                ctx.inc(v.tensor_scalar(sca[:, 8:9], sca[:, 8:9], -1.0, 1.0,
                                        ALU.mult, ALU.add), "ve")
                ctx.wall(v, "ve")
                ctx.inc(v.tensor_tensor(sca[:, 12:13], sca[:, 8:9],
                                        sca[:, 12:13], ALU.mult), "ve")
                ctx.wall(v, "ve")
                ctx.inc(v.tensor_tensor(sca[:, 2:3], sca[:, 10:11],
                                        sca[:, 12:13], ALU.add), "ve")
                ctx.inc(v.tensor_scalar(sca[:, 7:8], sca[:, 6:7], 0.005, None,
                                        ALU.mult), "ve")
                ctx.wall(v, "ve")
                ctx.inc(v.tensor_tensor(sca[:, 3:4], sca[:, 3:4], sca[:, 7:8],
                                        ALU.add), "ve")
                ctx.wall(v, "ve")
                ctx.inc(v.tensor_copy(scab[:, 2:3], sca[:, 6:7]), "ve")
                ctx.inc(v.tensor_copy(scab[:, 3:4], sca[:, 5:6]), "ve")
        ctx.inc(v.tensor_scalar(sca[:, 7:8], sca[:, 3:4], 0.0, None,
                                ALU.is_gt), "ve")
        ctx.inc(v.tensor_scalar(sca[:, 8:9], sca[:, 3:4], 1e-30, None,
                                ALU.max), "ve")
        ve_m[("bpre", it)] = ctx.cnt["ve"]
        ctx.wall(v, "ve")
        ctx.w(v, "se", se_pos[("tsq", it)])
        ctx.inc(v.reciprocal(sca[:, 8:9], sca[:, 8:9]), "ve")
        ctx.wall(v, "ve")
        ctx.inc(v.tensor_tensor(sca[:, 8:9], sca[:, 7:8], sca[:, 8:9],
                                ALU.mult), "ve")
        ctx.inc(v.tensor_scalar(sca[:, 7:8], sca[:, 7:8], -1.0, 1.0,
                                ALU.mult, ALU.add), "ve")
        ctx.wall(v, "ve")
        ctx.inc(v.tensor_tensor(sca[:, 4:5], sca[:, 8:9], sca[:, 7:8],
                                ALU.add), "ve")
        ctx.wall(v, "ve")
        ctx.inc(v.tensor_copy(scab[:, 0:1], sca[:, 4:5]), "ve")
        if it >= 1:
            ctx.inc(v.tensor_copy(scab[:, 1:2], sca[:, 2:3]), "ve")
        ve_m[("bmath", it)] = ctx.cnt["ve"]
        ctx.wall(v, "ve")
        ctx.w(v, "pe", pe_pos[("tsb", it)])
        ctx.inc(v.tensor_copy(tsrep[:, :], pbc[:, 0:1]), "ve")
        ctx.wall(v, "ve")
        ctx.inc(v.tensor_scalar(lncts[:, :], lnc64[:, :], tsrep[:, :], None,
                                ALU.mult), "ve")
        ve_m[("lncts", it)] = ctx.cnt["ve"]
        if it >= 1:
            ctx.w(v, "pe", pe_pos[("cwb", it)])
            ctx.inc(v.tensor_copy(cwrep[:, :], pbc[:, 1:2]), "ve")
            ctx.wall(v, "ve")
            ctx.inc(v.tensor_scalar(kmask[:, :], iota2[:, :], cwrep[:, :],
                                    None, ALU.is_lt), "ve")
            ctx.wall(v, "ve")
            ctx.inc(v.tensor_copy(kmb[:, :], kmask[:, :]), "ve")
            ve_m[("kmb", it)] = ctx.cnt["ve"]
            ctx.w(v, "pe", pe_pos[("qmT", it)])
            ctx.inc(v.tensor_copy(qmT[:, :], pT[0][0:16, 0:128]), "ve")
            ve_m[("qmT", it)] = ctx.cnt["ve"]
        if it == 2:
            ctx.w(v, "pe", pe_pos[("upb", it)])
            ctx.inc(v.tensor_copy(updrep[:, :], pbc[0:64, 0:1]), "ve")
            ctx.inc(v.tensor_copy(c1rep[:, :], pbc[0:64, 1:2]), "ve")
            ve_m[("upb", it)] = ctx.cnt["ve"]
            ctx.wall(v, "ve")

    def f_vector(v):
        ctx.inc(v.memset(onesr[:, :], 1.0), "ve")
        ctx.inc(v.memset(r65[:, RON:RON + 64], 1.0), "ve")
        ctx.inc(v.memset(scb_[:, :], 0.0), "ve")
        ctx.inc(v.memset(qmrow65[64:65, :], 1.0), "ve")
        for i in range(2):
            ctx.inc(v.memset(
                vpb[i].rearrange("p (a b) -> p a b", b=65)[:, :, 64:65],
                1.0), "ve")
        ve_m[("init",)] = ctx.cnt["ve"]
        ctx.wall(v, "ve")
        ctx.w(v, "qw", qw_pos)
        for c in range(NCH):
            ctx.inc(v.tensor_scalar(wq[:, c * D:(c + 1) * D],
                                    wq[:, c * D:(c + 1) * D],
                                    lna2[:, c:c + 1], SC,
                                    ALU.mult, ALU.mult), "ve")
            ctx.inc(v.tensor_scalar(wk[:, c * D:(c + 1) * D],
                                    wk[:, c * D:(c + 1) * D],
                                    lnb2[:, c:c + 1], SC,
                                    ALU.mult, ALU.mult), "ve")
            ctx.inc(v.tensor_scalar(wv[:, c * D:(c + 1) * D],
                                    wv[:, c * D:(c + 1) * D],
                                    lnb2[:, c:c + 1], None, ALU.mult), "ve")
        ctx.inc(v.tensor_scalar(bq2[:, :], bq2[:, :], SC, None, ALU.mult), "ve")
        ve_m[("wscale",)] = ctx.cnt["ve"]
        ctx.wall(v, "ve")
        for hp in range(NCH):
            for (kind, dst, bias) in (("q2", q2s, bq2), ("k2", k2s, None),
                                      ("v2", v2s, bv2)):
                for tg in range(NTG):
                    ctx.w(v, "pe", pe_pos[(kind, hp, tg)])
                    if kind == "q2" and tg == 0 and hp >= 1:
                        ctx.slot_w(v, "qe", 0, pl[("dwr", hp - 1, 2)] // 16)
                        ctx.slot_w(v, "qe", 1, pl[("dwr", hp - 1, 3)] // 16)
                    if bias is not None:
                        ctx.inc(v.tensor_scalar(
                            dst[:, tg * 512:(tg + 1) * 512],
                            pS[tg % 2][:, :], bias[:, hp:hp + 1], None,
                            ALU.add), "ve")
                    else:
                        ctx.inc(v.tensor_copy(dst[:, tg * 512:(tg + 1) * 512],
                                              pS[tg % 2][:, :]), "ve")
                    ve_m[(kind + "e", hp, tg)] = ctx.cnt["ve"]
            for sh4 in range(4):
                su = sh4 % 2
                idx = d_idx(hp, sh4)
                ctx.w(v, "pe", pe_pos[("kp", hp, sh4)])
                for hb in range(2):
                    ctx.inc(v.tensor_tensor(
                        qpal.rearrange("p (a b) -> p a b", b=64)[
                            :, hb * 8:(hb + 1) * 8, :],
                        pS[hb].rearrange("p (a b) -> p a b", b=64),
                        lkb64[:, :].unsqueeze(1).broadcast_to([128, 8, 64]),
                        ALU.add), "ve")
                ctx.wall(v, "ve")
                ln64(v, qpal[:, :], lnd64, sqb[:, :], ("lnk", hp, sh4))
                ctx.inc(v.reduce_max(strd[:, 0:1], st16[:, :], axis=AX.X),
                        "ve")
                ctx.wall(v, "ve")
                ctx.inc(v.tensor_copy(ncb[:, 0:1], strd[:, 0:1]), "ve")
                ve_m[("knmb", hp, sh4)] = ctx.cnt["ve"]
                ctx.w(v, "pe", pe_pos[("skT", hp, sh4)])
                if 4 * hp + sh4 >= 2:
                    pd = 4 * hp + sh4 - 2
                    ctx.slot_w(v, "qe", su, pl[("dwr", pd // 4, pd % 4)] // 16)
                ctx.inc(v.tensor_copy(skb[su][0:64, 0:1024],
                                      pT[0][0:64, :]), "ve")
                ctx.inc(v.tensor_copy(skb[su][0:64, 1024:2048],
                                      pT[1][0:64, :]), "ve")
                ctx.inc(v.memset(skb[su][64:65, :], 1.0), "ve")
                ve_m[("skb", hp, sh4)] = ctx.cnt["ve"]
                ctx.w(v, "pe", pe_pos[("knm", hp, sh4)])
                ctx.inc(v.reduce_max(scr65[0:1, 2:3], pT[0][0:1, 0:128],
                                     axis=AX.X), "ve")
                ve_m[("mn2", hp, sh4)] = ctx.cnt["ve"]
                ctx.wall(v, "ve")
                ctx.w(v, "se", se_pos[("mnsq", hp, sh4)])
                ctx.inc(v.tensor_scalar(scr65[0:1, 3:4], scr65[0:1, 4:5],
                                        -0.125, None, ALU.mult), "ve")
                ctx.wall(v, "ve")
                ctx.inc(v.tensor_copy(scab[0:1, 4:5], scr65[0:1, 3:4]), "ve")
                ve_m[("mnbf", hp, sh4)] = ctx.cnt["ve"]
                ctx.w(v, "pe", pe_pos[("mnb", hp, sh4)])
                ctx.inc(v.tensor_copy(negmn_all[:, idx:idx + 1],
                                      pbc[:, 2:3]), "ve")
                ve_m[("mnrep", hp, sh4)] = ctx.cnt["ve"]
                prev_sqrt[0] = se_pos[("mnsq", hp, sh4)]
                ctx.w(v, "pe", pe_pos[("vp", hp, sh4)])
                for hb in range(2):
                    ctx.inc(v.tensor_tensor(
                        vpb[su].rearrange("p (a b) -> p a b", b=65)[
                            :, hb * 8:(hb + 1) * 8, 0:64],
                        pS[hb].rearrange("p (a b) -> p a b", b=64),
                        lvb64[:, :].unsqueeze(1).broadcast_to([128, 8, 64]),
                        ALU.add), "ve")
                ve_m[("vpb", hp, sh4)] = ctx.cnt["ve"]
                ctx.wall(v, "ve")
        for it in range(3):
            emit_boundary(v, it)
            for sh in range(NSH):
                su = sh % 2
                ctx.slot_w(v, "qd", su, pl[("ld", it, sh)] // 16)
                if it == 2:
                    ctx.inc(v.scalar_tensor_tensor(
                        qcb[su][:, :], iob[su][:, :], updrep[:, :],
                        qcb[su][:, :], ALU.mult, ALU.add), "ve")
                    ve_m[("qfix", it, sh)] = ctx.cnt["ve"]
                    ctx.wall(v, "ve")
                ctx.w(v, "pe", pe_pos[("qp", it, sh)])
                for hb in range(2):
                    ctx.inc(v.tensor_tensor(
                        qpal.rearrange("p (a b) -> p a b", b=64)[
                            :, hb * 8:(hb + 1) * 8, :],
                        pS[hb].rearrange("p (a b) -> p a b", b=64),
                        lqb64[:, :].unsqueeze(1).broadcast_to([128, 8, 64]),
                        ALU.add), "ve")
                ctx.wall(v, "ve")
                ln64(v, qpal[:, :], lncts, sqb[:, :], ("lnq", it, sh))
                ctx.w(v, "pe", pe_pos[("sqT", it, sh)])
                ctx.inc(v.tensor_copy(sqT[su][0:64, 0:1024],
                                      pT[0][0:64, :]), "ve")
                ctx.inc(v.tensor_copy(sqT[su][0:64, 1024:2048],
                                      pT[1][0:64, :]), "ve")
                ve_m[("sqTe", it, sh)] = ctx.cnt["ve"]
                for j in range(16):
                    ctx.w(v, "pe", pe_pos[("sub", it, sh, j)])
                    ctx.inc(v.reduce_max(nsub[:, j:j + 1],
                                         pS[j % 2][:, 0:16], axis=AX.X,
                                         negate=True), "ve")
                    ve_m[("nsr", it, sh, j)] = ctx.cnt["ve"]
                ctx.wall(v, "ve")
                ctx.w(v, "se", se_pos[("nsq", it, sh)])
                ctx.inc(v.tensor_scalar(st16b[:, :], stm[:, :],
                                        negmn_all[:, sh:sh + 1],
                                        80.0, ALU.mult, ALU.add), "ve")
                ctx.wall(v, "ve")
                ctx.inc(v.tensor_tensor(st16b[:, :], nsub[:, :], st16b[:, :],
                                        ALU.min), "ve")
                ctx.wall(v, "ve")
                prev_sqrt[0] = se_pos[("nsq", it, sh)]
                p1 = prevk(it, sh, 1)
                if p1 is not None:
                    ctx.w(v, "pe", pe_pos[("ncT",) + p1])
                ctx.inc(v.tensor_copy(ncb[:, :], st16b[:, :]), "ve")
                ve_m[("ncb", it, sh)] = ctx.cnt["ve"]
                ctx.w(v, "pe", pe_pos[("ncT", it, sh)])
                ctx.inc(v.tensor_copy(negcT[su][:, :], pT[0][0:16, 0:128]),
                        "ve")
                ve_m[("ncT", it, sh)] = ctx.cnt["ve"]
                for qg in range(4):
                    if it >= 1:
                        for kc in range(12, 16):
                            ctx.w(v, "se", se_pos[("exp", it, sh, qg, kc)])
                            ctx.inc(v.tensor_scalar(
                                pbuf[kc % 3][:, :], pbuf[kc % 3][:, :],
                                kmask[:, kc:kc + 1], None, ALU.mult), "ve")
                            ve_m[("km", it, sh, qg, kc)] = ctx.cnt["ve"]
                        ctx.wall(v, "ve")
                    ctx.w(v, "pe", pe_pos[("PV", it, sh, qg, 15)])
                    ctx.inc(v.tensor_scalar(scr65[64:65, 0:512],
                                            ppv[64:65, :], 1.0, 1e-30,
                                            ALU.mult, ALU.add), "ve")
                    ctx.wall(v, "ve")
                    ctx.inc(v.reciprocal(scr65[64:65, 0:512],
                                         scr65[64:65, 0:512]), "ve")
                    ctx.wall(v, "ve")
                    if it >= 1 and qg == 3:
                        ctx.slot_w(v, "qv", 1, pl[("qm", it)] // 16)
                        ctx.inc(v.tensor_tensor(
                            scr65[64:65, 0:512], scr65[64:65, 0:512],
                            qmrow65[64:65, qg * 512:(qg + 1) * 512],
                            ALU.mult), "ve")
                        ctx.wall(v, "ve")
                    ctx.inc(v.tensor_copy(r65[64:65, RAL:RAL + 512],
                                          scr65[64:65, 0:512]), "ve")
                    ve_m[("rq", it, sh, qg)] = ctx.cnt["ve"]
                    ctx.w(v, "pe", pe_pos[("bca", it, sh, qg)])
                    ctx.inc(v.tensor_copy(arep[:, :], pbc[0:64, :]), "ve")
                    ctx.wall(v, "ve")
                    if qg == 0:
                        g2 = it * NSH + sh - 2
                        if g2 >= 0 and g2 // NSH <= 1:
                            it2, sh2 = divmod(g2, NSH)
                            ctx.slot_w(v, "qe", su,
                                       pl[("iw", it2, sh2)][su] // 16)
                    ctx.inc(v.tensor_tensor(
                        ionew[:, qg * 512:(qg + 1) * 512],
                        ppv[0:64, :], arep[:, :], ALU.mult), "ve")
                    ve_m[("ion", it, sh, qg)] = ctx.cnt["ve"]
                ctx.wall(v, "ve")
                if it == 0:
                    ctx.inc(v.reduce_sum(diffcols[:, sh:sh + 1], ionew[:, :],
                                         axis=AX.X,
                                         apply_absolute_value=True), "ve")
                    ctx.inc(v.tensor_tensor(qcb[su][:, :], qcb[su][:, :],
                                            ionew[:, :], ALU.add), "ve")
                elif it == 1:
                    ctx.inc(v.tensor_tensor(scrb[:, :], ionew[:, :],
                                            iob[su][:, :], ALU.subtract), "ve")
                    ctx.wall(v, "ve")
                    ctx.inc(v.reduce_sum(diffcols[:, sh:sh + 1], scrb[:, :],
                                         axis=AX.X,
                                         apply_absolute_value=True), "ve")
                else:
                    ctx.inc(v.tensor_tensor(scrb[:, :], iob[su][:, :],
                                            ionew[:, :], ALU.subtract), "ve")
                    ctx.wall(v, "ve")
                    if sh >= 2:
                        ctx.slot_w(v, "qp", su, qp_base[su] // 16 + sh // 2)
                    ctx.inc(v.scalar_tensor_tensor(
                        iob[su][:, :], scrb[:, :], c1rep[:, :],
                        ionew[:, :], ALU.mult, ALU.add), "ve")
                ve_m[("iodone", it, sh)] = ctx.cnt["ve"]
                ctx.wall(v, "ve")
            if it < 2:
                ctx.inc(v.reduce_sum(dcol1[:, :], diffcols[:, :],
                                     axis=AX.X), "ve")
                ve_m[("dcol", it)] = ctx.cnt["ve"]
                ctx.wall(v, "ve")
                ctx.slot_w(v, "qv", 0, pl[("drow", it)] // 16)
                ctx.inc(v.reduce_sum(sca[0:1, 10:11], drow[:, :],
                                     axis=AX.X), "ve")
                ctx.wall(v, "ve")
                ctx.inc(v.tensor_copy(scb_[:, 0:1], sca[0:1, 10:11]), "ve")
                ve_m[("dsum", it)] = ctx.cnt["ve"]
                ctx.wall(v, "ve")
        for j in range(NT):
            jm = j % 2
            pfo = pS if jm == 0 else pF
            ctx.w(v, "pe", pe_pos[("proj", j)])
            ctx.slot_w(v, "qi", jm, qi0[jm] // 16 + j // 2 + 1)
            for half in range(2):
                ctx.inc(v.tensor_tensor(scrap[:, half * 512:half * 512 + 512],
                                        pfo[half][:, :],
                                        borep[:, half * 512:half * 512 + 512],
                                        ALU.add), "ve")
            ctx.wall(v, "ve")
            for half in range(2):
                ctx.inc(v.tensor_tensor(
                    rin4[jm][:, half * 512:half * 512 + 512],
                    rin4[jm][:, half * 512:half * 512 + 512],
                    scrap[:, half * 512:half * 512 + 512], ALU.add), "ve")
            ctx.wall(v, "ve")
            ve_m[("comb", j)] = ctx.cnt["ve"]
    blk.vector(f_vector)

    def f_scalar(sc):
        for hp in range(NCH):
            for sh4 in range(4):
                ctx.w(sc, "ve", ve_m[("var", "lnk", hp, sh4)])
                ctx.inc(sc.activation(stm[:, :], st16b[:, :], AF.Sqrt,
                                      bias=ln["eps"][:, :]), "se")
                ctx.w(sc, "ve", ve_m[("mn2", hp, sh4)])
                ctx.inc(sc.activation(scr65[0:1, 4:5], scr65[0:1, 2:3],
                                      AF.Sqrt), "se")
        for it in range(3):
            ctx.w(sc, "ve", ve_m[("bpre", it)])
            ctx.inc(sc.activation(sca[:, 8:9], sca[:, 8:9], AF.Sqrt), "se")
            for sh in range(NSH):
                ctx.w(sc, "ve", ve_m[("var", "lnq", it, sh)])
                ctx.inc(sc.activation(stm[:, :], st16b[:, :], AF.Sqrt,
                                      bias=ln["eps"][:, :]), "se")
                ctx.w(sc, "ve", ve_m[("sq2", "lnq", it, sh)])
                ctx.inc(sc.activation(stm[:, :], st16[:, :], AF.Sqrt), "se")
                for qg in range(4):
                    for kc in range(16):
                        ctx.w(sc, "pe", pe_pos[("S", it, sh, qg, kc)])
                        if kc >= 3:
                            k3 = kc - 3
                            if it >= 1 and k3 >= 12:
                                ctx.w(sc, "ve", ve_m[("km", it, sh, qg, k3)])
                            else:
                                ctx.w(sc, "pe", pe_pos[("PV", it, sh, qg, k3)])
                        ctx.inc(sc.activation(pbuf[kc % 3][:, :],
                                              pS[kc % 2][:, :],
                                              AF.Exp, scale=0.125), "se")
    blk.scalar(f_scalar)


    def f_sync(sync):
        for hp in range(NCH):
            for sh4 in range(4):
                su = sh4 % 2
                s_, h01 = sh4 // 2, sh4 % 2
                idx = d_idx(hp, sh4)
                ctx.w(sync, "ve", ve_m[("q2e", hp, NTG - 1)])
                ctx.slot_inc(sync.dma_start(
                    out=qcur_d[idx], in_=q2s[h01 * 64:h01 * 64 + 64,
                                            s_ * SEQ:(s_ + 1) * SEQ]),
                    "qe", su)
                ctx.w(sync, "ve", ve_m[("skb", hp, sh4)])
                ctx.slot_inc(sync.dma_start(out=sk_d[idx],
                                            in_=skb[su][:, :]), "qe", su)
                ctx.w(sync, "ve", ve_m[("vpb", hp, sh4)])
                ctx.slot_inc(sync.dma_start(out=vp_d[idx],
                                            in_=vpb[su][:, :]), "qe", su)
                assert ctx.cnt[f"qe{su}"] == pl[("dwr", hp, sh4)]
                ctx.slot_w(sync, "qe", su, ctx.cnt[f"qe{su}"] // 16)
        for it in range(3):
            if it >= 1:
                ctx.w(sync, "ve", ve_m[("qmT", it)])
                ctx.slot_inc(sync.dma_start(out=qm_d, in_=qmT[:, :]), "qv", 1)
                ctx.slot_w(sync, "qv", 1, ctx.cnt["qv1"] // 16)
                ctx.slot_inc(sync.dma_start(
                    out=qmrow65[64:65, :],
                    in_=qm_d.rearrange("a b -> (a b)")[None, :]), "qv", 1)
                assert ctx.cnt["qv1"] == pl[("qm", it)]
            for sh in range(NSH):
                su = sh % 2
                p2 = prevk(it, sh, 2)
                if p2 is not None:
                    ctx.w(sync, "pe", pe_pos[("PV",) + p2 + (3, 15)])
                    ctx.w(sync, "ve", ve_m[("iodone",) + p2])
                    if it == 2 and sh >= 2:
                        ctx.slot_w(sync, "qp", su,
                                   qp_base[su] // 16 + sh // 2)
                else:
                    ctx.slot_w(sync, "qe", su,
                               pl[("dwr", NCH - 1, 2 + su)] // 16)
                ctx.slot_inc(sync.dma_start(out=qcb[su][:, :],
                                            in_=qcur_d[sh]), "qd", su)
                ctx.slot_inc(sync.dma_start(out=skb[su][:, :],
                                            in_=sk_d[sh]), "qd", su)
                ctx.slot_inc(sync.dma_start(out=vpb[su][:, :],
                                            in_=vp_d[sh]), "qd", su)
                if it == 1:
                    ctx.slot_inc(sync.dma_start(out=iob[su][:, :],
                                                in_=io_d[sh]), "qd", su)
                elif it == 2:
                    ctx.slot_inc(sync.dma_start(out=iob[su][:, :],
                                                in_=io1_d[sh]), "qd", su)
                assert ctx.cnt[f"qd{su}"] == pl[("ld", it, sh)]
                ctx.w(sync, "ve", ve_m[("ncT", it, sh)])
                ctx.slot_inc(sync.dma_start(out=negc_d[su],
                                            in_=negcT[su][:, :]), "qn", su)
                ctx.slot_w(sync, "qn", su, ctx.cnt[f"qn{su}"] // 16)
                if p2 is not None:
                    ctx.w(sync, "pe", pe_pos[("S",) + p2 + (3, 15)])
                ctx.slot_inc(sync.dma_start(
                    out=sqT[su][64:65, :],
                    in_=negc_d[su].rearrange("a b -> (a b)")[None, :]),
                    "qn", su)
                assert ctx.cnt[f"qn{su}"] == pl[("ncrow", it, sh)]
                if it == 0:
                    ctx.w(sync, "ve", ve_m[("iodone", it, sh)])
                    ctx.slot_inc(sync.dma_start(out=io_d[sh],
                                                in_=ionew[:, :]), "qe", su)
                    ctx.slot_inc(sync.dma_start(out=qcur_d[sh],
                                                in_=qcb[su][:, :]), "qe", su)
                elif it == 1:
                    ctx.w(sync, "ve", ve_m[("iodone", it, sh)])
                    ctx.slot_inc(sync.dma_start(out=io1_d[sh],
                                                in_=ionew[:, :]), "qe", su)
                if it < 2:
                    ctx.slot_w(sync, "qe", su, ctx.cnt[f"qe{su}"] // 16)
                assert (ctx.cnt["qe0"], ctx.cnt["qe1"]) == pl[("iw", it, sh)]
            if it < 2:
                ctx.w(sync, "ve", ve_m[("dcol", it)])
                ctx.slot_inc(sync.dma_start(out=dcol_d, in_=dcol1[:, :]),
                             "qv", 0)
                ctx.slot_w(sync, "qv", 0, ctx.cnt["qv0"] // 16)
                ctx.slot_inc(sync.dma_start(
                    out=drow[:, :],
                    in_=dcol_d.rearrange("a b -> (a b)")[None, :]), "qv", 0)
                assert ctx.cnt["qv0"] == pl[("drow", it)]
                ctx.w(sync, "ve", ve_m[("dsum", it)])
                ctx.inc(sync.dma_start(out=din_d, in_=scb_[:, :]), "qB", 16)
                assert ctx.cnt["qB"] == qa_pos[("din", it)]
                ctx.w(sync, "cc", cc_base + it + 1)
                ctx.inc(sync.dma_start(out=dres[:, :], in_=dout_d), "qB", 16)
                assert ctx.cnt["qB"] == qa_pos[("dout", it)]
        ctx.slot_w(sync, "qp", 0, qp_base[0] // 16 + 16)
        ctx.slot_w(sync, "qp", 1, qp_base[1] // 16 + 16)
        ctx.w(sync, "pe", last_v2)
        for c in range(NCH):
            ctx.inc(sync.dma_start(out=xnT[:, c * T:(c + 1) * T],
                                   in_=attn_park[c, :, :]), "qB", 16)
        assert ctx.cnt["qB"] == qa_pos[("stagef",)]
        for j in range(NT):
            jm = j % 2
            if j >= 2:
                ctx.w(sync, "ve", ve_m[("comb", j - 2)])
                ctx.slot_inc(sync.dma_start(
                    out=R_out[(j - 2) * 128:(j - 1) * 128, :],
                    in_=rin4[jm][:, :]), "qu", jm)
                ctx.slot_w(sync, "qu", jm, qo0[jm] // 16 + (j - 2) // 2 + 1)
            else:
                ctx.slot_w(sync, pqo, jm, qo_prev[jm] // 16)
            ctx.slot_inc(sync.dma_start(out=rin4[jm][:, :],
                                        in_=R_in[j * 128:(j + 1) * 128, :]),
                         "qi", jm)
        for j in (NT - 2, NT - 1):
            jm = j % 2
            ctx.w(sync, "ve", ve_m[("comb", j)])
            ctx.slot_inc(sync.dma_start(out=R_out[j * 128:(j + 1) * 128, :],
                                        in_=rin4[jm][:, :]), "qu", jm)
    blk.sync(f_sync)

    def f_gp(g):
        fence_wait(ctx, g, phv)
        ctx.w(g, "qw", qw_base)
        for (wsb, key) in ((wq, "l1_wq_t"), (wk, "l1_wk_t"), (wv, "l1_wv_t")):
            for c in range(NCH):
                ctx.inc(g.dma_start(out=wsb[:, c * D:(c + 1) * D],
                                    in_=wdr[key][c * 128:(c + 1) * 128, :]),
                        "qw", 16)
        for (dst, key) in ((lqT2, "l1_lq_t"), (lkT2, "l1_lk_t"),
                           (lvT2, "l1_lv_t")):
            ctx.inc(g.dma_start(out=dst[0:64, :], in_=wdr[key]), "qw", 16)
            ctx.inc(g.dma_start(out=dst[64:128, :], in_=wdr[key]), "qw", 16)
        for (dst, key) in ((iota2, "iota2d"), (lnc64, "l1_lnc64"),
                           (lnd64, "l1_lnd64"), (lqb64, "l1_lqb64"),
                           (lkb64, "l1_lkb64"), (lvb64, "l1_lvb64"),
                           (borep, "l1_borep"), (lna2, "l1_lna2"),
                           (lnb2, "l1_lnb2"), (bq2, "l1_bq2"),
                           (bv2, "l1_bv2"), (scalin, "l1_scal")):
            ctx.inc(g.dma_start(out=dst[:, :], in_=wdr[key]), "qw", 16)
        assert ctx.cnt["qw"] == qw_pos
        ctx.w(g, "pe", pe_pos[("q2", NCH - 1, NTG - 1)])
        for c in range(NCH):
            ctx.inc(g.dma_start(out=wo[:, c * D:(c + 1) * D],
                                in_=wdr["l1_wo_t"][c * 128:(c + 1) * 128, :]),
                    "qw", 16)
        wo_loaded[0] = ctx.cnt["qw"]
        for it in range(2):
            ctx.w(g, "qB", qa_pos[("din", it)])
            ctx.inc(g.collective_compute(
                "AllReduce", ALU.add, replica_groups=[list(range(NCORES))],
                ins=[din_d], outs=[dout_d]), "cc")
        for sh in range(NSH):
            su = sh % 2
            ctx.w(g, "ve", ve_m[("iodone", 2, sh)])
            s_, h = sh // 16, sh % 16
            hp, h01 = h // 2, h % 2
            ctx.slot_inc(g.dma_start(
                out=attn_park[hp, h01 * 64:h01 * 64 + 64,
                              s_ * SEQ:(s_ + 1) * SEQ],
                in_=iob[su][:, :]), "qp", su)
    blk.gpsimd(f_gp)

    def f_pe(t):
        ctx.w(t, "ve", ve_m[("wscale",)])
        for hp in range(NCH):
            for (kind, wsb) in (("q2", wq), ("k2", wk), ("v2", wv)):
                for tg in range(NTG):
                    if tg >= 2:
                        ctx.w(t, "ve", ve_m[(kind + "e", hp, tg - 2)])
                    elif kind == "q2" and hp >= 1:
                        ctx.w(t, "ve", ve_m[("vpb", hp - 1, 3)])
                    elif (hp, kind) != (0, "q2"):
                        prevkind = {"q2": "v2", "k2": "q2", "v2": "k2"}[kind]
                        ctx.w(t, "ve", ve_m[(prevkind + "e", hp,
                                             NTG - 2 + tg)])
                    for c in range(NCH):
                        i = t.matmul(pS[tg % 2][:, :],
                                     wsb[:, c * D + hp * 128:
                                         c * D + hp * 128 + 128],
                                     xnT[:, c * T + tg * 512:
                                         c * T + tg * 512 + 512],
                                     start=(c == 0), stop=(c == NCH - 1))
                    ctx.inc(i, "pe")
            for sh4 in range(4):
                su = sh4 % 2
                s_, h01 = sh4 // 2, sh4 % 2
                ctx.w(t, "ve", ve_m[("v2e", hp, NTG - 1)])
                if 4 * hp + sh4 >= 1:
                    pd = 4 * hp + sh4 - 1
                    ctx.w(t, "ve", ve_m[("vpb", pd // 4, pd % 4)])
                for j in range(16):
                    i = t.matmul(pS[j // 8][:, (j % 8) * 64:(j % 8) * 64 + 64],
                                 k2s[h01 * 64:h01 * 64 + 64,
                                     s_ * SEQ + j * 128:
                                     s_ * SEQ + j * 128 + 128],
                                 lkT2[h01 * 64:h01 * 64 + 64, :],
                                 start=True, stop=True)
                ctx.inc(i, "pe")
                ctx.w(t, "ve", ve_m[("knmb", hp, sh4)])
                for j in range(16):
                    i = t.transpose(
                        pT[j // 8][0:64, (j % 8) * 128:(j % 8) * 128 + 128],
                        sqb[:, j * 64:(j + 1) * 64], ident_bf[:, :])
                ctx.inc(i, "pe")
                ctx.w(t, "ve", ve_m[("skb", hp, sh4)])
                ctx.inc(t.transpose(pT[0][0:1, 0:128], ncb[:, 0:1],
                                    ident_bf[:, :]), "pe")
                ctx.w(t, "ve", ve_m[("mnbf", hp, sh4)])
                ctx.inc(t.matmul(pbc[:, 2:3], onesr[:, :], scab[0:1, 4:5],
                                 start=True, stop=True), "pe")
                ctx.w(t, "ve", ve_m[("mnrep", hp, sh4)])
                for j in range(16):
                    i = t.matmul(pS[j // 8][:, (j % 8) * 64:(j % 8) * 64 + 64],
                                 v2s[h01 * 64:h01 * 64 + 64,
                                     s_ * SEQ + j * 128:
                                     s_ * SEQ + j * 128 + 128],
                                 lvT2[h01 * 64:h01 * 64 + 64, :],
                                 start=True, stop=True)
                ctx.inc(i, "pe")
        for it in range(3):
            ctx.w(t, "ve", ve_m[("bmath", it)])
            ctx.inc(t.matmul(pbc[:, 0:1], onesr[:, :], scab[0:1, 0:1],
                             start=True, stop=True), "pe")
            if it >= 1:
                ctx.inc(t.matmul(pbc[:, 1:2], onesr[:, :], scab[0:1, 1:2],
                                 start=True, stop=True), "pe")
                ctx.w(t, "ve", ve_m[("kmb", it)])
                ctx.inc(t.transpose(pT[0][0:16, 0:128], kmb[:, :],
                                    ident_bf[:, :]), "pe")
            if it == 2:
                i = t.matmul(pbc[0:64, 0:1], onesr[:, 0:64], scab[0:1, 2:3],
                             start=True, stop=True)
                i = t.matmul(pbc[0:64, 1:2], onesr[:, 0:64], scab[0:1, 3:4],
                             start=True, stop=True)
                ctx.inc(i, "pe")
            for sh in range(NSH):
                su = sh % 2
                ctx.slot_w(t, "qd", su, pl[("ld", it, sh)] // 16)
                if it == 2:
                    ctx.w(t, "ve", ve_m[("qfix", it, sh)])
                p1 = prevk(it, sh, 1)
                if p1 is not None:
                    ctx.w(t, "ve", ve_m[("ion",) + p1 + (3,)])
                ctx.w(t, "ve", ve_m[("lncts", it)])
                for j in range(16):
                    i = t.matmul(pS[j // 8][:, (j % 8) * 64:(j % 8) * 64 + 64],
                                 qcb[su][:, j * 128:(j + 1) * 128],
                                 lqT2[0:64, :], start=True, stop=True)
                ctx.inc(i, "pe")
                ctx.w(t, "ve", ve_m[("sq2", "lnq", it, sh)])
                for j in range(16):
                    i = t.transpose(
                        pT[j // 8][0:64, (j % 8) * 128:(j % 8) * 128 + 128],
                        sqb[:, j * 64:(j + 1) * 64], ident_bf[:, :])
                ctx.inc(i, "pe")
                ctx.w(t, "ve", ve_m[("sqTe", it, sh)])
                for j in range(16):
                    if j >= 2:
                        ctx.w(t, "ve", ve_m[("nsr", it, sh, j - 2)])
                    rhs = skb[su][0:64, :].rearrange(
                        "p (a b) -> p a b", b=4)[:, 0:16, 0:1]
                    i = t.matmul(pS[j % 2][:, 0:16],
                                 sqT[su][0:64, j * 128:(j + 1) * 128],
                                 rhs, start=True, stop=True)
                    ctx.inc(i, "pe")
                ctx.w(t, "ve", ve_m[("ncb", it, sh)])
                ctx.inc(t.transpose(pT[0][0:16, 0:128], ncb[:, :],
                                    ident_bf[:, :]), "pe")
                ctx.slot_w(t, "qn", su, pl[("ncrow", it, sh)] // 16)
                for qg in range(4):
                    for kc in range(16):
                        if kc >= 2:
                            ctx.w(t, "se", se_pos[("exp", it, sh, qg, kc - 2)])
                        i = t.matmul(pS[kc % 2][:, :],
                                     skb[su][:, kc * 128:(kc + 1) * 128],
                                     sqT[su][:, qg * 512:(qg + 1) * 512],
                                     start=True, stop=True)
                        ctx.inc(i, "pe")
                        if kc == 0 and qg > 0:
                            ctx.w(t, "ve", ve_m[("ion", it, sh, qg - 1)])
                        if it >= 1 and kc >= 12:
                            ctx.w(t, "ve", ve_m[("km", it, sh, qg, kc)])
                        else:
                            ctx.w(t, "se", se_pos[("exp", it, sh, qg, kc)])
                        i = t.matmul(ppv[:, :],
                                     vpb[su][:, kc * 65:(kc + 1) * 65],
                                     pbuf[kc % 3][:, :],
                                     start=(kc == 0), stop=(kc == 15))
                        ctx.inc(i, "pe")
                    ctx.w(t, "ve", ve_m[("rq", it, sh, qg)])
                    ctx.inc(t.matmul(pbc[0:64, :], r65[64:65, RON:RON + 64],
                                     r65[64:65, RAL:RAL + 512],
                                     start=True, stop=True), "pe")
        ctx.w(t, "qB", qa_pos[("stagef",)])
        ctx.w(t, "qw", wo_loaded[0])
        ctx.w(t, "ve", ve_m[("ion", 2, NSH - 1, 3)])
        ctx.w(t, "se", se_pos[("exp", 2, NSH - 1, 3, 15)])
        for j in range(NT):
            jm = j % 2
            pfo = pS if jm == 0 else pF
            if j >= 2:
                ctx.w(t, "ve", ve_m[("comb", j - 2)])
            for half in range(2):
                for c in range(NCH):
                    i = t.matmul(pfo[half][:, :],
                                 xnT[:, c * T + j * 128: c * T + j * 128 + 128],
                                 wo[:, c * D + half * 512:
                                    c * D + half * 512 + 512],
                                 start=(c == 0), stop=(c == NCH - 1))
            ctx.inc(i, "pe")
    blk.tensor(f_pe)


def build_program(phases):
    from contextlib import ExitStack
    nc = bass.Bass("TRN2", target_bir_lowering=False, debug=False)

    dr = {}
    def din(name, shape):
        dr[name] = nc.dram_tensor(name, shape, F32, kind="ExternalInput").ap()
        return dr[name]

    x_in = din("x_in", [T, D])
    r_out = nc.dram_tensor("r_out", [T, D], F32, kind="ExternalOutput").ap()

    need = set(phases)
    if "l0ffn_a" in need or "l0ffn_b" in need:
        din("wr_t", [D, D]); din("wk_t", [D, D]); din("wv_t", [D, D])
        din("mu_r2", [128, NCH]); din("mu_k2", [128, NCH])
    if "l0attn" in need:
        din("wqkv_t", [D, 3 * D]); din("wproj_t", [D, D])
    if "l1ffn_a" in need or "l1ffn_b" in need:
        din("m_wr", [D, 8]); din("m_br", [8]); din("m_b1", [128, 128])
        din("m_b2", [8, D]); din("m_w1", [8, D, 2048]); din("m_w2", [8, 2048, D])
    if "l1attn" in need:
        for nm in ("l1_wq_t", "l1_wk_t", "l1_wv_t", "l1_wo_t"):
            din(nm, [D, D])
        for nm in ("l1_lq_t", "l1_lk_t", "l1_lv_t"):
            din(nm, [64, 64])
        for nm in ("l1_lnc64", "l1_lnd64", "l1_lqb64", "l1_lkb64", "l1_lvb64"):
            din(nm, [128, 64])
        din("l1_borep", [128, D])
        for nm in ("l1_lna2", "l1_lnb2", "l1_bq2", "l1_bv2"):
            din(nm, [128, NCH])
        din("l1_scal", [1, 4]); din("iota2d", [128, 16])
    din("ident", [128, 128])
    dscr = {}
    if "l0attn" in need:
        dscr["attn_park"] = nc.dram_tensor("attn_park", [NCH, 128, T], BF16,
                                           kind="Internal").ap()
        dscr["negc_d"] = nc.dram_tensor("negc_d", [2, 16, 128], BF16,
                                        kind="Internal").ap()
        dscr["vsum_d"] = nc.dram_tensor("vsum_d", [2, 1, 130], F32,
                                        kind="Internal").ap()
    if "l1ffn_a" in need or "l1ffn_b" in need:
        dscr["xnpark"] = nc.dram_tensor("xnpark", [NCH, 128, T], BF16,
                                        kind="Internal").ap()
        dscr["g_d"] = nc.dram_tensor("g_d", [8, T], BF16,
                                     kind="Internal").ap()
    if "l1attn" in need:
        if "attn_park" not in dscr:
            dscr["attn_park"] = nc.dram_tensor("attn_park", [NCH, 128, T],
                                               BF16, kind="Internal").ap()
            dscr["negc_d"] = nc.dram_tensor("negc_d", [2, 16, 128], BF16,
                                            kind="Internal").ap()
        dscr["qcur_d"] = nc.dram_tensor("qcur_d", [32, 64, SEQ], BF16,
                                        kind="Internal").ap()
        dscr["sk_d"] = nc.dram_tensor("sk_d", [32, 65, SEQ], BF16,
                                      kind="Internal").ap()
        dscr["vp_d"] = nc.dram_tensor("vp_d", [32, 128, 16 * 65], BF16,
                                      kind="Internal").ap()
        dscr["io_d"] = nc.dram_tensor("io_d", [32, 64, SEQ], BF16,
                                      kind="Internal").ap()
        dscr["io1_d"] = nc.dram_tensor("io1_d", [32, 64, SEQ], BF16,
                                       kind="Internal").ap()
        dscr["din_d"] = nc.dram_tensor("din_d", [1, 8], F32,
                                       kind="Internal").ap()
        dscr["dout_d"] = nc.dram_tensor("dout_d", [1, 8], F32,
                                        kind="Internal").ap()
        dscr["qm_d"] = nc.dram_tensor("qm_d", [16, 128], BF16,
                                      kind="Internal").ap()
        dscr["dcol_d"] = nc.dram_tensor("dcol_d", [64, 1], F32,
                                        kind="Internal").ap()

    es_outer = ExitStack()
    with es_outer as es:
        ctx = Ctx(nc)
        for s_ in ("qw", "qc", "qA", "ve", "se", "pe", "gp", "cc",
                   "qr0", "qr1", "qi0", "qi1", "qo0", "qo1",
                   "qp0", "qp1", "qn0", "qn1", "qv0", "qv1",
                   "qf0", "qf1", "qm0", "qm1", "qx0", "qx1", "qy0", "qy1",
                   "qz0", "qz1", "qd0", "qd1", "qe0", "qe1", "qB",
                   "qu0", "qu1"):
            ctx.add_sem(es, s_)

        ident_bf = es.enter_context(nc.sbuf_tensor("identb", [128, 128], BF16))
        onesb = es.enter_context(nc.sbuf_tensor("onesb", [128, 1], BF16))
        ln = {}
        for nm, shape, dt in (
            ("rt0", [128, D], F32), ("rt1", [128, D], F32),
            ("xn0", [128, D], BF16), ("xn1", [128, D], BF16),
            ("s", [128, 1], F32), ("negm", [128, 1], F32), ("ssq", [128, 1], F32),
            ("sd", [128, 1], F32), ("rstd", [128, 1], F32),
            ("eps", [128, 1], F32),
        ):
            ln[nm] = es.enter_context(nc.sbuf_tensor("ln_" + nm, shape, dt))
        pT = [es.enter_context(nc.psum_tensor(f"pT{i}", [128, 1024], BF16))
              for i in range(2)]

        with nc.Block() as blk:
            def f_const(g):
                ctx.inc(g.dma_start(out=ident_bf[:, :], in_=dr["ident"]), "qc", 16)
                ctx.inc(g.memset(ln["eps"][:, :], LN_EPS), "gp")
                ctx.inc(g.memset(onesb[:, :], 1.0), "gp")
            blk.gpsimd(f_const)
            ctx.setup = {"qc": ctx.cnt["qc"], "gp": ctx.cnt["gp"]}

            def f_cw(t):
                ctx.w(t, "qc", ctx.setup["qc"])
                ctx.w(t, "gp", ctx.setup["gp"])
            blk.tensor(f_cw)

            src = x_in
            psem = "qo"
            for ph in phases:
                if ph.startswith("l0ffn"):
                    tag = ph[-1]
                    with ExitStack() as es_ph:
                        emit_l0ffn(nc, ctx, blk, es_ph, src, r_out,
                                   ln, ident_bf, pT,
                                   {k: dr[k] for k in
                                    ("wr_t", "wk_t", "wv_t", "mu_r2", "mu_k2")}, tag)
                    src = r_out
                elif ph.startswith("l1ffn"):
                    tag = ph[-1]
                    with ExitStack() as es_ph:
                        emit_l1ffn(nc, ctx, blk, es_ph, src, r_out,
                                   ident_bf, ln, pT,
                                   {k: dr[k] for k in
                                    ("m_wr", "m_br", "m_b1", "m_b2",
                                     "m_w1", "m_w2")}, dscr, tag, pqo=psem)
                    src = r_out
                    psem = "qu"
                elif ph == "l1attn":
                    with ExitStack() as es_ph:
                        emit_l1attn(nc, ctx, blk, es_ph, src, r_out,
                                    ln, ident_bf, onesb, pT, psem,
                                    {kk: dr[kk] for kk in dr
                                     if kk.startswith(("l1_", "iota"))},
                                    dscr, "y")
                    src = r_out
                    psem = "qu"
                elif ph == "l0attn":
                    with ExitStack() as es_ph:
                        emit_l0attn(nc, ctx, blk, es_ph, src, r_out,
                                    ln, ident_bf, onesb, pT,
                                    {k: dr[k] for k in ("wqkv_t", "wproj_t")},
                                    dscr, "x")
                    src = r_out
                else:
                    raise ValueError(ph)

            def f_tail(sync):
                ctx.wall(sync, "qo0")
                ctx.wall(sync, "qo1")
                ctx.wall(sync, "qu0")
                ctx.wall(sync, "qu1")
            blk.sync(f_tail)

    names = [t for t in dr]
    return nc, names


def _prep_shared(inputs):
    f32 = np.float32
    t = lambda a: np.ascontiguousarray(np.asarray(a).T).astype(f32, copy=False)
    out = {
        "wr_t": t(inputs["L0_ffn_W_r"]),
        "wk_t": t(inputs["L0_ffn_W_k"]),
        "wv_t": t(inputs["L0_ffn_W_v"]),
        "mu_r2": np.ascontiguousarray(
            np.asarray(inputs["L0_ffn_mu_r"], dtype=f32).reshape(NCH, 128).T),
        "mu_k2": np.ascontiguousarray(
            np.asarray(inputs["L0_ffn_mu_k"], dtype=f32).reshape(NCH, 128).T),
        "wqkv_t": t(inputs["L0_attn_w_qkv"]),
        "wproj_t": t(inputs["L0_attn_w_proj"]),
        "m_wr": np.ascontiguousarray(np.asarray(inputs["L1_ffn_Wr"], dtype=f32)),
        "m_br": np.ascontiguousarray(np.asarray(inputs["L1_ffn_br"], dtype=f32)),
        "m_b1": np.ascontiguousarray(
            np.asarray(inputs["L1_ffn_b1"], dtype=f32).reshape(8, 16, 128)
            .transpose(2, 0, 1).reshape(128, 128)),
        "m_b2": np.ascontiguousarray(np.asarray(inputs["L1_ffn_b2"], dtype=f32)),
        "m_w1": np.ascontiguousarray(np.asarray(inputs["L1_ffn_W1"], dtype=f32)),
        "m_w2": np.ascontiguousarray(np.asarray(inputs["L1_ffn_W2"], dtype=f32)),
        "l1_wq_t": t(inputs["L1_attn_Wq"]),
        "l1_wk_t": t(inputs["L1_attn_Wk"]),
        "l1_wv_t": t(inputs["L1_attn_Wv"]),
        "l1_wo_t": t(inputs["L1_attn_Wo"]),
        "l1_lq_t": t(inputs["L1_attn_lq_w"]),
        "l1_lk_t": t(inputs["L1_attn_lk_w"]),
        "l1_lv_t": t(inputs["L1_attn_lv_w"]),
        "l1_lnc64": np.ascontiguousarray(np.broadcast_to(
            np.asarray(inputs["L1_attn_lnc_w"], f32)[None, :], (128, 64))),
        "l1_lnd64": np.ascontiguousarray(np.broadcast_to(
            np.asarray(inputs["L1_attn_lnd_w"], f32)[None, :], (128, 64))),
        "l1_lqb64": np.ascontiguousarray(np.broadcast_to(
            np.asarray(inputs["L1_attn_lq_b"], f32)[None, :], (128, 64))),
        "l1_lkb64": np.ascontiguousarray(np.broadcast_to(
            np.asarray(inputs["L1_attn_lk_b"], f32)[None, :], (128, 64))),
        "l1_lvb64": np.ascontiguousarray(np.broadcast_to(
            np.asarray(inputs["L1_attn_lv_b"], f32)[None, :], (128, 64))),
        "l1_borep": np.ascontiguousarray(np.broadcast_to(
            np.asarray(inputs["L1_attn_bo"], f32)[None, :], (128, D))),
        "l1_lna2": np.ascontiguousarray(
            np.asarray(inputs["L1_attn_lna_w"], f32).reshape(NCH, 128).T),
        "l1_lnb2": np.ascontiguousarray(
            np.asarray(inputs["L1_attn_lnb_w"], f32).reshape(NCH, 128).T),
        "l1_bq2": np.ascontiguousarray(
            np.asarray(inputs["L1_attn_bq"], f32).reshape(NCH, 128).T),
        "l1_bv2": np.ascontiguousarray(
            np.asarray(inputs["L1_attn_bv"], f32).reshape(NCH, 128).T),
        "l1_scal": np.array([[np.float32(inputs["L1_attn_threshold"]),
                              np.float32(inputs["L1_attn_temp_base"]),
                              np.float32(inputs["L1_attn_factor"]), 0.0]],
                            dtype=f32),
        "iota2d": np.ascontiguousarray(
            (np.arange(128, dtype=f32)[:, None]
             + 128.0 * np.arange(16, dtype=f32)[None, :])),
        "ident": np.eye(128, dtype=f32),
    }
    return out


def kernel(**inputs):
    x = np.asarray(inputs["L0_attn_x"], dtype=np.float32)
    shared = _prep_shared(inputs)

    nc, names = build_program(["l0ffn_a", "l0attn", "l0ffn_b",
                               "l1ffn_a", "l1attn", "l1ffn_b"])
    in_maps = []
    for c in range(NCORES):
        m = {k: shared[k] for k in names if k in shared}
        m["x_in"] = np.ascontiguousarray(
            x[c * NSEQ:(c + 1) * NSEQ].reshape(T, D))
        in_maps.append(m)
    res = run_bass_kernel_spmd(nc, in_maps, core_ids=list(range(NCORES)))
    out = np.concatenate(
        [res.results[c]["r_out"].reshape(NSEQ, SEQ, D) for c in range(NCORES)],
        axis=0)
    return out


# version 02
# speedup vs baseline: 1.2393x; 1.2393x over previous
import math
import numpy as np
import ml_dtypes
import concourse.bass as bass
import concourse.mybir as mybir
from concourse.bass_utils import run_bass_kernel_spmd

F32 = mybir.dt.float32
BF16 = mybir.dt.bfloat16
AF = mybir.ActivationFunctionType
ALU = mybir.AluOpType
AX = mybir.AxisListType

NCORES = 8
SEQ = 2048
NSEQ = 2
T = NSEQ * SEQ
NT = T // 128
NT_SEQ = SEQ // 128
D = 1024
NCH = D // 128
NTG = T // 512
H = 16
HD = 64
LN_EPS = 1e-5



DMA_SEMS = ("qr0", "qr1", "qi0", "qi1", "qo0", "qo1", "qp0", "qp1",
            "qn0", "qn1", "qv0", "qv1", "qf0", "qf1", "qm0", "qm1",
            "qx0", "qx1", "qy0", "qy1", "qz0", "qz1", "qd0", "qd1",
            "qe0", "qe1", "qu0", "qu1", "qA", "qB", "qw", "qc")


def phase_fence(ctx):
    return {k: ctx.cnt[k] for k in DMA_SEMS + ("ve", "se", "pe", "gp")
            if k in ctx.cnt}


def fence_wait(ctx, eng, pv):
    for k, val in pv.items():
        if val:
            ctx.w(eng, k, val)


class Ctx:

    def __init__(self, nc):
        self.nc = nc
        self.sem = {}
        self.cnt = {}

    def add_sem(self, es, name):
        self.sem[name] = es.enter_context(self.nc.semaphore(name))
        self.cnt[name] = 0

    def inc(self, instr, name, n=1):
        instr.then_inc(self.sem[name], n)
        self.cnt[name] += n
        return self.cnt[name]

    def w(self, eng, name, value):
        eng.wait_ge(self.sem[name], value)

    def wall(self, eng, name):
        eng.wait_ge(self.sem[name], self.cnt[name])

    def slot_inc(self, instr, name, slot):
        return self.inc(instr, f"{name}{slot}", 16)

    def slot_w(self, eng, name, slot, k):
        eng.wait_ge(self.sem[f"{name}{slot}"], 16 * k)


def emit_ln_transpose(nc, ctx, blk, src_dram, xnT, ln, ident_bf, pT,
                      dest_of_tile=None, dest_war=None, pqo="qo"):
    b = {k: ctx.cnt[k] for k in ("ve", "se", "pe")}
    b["qr0"] = ctx.cnt["qr0"]; b["qr1"] = ctx.cnt["qr1"]
    qo_prev = (ctx.cnt[pqo + "0"], ctx.cnt[pqo + "1"])

    def f_sync(sync):
        ctx.w(sync, pqo + "0", qo_prev[0])
        ctx.w(sync, pqo + "1", qo_prev[1])
        for j in range(NT):
            if j >= 2:
                ctx.w(sync, "ve", b["ve"] + (j - 2) * 5 + 5)
            ctx.slot_inc(sync.dma_start(out=ln[f"rt{j % 2}"][:, :],
                                        in_=src_dram[j * 128:(j + 1) * 128, :]),
                         "qr", j % 2)
    blk.sync(f_sync)

    def f_vec(v):
        for j in range(NT):
            rt = ln[f"rt{j % 2}"]
            xn = ln[f"xn{j % 2}"]
            ctx.slot_w(v, "qr", j % 2, b[f"qr{j % 2}"] // 16 + j // 2 + 1)
            ctx.inc(v.reduce_sum(ln["s"][:, :], rt[:, :], axis=AX.X), "ve")
            ctx.wall(v, "ve")
            ctx.inc(v.tensor_scalar(ln["negm"][:, :], ln["s"][:, :], -1.0 / D, None,
                                    ALU.mult), "ve")
            ctx.w(v, "se", b["se"] + (j + 1) * 2)
            ctx.inc(v.reciprocal(ln["rstd"][:, :], ln["sd"][:, :]), "ve")
            ctx.wall(v, "ve")
            if j >= 2:
                ctx.w(v, "pe", b["pe"] + j - 1)
            ctx.inc(v.tensor_scalar(xn[:, :], rt[:, :], ln["negm"][:, :],
                                    ln["rstd"][:, :], ALU.add, ALU.mult), "ve")
            ctx.w(v, "pe", b["pe"] + j + 1)
            if dest_war is not None:
                dest_war(v, j)
            dst = (xnT.rearrange("p (c t) -> p c t", c=NCH)[:, :,
                                 j * 128:(j + 1) * 128]
                   if dest_of_tile is None else dest_of_tile(j))
            ctx.inc(v.tensor_copy(
                dst, pT[j % 2].rearrange("p (c t) -> p c t", c=NCH)[:, :, :]),
                "ve")
    blk.vector(f_vec)

    def f_scal(sc):
        ctx.w(sc, "gp", ctx.setup["gp"])
        for j in range(NT):
            rt = ln[f"rt{j % 2}"]
            ctx.w(sc, "ve", b["ve"] + j * 5 + 2)
            ctx.inc(sc.activation(ln[f"xn{j % 2}"][:, :], rt[:, :], AF.Square,
                                  bias=ln["negm"][:, :], scale=1.0,
                                  accum_out=ln["ssq"][:, :]), "se")
            ctx.wall(sc, "se")
            ctx.inc(sc.activation(ln["sd"][:, :], ln["ssq"][:, :], AF.Sqrt,
                                  bias=ln["eps"][:, :], scale=1.0 / D), "se")
    blk.scalar(f_scal)

    def f_pe(t):
        for j in range(NT):
            xn = ln[f"xn{j % 2}"]
            ctx.w(t, "ve", b["ve"] + j * 5 + 4)
            if j >= 2:
                ctx.w(t, "ve", b["ve"] + (j - 2) * 5 + 5)
            for c in range(NCH):
                i = t.transpose(pT[j % 2][:, c * 128:(c + 1) * 128],
                                xn[:, c * 128:(c + 1) * 128], ident_bf[:, :])
            ctx.inc(i, "pe")
    blk.tensor(f_pe)


def emit_l0ffn(nc, ctx, blk, es, R_in, R_out, ln, ident_bf, pT, wdr, tag):
    E = es.enter_context
    xnT = E(nc.sbuf_tensor(f"f_xnT{tag}", [128, NCH * T], BF16))
    emit_ln_transpose(nc, ctx, blk, R_in, xnT, ln, ident_bf, pT)
    wr = E(nc.sbuf_tensor(f"f_wr{tag}", [128, NCH * D], BF16))
    wk = E(nc.sbuf_tensor(f"f_wk{tag}", [128, NCH * D], BF16))
    wv = E(nc.sbuf_tensor(f"f_wv{tag}", [128, NCH * D], BF16))
    mur = E(nc.sbuf_tensor(f"f_mur{tag}", [128, NCH], F32))
    muk = E(nc.sbuf_tensor(f"f_muk{tag}", [128, NCH], F32))
    omur = E(nc.sbuf_tensor(f"f_omur{tag}", [128, NCH], F32))
    omuk = E(nc.sbuf_tensor(f"f_omuk{tag}", [128, NCH], F32))
    xrg = [E(nc.sbuf_tensor(f"f_xrg{i}{tag}", [128, NCH * 512], BF16)) for i in range(2)]
    xkg = [E(nc.sbuf_tensor(f"f_xkg{i}{tag}", [128, NCH * 512], BF16)) for i in range(2)]
    kag = [E(nc.sbuf_tensor(f"f_kag{i}{tag}", [128, NCH * 512], BF16)) for i in range(2)]
    tmix = E(nc.sbuf_tensor(f"f_tmix{tag}", [128, 512], F32))
    kscr = [E(nc.sbuf_tensor(f"f_kscr{i}{tag}", [128, 512], BF16)) for i in range(2)]
    rsig = [E(nc.sbuf_tensor(f"f_rsig{i}{tag}", [128, D], BF16)) for i in range(2)]
    rvt = [E(nc.sbuf_tensor(f"f_rvt{i}{tag}", [128, 512], F32)) for i in range(2)]
    rin = [E(nc.sbuf_tensor(f"f_rin{i}{tag}", [128, D], F32)) for i in range(2)]
    pr = [E(nc.psum_tensor(f"f_pr{i}{tag}", [128, 512], F32)) for i in range(4)]
    pk = [E(nc.psum_tensor(f"f_pk{i}{tag}", [128, 512], F32)) for i in range(2)]

    def f_gw(g):
        ctx.wall(g, "qw")
        for (wsb, wd) in ((wr, wdr["wr_t"]), (wk, wdr["wk_t"]), (wv, wdr["wv_t"])):
            for c in range(NCH):
                ctx.inc(g.dma_start(out=wsb[:, c * D:(c + 1) * D],
                                    in_=wd[c * 128:(c + 1) * 128, :]), "qw", 16)
        ctx.inc(g.dma_start(out=mur[:, :], in_=wdr["mu_r2"]), "qw", 16)
        ctx.inc(g.dma_start(out=muk[:, :], in_=wdr["mu_k2"]), "qw", 16)
    blk.gpsimd(f_gw)
    w_loaded = ctx.cnt["qw"]

    pe0 = ctx.cnt["pe"]
    qi0 = (ctx.cnt["qi0"], ctx.cnt["qi1"])
    qo0 = (ctx.cnt["qo0"], ctx.cnt["qo1"])

    R_POS = [0, 1, 3, 5]
    V_POS = [2, 4, 6, 7]

    def pe_r_done(tg, jj):
        return pe0 + tg * 16 + 8 + R_POS[jj] + 1

    def pe_v_done(tg, jj):
        return pe0 + tg * 16 + 8 + V_POS[jj] + 1

    mix_done = {}
    comb_done = {}

    def f_vec(v):
        ctx.wall(v, "ve")
        ctx.w(v, "qw", w_loaded)
        ctx.inc(v.tensor_scalar(omur[:, :], mur[:, :], -1.0, 1.0, ALU.mult, ALU.add), "ve")
        ctx.inc(v.tensor_scalar(omuk[:, :], muk[:, :], -1.0, 1.0, ALU.mult, ALU.add), "ve")
        ctx.wall(v, "ve")

        def emit_mix(tg):
            if tg >= 2:
                ctx.w(v, "pe", pe0 + (tg - 2) * 16 + 16)
            for (dst, mu, omu) in ((xkg[tg % 2], muk, omuk), (xrg[tg % 2], mur, omur)):
                for c in range(NCH):
                    src = xnT[:, c * T + tg * 512: c * T + tg * 512 + 512]
                    d = dst[:, c * 512:(c + 1) * 512]
                    if tg % (SEQ // 512) == 0:
                        ctx.inc(v.tensor_scalar(
                            tmix[:, 1:], xnT[:, c * T + tg * 512: c * T + tg * 512 + 511],
                            omu[:, c:c + 1], None, ALU.mult), "ve")
                        ctx.inc(v.memset(tmix[:, 0:1], 0.0), "ve")
                    else:
                        ctx.inc(v.tensor_scalar(
                            tmix[:, :], xnT[:, c * T + tg * 512 - 1: c * T + tg * 512 + 511],
                            omu[:, c:c + 1], None, ALU.mult), "ve")
                    ctx.wall(v, "ve")
                    ctx.inc(v.scalar_tensor_tensor(d, src, mu[:, c:c + 1], tmix[:, :],
                                                   ALU.mult, ALU.add), "ve")
                    ctx.wall(v, "ve")
            mix_done[tg] = ctx.cnt["ve"]

        def emit_comb(tg):
            for jj in range(4):
                j = tg * 4 + jj
                jm = j % 2
                ctx.w(v, "pe", pe_v_done(tg, jj))
                ctx.slot_w(v, "qi", jm, qi0[jm] // 16 + j // 2 + 1)
                for half in range(2):
                    ctx.inc(v.tensor_tensor(rvt[half][:, :],
                                            rsig[jm][:, half * 512:half * 512 + 512],
                                            pr[2 * jm + half][:, :], ALU.mult), "ve")
                ctx.wall(v, "ve")
                for half in range(2):
                    ctx.inc(v.scalar_tensor_tensor(
                        rin[jm][:, half * 512:half * 512 + 512], rvt[half][:, :], 0.5,
                        rin[jm][:, half * 512:half * 512 + 512], ALU.mult, ALU.add), "ve")
                ctx.wall(v, "ve")
                comb_done[j] = ctx.cnt["ve"]

        emit_mix(0)
        emit_mix(1)
        for tg in range(NTG):
            emit_comb(tg)
            if tg + 2 < NTG:
                emit_mix(tg + 2)
    blk.vector(f_vec)

    se0 = ctx.cnt["se"]

    def se_ka(tg, dn):
        return se0 + tg * 24 + (dn + 1) * 2

    def se_sig(tg, jj):
        return se0 + tg * 24 + 16 + (jj + 1) * 2

    def f_pe(t):
        for tg in range(NTG):
            ctx.w(t, "ve", mix_done[tg])
            for dn in range(NCH):
                gi = tg * NCH + dn
                bnk = pk[gi % 2]
                if gi >= 2:
                    tg2, dn2 = divmod(gi - 2, NCH)
                    ctx.w(t, "se", se_ka(tg2, dn2))
                for c in range(NCH):
                    i = t.matmul(bnk[:, :],
                                 wk[:, c * D + dn * 128: c * D + dn * 128 + 128],
                                 xkg[tg % 2][:, c * 512: c * 512 + 512],
                                 start=(c == 0), stop=(c == NCH - 1))
                ctx.inc(i, "pe")
            def emit_rmm(jj):
                j = tg * 4 + jj
                jm = j % 2
                if j >= 2:
                    ctx.w(t, "ve", comb_done[j - 2])
                for half in range(2):
                    for c in range(NCH):
                        i = t.matmul(pr[2 * jm + half][:, :],
                                     xrg[tg % 2][:, c * 512 + jj * 128: c * 512 + jj * 128 + 128],
                                     wr[:, c * D + half * 512: c * D + half * 512 + 512],
                                     start=(c == 0), stop=(c == NCH - 1))
                ctx.inc(i, "pe")

            def emit_vmm(jj):
                j = tg * 4 + jj
                jm = j % 2
                ctx.w(t, "se", se_sig(tg, jj))
                for half in range(2):
                    for dn in range(NCH):
                        i = t.matmul(pr[2 * jm + half][:, :],
                                     kag[tg % 2][:, dn * 512 + jj * 128: dn * 512 + jj * 128 + 128],
                                     wv[:, dn * D + half * 512: dn * D + half * 512 + 512],
                                     start=(dn == 0), stop=(dn == NCH - 1))
                ctx.inc(i, "pe")

            emit_rmm(0); emit_rmm(1); emit_vmm(0); emit_rmm(2)
            emit_vmm(1); emit_rmm(3); emit_vmm(2); emit_vmm(3)
    blk.tensor(f_pe)

    def f_scal(sc):
        for tg in range(NTG):
            for dn in range(NCH):
                gi = tg * NCH + dn
                bnk = pk[gi % 2]
                ctx.w(sc, "pe", pe0 + tg * 16 + dn + 1)
                if gi >= 2:
                    ctx.w(sc, "se", se0 + ((gi - 2) // NCH) * 24 + ((gi - 2) % NCH + 1) * 2)
                ctx.inc(sc.activation(kscr[gi % 2][:, :], bnk[:, :], AF.Relu), "se")
                ctx.wall(sc, "se")
                if tg >= 2 and dn == 0:
                    ctx.w(sc, "pe", pe0 + (tg - 2) * 16 + 16)
                ctx.inc(sc.activation(kag[tg % 2][:, dn * 512:(dn + 1) * 512],
                                      kscr[gi % 2][:, :], AF.Square), "se")
            for jj in range(4):
                j = tg * 4 + jj
                jm = j % 2
                ctx.w(sc, "pe", pe_r_done(tg, jj))
                for half in range(2):
                    ctx.inc(sc.activation(rsig[jm][:, half * 512:half * 512 + 512],
                                          pr[2 * jm + half][:, :], AF.Sigmoid), "se")
    blk.scalar(f_scal)

    def f_rload(sync):
        for j in range(NT):
            jm = j % 2
            if j < 2:
                ctx.slot_w(sync, "qo", jm, qo0[jm] // 16)
            else:
                ctx.slot_w(sync, "qo", jm, qo0[jm] // 16 + (j - 2) // 2 + 1)
            ctx.slot_inc(sync.dma_start(out=rin[jm][:, :],
                                        in_=R_in[j * 128:(j + 1) * 128, :]),
                         "qi", jm)
    blk.sync(f_rload)

    def f_rstore(g):
        for j in range(NT):
            ctx.w(g, "ve", comb_done[j])
            ctx.slot_inc(g.dma_start(out=R_out[j * 128:(j + 1) * 128, :],
                                     in_=rin[j % 2][:, :]), "qo", j % 2)
    blk.gpsimd(f_rstore)


def emit_l0attn(nc, ctx, blk, es, R_in, R_out, ln, ident_bf, onesb, pT,
                wdr, dscr, tag):
    E = es.enter_context
    xnT = E(nc.sbuf_tensor(f"a_xnT{tag}", [128, NCH * T], BF16))
    emit_ln_transpose(nc, ctx, blk, R_in, xnT, ln, ident_bf, pT)
    wqkv = [E(nc.sbuf_tensor(f"a_wqkv{i}{tag}", [128, NCH * 384], BF16))
            for i in range(2)]
    wpj = E(nc.sbuf_tensor(f"a_wpj{tag}", [128, NCH * D], BF16))
    v2 = E(nc.sbuf_tensor(f"a_v2{tag}", [128, NT * 130], BF16))
    q65 = [E(nc.sbuf_tensor(f"a_q65{i}{tag}", [65, SEQ], BF16)) for i in range(2)]
    k65 = [E(nc.sbuf_tensor(f"a_k65{i}{tag}", [65, SEQ], BF16)) for i in range(2)]
    pbuf = [E(nc.sbuf_tensor(f"a_pb{i}{tag}", [128, 512], BF16)) for i in range(3)]
    pmask = E(nc.sbuf_tensor(f"a_pm{tag}", [128, 512], BF16))
    att = [E(nc.sbuf_tensor(f"a_att{i}{tag}", [64, SEQ], BF16)) for i in range(2)]
    negc = E(nc.sbuf_tensor(f"a_negc{tag}", [128, 16], F32))
    negcb = E(nc.sbuf_tensor(f"a_negcb{tag}", [128, 16], BF16))
    negcT = [E(nc.sbuf_tensor(f"a_negcT{i}{tag}", [16, 128], BF16)) for i in range(2)]
    scr65 = E(nc.sbuf_tensor(f"a_scr65{tag}", [65, 1024], F32))
    r65 = E(nc.sbuf_tensor(f"a_r65{tag}", [65, 1216], BF16))
    vs1 = E(nc.sbuf_tensor(f"a_vs1{tag}", [1, 130], F32))
    vscol = E(nc.sbuf_tensor(f"a_vscol{tag}", [64, 1], F32))
    s64 = E(nc.sbuf_tensor(f"a_s64{tag}", [64, 512], F32))
    corr2 = E(nc.sbuf_tensor(f"a_corr2{tag}", [64, 512], F32))
    arep = E(nc.sbuf_tensor(f"a_arep{tag}", [64, 512], F32))
    rin2 = [E(nc.sbuf_tensor(f"a_rin{i}{tag}", [128, D], F32)) for i in range(2)]
    pA = [E(nc.psum_tensor(f"a_pA{i}{tag}", [128, 512], F32)) for i in range(4)]
    ppv = E(nc.psum_tensor(f"a_ppv{tag}", [65, 512], F32))
    pqk = E(nc.psum_tensor(f"a_pqk{tag}", [64, 512], F32))

    attn_park = dscr["attn_park"]
    negc_d = dscr["negc_d"]
    vsum_d = dscr["vsum_d"]

    GAM, ALP, VSU, ONE = 0, 512, 1024, 1089

    pe_pos = {}

    def plan_pe():
        n = ctx.cnt["pe"]
        for hp in range(NCH):
            for tj in range(NT):
                n += 1; pe_pos[("vproj", hp, tj)] = n
            for sh in range(4):
                for tg in range(4):
                    n += 1; pe_pos[("qproj", hp, sh, tg)] = n
                for tg in range(4):
                    n += 1; pe_pos[("kproj", hp, sh, tg)] = n
                if sh % 2 == 0:
                    n += 1; pe_pos[("vsum", hp, sh)] = n
                for j in range(16):
                    n += 1; pe_pos[("sub", hp, sh, j)] = n
                n += 1; pe_pos[("negcT", hp, sh)] = n
                for qg in range(4):
                    for kc in range(4 * qg + 4):
                        n += 1; pe_pos[("S", hp, sh, qg, kc)] = n
                        n += 1; pe_pos[("PV", hp, sh, qg, kc)] = n
                    n += 1; pe_pos[("bcast_a", hp, sh, qg)] = n
                    n += 1; pe_pos[("bcast_g", hp, sh, qg)] = n
        for j in range(NT):
            n += 1; pe_pos[("proj", j)] = n
        return n

    plan_pe()
    last_kproj = pe_pos[("kproj", NCH - 1, 3, 3)]
    last_bcast = pe_pos[("bcast_g", NCH - 1, 3, 3)]

    gp_pos = {}

    def plan_gp():
        n = ctx.cnt["gp"]
        for hp in range(NCH):
            for sh in range(4):
                for qg in range(4):
                    for kc in range(4 * qg, 4 * qg + 4):
                        n += 1; gp_pos[(hp, sh, qg, kc)] = n
    plan_gp()

    def prev_shk(hp, sh, k=1):
        i = 4 * hp + sh - k
        return (i // 4, i % 4) if i >= 0 else None

    ve_m, se_m, dm_m = {}, {}, {}
    qo_prev = (ctx.cnt["qo0"], ctx.cnt["qo1"])
    qi0 = (ctx.cnt["qi0"], ctx.cnt["qi1"])
    qo0 = (ctx.cnt["qo0"], ctx.cnt["qo1"])
    qp0 = (ctx.cnt["qp0"], ctx.cnt["qp1"])
    qw_base = ctx.cnt["qw"]

    def qp_parks_before(hp, sh):
        return 2 * hp + sh // 2

    def f_vector(v):
        ctx.inc(v.memset(r65[:, ONE:ONE + 64], 1.0), "ve")
        ctx.wall(v, "ve")
        for hp in range(NCH):
            for tj in range(NT):
                ctx.w(v, "pe", pe_pos[("vproj", hp, tj)])
                ctx.inc(v.tensor_copy(v2[:, tj * 130: tj * 130 + 64],
                                      pA[2 + tj % 2][:, 0:64]), "ve")
                ctx.inc(v.tensor_copy(v2[:, tj * 130 + 65: tj * 130 + 129],
                                      pA[2 + tj % 2][:, 64:128]), "ve")
                ve_m[("v2", hp, tj)] = ctx.cnt["ve"]
            ctx.inc(v.memset(
                v2.rearrange("p (t c) -> p t c", c=130)[:, :, 64:65], 1.0), "ve")
            ctx.inc(v.memset(
                v2.rearrange("p (t c) -> p t c", c=130)[:, :, 129:130], 1.0), "ve")
            ve_m[("v2done", hp)] = ctx.cnt["ve"]
            for sh in range(4):
                su = sh % 2
                for tg in range(4):
                    ctx.w(v, "pe", pe_pos[("qproj", hp, sh, tg)])
                    ctx.inc(v.tensor_copy(q65[su][0:64, tg * 512:(tg + 1) * 512],
                                          pqk[:, :]), "ve")
                    ve_m[("qev", hp, sh, tg)] = ctx.cnt["ve"]
                for tg in range(4):
                    ctx.w(v, "pe", pe_pos[("kproj", hp, sh, tg)])
                    ctx.inc(v.tensor_copy(k65[su][0:64, tg * 512:(tg + 1) * 512],
                                          pqk[:, :]), "ve")
                    ve_m[("kev", hp, sh, tg)] = ctx.cnt["ve"]
                ctx.inc(v.memset(k65[su][64:65, :], 1.0), "ve")
                ve_m[("qk", hp, sh)] = ctx.cnt["ve"]
                if sh % 2 == 0:
                    ctx.w(v, "pe", pe_pos[("vsum", hp, sh)])
                    ctx.inc(v.tensor_copy(vs1[:, :], pA[2][0:1, 0:130]), "ve")
                    ve_m[("vs1", hp, sh)] = ctx.cnt["ve"]
                for j in range(16):
                    ctx.w(v, "pe", pe_pos[("sub", hp, sh, j)])
                    ctx.inc(v.reduce_max(negc[:, j:j + 1],
                                         pA[3 - j % 2][:, 0:8 * j + 1], axis=AX.X,
                                         negate=True), "ve")
                    ve_m[("sub", hp, sh, j)] = ctx.cnt["ve"]
                ctx.wall(v, "ve")
                p = prev_shk(hp, sh)
                if p is not None:
                    ctx.w(v, "pe", pe_pos[("negcT", p[0], p[1])])
                ctx.inc(v.tensor_copy(negcb[:, :], negc[:, :]), "ve")
                ve_m[("negcb", hp, sh)] = ctx.cnt["ve"]
                ctx.w(v, "pe", pe_pos[("negcT", hp, sh)])
                ctx.inc(v.tensor_copy(negcT[su][:, :], pT[0][0:16, 0:128]), "ve")
                ve_m[("negcT", hp, sh)] = ctx.cnt["ve"]
                for qg in range(4):
                    nk = 4 * qg + 4
                    ctx.w(v, "pe", pe_pos[("PV", hp, sh, qg, nk - 1)])
                    ctx.inc(v.tensor_scalar(scr65[64:65, 0:512], ppv[64:65, :],
                                            1.0 + 2048e-8, 2.048e-11,
                                            ALU.mult, ALU.add), "ve")
                    ctx.wall(v, "ve")
                    ctx.inc(v.reciprocal(scr65[64:65, 0:512],
                                         scr65[64:65, 0:512]), "ve")
                    ctx.wall(v, "ve")
                    ctx.inc(v.tensor_copy(r65[64:65, ALP:ALP + 512],
                                          scr65[64:65, 0:512]), "ve")
                    ctx.inc(v.tensor_scalar(scr65[64:65, 512:1024], ppv[64:65, :],
                                            1e-8, 1e-14, ALU.mult, ALU.add), "ve")
                    ctx.wall(v, "ve")
                    ctx.inc(v.tensor_tensor(r65[64:65, GAM:GAM + 512],
                                            scr65[64:65, 512:1024],
                                            scr65[64:65, 0:512], ALU.mult), "ve")
                    ve_m[("ag", hp, sh, qg)] = ctx.cnt["ve"]
                    ctx.w(v, "pe", pe_pos[("bcast_g", hp, sh, qg)])
                    ctx.inc(v.tensor_copy(arep[:, :], pA[2][0:64, :]), "ve")
                    ctx.wall(v, "ve")
                    ctx.wall(v, "ve")
                    ctx.inc(v.tensor_scalar(corr2[:, :], pA[3][0:64, 0:512],
                                            vscol[:, :], None, ALU.mult), "ve")
                    ctx.wall(v, "ve")
                    ctx.inc(v.tensor_tensor(s64[:, :], ppv[0:64, :],
                                            arep[:, :], ALU.mult), "ve")
                    ctx.wall(v, "ve")
                    if qg == 0:
                        ctx.slot_w(v, "qp", su,
                                   qp0[su] // 16 + qp_parks_before(hp, sh))
                    ctx.inc(v.tensor_tensor(att[su][:, qg * 512:(qg + 1) * 512],
                                            s64[:, :], corr2[:, :], ALU.add),
                            "ve")
                    ve_m[("att", hp, sh, qg)] = ctx.cnt["ve"]
        for j in range(NT):
            jm = j % 2
            ctx.w(v, "pe", pe_pos[("proj", j)])
            ctx.slot_w(v, "qi", jm, qi0[jm] // 16 + j // 2 + 1)
            for half in range(2):
                ctx.inc(v.scalar_tensor_tensor(
                    rin2[jm][:, half * 512:half * 512 + 512],
                    pA[2 * jm + half][:, :], 1.0,
                    rin2[jm][:, half * 512:half * 512 + 512], ALU.mult, ALU.add),
                    "ve")
            ctx.wall(v, "ve")
            ve_m[("comb", j)] = ctx.cnt["ve"]
    blk.vector(f_vector)

    def f_scalar(sc):
        for hp in range(NCH):
            for sh in range(4):
                for qg in range(4):
                    for kc in range(4 * qg + 4):
                        ctx.w(sc, "pe", pe_pos[("S", hp, sh, qg, kc)])
                        if kc >= 3:
                            k3 = kc - 3
                            if k3 >= 4 * qg:
                                ctx.w(sc, "gp", gp_pos[(hp, sh, qg, k3)])
                            else:
                                ctx.w(sc, "pe", pe_pos[("PV", hp, sh, qg, k3)])
                        ctx.inc(sc.activation(pbuf[kc % 3][:, :],
                                              pA[kc % 2][:, :],
                                              AF.Exp, scale=0.125), "se")
                        se_m[("exp", hp, sh, qg, kc)] = ctx.cnt["se"]
    blk.scalar(f_scalar)

    def f_gp(g):
        ctx.w(g, "qw", qw_base)
        for c in range(NCH):
            ctx.inc(g.dma_start(out=wpj[:, c * D:(c + 1) * D],
                                in_=wdr["wproj_t"][c * 128:(c + 1) * 128, :]),
                    "qw", 16)

        def load_hp(hp):
            for c in range(NCH):
                for (slot, off) in ((0, 0), (1, D), (2, 2 * D)):
                    ctx.inc(g.dma_start(
                        out=wqkv[hp % 2][:, c * 384 + slot * 128:
                                         c * 384 + slot * 128 + 128],
                        in_=wdr["wqkv_t"][c * 128:(c + 1) * 128,
                                          off + hp * 128: off + hp * 128 + 128]),
                        "qw", 16)
            dm_m[("w", hp)] = ctx.cnt["qw"]

        load_hp(0)
        load_hp(1)
        last_diag = None
        for hp in range(NCH):
            for sh in range(4):
                su = sh % 2
                for qg in range(4):
                    for kc in range(4 * qg, 4 * qg + 4):
                        ctx.w(g, "se", se_m[("exp", hp, sh, qg, kc)])
                        if last_diag is not None:
                            ctx.w(g, "pe", pe_pos[("PV", *last_diag)])
                        ctx.inc(g.affine_select(
                            pmask[:, :], pbuf[kc % 3][:, :], pattern=[[1, 512]],
                            compare_op=ALU.is_ge, fill=0.0,
                            base=qg * 512 - kc * 128,
                            channel_multiplier=-1), "gp")
                        last_diag = (hp, sh, qg, kc)
                ctx.w(g, "ve", ve_m[("att", hp, sh, 3)])
                ctx.slot_inc(g.dma_start(
                    out=attn_park[hp, (sh % 2) * 64:(sh % 2) * 64 + 64,
                                  (sh // 2) * SEQ:(sh // 2 + 1) * SEQ],
                    in_=att[su][:, :]), "qp", su)
            if hp + 2 < NCH:
                ctx.w(g, "pe", pe_pos[("kproj", hp, 3, 3)])
                ctx.w(g, "qw", dm_m[("w", hp + 1)])
                load_hp(hp + 2)
        ctx.w(g, "pe", last_kproj)
        ctx.slot_w(g, "qp", 0, ctx.cnt["qp0"] // 16)
        ctx.slot_w(g, "qp", 1, ctx.cnt["qp1"] // 16)
        for c in range(NCH):
            ctx.inc(g.dma_start(out=xnT[:, c * T:(c + 1) * T],
                                in_=attn_park[c, :, :]), "qA", 16)
        dm_m[("qA",)] = ctx.cnt["qA"]
        for j in range(NT):
            ctx.w(g, "ve", ve_m[("comb", j)])
            ctx.slot_inc(g.dma_start(out=R_out[j * 128:(j + 1) * 128, :],
                                     in_=rin2[j % 2][:, :]), "qo", j % 2)
    blk.gpsimd(f_gp)

    def f_sync(sync):
        for hp in range(NCH):
            for sh in range(4):
                su = sh % 2
                s_ = sh // 2
                h01 = sh % 2
                if h01 == 0:
                    ctx.w(sync, "ve", ve_m[("vs1", hp, sh)])
                    ctx.slot_inc(sync.dma_start(out=vsum_d[s_], in_=vs1[:, :]),
                                 "qv", s_)
                ctx.slot_w(sync, "qv", s_, ctx.cnt[f"qv{s_}"] // 16)
                p = prev_shk(hp, sh)
                if p is not None:
                    ctx.w(sync, "ve", ve_m[("att", p[0], p[1], 3)])
                ctx.slot_inc(sync.dma_start(
                    out=vscol[:, :],
                    in_=vsum_d[s_][:, h01 * 65: h01 * 65 + 64].rearrange(
                        "o n -> (o n)")[:, None]), "qv", s_)
                dm_m[("vsrow", hp, sh)] = ctx.cnt[f"qv{s_}"]
                ctx.w(sync, "ve", ve_m[("negcT", hp, sh)])
                ctx.slot_inc(sync.dma_start(out=negc_d[su], in_=negcT[su][:, :]),
                             "qn", su)
                ctx.slot_w(sync, "qn", su, ctx.cnt[f"qn{su}"] // 16)
                p2 = prev_shk(hp, sh, 2)
                if p2 is not None:
                    ctx.w(sync, "pe", pe_pos[("S", p2[0], p2[1], 3, 15)])
                ctx.slot_inc(sync.dma_start(
                    out=q65[su][64:65, :],
                    in_=negc_d[su].rearrange("a b -> (a b)")[None, :]), "qn", su)
                dm_m[("negcrow", hp, sh)] = ctx.cnt[f"qn{su}"]
        for j in range(NT):
            jm = j % 2
            if j < 2:
                ctx.slot_w(sync, "qo", jm, qo_prev[jm] // 16)
            else:
                ctx.slot_w(sync, "qo", jm, qo0[jm] // 16 + (j - 2) // 2 + 1)
            ctx.slot_inc(sync.dma_start(out=rin2[jm][:, :],
                                        in_=R_in[j * 128:(j + 1) * 128, :]),
                         "qi", jm)
    blk.sync(f_sync)

    def f_pe(t):
        for hp in range(NCH):
            ctx.w(t, "qw", dm_m[("w", max(hp, 1))])
            if hp > 0:
                ctx.w(t, "ve", ve_m[("att", hp - 1, 3, 3)])
            for tj in range(NT):
                if tj >= 2:
                    ctx.w(t, "ve", ve_m[("v2", hp, tj - 2)])
                for c in range(NCH):
                    i = t.matmul(pA[2 + tj % 2][:, 0:128],
                                 xnT[:, c * T + tj * 128: c * T + tj * 128 + 128],
                                 wqkv[hp % 2][:, c * 384 + 256: c * 384 + 384],
                                 start=(c == 0), stop=(c == NCH - 1))
                ctx.inc(i, "pe")
            for sh in range(4):
                su = sh % 2
                s_ = sh // 2
                h01 = sh % 2
                scol = s_ * SEQ
                p2 = prev_shk(hp, sh, 2)
                if p2 is not None:
                    ctx.w(t, "pe", pe_pos[("S", p2[0], p2[1], 3, 15)])
                for tg in range(4):
                    if tg > 0:
                        ctx.w(t, "ve", ve_m[("qev", hp, sh, tg - 1)])
                    elif (hp, sh) != (0, 0):
                        p = prev_shk(hp, sh)
                        ctx.w(t, "ve", ve_m[("kev", p[0], p[1], 3)])
                    for c in range(NCH):
                        i = t.matmul(pqk[:, :],
                                     wqkv[hp % 2][:, c * 384 + h01 * 64:
                                                  c * 384 + h01 * 64 + 64],
                                     xnT[:, c * T + scol + tg * 512:
                                         c * T + scol + tg * 512 + 512],
                                     start=(c == 0), stop=(c == NCH - 1))
                    ctx.inc(i, "pe")
                for tg in range(4):
                    ctx.w(t, "ve", ve_m[("qev", hp, sh, 3)] if tg == 0 else
                          ve_m[("kev", hp, sh, tg - 1)])
                    for c in range(NCH):
                        i = t.matmul(pqk[:, :],
                                     wqkv[hp % 2][:, c * 384 + 128 + h01 * 64:
                                                  c * 384 + 128 + h01 * 64 + 64],
                                     xnT[:, c * T + scol + tg * 512:
                                         c * T + scol + tg * 512 + 512],
                                     start=(c == 0), stop=(c == NCH - 1))
                    ctx.inc(i, "pe")
                if h01 == 0:
                    ctx.w(t, "ve", ve_m[("v2done", hp)])
                    if (hp, sh) != (0, 0):
                        p = prev_shk(hp, sh)
                        ctx.w(t, "ve", ve_m[("att", p[0], p[1], 3)])
                    for kt in range(16):
                        i = t.matmul(pA[2][0:1, 0:130], onesb[:, :],
                                     v2[:, (s_ * 16 + kt) * 130:
                                        (s_ * 16 + kt) * 130 + 130],
                                     start=(kt == 0), stop=(kt == 15))
                    ctx.inc(i, "pe")
                ctx.w(t, "ve", ve_m[("qk", hp, sh)])
                if h01 == 1 and (hp, sh) != (0, 1):
                    p = prev_shk(hp, sh)
                    ctx.w(t, "ve", ve_m[("att", p[0], p[1], 3)])
                for j in range(16):
                    if j == 1:
                        ctx.w(t, "ve", ve_m.get(("vs1", hp, sh - h01), 0))
                    if j >= 2:
                        ctx.w(t, "ve", ve_m[("sub", hp, sh, j - 2)])
                    rhs = k65[su][0:64, :].rearrange(
                        "p (a b) -> p a b", b=16)[:, 0:8 * j + 1, 0:1]
                    i = t.matmul(pA[3 - j % 2][:, 0:8 * j + 1],
                                 q65[su][0:64, j * 128:(j + 1) * 128],
                                 rhs, start=True, stop=True)
                    ctx.inc(i, "pe")
                ctx.w(t, "ve", ve_m[("negcb", hp, sh)])
                ctx.inc(t.transpose(pT[0][0:16, 0:128], negcb[:, :],
                                    ident_bf[:, :]), "pe")
                ctx.slot_w(t, "qn", su, dm_m[("negcrow", hp, sh)] // 16)
                ctx.slot_w(t, "qv", s_, dm_m[("vsrow", hp, sh)] // 16)
                for qg in range(4):
                    nk = 4 * qg + 4
                    for kc in range(nk):
                        if kc >= 2:
                            ctx.w(t, "se", se_m[("exp", hp, sh, qg, kc - 2)])
                        elif qg == 0 and kc == 0:
                            pass
                        i = t.matmul(pA[kc % 2][:, :],
                                     k65[su][:, kc * 128:(kc + 1) * 128],
                                     q65[su][:, qg * 512:(qg + 1) * 512],
                                     start=True, stop=True)
                        ctx.inc(i, "pe")
                        if kc == 0 and qg > 0:
                            ctx.w(t, "ve", ve_m[("att", hp, sh, qg - 1)])
                        if kc >= 4 * qg:
                            ctx.w(t, "gp", gp_pos[(hp, sh, qg, kc)])
                            pv_rhs = pmask[:, :]
                        else:
                            ctx.w(t, "se", se_m[("exp", hp, sh, qg, kc)])
                            pv_rhs = pbuf[kc % 3][:, :]
                        i = t.matmul(ppv[:, :],
                                     v2[:, (s_ * 16 + kc) * 130 + h01 * 65:
                                        (s_ * 16 + kc) * 130 + h01 * 65 + 65],
                                     pv_rhs, start=(kc == 0), stop=(kc == nk - 1))
                        ctx.inc(i, "pe")
                    ctx.w(t, "ve", ve_m[("ag", hp, sh, qg)])
                    ctx.inc(t.matmul(pA[2][0:64, :], r65[64:65, ONE:ONE + 64],
                                     r65[64:65, ALP:ALP + 512],
                                     start=True, stop=True), "pe")
                    ctx.inc(t.matmul(pA[3][0:64, 0:512],
                                     r65[64:65, ONE:ONE + 64],
                                     r65[64:65, GAM:GAM + 512],
                                     start=True, stop=True), "pe")
        ctx.w(t, "qA", dm_m[("qA",)])
        ctx.w(t, "ve", ve_m[("att", NCH - 1, 3, 3)])
        for j in range(NT):
            jm = j % 2
            if j >= 2:
                ctx.w(t, "ve", ve_m[("comb", j - 2)])
            for half in range(2):
                for c in range(NCH):
                    i = t.matmul(pA[2 * jm + half][:, :],
                                 xnT[:, c * T + j * 128: c * T + j * 128 + 128],
                                 wpj[:, c * D + half * 512: c * D + half * 512 + 512],
                                 start=(c == 0), stop=(c == NCH - 1))
            ctx.inc(i, "pe")
    blk.tensor(f_pe)


def emit_l1ffn(nc, ctx, blk, es, R_in, R_out, ident_bf, ln, pT, wdr, dscr, tag,
               pqo="qo"):
    phv = phase_fence(ctx)
    from contextlib import ExitStack
    E = es.enter_context
    HT = T // 2
    NST = HT // 512
    gallT = E(nc.sbuf_tensor(f"m_gallT{tag}", [8, T], BF16))
    gb = E(nc.sbuf_tensor(f"m_gb{tag}", [128, 8], BF16))
    wrsb = E(nc.sbuf_tensor(f"m_wr{tag}", [128, NCH * 8], BF16))
    brrep = E(nc.sbuf_tensor(f"m_brrep{tag}", [128, 8], F32))
    brsb = E(nc.sbuf_tensor(f"m_brs{tag}", [1, 8], BF16))
    onesr = E(nc.sbuf_tensor(f"m_onesr{tag}", [1, 128], BF16))
    b1sb = E(nc.sbuf_tensor(f"m_b1{tag}", [128, 8 * 16], F32))
    b2sb = E(nc.sbuf_tensor(f"m_b2{tag}", [8, D], BF16))
    yaccT = E(nc.sbuf_tensor(f"m_yacc{tag}", [128, NCH * HT], BF16))
    lg = E(nc.sbuf_tensor(f"m_lg{tag}", [128, 8], F32))
    gs1 = E(nc.sbuf_tensor(f"m_gs1{tag}", [128, 8], F32))
    gs2 = E(nc.sbuf_tensor(f"m_gs2{tag}", [128, 8], F32))
    gm = E(nc.sbuf_tensor(f"m_gm{tag}", [128, 1], F32))
    gm2 = E(nc.sbuf_tensor(f"m_gm2{tag}", [128, 1], F32))
    gnm = E(nc.sbuf_tensor(f"m_gnm{tag}", [128, 1], F32))
    ged = E(nc.sbuf_tensor(f"m_ged{tag}", [128, 1], F32))
    grd = E(nc.sbuf_tensor(f"m_grd{tag}", [128, 1], F32))
    w2b = E(nc.sbuf_tensor(f"m_w2{tag}", [128, 16 * D], BF16))
    hT = E(nc.sbuf_tensor(f"m_hT{tag}", [128, 16 * 512], BF16))
    xst = [E(nc.sbuf_tensor(f"m_xst{i}{tag}", [128, NCH * 512], BF16))
           for i in range(2)]
    gerow = E(nc.sbuf_tensor(f"m_gerow{tag}", [1, HT], BF16))
    grep = E(nc.sbuf_tensor(f"m_grep{tag}", [128, 512], BF16))
    ytmp = E(nc.sbuf_tensor(f"m_ytmp{tag}", [128, 512], F32))
    ytile = E(nc.sbuf_tensor(f"m_ytile{tag}", [128, D], BF16))
    rin3 = [ln["rt0"], ln["rt1"]]
    ph = [E(nc.psum_tensor(f"m_ph{i}{tag}", [128, 512], F32)) for i in range(2)]
    py = [E(nc.psum_tensor(f"m_py{i}{tag}", [128, 512], F32)) for i in range(4)]

    xnpark = dscr["xnpark"]
    g_d = dscr["g_d"]

    xtile = [E(nc.sbuf_tensor(f"m_xt{i}{tag}", [128, NCH * 128], BF16))
             for i in range(2)]
    xg = [E(nc.sbuf_tensor(f"m_xg{i}{tag}", [128, NCH * 128], BF16))
          for i in range(2)]
    w1b = [E(nc.sbuf_tensor(f"m_w1{i}{tag}", [128, NCH * 2048], BF16))
           for i in range(2)]

    def eidx(hf, e):
        return hf * 8 + e

    def sidx(hf, e, st):
        return (hf * 8 + e) * NST + st

    lnve0 = ctx.cnt["ve"]
    qz_base = (ctx.cnt["qz0"], ctx.cnt["qz1"])

    def ln_dest(j):
        return xtile[j % 2].rearrange("p (c t) -> p c t", c=NCH)[:, :, :]

    def ln_war(v, j):
        if j >= 2:
            ctx.slot_w(v, "qz", j % 2, qz_base[j % 2] // 16 + j // 2)

    emit_ln_transpose(nc, ctx, blk, R_in, None, ln, ident_bf, pT, pqo=pqo,
                      dest_of_tile=ln_dest, dest_war=ln_war)

    def ln_evac(j):
        return lnve0 + 5 * j + 5

    pe_pos = {}

    def plan_pe():
        n = ctx.cnt["pe"]
        n += 1; pe_pos[("brbc",)] = n
        for tj in range(NT):
            n += 1; pe_pos[("lg", tj)] = n
            n += 1; pe_pos[("gT", tj)] = n
        for hf in range(2):
            for dn in range(NCH):
                for tg in range(NST):
                    n += 1; pe_pos[("b2g", hf, dn, tg)] = n
            for e in range(8):
                for st in range(NST):
                    n += 1; pe_pos[("grep", hf, e, st)] = n
                    for fn in range(16):
                        n += 1; pe_pos[("h", hf, e, st, fn)] = n
                        n += 1; pe_pos[("yA", hf, e, st, fn)] = n
                    for fn in range(16):
                        n += 1; pe_pos[("yB", hf, e, st, fn)] = n
            for j in range(16):
                n += 1; pe_pos[("yt", hf, j)] = n
        return n
    plan_pe()

    ve_m, se_m, dm_m = {}, {}, {}
    qi0 = (ctx.cnt["qi0"], ctx.cnt["qi1"])
    qo0 = (ctx.cnt["qu0"], ctx.cnt["qu1"])
    qo_prev = (ctx.cnt[pqo + "0"], ctx.cnt[pqo + "1"])
    qw_base = ctx.cnt["qw"]
    se_gate0 = ctx.cnt["se"]
    assert se_gate0 >= 2 * NT

    def f_vector(v):
        ctx.inc(v.memset(onesr[:, :], 1.0), "ve")
        ve_m[("onesr",)] = ctx.cnt["ve"]
        ctx.wall(v, "ve")
        ctx.w(v, "pe", pe_pos[("brbc",)])
        ctx.inc(v.tensor_copy(brrep[:, :], ph[1][:, 0:8]), "ve")
        ve_m[("brrep",)] = ctx.cnt["ve"]
        ctx.wall(v, "ve")
        for tj in range(NT):
            ctx.w(v, "pe", pe_pos[("lg", tj)])
            ctx.inc(v.tensor_tensor(lg[:, :], ph[0][:, 0:8], brrep[:, :],
                                    ALU.add), "ve")
            ve_m[("glg", tj)] = ctx.cnt["ve"]
            ctx.wall(v, "ve")
            ctx.inc(v.reduce_max(gm[:, :], lg[:, :], axis=AX.X), "ve")
            ctx.wall(v, "ve")
            ctx.inc(v.tensor_scalar(gs1[:, :], lg[:, :], gm[:, :], None,
                                    ALU.is_lt), "ve")
            ctx.inc(v.tensor_scalar(gnm[:, :], gm[:, :], -1.0, None,
                                    ALU.mult), "ve")
            ctx.wall(v, "ve")
            ctx.inc(v.tensor_tensor(gs2[:, :], lg[:, :], gs1[:, :],
                                    ALU.mult), "ve")
            ctx.wall(v, "ve")
            ctx.inc(v.tensor_scalar(gs1[:, :], gs1[:, :], 1e30, 1e30,
                                    ALU.mult, ALU.subtract), "ve")
            ctx.wall(v, "ve")
            ctx.inc(v.tensor_tensor(gs2[:, :], gs2[:, :], gs1[:, :],
                                    ALU.add), "ve")
            ctx.wall(v, "ve")
            ctx.inc(v.reduce_max(gm2[:, :], gs2[:, :], axis=AX.X), "ve")
            ve_m[("gm2", tj)] = ctx.cnt["ve"]
            ctx.w(v, "se", 2 * (tj + 1) + se_gate0)
            ctx.inc(v.tensor_scalar(grd[:, :], ged[:, :], 1.0, None,
                                    ALU.add), "ve")
            ctx.wall(v, "ve")
            ctx.inc(v.reciprocal(grd[:, :], grd[:, :]), "ve")
            ctx.wall(v, "ve")
            ctx.inc(v.tensor_scalar(gs2[:, :], lg[:, :], gm2[:, :], None,
                                    ALU.is_ge), "ve")
            ctx.wall(v, "ve")
            ctx.inc(v.tensor_tensor(gs2[:, :], gs2[:, :], gs1[:, :],
                                    ALU.mult), "ve")
            ctx.wall(v, "ve")
            ctx.inc(v.tensor_scalar(gb[:, :], gs2[:, :],
                                    grd[:, :], 0.5, ALU.mult, ALU.mult), "ve")
            ve_m[("gate", tj)] = ctx.cnt["ve"]
            ctx.w(v, "pe", pe_pos[("gT", tj)])
            ctx.inc(v.tensor_copy(gallT[:, tj * 128:(tj + 1) * 128],
                                  pT[0][0:8, 0:128]), "ve")
            ve_m[("gT", tj)] = ctx.cnt["ve"]
        for hf in range(2):
            for dn in range(NCH):
                for tg in range(NST):
                    ctx.w(v, "pe", pe_pos[("b2g", hf, dn, tg)])
                    ctx.inc(v.tensor_copy(
                        yaccT[:, dn * HT + tg * 512: dn * HT + tg * 512 + 512],
                        py[tg % 4][:, :]), "ve")
                    ve_m[("b2g", hf, dn, tg)] = ctx.cnt["ve"]
            for e in range(8):
                for st in range(NST):
                    ctx.w(v, "pe", pe_pos[("grep", hf, e, st)])
                    ctx.inc(v.tensor_copy(grep[:, :], ph[0][:, :]), "ve")
                    ve_m[("grep", hf, e, st)] = ctx.cnt["ve"]
                    for pss, key in ((0, "yA"), (1, "yB")):
                        ctx.w(v, "pe", pe_pos[(key, hf, e, st, 15)])
                        for dn4 in range(4):
                            dn = pss * 4 + dn4
                            ctx.inc(v.tensor_tensor(ytmp[:, :], py[dn4][:, :],
                                                    grep[:, :], ALU.mult), "ve")
                            ctx.wall(v, "ve")
                            sl = yaccT[:, dn * HT + st * 512:
                                       dn * HT + st * 512 + 512]
                            ctx.inc(v.tensor_tensor(sl, sl, ytmp[:, :],
                                                    ALU.add), "ve")
                            ve_m[("yev", hf, e, st, dn)] = ctx.cnt["ve"]
                            ctx.wall(v, "ve")
            for j in range(16):
                jg = hf * 16 + j
                jm = jg % 2
                ctx.w(v, "pe", pe_pos[("yt", hf, j)])
                ctx.inc(v.tensor_copy(
                    ytile.rearrange("p (c n) -> p c n", c=NCH)[:, :, :],
                    pT[j % 2].rearrange("p (c n) -> p c n", c=NCH)[:, :, :]),
                    "ve")
                ctx.wall(v, "ve")
                ctx.slot_w(v, "qi", jm, qi0[jm] // 16 + jg // 2 + 1)
                for half in range(2):
                    ctx.inc(v.tensor_tensor(
                        rin3[jm][:, half * 512:half * 512 + 512],
                        rin3[jm][:, half * 512:half * 512 + 512],
                        ytile[:, half * 512:half * 512 + 512], ALU.add), "ve")
                ctx.wall(v, "ve")
                ve_m[("comb", jg)] = ctx.cnt["ve"]

    def f_scalar(sc):
        for tj in range(NT):
            ctx.w(sc, "ve", ve_m[("gm2", tj)])
            ctx.inc(sc.activation(gs1[:, :], lg[:, :], AF.Exp,
                                  bias=gnm[:, :]), "se")
            ctx.wall(sc, "se")
            ctx.inc(sc.activation(ged[:, :], gm2[:, :], AF.Exp,
                                  bias=gnm[:, :]), "se")
            se_m[("ged", tj)] = ctx.cnt["se"]
        for hf in range(2):
            for e in range(8):
                for st in range(NST):
                    for fn in range(16):
                        ctx.w(sc, "pe", pe_pos[("h", hf, e, st, fn)])
                        if fn == 0 and sidx(hf, e, st) > 0:
                            i2 = sidx(hf, e, st) - 1
                            h2, r2 = divmod(i2, 8 * NST)
                            e2, s2 = divmod(r2, NST)
                            ctx.w(sc, "pe", pe_pos[("yB", h2, e2, s2, 15)])
                        ctx.inc(sc.activation(
                            hT[:, fn * 512:(fn + 1) * 512],
                            ph[fn % 2][:, :], AF.Relu,
                            bias=b1sb[:, e * 16 + fn: e * 16 + fn + 1]), "se")
                        se_m[("relu", hf, e, st, fn)] = ctx.cnt["se"]

    def f_gp(g):
        fence_wait(ctx, g, phv)
        ctx.w(g, "qw", qw_base)
        for c in range(NCH):
            ctx.inc(g.dma_start(out=wrsb[:, c * 8:(c + 1) * 8],
                                in_=wdr["m_wr"][c * 128:(c + 1) * 128, :]),
                    "qw", 16)
        ctx.inc(g.dma_start(out=brsb[:, :], in_=wdr["m_br"][None, :]), "qw", 16)
        ctx.inc(g.dma_start(out=b1sb[:, :], in_=wdr["m_b1"]), "qw", 16)
        ctx.inc(g.dma_start(out=b2sb[:, :], in_=wdr["m_b2"]), "qw", 16)
        dm_m[("wconst",)] = ctx.cnt["qw"]

        def load_w1(hf, e, slot):
            ctx.slot_w(g, "qf", slot, ctx.cnt[f"qf{slot}"] // 16)
            for c in range(NCH):
                ctx.slot_inc(g.dma_start(
                    out=w1b[slot][:, c * 2048:(c + 1) * 2048],
                    in_=wdr["m_w1"][e, c * 128:(c + 1) * 128, :]), "qf", slot)
            dm_m[("w1", hf, e)] = ctx.cnt[f"qf{slot}"]

        def load_w2(hf, e):
            ctx.slot_w(g, "qm", 0, ctx.cnt["qm0"] // 16)
            for fn in range(16):
                ctx.slot_inc(g.dma_start(
                    out=w2b[:, fn * D:(fn + 1) * D],
                    in_=wdr["m_w2"][e, fn * 128:(fn + 1) * 128, :]), "qm", 0)
            dm_m[("w2", hf, e)] = ctx.cnt["qm0"]

        load_w1(0, 0, 0)
        load_w2(0, 0)
        for hf in range(2):
            for e in range(8):
                i = eidx(hf, e)
                if i + 1 < 16:
                    nhf, ne = divmod(i + 1, 8)
                    if i >= 1:
                        p = i - 1
                        ctx.w(g, "pe", pe_pos[("yA", p // 8, p % 8,
                                               NST - 1, 15)])
                    load_w1(nhf, ne, (i + 1) % 2)
                    ctx.w(g, "pe", pe_pos[("yB", hf, e, NST - 1, 15)])
                    load_w2(nhf, ne)
    xpark_cnt = [0]

    def f_park(g):
        for j in range(NT):
            ctx.w(g, "ve", ln_evac(j))
            ctx.slot_inc(g.dma_start(
                out=xnpark[:, :, j * 128:(j + 1) * 128].rearrange(
                    "c p t -> p c t"),
                in_=xtile[j % 2].rearrange("p (c t) -> p c t", c=NCH)),
                "qz", j % 2)
        xpark_cnt[0] = (ctx.cnt["qz0"], ctx.cnt["qz1"])

    def f_sync(sync):
        for j in range(NT):
            jm = j % 2
            if j >= 2:
                ctx.w(sync, "pe", pe_pos[("lg", j - 2)])
            ctx.slot_w(sync, "qz", jm, qz_base[jm] // 16 + j // 2 + 1)
            ctx.slot_inc(sync.dma_start(
                out=xg[jm].rearrange("p (c t) -> p c t", c=NCH),
                in_=xnpark[:, :, j * 128:(j + 1) * 128].rearrange(
                    "c p t -> p c t")), "qy", jm)
            dm_m[("xg", j)] = ctx.cnt[f"qy{jm}"]
        ctx.w(sync, "ve", ve_m[("gT", NT - 1)])
        ctx.inc(sync.dma_start(out=g_d[:, :], in_=gallT[:, :]), "qB", 16)
        dm_m[("gd",)] = ctx.cnt["qB"]
        for hf in range(2):
            for e in range(8):
                i = eidx(hf, e)
                if i >= 1:
                    p = i - 1
                    ctx.w(sync, "pe", pe_pos[("grep", p // 8, p % 8, NST - 1)])
                ctx.slot_w(sync, "qm", 1, ctx.cnt["qm1"] // 16)
                ctx.w(sync, "qB", dm_m[("gd",)])
                ctx.slot_inc(sync.dma_start(
                    out=gerow[:, :],
                    in_=g_d[e:e + 1, hf * HT:(hf + 1) * HT]), "qm", 1)
                dm_m[("ge", hf, e)] = ctx.cnt["qm1"]
                for st in range(NST):
                    sl = st % 2
                    if sidx(hf, e, st) >= 2:
                        i2 = sidx(hf, e, st) - 2
                        h2, r2 = divmod(i2, 8 * NST)
                        e2, s2 = divmod(r2, NST)
                        ctx.w(sync, "pe", pe_pos[("h", h2, e2, s2, 15)])
                    else:
                        ctx.slot_w(sync, "qz", 0, xpark_cnt[0][0] // 16)
                        ctx.slot_w(sync, "qz", 1, xpark_cnt[0][1] // 16)
                    for c in range(NCH):
                        ctx.slot_inc(sync.dma_start(
                            out=xst[sl][:, c * 512:(c + 1) * 512],
                            in_=xnpark[c, :, hf * HT + st * 512:
                                       hf * HT + st * 512 + 512]), "qx", sl)
                    dm_m[("xst", hf, e, st)] = ctx.cnt[f"qx{sl}"]
            for j in range(16):
                jg = hf * 16 + j
                jm = jg % 2
                if jg >= 2:
                    ctx.w(sync, "ve", ve_m[("comb", jg - 2)])
                    ctx.slot_inc(sync.dma_start(
                        out=R_out[(jg - 2) * 128:(jg - 1) * 128, :],
                        in_=rin3[jm][:, :]), "qu", jm)
                    ctx.slot_w(sync, "qu", jm,
                               qo0[jm] // 16 + (jg - 2) // 2 + 1)
                else:
                    ctx.slot_w(sync, pqo, jm, qo_prev[jm] // 16)
                ctx.slot_inc(sync.dma_start(
                    out=rin3[jm][:, :],
                    in_=R_in[jg * 128:(jg + 1) * 128, :]), "qi", jm)
        for jg in (NT - 2, NT - 1):
            jm = jg % 2
            ctx.w(sync, "ve", ve_m[("comb", jg)])
            ctx.slot_inc(sync.dma_start(
                out=R_out[jg * 128:(jg + 1) * 128, :],
                in_=rin3[jm][:, :]), "qu", jm)

    def f_pe(t):
        ctx.w(t, "qw", dm_m[("wconst",)])
        ctx.w(t, "ve", ve_m[("onesr",)])
        ctx.inc(t.matmul(ph[1][:, 0:8], onesr[:, :], brsb[:, :],
                         start=True, stop=True), "pe")
        for tj in range(NT):
            if tj > 0:
                ctx.w(t, "ve", ve_m[("glg", tj - 1)])
            ctx.slot_w(t, "qy", tj % 2, dm_m[("xg", tj)] // 16)
            for c in range(NCH):
                i = t.matmul(ph[0][:, 0:8],
                             xg[tj % 2][:, c * 128:(c + 1) * 128],
                             wrsb[:, c * 8:(c + 1) * 8],
                             start=(c == 0), stop=(c == NCH - 1))
            ctx.inc(i, "pe")
            ctx.w(t, "ve", ve_m[("gate", tj)])
            if tj >= 1:
                ctx.w(t, "ve", ve_m[("gT", tj - 1)])
            ctx.inc(t.transpose(pT[0][0:8, 0:128], gb[:, :], ident_bf[:, :]),
                    "pe")
        for hf in range(2):
            ctx.w(t, "ve", ve_m[("gT", NT - 1)])
            for dn in range(NCH):
                for tg in range(NST):
                    g4 = dn * NST + tg - 4
                    if g4 >= 0:
                        ctx.w(t, "ve", ve_m[("b2g", hf, g4 // NST, g4 % NST)])
                    elif hf == 1:
                        ctx.w(t, "ve", ve_m[("yev", 0, 7, NST - 1,
                                             4 + (dn * NST + tg) % 4)])
                    i = t.matmul(py[(dn * NST + tg) % 4][:, :],
                                 b2sb[:, dn * 128:(dn + 1) * 128],
                                 gallT[:, hf * HT + tg * 512:
                                       hf * HT + tg * 512 + 512],
                                 start=True, stop=True)
                    ctx.inc(i, "pe")
            for e in range(8):
                i_e = eidx(hf, e)
                ctx.slot_w(t, "qf", i_e % 2, dm_m[("w1", hf, e)] // 16)
                ctx.slot_w(t, "qm", 0, dm_m[("w2", hf, e)] // 16)
                for st in range(NST):
                    sl = st % 2
                    ctx.slot_w(t, "qx", sl, dm_m[("xst", hf, e, st)] // 16)
                    ctx.slot_w(t, "qm", 1, dm_m[("ge", hf, e)] // 16)
                    if sidx(hf, e, st) == 0:
                        ctx.w(t, "ve", ve_m[("glg", NT - 1)])
                        ctx.w(t, "ve", ve_m[("brrep",)])
                    else:
                        i2 = sidx(hf, e, st) - 1
                        h2, r2 = divmod(i2, 8 * NST)
                        e2, s2 = divmod(r2, NST)
                        ctx.w(t, "ve", ve_m[("grep", h2, e2, s2)])
                    ctx.inc(t.matmul(ph[0][:, :], onesr[:, :],
                                     gerow[:, st * 512:(st + 1) * 512],
                                     start=True, stop=True), "pe")

                    w1s = w1b[i_e % 2]

                    def hmm(fn):
                        if fn >= 2:
                            ctx.w(t, "se", se_m[("relu", hf, e, st, fn - 2)])
                        if fn == 0:
                            ctx.w(t, "ve", ve_m[("grep", hf, e, st)])
                        if fn == 1 and sidx(hf, e, st) == 0:
                            ctx.w(t, "ve", ve_m[("brrep",)])
                        for c in range(NCH):
                            i = t.matmul(ph[fn % 2][:, :],
                                         w1s[:, c * 2048 + fn * 128:
                                             c * 2048 + fn * 128 + 128],
                                         xst[sl][:, c * 512:(c + 1) * 512],
                                         start=(c == 0), stop=(c == NCH - 1))
                        ctx.inc(i, "pe")

                    def ymm(fn, pss):
                        ctx.w(t, "se", se_m[("relu", hf, e, st, fn)])
                        if fn == 0:
                            if pss == 0:
                                i2 = sidx(hf, e, st) - 1
                                if i2 >= 0:
                                    h2, r2 = divmod(i2, 8 * NST)
                                    e2, s2 = divmod(r2, NST)
                                    ctx.w(t, "ve",
                                          ve_m[("yev", h2, e2, s2, 7)])
                                else:
                                    ctx.w(t, "ve",
                                          ve_m[("b2g", hf, NCH - 1, NST - 1)])
                            else:
                                ctx.w(t, "ve", ve_m[("yev", hf, e, st, 3)])
                        for dn4 in range(4):
                            dn = pss * 4 + dn4
                            i = t.matmul(py[dn4][:, :],
                                         w2b[:, fn * D + dn * 128:
                                             fn * D + dn * 128 + 128],
                                         hT[:, fn * 512:(fn + 1) * 512],
                                         start=(fn == 0), stop=(fn == 15))
                        ctx.inc(i, "pe")

                    hmm(0)
                    for fn in range(1, 16):
                        hmm(fn)
                        ymm(fn - 1, 0)
                    ymm(15, 0)
                    for fn in range(16):
                        ymm(fn, 1)
            for j in range(16):
                ctx.w(t, "ve", ve_m[("yev", hf, 7, NST - 1, 7)])
                if j >= 2:
                    ctx.w(t, "ve", ve_m[("comb", hf * 16 + j - 2)])
                for c in range(NCH):
                    i = t.transpose(pT[j % 2][:, c * 128:(c + 1) * 128],
                                    yaccT[:, c * HT + j * 128:
                                          c * HT + j * 128 + 128],
                                    ident_bf[:, :])
                ctx.inc(i, "pe")

    blk.vector(f_vector)
    blk.scalar(f_scalar)
    blk.gpsimd(f_park)
    blk.sync(f_sync)
    blk.gpsimd(f_gp)
    blk.tensor(f_pe)


def emit_l1attn(nc, ctx, blk, es, R_in, R_out, ln, ident_bf, onesb, pT, pqo,
                wdr, dscr, tag):
    phv = phase_fence(ctx)
    E = es.enter_context
    xnT = E(nc.sbuf_tensor(f"t_xnT{tag}", [128, NCH * T], BF16))
    emit_ln_transpose(nc, ctx, blk, R_in, xnT, ln, ident_bf, pT, pqo=pqo)

    NSH = 32
    INV_CNT = 1.0 / 33554432.0
    SC = 64.0 ** (-0.25)

    wq = E(nc.sbuf_tensor(f"t_wq{tag}", [128, NCH * D], BF16))
    wk = E(nc.sbuf_tensor(f"t_wk{tag}", [128, NCH * D], BF16))
    wv = E(nc.sbuf_tensor(f"t_wv{tag}", [128, NCH * D], BF16))
    wo = wq
    q2s = E(nc.sbuf_tensor(f"t_q2s{tag}", [128, T], BF16))
    k2s = E(nc.sbuf_tensor(f"t_k2s{tag}", [128, T], BF16))
    v2s = E(nc.sbuf_tensor(f"t_v2s{tag}", [128, T], BF16))
    lqT2 = E(nc.sbuf_tensor(f"t_lq{tag}", [128, 64], BF16))
    lkT2 = E(nc.sbuf_tensor(f"t_lk{tag}", [128, 64], BF16))
    lvT2 = E(nc.sbuf_tensor(f"t_lv{tag}", [128, 64], BF16))
    onesr = E(nc.sbuf_tensor(f"t_onesr{tag}", [1, 128], BF16))
    iota2 = E(nc.sbuf_tensor(f"t_iota{tag}", [128, 16], F32))
    lnc64 = E(nc.sbuf_tensor(f"t_lnc64{tag}", [128, 64], F32))
    lnd64 = E(nc.sbuf_tensor(f"t_lnd64{tag}", [128, 64], F32))
    lqb64 = E(nc.sbuf_tensor(f"t_lqb64{tag}", [128, 64], F32))
    lkb64 = E(nc.sbuf_tensor(f"t_lkb64{tag}", [128, 64], F32))
    lvb64 = E(nc.sbuf_tensor(f"t_lvb64{tag}", [128, 64], F32))
    borep = E(nc.sbuf_tensor(f"t_borep{tag}", [128, D], F32))
    lna2 = E(nc.sbuf_tensor(f"t_lna2{tag}", [128, NCH], F32))
    lnb2 = E(nc.sbuf_tensor(f"t_lnb2{tag}", [128, NCH], F32))
    bq2 = E(nc.sbuf_tensor(f"t_bq2{tag}", [128, NCH], F32))
    bv2 = E(nc.sbuf_tensor(f"t_bv2{tag}", [128, NCH], F32))
    scalin = E(nc.sbuf_tensor(f"t_scalin{tag}", [1, 4], F32))
    lncts = E(nc.sbuf_tensor(f"t_lncts{tag}", [128, 64], F32))
    negmn_all = E(nc.sbuf_tensor(f"t_negmn{tag}", [128, NSH], F32))
    diffcols = E(nc.sbuf_tensor(f"t_dcols{tag}", [64, NSH], F32))
    dcol1 = E(nc.sbuf_tensor(f"t_dcol1{tag}", [64, 1], F32))
    drow = E(nc.sbuf_tensor(f"t_drow{tag}", [1, 64], F32))
    sca = E(nc.sbuf_tensor(f"t_sca{tag}", [1, 32], F32))
    scab = E(nc.sbuf_tensor(f"t_scab{tag}", [1, 8], BF16))
    scb_ = E(nc.sbuf_tensor(f"t_scb{tag}", [1, 8], F32))
    dres = E(nc.sbuf_tensor(f"t_dres{tag}", [1, 8], F32))
    kmask = E(nc.sbuf_tensor(f"t_kmask{tag}", [128, 16], F32))
    kmb = E(nc.sbuf_tensor(f"t_kmb{tag}", [128, 16], BF16))
    qmT = E(nc.sbuf_tensor(f"t_qmT{tag}", [16, 128], BF16))
    qmrow65 = E(nc.sbuf_tensor(f"t_qmrow{tag}", [65, SEQ], BF16))
    tsrep = E(nc.sbuf_tensor(f"t_tsrep{tag}", [128, 1], F32))
    cwrep = E(nc.sbuf_tensor(f"t_cwrep{tag}", [128, 1], F32))
    updrep = E(nc.sbuf_tensor(f"t_updrep{tag}", [64, 1], F32))
    c1rep = E(nc.sbuf_tensor(f"t_c1rep{tag}", [64, 1], F32))
    qcb = [E(nc.sbuf_tensor(f"t_qc{i}{tag}", [64, SEQ], BF16)) for i in range(2)]
    skb = [E(nc.sbuf_tensor(f"t_sk{i}{tag}", [65, SEQ], BF16)) for i in range(2)]
    vpb = [E(nc.sbuf_tensor(f"t_vp{i}{tag}", [128, 16 * 65], BF16))
           for i in range(2)]
    iob = [E(nc.sbuf_tensor(f"t_io{i}{tag}", [64, SEQ], BF16)) for i in range(2)]
    ionew = v2s[0:64, 0:SEQ]
    sqT = [k2s[0:65, 0:SEQ], k2s[0:65, SEQ:2 * SEQ]]
    qpal = E(nc.sbuf_tensor(f"t_qpal{tag}", [128, 1024], F32))
    scrap = E(nc.sbuf_tensor(f"t_scrap{tag}", [128, 1024], F32))
    scrb = v2s[0:64, SEQ:2 * SEQ]
    sqb = E(nc.sbuf_tensor(f"t_sqb{tag}", [128, 1024], BF16))
    st16 = E(nc.sbuf_tensor(f"t_st16{tag}", [128, 16], F32))
    st16b = E(nc.sbuf_tensor(f"t_st16c{tag}", [128, 16], F32))
    stm = E(nc.sbuf_tensor(f"t_stm{tag}", [128, 16], F32))
    strd = E(nc.sbuf_tensor(f"t_strd{tag}", [128, 16], F32))
    nsub = E(nc.sbuf_tensor(f"t_nsub{tag}", [128, 16], F32))
    ncb = E(nc.sbuf_tensor(f"t_ncb{tag}", [128, 16], BF16))
    negcT = [E(nc.sbuf_tensor(f"t_ncT{i}{tag}", [16, 128], BF16))
             for i in range(2)]
    pbuf = [E(nc.sbuf_tensor(f"t_pb{i}{tag}", [128, 512], BF16))
            for i in range(3)]
    arep = E(nc.sbuf_tensor(f"t_arep{tag}", [64, 512], F32))
    r65 = E(nc.sbuf_tensor(f"t_r65{tag}", [65, 640], BF16))
    RAL, RON = 0, 512
    scr65 = E(nc.sbuf_tensor(f"t_scr65{tag}", [65, 512], F32))
    rin4 = [ln["rt0"], ln["rt1"]]
    pS = [E(nc.psum_tensor(f"t_pS{i}{tag}", [128, 512], F32)) for i in range(2)]
    pF = [E(nc.psum_tensor(f"t_pF{i}{tag}", [128, 512], F32)) for i in range(2)]
    ppv = E(nc.psum_tensor(f"t_ppv{tag}", [65, 512], F32))
    pbc = E(nc.psum_tensor(f"t_pbc{tag}", [128, 512], F32))

    qcur_d = dscr["qcur_d"]; sk_d = dscr["sk_d"]; vp_d = dscr["vp_d"]
    io_d = dscr["io_d"]; io1_d = dscr["io1_d"]
    attn_park = dscr["attn_park"]; negc_d = dscr["negc_d"]
    din_d = dscr["din_d"]; dout_d = dscr["dout_d"]; qm_d = dscr["qm_d"]
    dcol_d = dscr["dcol_d"]

    def d_idx(hp, sh4):
        return (sh4 // 2) * 16 + hp * 2 + (sh4 % 2)

    def prevk(it, sh, k):
        i = it * NSH + sh - k
        return (i // NSH, i % NSH) if i >= 0 else None

    pe_pos, se_pos = {}, {}
    cc_base = ctx.cnt["cc"]

    def plan_pe_se():
        n = ctx.cnt["pe"]
        for hp in range(NCH):
            for kind in ("q2", "k2", "v2"):
                for tg in range(NTG):
                    n += 1; pe_pos[(kind, hp, tg)] = n
            for sh4 in range(4):
                n += 1; pe_pos[("kp", hp, sh4)] = n
                n += 1; pe_pos[("skT", hp, sh4)] = n
                n += 1; pe_pos[("knm", hp, sh4)] = n
                n += 1; pe_pos[("mnb", hp, sh4)] = n
                n += 1; pe_pos[("vp", hp, sh4)] = n
        for it in range(3):
            n += 1; pe_pos[("tsb", it)] = n
            if it >= 1:
                n += 1; pe_pos[("cwb", it)] = n
                n += 1; pe_pos[("qmT", it)] = n
            if it == 2:
                n += 1; pe_pos[("upb", it)] = n
            for sh in range(NSH):
                n += 1; pe_pos[("qp", it, sh)] = n
                n += 1; pe_pos[("sqT", it, sh)] = n
                for j in range(16):
                    n += 1; pe_pos[("sub", it, sh, j)] = n
                n += 1; pe_pos[("ncT", it, sh)] = n
                for qg in range(4):
                    for kc in range(16):
                        n += 1; pe_pos[("S", it, sh, qg, kc)] = n
                        if kc >= 3:
                            n += 1; pe_pos[("PV", it, sh, qg, kc - 3)] = n
                    for k2 in (13, 14, 15):
                        n += 1; pe_pos[("PV", it, sh, qg, k2)] = n
                    n += 1; pe_pos[("bca", it, sh, qg)] = n
        for j in range(NT):
            n += 1; pe_pos[("proj", j)] = n

        m = ctx.cnt["se"]
        for hp in range(NCH):
            for sh4 in range(4):
                m += 1; se_pos[("lnk", hp, sh4)] = m
                m += 1; se_pos[("mnsq", hp, sh4)] = m
        for it in range(3):
            m += 1; se_pos[("tsq", it)] = m
            for sh in range(NSH):
                m += 1; se_pos[("lnq", it, sh)] = m
                m += 1; se_pos[("nsq", it, sh)] = m
                for qg in range(4):
                    for kc in range(16):
                        m += 1; se_pos[("exp", it, sh, qg, kc)] = m
    plan_pe_se()
    emit_l1attn._pe_pos = pe_pos

    pl = {}
    qa_pos = {}

    def plan_sync():
        c = {k: ctx.cnt[k] for k in ("qd0", "qd1", "qe0", "qe1", "qn0", "qn1",
                                     "qv0", "qv1", "qB")}
        for hp in range(NCH):
            for sh4 in range(4):
                su = sh4 % 2
                c[f"qe{su}"] += 48
                pl[("dwr", hp, sh4)] = c[f"qe{su}"]
        for it in range(3):
            if it >= 1:
                c["qv1"] += 32
                pl[("qm", it)] = c["qv1"]
            for sh in range(NSH):
                su = sh % 2
                c[f"qd{su}"] += 48 if it == 0 else 64
                pl[("ld", it, sh)] = c[f"qd{su}"]
                c[f"qn{su}"] += 32
                pl[("ncrow", it, sh)] = c[f"qn{su}"]
                if it == 0:
                    c[f"qe{su}"] += 32
                elif it == 1:
                    c[f"qe{su}"] += 16
                pl[("iw", it, sh)] = (c["qe0"], c["qe1"])
            if it < 2:
                c["qv0"] += 32
                pl[("drow", it)] = c["qv0"]
                c["qB"] += 16
                qa_pos[("din", it)] = c["qB"]
                c["qB"] += 16
                qa_pos[("dout", it)] = c["qB"]
        c["qB"] += NCH * 16
        qa_pos[("stagef",)] = c["qB"]
    plan_sync()

    qw_pos = ctx.cnt["qw"] + 16 * (3 * NCH + 6 + 12)
    wo_loaded = [0]
    ve_m, dm_m = {}, {}
    qi0 = (ctx.cnt["qi0"], ctx.cnt["qi1"])
    qo0 = (ctx.cnt["qu0"], ctx.cnt["qu1"])
    qo_prev = (ctx.cnt[pqo + "0"], ctx.cnt[pqo + "1"])
    qw_base = ctx.cnt["qw"]
    qp_base = (ctx.cnt["qp0"], ctx.cnt["qp1"])
    last_v2 = pe_pos[("v2", NCH - 1, NTG - 1)]
    prev_sqrt = [None]

    def ln64(v, x16, wrep, outb, key):
        x3 = x16.rearrange("p (a b) -> p a b", b=64)
        if prev_sqrt[0] is not None:
            ctx.w(v, "se", prev_sqrt[0])
        ctx.inc(v.reduce_sum(st16[:, :], x3, axis=AX.X), "ve")
        ctx.wall(v, "ve")
        ctx.inc(v.tensor_scalar(st16[:, :], st16[:, :], 1.0 / 64, None,
                                ALU.mult), "ve")
        ctx.inc(v.tensor_tensor(scrap[:, :], x16, x16, ALU.mult), "ve")
        ctx.wall(v, "ve")
        ctx.inc(v.reduce_sum(st16b[:, :],
                             scrap.rearrange("p (a b) -> p a b", b=64),
                             axis=AX.X), "ve")
        ctx.inc(v.tensor_tensor(stm[:, :], st16[:, :], st16[:, :],
                                ALU.mult), "ve")
        ctx.wall(v, "ve")
        ctx.inc(v.tensor_scalar(st16b[:, :], st16b[:, :], 1.0 / 64, None,
                                ALU.mult), "ve")
        ctx.wall(v, "ve")
        ctx.inc(v.tensor_tensor(st16b[:, :], st16b[:, :], stm[:, :],
                                ALU.subtract), "ve")
        ve_m[("var",) + key] = ctx.cnt["ve"]
        ctx.wall(v, "ve")
        ctx.w(v, "se", se_pos[key])
        ctx.inc(v.reciprocal(strd[:, :], stm[:, :]), "ve")
        ctx.wall(v, "ve")
        m_bc = st16[:, :].unsqueeze(2).broadcast_to([128, 16, 64])
        r_bc = strd[:, :].unsqueeze(2).broadcast_to([128, 16, 64])
        w_bc = wrep[:, :].unsqueeze(1).broadcast_to([128, 16, 64])
        s3 = scrap.rearrange("p (a b) -> p a b", b=64)
        ctx.inc(v.tensor_tensor(s3, x3, m_bc, ALU.subtract), "ve")
        ctx.wall(v, "ve")
        ctx.inc(v.tensor_tensor(s3, s3, r_bc, ALU.mult), "ve")
        ctx.wall(v, "ve")
        ctx.inc(v.tensor_tensor(outb.rearrange("p (a b) -> p a b", b=64),
                                s3, w_bc, ALU.mult), "ve")
        ctx.wall(v, "ve")
        ctx.inc(v.tensor_tensor(scrap[:, :], outb, outb, ALU.mult), "ve")
        ctx.wall(v, "ve")
        ctx.inc(v.reduce_sum(st16[:, :],
                             scrap.rearrange("p (a b) -> p a b", b=64),
                             axis=AX.X), "ve")
        ve_m[("sq2",) + key] = ctx.cnt["ve"]
        ctx.wall(v, "ve")

    def emit_boundary(v, it):
        if it == 0:
            ctx.inc(v.tensor_copy(sca[:, 13:16], scalin[:, 0:3]), "ve")
            ctx.wall(v, "ve")
            ctx.inc(v.tensor_copy(sca[:, 3:4], sca[:, 14:15]), "ve")
            ctx.wall(v, "ve")
        else:
            ctx.w(v, "qB", qa_pos[("dout", it - 1)])
            ctx.inc(v.tensor_scalar(sca[:, 0:1], dres[:, 0:1], INV_CNT, None,
                                    ALU.mult), "ve")
            ctx.wall(v, "ve")
            if it == 1:
                ctx.inc(v.tensor_scalar(sca[:, 1:2], sca[:, 0:1], 0.001, None,
                                        ALU.is_lt), "ve")
                ctx.wall(v, "ve")
                ctx.inc(v.tensor_scalar(sca[:, 2:3], sca[:, 1:2], -205.0,
                                        2048.0, ALU.mult, ALU.add), "ve")
                ctx.inc(v.tensor_scalar(sca[:, 3:4], sca[:, 3:4], 1.0, 0.005,
                                        ALU.mult, ALU.add), "ve")
                ctx.wall(v, "ve")
            else:
                ctx.inc(v.tensor_tensor(sca[:, 7:8], sca[:, 0:1],
                                        sca[:, 15:16], ALU.mult), "ve")
                ctx.wall(v, "ve")
                ctx.inc(v.tensor_tensor(sca[:, 7:8], sca[:, 7:8],
                                        sca[:, 13:14], ALU.add), "ve")
                ctx.wall(v, "ve")
                ctx.inc(v.tensor_tensor(sca[:, 5:6], sca[:, 0:1], sca[:, 7:8],
                                        ALU.is_lt), "ve")
                ctx.wall(v, "ve")
                ctx.inc(v.tensor_scalar(sca[:, 6:7], sca[:, 5:6], -1.0, 1.0,
                                        ALU.mult, ALU.add), "ve")
                ctx.inc(v.tensor_scalar(sca[:, 8:9], sca[:, 0:1], 0.05, None,
                                        ALU.is_gt), "ve")
                ctx.inc(v.tensor_scalar(sca[:, 9:10], sca[:, 0:1], 0.001, None,
                                        ALU.is_lt), "ve")
                ctx.inc(v.tensor_scalar(sca[:, 10:11], sca[:, 1:2], -21.0,
                                        2048.0, ALU.mult, ALU.add), "ve")
                ctx.inc(v.tensor_scalar(sca[:, 11:12], sca[:, 1:2], -185.0,
                                        1843.0, ALU.mult, ALU.add), "ve")
                ctx.wall(v, "ve")
                ctx.inc(v.tensor_tensor(sca[:, 12:13], sca[:, 9:10],
                                        sca[:, 11:12], ALU.mult), "ve")
                ctx.inc(v.tensor_scalar(sca[:, 9:10], sca[:, 9:10], -1.0, 1.0,
                                        ALU.mult, ALU.add), "ve")
                ctx.wall(v, "ve")
                ctx.inc(v.tensor_tensor(sca[:, 9:10], sca[:, 9:10],
                                        sca[:, 2:3], ALU.mult), "ve")
                ctx.wall(v, "ve")
                ctx.inc(v.tensor_tensor(sca[:, 12:13], sca[:, 12:13],
                                        sca[:, 9:10], ALU.add), "ve")
                ctx.wall(v, "ve")
                ctx.inc(v.tensor_tensor(sca[:, 10:11], sca[:, 8:9],
                                        sca[:, 10:11], ALU.mult), "ve")
                ctx.inc(v.tensor_scalar(sca[:, 8:9], sca[:, 8:9], -1.0, 1.0,
                                        ALU.mult, ALU.add), "ve")
                ctx.wall(v, "ve")
                ctx.inc(v.tensor_tensor(sca[:, 12:13], sca[:, 8:9],
                                        sca[:, 12:13], ALU.mult), "ve")
                ctx.wall(v, "ve")
                ctx.inc(v.tensor_tensor(sca[:, 2:3], sca[:, 10:11],
                                        sca[:, 12:13], ALU.add), "ve")
                ctx.inc(v.tensor_scalar(sca[:, 7:8], sca[:, 6:7], 0.005, None,
                                        ALU.mult), "ve")
                ctx.wall(v, "ve")
                ctx.inc(v.tensor_tensor(sca[:, 3:4], sca[:, 3:4], sca[:, 7:8],
                                        ALU.add), "ve")
                ctx.wall(v, "ve")
                ctx.inc(v.tensor_copy(scab[:, 2:3], sca[:, 6:7]), "ve")
                ctx.inc(v.tensor_copy(scab[:, 3:4], sca[:, 5:6]), "ve")
        ctx.inc(v.tensor_scalar(sca[:, 7:8], sca[:, 3:4], 0.0, None,
                                ALU.is_gt), "ve")
        ctx.inc(v.tensor_scalar(sca[:, 8:9], sca[:, 3:4], 1e-30, None,
                                ALU.max), "ve")
        ve_m[("bpre", it)] = ctx.cnt["ve"]
        ctx.wall(v, "ve")
        ctx.w(v, "se", se_pos[("tsq", it)])
        ctx.inc(v.reciprocal(sca[:, 8:9], sca[:, 8:9]), "ve")
        ctx.wall(v, "ve")
        ctx.inc(v.tensor_tensor(sca[:, 8:9], sca[:, 7:8], sca[:, 8:9],
                                ALU.mult), "ve")
        ctx.inc(v.tensor_scalar(sca[:, 7:8], sca[:, 7:8], -1.0, 1.0,
                                ALU.mult, ALU.add), "ve")
        ctx.wall(v, "ve")
        ctx.inc(v.tensor_tensor(sca[:, 4:5], sca[:, 8:9], sca[:, 7:8],
                                ALU.add), "ve")
        ctx.wall(v, "ve")
        ctx.inc(v.tensor_copy(scab[:, 0:1], sca[:, 4:5]), "ve")
        if it >= 1:
            ctx.inc(v.tensor_copy(scab[:, 1:2], sca[:, 2:3]), "ve")
        ve_m[("bmath", it)] = ctx.cnt["ve"]
        ctx.wall(v, "ve")
        ctx.w(v, "pe", pe_pos[("tsb", it)])
        ctx.inc(v.tensor_copy(tsrep[:, :], pbc[:, 0:1]), "ve")
        ctx.wall(v, "ve")
        ctx.inc(v.tensor_scalar(lncts[:, :], lnc64[:, :], tsrep[:, :], None,
                                ALU.mult), "ve")
        ve_m[("lncts", it)] = ctx.cnt["ve"]
        if it >= 1:
            ctx.w(v, "pe", pe_pos[("cwb", it)])
            ctx.inc(v.tensor_copy(cwrep[:, :], pbc[:, 1:2]), "ve")
            ctx.wall(v, "ve")
            ctx.inc(v.tensor_scalar(kmask[:, :], iota2[:, :], cwrep[:, :],
                                    None, ALU.is_lt), "ve")
            ctx.wall(v, "ve")
            ctx.inc(v.tensor_copy(kmb[:, :], kmask[:, :]), "ve")
            ve_m[("kmb", it)] = ctx.cnt["ve"]
            ctx.w(v, "pe", pe_pos[("qmT", it)])
            ctx.inc(v.tensor_copy(qmT[:, :], pT[0][0:16, 0:128]), "ve")
            ve_m[("qmT", it)] = ctx.cnt["ve"]
        if it == 2:
            ctx.w(v, "pe", pe_pos[("upb", it)])
            ctx.inc(v.tensor_copy(updrep[:, :], pbc[0:64, 0:1]), "ve")
            ctx.inc(v.tensor_copy(c1rep[:, :], pbc[0:64, 1:2]), "ve")
            ve_m[("upb", it)] = ctx.cnt["ve"]
            ctx.wall(v, "ve")

    def f_vector(v):
        ctx.inc(v.memset(onesr[:, :], 1.0), "ve")
        ctx.inc(v.memset(r65[:, RON:RON + 64], 1.0), "ve")
        ctx.inc(v.memset(scb_[:, :], 0.0), "ve")
        ctx.inc(v.memset(qmrow65[64:65, :], 1.0), "ve")
        for i in range(2):
            ctx.inc(v.memset(
                vpb[i].rearrange("p (a b) -> p a b", b=65)[:, :, 64:65],
                1.0), "ve")
        ve_m[("init",)] = ctx.cnt["ve"]
        ctx.wall(v, "ve")
        ctx.w(v, "qw", qw_pos)
        for c in range(NCH):
            ctx.inc(v.tensor_scalar(wq[:, c * D:(c + 1) * D],
                                    wq[:, c * D:(c + 1) * D],
                                    lna2[:, c:c + 1], SC,
                                    ALU.mult, ALU.mult), "ve")
            ctx.inc(v.tensor_scalar(wk[:, c * D:(c + 1) * D],
                                    wk[:, c * D:(c + 1) * D],
                                    lnb2[:, c:c + 1], SC,
                                    ALU.mult, ALU.mult), "ve")
            ctx.inc(v.tensor_scalar(wv[:, c * D:(c + 1) * D],
                                    wv[:, c * D:(c + 1) * D],
                                    lnb2[:, c:c + 1], None, ALU.mult), "ve")
        ctx.inc(v.tensor_scalar(bq2[:, :], bq2[:, :], SC, None, ALU.mult), "ve")
        ve_m[("wscale",)] = ctx.cnt["ve"]
        ctx.wall(v, "ve")
        for hp in range(NCH):
            for (kind, dst, bias) in (("q2", q2s, bq2), ("k2", k2s, None),
                                      ("v2", v2s, bv2)):
                for tg in range(NTG):
                    ctx.w(v, "pe", pe_pos[(kind, hp, tg)])
                    if kind == "q2" and tg == 0 and hp >= 1:
                        ctx.slot_w(v, "qe", 0, pl[("dwr", hp - 1, 2)] // 16)
                        ctx.slot_w(v, "qe", 1, pl[("dwr", hp - 1, 3)] // 16)
                    if bias is not None:
                        ctx.inc(v.tensor_scalar(
                            dst[:, tg * 512:(tg + 1) * 512],
                            pS[tg % 2][:, :], bias[:, hp:hp + 1], None,
                            ALU.add), "ve")
                    else:
                        ctx.inc(v.tensor_copy(dst[:, tg * 512:(tg + 1) * 512],
                                              pS[tg % 2][:, :]), "ve")
                    ve_m[(kind + "e", hp, tg)] = ctx.cnt["ve"]
            for sh4 in range(4):
                su = sh4 % 2
                idx = d_idx(hp, sh4)
                ctx.w(v, "pe", pe_pos[("kp", hp, sh4)])
                for hb in range(2):
                    ctx.inc(v.tensor_tensor(
                        qpal.rearrange("p (a b) -> p a b", b=64)[
                            :, hb * 8:(hb + 1) * 8, :],
                        pS[hb].rearrange("p (a b) -> p a b", b=64),
                        lkb64[:, :].unsqueeze(1).broadcast_to([128, 8, 64]),
                        ALU.add), "ve")
                ctx.wall(v, "ve")
                ln64(v, qpal[:, :], lnd64, sqb[:, :], ("lnk", hp, sh4))
                ctx.inc(v.reduce_max(strd[:, 0:1], st16[:, :], axis=AX.X),
                        "ve")
                ctx.wall(v, "ve")
                ctx.inc(v.tensor_copy(ncb[:, 0:1], strd[:, 0:1]), "ve")
                ve_m[("knmb", hp, sh4)] = ctx.cnt["ve"]
                ctx.w(v, "pe", pe_pos[("skT", hp, sh4)])
                if 4 * hp + sh4 >= 2:
                    pd = 4 * hp + sh4 - 2
                    ctx.slot_w(v, "qe", su, pl[("dwr", pd // 4, pd % 4)] // 16)
                ctx.inc(v.tensor_copy(skb[su][0:64, 0:1024],
                                      pT[0][0:64, :]), "ve")
                ctx.inc(v.tensor_copy(skb[su][0:64, 1024:2048],
                                      pT[1][0:64, :]), "ve")
                ctx.inc(v.memset(skb[su][64:65, :], 1.0), "ve")
                ve_m[("skb", hp, sh4)] = ctx.cnt["ve"]
                ctx.w(v, "pe", pe_pos[("knm", hp, sh4)])
                ctx.inc(v.reduce_max(scr65[0:1, 2:3], pT[0][0:1, 0:128],
                                     axis=AX.X), "ve")
                ve_m[("mn2", hp, sh4)] = ctx.cnt["ve"]
                ctx.wall(v, "ve")
                ctx.w(v, "se", se_pos[("mnsq", hp, sh4)])
                ctx.inc(v.tensor_scalar(scr65[0:1, 3:4], scr65[0:1, 4:5],
                                        -0.125, None, ALU.mult), "ve")
                ctx.wall(v, "ve")
                ctx.inc(v.tensor_copy(scab[0:1, 4:5], scr65[0:1, 3:4]), "ve")
                ve_m[("mnbf", hp, sh4)] = ctx.cnt["ve"]
                ctx.w(v, "pe", pe_pos[("mnb", hp, sh4)])
                ctx.inc(v.tensor_copy(negmn_all[:, idx:idx + 1],
                                      pbc[:, 2:3]), "ve")
                ve_m[("mnrep", hp, sh4)] = ctx.cnt["ve"]
                prev_sqrt[0] = se_pos[("mnsq", hp, sh4)]
                ctx.w(v, "pe", pe_pos[("vp", hp, sh4)])
                for hb in range(2):
                    ctx.inc(v.tensor_tensor(
                        vpb[su].rearrange("p (a b) -> p a b", b=65)[
                            :, hb * 8:(hb + 1) * 8, 0:64],
                        pS[hb].rearrange("p (a b) -> p a b", b=64),
                        lvb64[:, :].unsqueeze(1).broadcast_to([128, 8, 64]),
                        ALU.add), "ve")
                ve_m[("vpb", hp, sh4)] = ctx.cnt["ve"]
                ctx.wall(v, "ve")
        for it in range(3):
            emit_boundary(v, it)
            for sh in range(NSH):
                su = sh % 2
                ctx.slot_w(v, "qd", su, pl[("ld", it, sh)] // 16)
                if it == 2:
                    ctx.inc(v.scalar_tensor_tensor(
                        qcb[su][:, :], iob[su][:, :], updrep[:, :],
                        qcb[su][:, :], ALU.mult, ALU.add), "ve")
                    ve_m[("qfix", it, sh)] = ctx.cnt["ve"]
                    ctx.wall(v, "ve")
                ctx.w(v, "pe", pe_pos[("qp", it, sh)])
                for hb in range(2):
                    ctx.inc(v.tensor_tensor(
                        qpal.rearrange("p (a b) -> p a b", b=64)[
                            :, hb * 8:(hb + 1) * 8, :],
                        pS[hb].rearrange("p (a b) -> p a b", b=64),
                        lqb64[:, :].unsqueeze(1).broadcast_to([128, 8, 64]),
                        ALU.add), "ve")
                ctx.wall(v, "ve")
                ln64(v, qpal[:, :], lncts, sqb[:, :], ("lnq", it, sh))
                ctx.w(v, "pe", pe_pos[("sqT", it, sh)])
                ctx.inc(v.tensor_copy(sqT[su][0:64, 0:1024],
                                      pT[0][0:64, :]), "ve")
                ctx.inc(v.tensor_copy(sqT[su][0:64, 1024:2048],
                                      pT[1][0:64, :]), "ve")
                ve_m[("sqTe", it, sh)] = ctx.cnt["ve"]
                for j in range(16):
                    ctx.w(v, "pe", pe_pos[("sub", it, sh, j)])
                    ctx.inc(v.reduce_max(nsub[:, j:j + 1],
                                         pS[j % 2][:, 0:16], axis=AX.X,
                                         negate=True), "ve")
                    ve_m[("nsr", it, sh, j)] = ctx.cnt["ve"]
                ctx.wall(v, "ve")
                ctx.w(v, "se", se_pos[("nsq", it, sh)])
                ctx.inc(v.tensor_scalar(st16b[:, :], stm[:, :],
                                        negmn_all[:, sh:sh + 1],
                                        80.0, ALU.mult, ALU.add), "ve")
                ctx.wall(v, "ve")
                ctx.inc(v.tensor_tensor(st16b[:, :], nsub[:, :], st16b[:, :],
                                        ALU.min), "ve")
                ctx.wall(v, "ve")
                prev_sqrt[0] = se_pos[("nsq", it, sh)]
                p1 = prevk(it, sh, 1)
                if p1 is not None:
                    ctx.w(v, "pe", pe_pos[("ncT",) + p1])
                ctx.inc(v.tensor_copy(ncb[:, :], st16b[:, :]), "ve")
                ve_m[("ncb", it, sh)] = ctx.cnt["ve"]
                ctx.w(v, "pe", pe_pos[("ncT", it, sh)])
                ctx.inc(v.tensor_copy(negcT[su][:, :], pT[0][0:16, 0:128]),
                        "ve")
                ve_m[("ncT", it, sh)] = ctx.cnt["ve"]
                for qg in range(4):
                    if it >= 1:
                        for kc in range(12, 16):
                            ctx.w(v, "se", se_pos[("exp", it, sh, qg, kc)])
                            ctx.inc(v.tensor_scalar(
                                pbuf[kc % 3][:, :], pbuf[kc % 3][:, :],
                                kmask[:, kc:kc + 1], None, ALU.mult), "ve")
                            ve_m[("km", it, sh, qg, kc)] = ctx.cnt["ve"]
                        ctx.wall(v, "ve")
                    ctx.w(v, "pe", pe_pos[("PV", it, sh, qg, 15)])
                    ctx.inc(v.tensor_scalar(scr65[64:65, 0:512],
                                            ppv[64:65, :], 1.0, 1e-30,
                                            ALU.mult, ALU.add), "ve")
                    ctx.wall(v, "ve")
                    ctx.inc(v.reciprocal(scr65[64:65, 0:512],
                                         scr65[64:65, 0:512]), "ve")
                    ctx.wall(v, "ve")
                    if it >= 1 and qg == 3:
                        ctx.slot_w(v, "qv", 1, pl[("qm", it)] // 16)
                        ctx.inc(v.tensor_tensor(
                            scr65[64:65, 0:512], scr65[64:65, 0:512],
                            qmrow65[64:65, qg * 512:(qg + 1) * 512],
                            ALU.mult), "ve")
                        ctx.wall(v, "ve")
                    ctx.inc(v.tensor_copy(r65[64:65, RAL:RAL + 512],
                                          scr65[64:65, 0:512]), "ve")
                    ve_m[("rq", it, sh, qg)] = ctx.cnt["ve"]
                    ctx.w(v, "pe", pe_pos[("bca", it, sh, qg)])
                    ctx.inc(v.tensor_copy(arep[:, :], pbc[0:64, :]), "ve")
                    ctx.wall(v, "ve")
                    if qg == 0:
                        g2 = it * NSH + sh - 2
                        if g2 >= 0 and g2 // NSH <= 1:
                            it2, sh2 = divmod(g2, NSH)
                            ctx.slot_w(v, "qe", su,
                                       pl[("iw", it2, sh2)][su] // 16)
                    ctx.inc(v.tensor_tensor(
                        ionew[:, qg * 512:(qg + 1) * 512],
                        ppv[0:64, :], arep[:, :], ALU.mult), "ve")
                    ve_m[("ion", it, sh, qg)] = ctx.cnt["ve"]
                ctx.wall(v, "ve")
                if it == 0:
                    ctx.inc(v.reduce_sum(diffcols[:, sh:sh + 1], ionew[:, :],
                                         axis=AX.X,
                                         apply_absolute_value=True), "ve")
                    ctx.inc(v.tensor_tensor(qcb[su][:, :], qcb[su][:, :],
                                            ionew[:, :], ALU.add), "ve")
                elif it == 1:
                    ctx.inc(v.tensor_tensor(scrb[:, :], ionew[:, :],
                                            iob[su][:, :], ALU.subtract), "ve")
                    ctx.wall(v, "ve")
                    ctx.inc(v.reduce_sum(diffcols[:, sh:sh + 1], scrb[:, :],
                                         axis=AX.X,
                                         apply_absolute_value=True), "ve")
                else:
                    ctx.inc(v.tensor_tensor(scrb[:, :], iob[su][:, :],
                                            ionew[:, :], ALU.subtract), "ve")
                    ctx.wall(v, "ve")
                    if sh >= 2:
                        ctx.slot_w(v, "qp", su, qp_base[su] // 16 + sh // 2)
                    ctx.inc(v.scalar_tensor_tensor(
                        iob[su][:, :], scrb[:, :], c1rep[:, :],
                        ionew[:, :], ALU.mult, ALU.add), "ve")
                ve_m[("iodone", it, sh)] = ctx.cnt["ve"]
                ctx.wall(v, "ve")
            if it < 2:
                ctx.inc(v.reduce_sum(dcol1[:, :], diffcols[:, :],
                                     axis=AX.X), "ve")
                ve_m[("dcol", it)] = ctx.cnt["ve"]
                ctx.wall(v, "ve")
                ctx.slot_w(v, "qv", 0, pl[("drow", it)] // 16)
                ctx.inc(v.reduce_sum(sca[0:1, 10:11], drow[:, :],
                                     axis=AX.X), "ve")
                ctx.wall(v, "ve")
                ctx.inc(v.tensor_copy(scb_[:, 0:1], sca[0:1, 10:11]), "ve")
                ve_m[("dsum", it)] = ctx.cnt["ve"]
                ctx.wall(v, "ve")
        for j in range(NT):
            jm = j % 2
            pfo = pS if jm == 0 else pF
            ctx.w(v, "pe", pe_pos[("proj", j)])
            ctx.slot_w(v, "qi", jm, qi0[jm] // 16 + j // 2 + 1)
            for half in range(2):
                ctx.inc(v.tensor_tensor(scrap[:, half * 512:half * 512 + 512],
                                        pfo[half][:, :],
                                        borep[:, half * 512:half * 512 + 512],
                                        ALU.add), "ve")
            ctx.wall(v, "ve")
            for half in range(2):
                ctx.inc(v.tensor_tensor(
                    rin4[jm][:, half * 512:half * 512 + 512],
                    rin4[jm][:, half * 512:half * 512 + 512],
                    scrap[:, half * 512:half * 512 + 512], ALU.add), "ve")
            ctx.wall(v, "ve")
            ve_m[("comb", j)] = ctx.cnt["ve"]
    blk.vector(f_vector)

    def f_scalar(sc):
        for hp in range(NCH):
            for sh4 in range(4):
                ctx.w(sc, "ve", ve_m[("var", "lnk", hp, sh4)])
                ctx.inc(sc.activation(stm[:, :], st16b[:, :], AF.Sqrt,
                                      bias=ln["eps"][:, :]), "se")
                ctx.w(sc, "ve", ve_m[("mn2", hp, sh4)])
                ctx.inc(sc.activation(scr65[0:1, 4:5], scr65[0:1, 2:3],
                                      AF.Sqrt), "se")
        for it in range(3):
            ctx.w(sc, "ve", ve_m[("bpre", it)])
            ctx.inc(sc.activation(sca[:, 8:9], sca[:, 8:9], AF.Sqrt), "se")
            for sh in range(NSH):
                ctx.w(sc, "ve", ve_m[("var", "lnq", it, sh)])
                ctx.inc(sc.activation(stm[:, :], st16b[:, :], AF.Sqrt,
                                      bias=ln["eps"][:, :]), "se")
                ctx.w(sc, "ve", ve_m[("sq2", "lnq", it, sh)])
                ctx.inc(sc.activation(stm[:, :], st16[:, :], AF.Sqrt), "se")
                for qg in range(4):
                    for kc in range(16):
                        ctx.w(sc, "pe", pe_pos[("S", it, sh, qg, kc)])
                        if kc >= 3:
                            ctx.w(sc, "pe", pe_pos[("PV", it, sh, qg, kc - 3)])
                        ctx.inc(sc.activation(pbuf[kc % 3][:, :],
                                              (pS[0], pS[1], pF[0],
                                               pF[1])[kc % 4][:, :],
                                              AF.Exp, scale=0.125), "se")
    blk.scalar(f_scalar)


    def f_sync(sync):
        for hp in range(NCH):
            for sh4 in range(4):
                su = sh4 % 2
                s_, h01 = sh4 // 2, sh4 % 2
                idx = d_idx(hp, sh4)
                ctx.w(sync, "ve", ve_m[("q2e", hp, NTG - 1)])
                ctx.slot_inc(sync.dma_start(
                    out=qcur_d[idx], in_=q2s[h01 * 64:h01 * 64 + 64,
                                            s_ * SEQ:(s_ + 1) * SEQ]),
                    "qe", su)
                ctx.w(sync, "ve", ve_m[("skb", hp, sh4)])
                ctx.slot_inc(sync.dma_start(out=sk_d[idx],
                                            in_=skb[su][:, :]), "qe", su)
                ctx.w(sync, "ve", ve_m[("vpb", hp, sh4)])
                ctx.slot_inc(sync.dma_start(out=vp_d[idx],
                                            in_=vpb[su][:, :]), "qe", su)
                assert ctx.cnt[f"qe{su}"] == pl[("dwr", hp, sh4)]
                ctx.slot_w(sync, "qe", su, ctx.cnt[f"qe{su}"] // 16)
        for it in range(3):
            if it >= 1:
                ctx.w(sync, "ve", ve_m[("qmT", it)])
                ctx.slot_inc(sync.dma_start(out=qm_d, in_=qmT[:, :]), "qv", 1)
                ctx.slot_w(sync, "qv", 1, ctx.cnt["qv1"] // 16)
                ctx.slot_inc(sync.dma_start(
                    out=qmrow65[64:65, :],
                    in_=qm_d.rearrange("a b -> (a b)")[None, :]), "qv", 1)
                assert ctx.cnt["qv1"] == pl[("qm", it)]
            for sh in range(NSH):
                su = sh % 2
                p2 = prevk(it, sh, 2)
                if p2 is not None:
                    ctx.w(sync, "pe", pe_pos[("PV",) + p2 + (3, 15)])
                    ctx.w(sync, "ve", ve_m[("iodone",) + p2])
                    if it == 2 and sh >= 2:
                        ctx.slot_w(sync, "qp", su,
                                   qp_base[su] // 16 + sh // 2)
                else:
                    ctx.slot_w(sync, "qe", su,
                               pl[("dwr", NCH - 1, 2 + su)] // 16)
                ctx.slot_inc(sync.dma_start(out=qcb[su][:, :],
                                            in_=qcur_d[sh]), "qd", su)
                ctx.slot_inc(sync.dma_start(out=skb[su][:, :],
                                            in_=sk_d[sh]), "qd", su)
                ctx.slot_inc(sync.dma_start(out=vpb[su][:, :],
                                            in_=vp_d[sh]), "qd", su)
                if it == 1:
                    ctx.slot_inc(sync.dma_start(out=iob[su][:, :],
                                                in_=io_d[sh]), "qd", su)
                elif it == 2:
                    ctx.slot_inc(sync.dma_start(out=iob[su][:, :],
                                                in_=io1_d[sh]), "qd", su)
                assert ctx.cnt[f"qd{su}"] == pl[("ld", it, sh)]
                ctx.w(sync, "ve", ve_m[("ncT", it, sh)])
                ctx.slot_inc(sync.dma_start(out=negc_d[su],
                                            in_=negcT[su][:, :]), "qn", su)
                ctx.slot_w(sync, "qn", su, ctx.cnt[f"qn{su}"] // 16)
                if p2 is not None:
                    ctx.w(sync, "pe", pe_pos[("S",) + p2 + (3, 15)])
                ctx.slot_inc(sync.dma_start(
                    out=sqT[su][64:65, :],
                    in_=negc_d[su].rearrange("a b -> (a b)")[None, :]),
                    "qn", su)
                assert ctx.cnt[f"qn{su}"] == pl[("ncrow", it, sh)]
                if it == 0:
                    ctx.w(sync, "ve", ve_m[("iodone", it, sh)])
                    ctx.slot_inc(sync.dma_start(out=io_d[sh],
                                                in_=ionew[:, :]), "qe", su)
                    ctx.slot_inc(sync.dma_start(out=qcur_d[sh],
                                                in_=qcb[su][:, :]), "qe", su)
                elif it == 1:
                    ctx.w(sync, "ve", ve_m[("iodone", it, sh)])
                    ctx.slot_inc(sync.dma_start(out=io1_d[sh],
                                                in_=ionew[:, :]), "qe", su)
                if it < 2:
                    ctx.slot_w(sync, "qe", su, ctx.cnt[f"qe{su}"] // 16)
                assert (ctx.cnt["qe0"], ctx.cnt["qe1"]) == pl[("iw", it, sh)]
            if it < 2:
                ctx.w(sync, "ve", ve_m[("dcol", it)])
                ctx.slot_inc(sync.dma_start(out=dcol_d, in_=dcol1[:, :]),
                             "qv", 0)
                ctx.slot_w(sync, "qv", 0, ctx.cnt["qv0"] // 16)
                ctx.slot_inc(sync.dma_start(
                    out=drow[:, :],
                    in_=dcol_d.rearrange("a b -> (a b)")[None, :]), "qv", 0)
                assert ctx.cnt["qv0"] == pl[("drow", it)]
                ctx.w(sync, "ve", ve_m[("dsum", it)])
                ctx.inc(sync.dma_start(out=din_d, in_=scb_[:, :]), "qB", 16)
                assert ctx.cnt["qB"] == qa_pos[("din", it)]
                ctx.w(sync, "cc", cc_base + it + 1)
                ctx.inc(sync.dma_start(out=dres[:, :], in_=dout_d), "qB", 16)
                assert ctx.cnt["qB"] == qa_pos[("dout", it)]
        ctx.slot_w(sync, "qp", 0, qp_base[0] // 16 + 16)
        ctx.slot_w(sync, "qp", 1, qp_base[1] // 16 + 16)
        ctx.w(sync, "pe", last_v2)
        for c in range(NCH):
            ctx.inc(sync.dma_start(out=xnT[:, c * T:(c + 1) * T],
                                   in_=attn_park[c, :, :]), "qB", 16)
        assert ctx.cnt["qB"] == qa_pos[("stagef",)]
        for j in range(NT):
            jm = j % 2
            if j >= 2:
                ctx.w(sync, "ve", ve_m[("comb", j - 2)])
                ctx.slot_inc(sync.dma_start(
                    out=R_out[(j - 2) * 128:(j - 1) * 128, :],
                    in_=rin4[jm][:, :]), "qu", jm)
                ctx.slot_w(sync, "qu", jm, qo0[jm] // 16 + (j - 2) // 2 + 1)
            else:
                ctx.slot_w(sync, pqo, jm, qo_prev[jm] // 16)
            ctx.slot_inc(sync.dma_start(out=rin4[jm][:, :],
                                        in_=R_in[j * 128:(j + 1) * 128, :]),
                         "qi", jm)
        for j in (NT - 2, NT - 1):
            jm = j % 2
            ctx.w(sync, "ve", ve_m[("comb", j)])
            ctx.slot_inc(sync.dma_start(out=R_out[j * 128:(j + 1) * 128, :],
                                        in_=rin4[jm][:, :]), "qu", jm)
    blk.sync(f_sync)

    def f_gp(g):
        fence_wait(ctx, g, phv)
        ctx.w(g, "qw", qw_base)
        for (wsb, key) in ((wq, "l1_wq_t"), (wk, "l1_wk_t"), (wv, "l1_wv_t")):
            for c in range(NCH):
                ctx.inc(g.dma_start(out=wsb[:, c * D:(c + 1) * D],
                                    in_=wdr[key][c * 128:(c + 1) * 128, :]),
                        "qw", 16)
        for (dst, key) in ((lqT2, "l1_lq_t"), (lkT2, "l1_lk_t"),
                           (lvT2, "l1_lv_t")):
            ctx.inc(g.dma_start(out=dst[0:64, :], in_=wdr[key]), "qw", 16)
            ctx.inc(g.dma_start(out=dst[64:128, :], in_=wdr[key]), "qw", 16)
        for (dst, key) in ((iota2, "iota2d"), (lnc64, "l1_lnc64"),
                           (lnd64, "l1_lnd64"), (lqb64, "l1_lqb64"),
                           (lkb64, "l1_lkb64"), (lvb64, "l1_lvb64"),
                           (borep, "l1_borep"), (lna2, "l1_lna2"),
                           (lnb2, "l1_lnb2"), (bq2, "l1_bq2"),
                           (bv2, "l1_bv2"), (scalin, "l1_scal")):
            ctx.inc(g.dma_start(out=dst[:, :], in_=wdr[key]), "qw", 16)
        assert ctx.cnt["qw"] == qw_pos
        ctx.w(g, "pe", pe_pos[("q2", NCH - 1, NTG - 1)])
        for c in range(NCH):
            ctx.inc(g.dma_start(out=wo[:, c * D:(c + 1) * D],
                                in_=wdr["l1_wo_t"][c * 128:(c + 1) * 128, :]),
                    "qw", 16)
        wo_loaded[0] = ctx.cnt["qw"]
        for it in range(2):
            ctx.w(g, "qB", qa_pos[("din", it)])
            ctx.inc(g.collective_compute(
                "AllReduce", ALU.add, replica_groups=[list(range(NCORES))],
                ins=[din_d], outs=[dout_d]), "cc")
        for sh in range(NSH):
            su = sh % 2
            ctx.w(g, "ve", ve_m[("iodone", 2, sh)])
            s_, h = sh // 16, sh % 16
            hp, h01 = h // 2, h % 2
            ctx.slot_inc(g.dma_start(
                out=attn_park[hp, h01 * 64:h01 * 64 + 64,
                              s_ * SEQ:(s_ + 1) * SEQ],
                in_=iob[su][:, :]), "qp", su)
    blk.gpsimd(f_gp)

    def f_pe(t):
        ctx.w(t, "ve", ve_m[("wscale",)])
        for hp in range(NCH):
            for (kind, wsb) in (("q2", wq), ("k2", wk), ("v2", wv)):
                for tg in range(NTG):
                    if tg >= 2:
                        ctx.w(t, "ve", ve_m[(kind + "e", hp, tg - 2)])
                    elif kind == "q2" and hp >= 1:
                        ctx.w(t, "ve", ve_m[("vpb", hp - 1, 3)])
                    elif (hp, kind) != (0, "q2"):
                        prevkind = {"q2": "v2", "k2": "q2", "v2": "k2"}[kind]
                        ctx.w(t, "ve", ve_m[(prevkind + "e", hp,
                                             NTG - 2 + tg)])
                    for c in range(NCH):
                        i = t.matmul(pS[tg % 2][:, :],
                                     wsb[:, c * D + hp * 128:
                                         c * D + hp * 128 + 128],
                                     xnT[:, c * T + tg * 512:
                                         c * T + tg * 512 + 512],
                                     start=(c == 0), stop=(c == NCH - 1))
                    ctx.inc(i, "pe")
            for sh4 in range(4):
                su = sh4 % 2
                s_, h01 = sh4 // 2, sh4 % 2
                ctx.w(t, "ve", ve_m[("v2e", hp, NTG - 1)])
                if 4 * hp + sh4 >= 1:
                    pd = 4 * hp + sh4 - 1
                    ctx.w(t, "ve", ve_m[("vpb", pd // 4, pd % 4)])
                for j in range(16):
                    i = t.matmul(pS[j // 8][:, (j % 8) * 64:(j % 8) * 64 + 64],
                                 k2s[h01 * 64:h01 * 64 + 64,
                                     s_ * SEQ + j * 128:
                                     s_ * SEQ + j * 128 + 128],
                                 lkT2[h01 * 64:h01 * 64 + 64, :],
                                 start=True, stop=True)
                ctx.inc(i, "pe")
                ctx.w(t, "ve", ve_m[("knmb", hp, sh4)])
                for j in range(16):
                    i = t.transpose(
                        pT[j // 8][0:64, (j % 8) * 128:(j % 8) * 128 + 128],
                        sqb[:, j * 64:(j + 1) * 64], ident_bf[:, :])
                ctx.inc(i, "pe")
                ctx.w(t, "ve", ve_m[("skb", hp, sh4)])
                ctx.inc(t.transpose(pT[0][0:1, 0:128], ncb[:, 0:1],
                                    ident_bf[:, :]), "pe")
                ctx.w(t, "ve", ve_m[("mnbf", hp, sh4)])
                ctx.inc(t.matmul(pbc[:, 2:3], onesr[:, :], scab[0:1, 4:5],
                                 start=True, stop=True), "pe")
                ctx.w(t, "ve", ve_m[("mnrep", hp, sh4)])
                for j in range(16):
                    i = t.matmul(pS[j // 8][:, (j % 8) * 64:(j % 8) * 64 + 64],
                                 v2s[h01 * 64:h01 * 64 + 64,
                                     s_ * SEQ + j * 128:
                                     s_ * SEQ + j * 128 + 128],
                                 lvT2[h01 * 64:h01 * 64 + 64, :],
                                 start=True, stop=True)
                ctx.inc(i, "pe")
        for it in range(3):
            ctx.w(t, "ve", ve_m[("bmath", it)])
            ctx.inc(t.matmul(pbc[:, 0:1], onesr[:, :], scab[0:1, 0:1],
                             start=True, stop=True), "pe")
            if it >= 1:
                ctx.inc(t.matmul(pbc[:, 1:2], onesr[:, :], scab[0:1, 1:2],
                                 start=True, stop=True), "pe")
                ctx.w(t, "ve", ve_m[("kmb", it)])
                ctx.inc(t.transpose(pT[0][0:16, 0:128], kmb[:, :],
                                    ident_bf[:, :]), "pe")
            if it == 2:
                i = t.matmul(pbc[0:64, 0:1], onesr[:, 0:64], scab[0:1, 2:3],
                             start=True, stop=True)
                i = t.matmul(pbc[0:64, 1:2], onesr[:, 0:64], scab[0:1, 3:4],
                             start=True, stop=True)
                ctx.inc(i, "pe")
            for sh in range(NSH):
                su = sh % 2
                ctx.slot_w(t, "qd", su, pl[("ld", it, sh)] // 16)
                if it == 2:
                    ctx.w(t, "ve", ve_m[("qfix", it, sh)])
                p1 = prevk(it, sh, 1)
                if p1 is not None:
                    ctx.w(t, "ve", ve_m[("ion",) + p1 + (3,)])
                ctx.w(t, "ve", ve_m[("lncts", it)])
                for j in range(16):
                    i = t.matmul(pS[j // 8][:, (j % 8) * 64:(j % 8) * 64 + 64],
                                 qcb[su][:, j * 128:(j + 1) * 128],
                                 lqT2[0:64, :], start=True, stop=True)
                ctx.inc(i, "pe")
                ctx.w(t, "ve", ve_m[("sq2", "lnq", it, sh)])
                for j in range(16):
                    i = t.transpose(
                        pT[j // 8][0:64, (j % 8) * 128:(j % 8) * 128 + 128],
                        sqb[:, j * 64:(j + 1) * 64], ident_bf[:, :])
                ctx.inc(i, "pe")
                ctx.w(t, "ve", ve_m[("sqTe", it, sh)])
                for j in range(16):
                    if j >= 2:
                        ctx.w(t, "ve", ve_m[("nsr", it, sh, j - 2)])
                    rhs = skb[su][0:64, :].rearrange(
                        "p (a b) -> p a b", b=4)[:, 0:16, 0:1]
                    i = t.matmul(pS[j % 2][:, 0:16],
                                 sqT[su][0:64, j * 128:(j + 1) * 128],
                                 rhs, start=True, stop=True)
                    ctx.inc(i, "pe")
                ctx.w(t, "ve", ve_m[("ncb", it, sh)])
                ctx.inc(t.transpose(pT[0][0:16, 0:128], ncb[:, :],
                                    ident_bf[:, :]), "pe")
                ctx.slot_w(t, "qn", su, pl[("ncrow", it, sh)] // 16)
                sb4 = (pS[0], pS[1], pF[0], pF[1])
                for qg in range(4):
                    order = []
                    for kc in range(16):
                        order.append(("S", kc))
                        if kc >= 3:
                            order.append(("PV", kc - 3))
                    order += [("PV", 13), ("PV", 14), ("PV", 15)]
                    for (kind, kc) in order:
                        if kind == "S":
                            if kc >= 4:
                                ctx.w(t, "se",
                                      se_pos[("exp", it, sh, qg, kc - 4)])
                            i = t.matmul(sb4[kc % 4][:, :],
                                         skb[su][:, kc * 128:(kc + 1) * 128],
                                         sqT[su][:, qg * 512:(qg + 1) * 512],
                                         start=True, stop=True)
                            ctx.inc(i, "pe")
                            assert ctx.cnt["pe"] == pe_pos[
                                ("S", it, sh, qg, kc)], (it, sh, qg, kc)
                        else:
                            if kc == 0 and qg > 0:
                                ctx.w(t, "ve", ve_m[("ion", it, sh, qg - 1)])
                            if it >= 1 and kc >= 12:
                                ctx.w(t, "ve", ve_m[("km", it, sh, qg, kc)])
                            else:
                                ctx.w(t, "se", se_pos[("exp", it, sh, qg, kc)])
                            i = t.matmul(ppv[:, :],
                                         vpb[su][:, kc * 65:(kc + 1) * 65],
                                         pbuf[kc % 3][:, :],
                                         start=(kc == 0), stop=(kc == 15))
                            ctx.inc(i, "pe")
                            assert ctx.cnt["pe"] == pe_pos[
                                ("PV", it, sh, qg, kc)], (it, sh, qg, kc)
                    ctx.w(t, "ve", ve_m[("rq", it, sh, qg)])
                    ctx.inc(t.matmul(pbc[0:64, :], r65[64:65, RON:RON + 64],
                                     r65[64:65, RAL:RAL + 512],
                                     start=True, stop=True), "pe")
        ctx.w(t, "qB", qa_pos[("stagef",)])
        ctx.w(t, "qw", wo_loaded[0])
        ctx.w(t, "ve", ve_m[("ion", 2, NSH - 1, 3)])
        ctx.w(t, "se", se_pos[("exp", 2, NSH - 1, 3, 15)])
        for j in range(NT):
            jm = j % 2
            pfo = pS if jm == 0 else pF
            if j >= 2:
                ctx.w(t, "ve", ve_m[("comb", j - 2)])
            for half in range(2):
                for c in range(NCH):
                    i = t.matmul(pfo[half][:, :],
                                 xnT[:, c * T + j * 128: c * T + j * 128 + 128],
                                 wo[:, c * D + half * 512:
                                    c * D + half * 512 + 512],
                                 start=(c == 0), stop=(c == NCH - 1))
            ctx.inc(i, "pe")
    blk.tensor(f_pe)


def build_program(phases):
    from contextlib import ExitStack
    nc = bass.Bass("TRN2", target_bir_lowering=False, debug=False)

    dr = {}
    def din(name, shape):
        dr[name] = nc.dram_tensor(name, shape, F32, kind="ExternalInput").ap()
        return dr[name]

    x_in = din("x_in", [T, D])
    r_out = nc.dram_tensor("r_out", [T, D], F32, kind="ExternalOutput").ap()

    need = set(phases)
    if "l0ffn_a" in need or "l0ffn_b" in need:
        din("wr_t", [D, D]); din("wk_t", [D, D]); din("wv_t", [D, D])
        din("mu_r2", [128, NCH]); din("mu_k2", [128, NCH])
    if "l0attn" in need:
        din("wqkv_t", [D, 3 * D]); din("wproj_t", [D, D])
    if "l1ffn_a" in need or "l1ffn_b" in need:
        din("m_wr", [D, 8]); din("m_br", [8]); din("m_b1", [128, 128])
        din("m_b2", [8, D]); din("m_w1", [8, D, 2048]); din("m_w2", [8, 2048, D])
    if "l1attn" in need:
        for nm in ("l1_wq_t", "l1_wk_t", "l1_wv_t", "l1_wo_t"):
            din(nm, [D, D])
        for nm in ("l1_lq_t", "l1_lk_t", "l1_lv_t"):
            din(nm, [64, 64])
        for nm in ("l1_lnc64", "l1_lnd64", "l1_lqb64", "l1_lkb64", "l1_lvb64"):
            din(nm, [128, 64])
        din("l1_borep", [128, D])
        for nm in ("l1_lna2", "l1_lnb2", "l1_bq2", "l1_bv2"):
            din(nm, [128, NCH])
        din("l1_scal", [1, 4]); din("iota2d", [128, 16])
    din("ident", [128, 128])
    dscr = {}
    if "l0attn" in need:
        dscr["attn_park"] = nc.dram_tensor("attn_park", [NCH, 128, T], BF16,
                                           kind="Internal").ap()
        dscr["negc_d"] = nc.dram_tensor("negc_d", [2, 16, 128], BF16,
                                        kind="Internal").ap()
        dscr["vsum_d"] = nc.dram_tensor("vsum_d", [2, 1, 130], F32,
                                        kind="Internal").ap()
    if "l1ffn_a" in need or "l1ffn_b" in need:
        dscr["xnpark"] = nc.dram_tensor("xnpark", [NCH, 128, T], BF16,
                                        kind="Internal").ap()
        dscr["g_d"] = nc.dram_tensor("g_d", [8, T], BF16,
                                     kind="Internal").ap()
    if "l1attn" in need:
        if "attn_park" not in dscr:
            dscr["attn_park"] = nc.dram_tensor("attn_park", [NCH, 128, T],
                                               BF16, kind="Internal").ap()
            dscr["negc_d"] = nc.dram_tensor("negc_d", [2, 16, 128], BF16,
                                            kind="Internal").ap()
        dscr["qcur_d"] = nc.dram_tensor("qcur_d", [32, 64, SEQ], BF16,
                                        kind="Internal").ap()
        dscr["sk_d"] = nc.dram_tensor("sk_d", [32, 65, SEQ], BF16,
                                      kind="Internal").ap()
        dscr["vp_d"] = nc.dram_tensor("vp_d", [32, 128, 16 * 65], BF16,
                                      kind="Internal").ap()
        dscr["io_d"] = nc.dram_tensor("io_d", [32, 64, SEQ], BF16,
                                      kind="Internal").ap()
        dscr["io1_d"] = nc.dram_tensor("io1_d", [32, 64, SEQ], BF16,
                                       kind="Internal").ap()
        dscr["din_d"] = nc.dram_tensor("din_d", [1, 8], F32,
                                       kind="Internal").ap()
        dscr["dout_d"] = nc.dram_tensor("dout_d", [1, 8], F32,
                                        kind="Internal").ap()
        dscr["qm_d"] = nc.dram_tensor("qm_d", [16, 128], BF16,
                                      kind="Internal").ap()
        dscr["dcol_d"] = nc.dram_tensor("dcol_d", [64, 1], F32,
                                        kind="Internal").ap()

    es_outer = ExitStack()
    with es_outer as es:
        ctx = Ctx(nc)
        for s_ in ("qw", "qc", "qA", "ve", "se", "pe", "gp", "cc",
                   "qr0", "qr1", "qi0", "qi1", "qo0", "qo1",
                   "qp0", "qp1", "qn0", "qn1", "qv0", "qv1",
                   "qf0", "qf1", "qm0", "qm1", "qx0", "qx1", "qy0", "qy1",
                   "qz0", "qz1", "qd0", "qd1", "qe0", "qe1", "qB",
                   "qu0", "qu1"):
            ctx.add_sem(es, s_)

        ident_bf = es.enter_context(nc.sbuf_tensor("identb", [128, 128], BF16))
        onesb = es.enter_context(nc.sbuf_tensor("onesb", [128, 1], BF16))
        ln = {}
        for nm, shape, dt in (
            ("rt0", [128, D], F32), ("rt1", [128, D], F32),
            ("xn0", [128, D], BF16), ("xn1", [128, D], BF16),
            ("s", [128, 1], F32), ("negm", [128, 1], F32), ("ssq", [128, 1], F32),
            ("sd", [128, 1], F32), ("rstd", [128, 1], F32),
            ("eps", [128, 1], F32),
        ):
            ln[nm] = es.enter_context(nc.sbuf_tensor("ln_" + nm, shape, dt))
        pT = [es.enter_context(nc.psum_tensor(f"pT{i}", [128, 1024], BF16))
              for i in range(2)]

        with nc.Block() as blk:
            def f_const(g):
                ctx.inc(g.dma_start(out=ident_bf[:, :], in_=dr["ident"]), "qc", 16)
                ctx.inc(g.memset(ln["eps"][:, :], LN_EPS), "gp")
                ctx.inc(g.memset(onesb[:, :], 1.0), "gp")
            blk.gpsimd(f_const)
            ctx.setup = {"qc": ctx.cnt["qc"], "gp": ctx.cnt["gp"]}

            def f_cw(t):
                ctx.w(t, "qc", ctx.setup["qc"])
                ctx.w(t, "gp", ctx.setup["gp"])
            blk.tensor(f_cw)

            src = x_in
            psem = "qo"
            for ph in phases:
                nc.push_named_scope(ph)
                if ph.startswith("l0ffn"):
                    tag = ph[-1]
                    with ExitStack() as es_ph:
                        emit_l0ffn(nc, ctx, blk, es_ph, src, r_out,
                                   ln, ident_bf, pT,
                                   {k: dr[k] for k in
                                    ("wr_t", "wk_t", "wv_t", "mu_r2", "mu_k2")}, tag)
                    src = r_out
                elif ph.startswith("l1ffn"):
                    tag = ph[-1]
                    with ExitStack() as es_ph:
                        emit_l1ffn(nc, ctx, blk, es_ph, src, r_out,
                                   ident_bf, ln, pT,
                                   {k: dr[k] for k in
                                    ("m_wr", "m_br", "m_b1", "m_b2",
                                     "m_w1", "m_w2")}, dscr, tag, pqo=psem)
                    src = r_out
                    psem = "qu"
                elif ph == "l1attn":
                    with ExitStack() as es_ph:
                        emit_l1attn(nc, ctx, blk, es_ph, src, r_out,
                                    ln, ident_bf, onesb, pT, psem,
                                    {kk: dr[kk] for kk in dr
                                     if kk.startswith(("l1_", "iota"))},
                                    dscr, "y")
                    src = r_out
                    psem = "qu"
                elif ph == "l0attn":
                    with ExitStack() as es_ph:
                        emit_l0attn(nc, ctx, blk, es_ph, src, r_out,
                                    ln, ident_bf, onesb, pT,
                                    {k: dr[k] for k in ("wqkv_t", "wproj_t")},
                                    dscr, "x")
                    src = r_out
                else:
                    raise ValueError(ph)
                nc.pop_named_scope(ph)

            def f_tail(sync):
                ctx.wall(sync, "qo0")
                ctx.wall(sync, "qo1")
                ctx.wall(sync, "qu0")
                ctx.wall(sync, "qu1")
            blk.sync(f_tail)

    names = [t for t in dr]
    return nc, names


def _prep_shared(inputs):
    f32 = np.float32
    t = lambda a: np.ascontiguousarray(np.asarray(a).T).astype(f32, copy=False)
    out = {
        "wr_t": t(inputs["L0_ffn_W_r"]),
        "wk_t": t(inputs["L0_ffn_W_k"]),
        "wv_t": t(inputs["L0_ffn_W_v"]),
        "mu_r2": np.ascontiguousarray(
            np.asarray(inputs["L0_ffn_mu_r"], dtype=f32).reshape(NCH, 128).T),
        "mu_k2": np.ascontiguousarray(
            np.asarray(inputs["L0_ffn_mu_k"], dtype=f32).reshape(NCH, 128).T),
        "wqkv_t": t(inputs["L0_attn_w_qkv"]),
        "wproj_t": t(inputs["L0_attn_w_proj"]),
        "m_wr": np.ascontiguousarray(np.asarray(inputs["L1_ffn_Wr"], dtype=f32)),
        "m_br": np.ascontiguousarray(np.asarray(inputs["L1_ffn_br"], dtype=f32)),
        "m_b1": np.ascontiguousarray(
            np.asarray(inputs["L1_ffn_b1"], dtype=f32).reshape(8, 16, 128)
            .transpose(2, 0, 1).reshape(128, 128)),
        "m_b2": np.ascontiguousarray(np.asarray(inputs["L1_ffn_b2"], dtype=f32)),
        "m_w1": np.ascontiguousarray(np.asarray(inputs["L1_ffn_W1"], dtype=f32)),
        "m_w2": np.ascontiguousarray(np.asarray(inputs["L1_ffn_W2"], dtype=f32)),
        "l1_wq_t": t(inputs["L1_attn_Wq"]),
        "l1_wk_t": t(inputs["L1_attn_Wk"]),
        "l1_wv_t": t(inputs["L1_attn_Wv"]),
        "l1_wo_t": t(inputs["L1_attn_Wo"]),
        "l1_lq_t": t(inputs["L1_attn_lq_w"]),
        "l1_lk_t": t(inputs["L1_attn_lk_w"]),
        "l1_lv_t": t(inputs["L1_attn_lv_w"]),
        "l1_lnc64": np.ascontiguousarray(np.broadcast_to(
            np.asarray(inputs["L1_attn_lnc_w"], f32)[None, :], (128, 64))),
        "l1_lnd64": np.ascontiguousarray(np.broadcast_to(
            np.asarray(inputs["L1_attn_lnd_w"], f32)[None, :], (128, 64))),
        "l1_lqb64": np.ascontiguousarray(np.broadcast_to(
            np.asarray(inputs["L1_attn_lq_b"], f32)[None, :], (128, 64))),
        "l1_lkb64": np.ascontiguousarray(np.broadcast_to(
            np.asarray(inputs["L1_attn_lk_b"], f32)[None, :], (128, 64))),
        "l1_lvb64": np.ascontiguousarray(np.broadcast_to(
            np.asarray(inputs["L1_attn_lv_b"], f32)[None, :], (128, 64))),
        "l1_borep": np.ascontiguousarray(np.broadcast_to(
            np.asarray(inputs["L1_attn_bo"], f32)[None, :], (128, D))),
        "l1_lna2": np.ascontiguousarray(
            np.asarray(inputs["L1_attn_lna_w"], f32).reshape(NCH, 128).T),
        "l1_lnb2": np.ascontiguousarray(
            np.asarray(inputs["L1_attn_lnb_w"], f32).reshape(NCH, 128).T),
        "l1_bq2": np.ascontiguousarray(
            np.asarray(inputs["L1_attn_bq"], f32).reshape(NCH, 128).T),
        "l1_bv2": np.ascontiguousarray(
            np.asarray(inputs["L1_attn_bv"], f32).reshape(NCH, 128).T),
        "l1_scal": np.array([[np.float32(inputs["L1_attn_threshold"]),
                              np.float32(inputs["L1_attn_temp_base"]),
                              np.float32(inputs["L1_attn_factor"]), 0.0]],
                            dtype=f32),
        "iota2d": np.ascontiguousarray(
            (np.arange(128, dtype=f32)[:, None]
             + 128.0 * np.arange(16, dtype=f32)[None, :])),
        "ident": np.eye(128, dtype=f32),
    }
    return out


def kernel(**inputs):
    x = np.asarray(inputs["L0_attn_x"], dtype=np.float32)
    shared = _prep_shared(inputs)

    nc, names = build_program(["l0ffn_a", "l0attn", "l0ffn_b",
                               "l1ffn_a", "l1attn", "l1ffn_b"])
    in_maps = []
    for c in range(NCORES):
        m = {k: shared[k] for k in names if k in shared}
        m["x_in"] = np.ascontiguousarray(
            x[c * NSEQ:(c + 1) * NSEQ].reshape(T, D))
        in_maps.append(m)
    res = run_bass_kernel_spmd(nc, in_maps, core_ids=list(range(NCORES)))
    out = np.concatenate(
        [res.results[c]["r_out"].reshape(NSEQ, SEQ, D) for c in range(NCORES)],
        axis=0)
    return out


# version 03
# speedup vs baseline: 1.2694x; 1.0243x over previous
import math
import numpy as np
import ml_dtypes
import concourse.bass as bass
import concourse.mybir as mybir
from concourse.bass_utils import run_bass_kernel_spmd

F32 = mybir.dt.float32
BF16 = mybir.dt.bfloat16
AF = mybir.ActivationFunctionType
ALU = mybir.AluOpType
AX = mybir.AxisListType

NCORES = 8
SEQ = 2048
NSEQ = 2
T = NSEQ * SEQ
NT = T // 128
NT_SEQ = SEQ // 128
D = 1024
NCH = D // 128
NTG = T // 512
H = 16
HD = 64
LN_EPS = 1e-5



DMA_SEMS = ("qr0", "qr1", "qi0", "qi1", "qo0", "qo1", "qp0", "qp1",
            "qn0", "qn1", "qv0", "qv1", "qf0", "qf1", "qm0", "qm1",
            "qx0", "qx1", "qy0", "qy1", "qz0", "qz1", "qd0", "qd1",
            "qe0", "qe1", "qu0", "qu1", "qA", "qB", "qw", "qc")


def phase_fence(ctx):
    return {k: ctx.cnt[k] for k in DMA_SEMS + ("ve", "se", "pe", "gp")
            if k in ctx.cnt}


def fence_wait(ctx, eng, pv):
    for k, val in pv.items():
        if val:
            ctx.w(eng, k, val)


class Ctx:

    def __init__(self, nc):
        self.nc = nc
        self.sem = {}
        self.cnt = {}

    def add_sem(self, es, name):
        self.sem[name] = es.enter_context(self.nc.semaphore(name))
        self.cnt[name] = 0

    def inc(self, instr, name, n=1):
        instr.then_inc(self.sem[name], n)
        self.cnt[name] += n
        return self.cnt[name]

    def w(self, eng, name, value):
        eng.wait_ge(self.sem[name], value)

    def wall(self, eng, name):
        eng.wait_ge(self.sem[name], self.cnt[name])

    def slot_inc(self, instr, name, slot):
        return self.inc(instr, f"{name}{slot}", 16)

    def slot_w(self, eng, name, slot, k):
        eng.wait_ge(self.sem[f"{name}{slot}"], 16 * k)


def emit_ln_transpose(nc, ctx, blk, src_dram, xnT, ln, ident_bf, pT,
                      dest_of_tile=None, dest_war=None, pqo="qo"):
    b = {k: ctx.cnt[k] for k in ("ve", "se", "pe")}
    b["qr0"] = ctx.cnt["qr0"]; b["qr1"] = ctx.cnt["qr1"]
    qo_prev = (ctx.cnt[pqo + "0"], ctx.cnt[pqo + "1"])

    def f_sync(sync):
        ctx.w(sync, pqo + "0", qo_prev[0])
        ctx.w(sync, pqo + "1", qo_prev[1])
        for j in range(NT):
            if j >= 2:
                ctx.w(sync, "ve", b["ve"] + (j - 2) * 5 + 5)
            ctx.slot_inc(sync.dma_start(out=ln[f"rt{j % 2}"][:, :],
                                        in_=src_dram[j * 128:(j + 1) * 128, :]),
                         "qr", j % 2)
    blk.sync(f_sync)

    def f_vec(v):
        for j in range(NT):
            rt = ln[f"rt{j % 2}"]
            xn = ln[f"xn{j % 2}"]
            ctx.slot_w(v, "qr", j % 2, b[f"qr{j % 2}"] // 16 + j // 2 + 1)
            ctx.inc(v.reduce_sum(ln["s"][:, :], rt[:, :], axis=AX.X), "ve")
            ctx.wall(v, "ve")
            ctx.inc(v.tensor_scalar(ln["negm"][:, :], ln["s"][:, :], -1.0 / D, None,
                                    ALU.mult), "ve")
            ctx.w(v, "se", b["se"] + (j + 1) * 2)
            ctx.inc(v.reciprocal(ln["rstd"][:, :], ln["sd"][:, :]), "ve")
            ctx.wall(v, "ve")
            if j >= 2:
                ctx.w(v, "pe", b["pe"] + j - 1)
            ctx.inc(v.tensor_scalar(xn[:, :], rt[:, :], ln["negm"][:, :],
                                    ln["rstd"][:, :], ALU.add, ALU.mult), "ve")
            ctx.w(v, "pe", b["pe"] + j + 1)
            if dest_war is not None:
                dest_war(v, j)
            dst = (xnT.rearrange("p (c t) -> p c t", c=NCH)[:, :,
                                 j * 128:(j + 1) * 128]
                   if dest_of_tile is None else dest_of_tile(j))
            ctx.inc(v.tensor_copy(
                dst, pT[j % 2].rearrange("p (c t) -> p c t", c=NCH)[:, :, :]),
                "ve")
    blk.vector(f_vec)

    def f_scal(sc):
        ctx.w(sc, "gp", ctx.setup["gp"])
        for j in range(NT):
            rt = ln[f"rt{j % 2}"]
            ctx.w(sc, "ve", b["ve"] + j * 5 + 2)
            ctx.inc(sc.activation(ln[f"xn{j % 2}"][:, :], rt[:, :], AF.Square,
                                  bias=ln["negm"][:, :], scale=1.0,
                                  accum_out=ln["ssq"][:, :]), "se")
            ctx.wall(sc, "se")
            ctx.inc(sc.activation(ln["sd"][:, :], ln["ssq"][:, :], AF.Sqrt,
                                  bias=ln["eps"][:, :], scale=1.0 / D), "se")
    blk.scalar(f_scal)

    def f_pe(t):
        for j in range(NT):
            xn = ln[f"xn{j % 2}"]
            ctx.w(t, "ve", b["ve"] + j * 5 + 4)
            if j >= 2:
                ctx.w(t, "ve", b["ve"] + (j - 2) * 5 + 5)
            for c in range(NCH):
                i = t.transpose(pT[j % 2][:, c * 128:(c + 1) * 128],
                                xn[:, c * 128:(c + 1) * 128], ident_bf[:, :])
            ctx.inc(i, "pe")
    blk.tensor(f_pe)


def emit_l0ffn(nc, ctx, blk, es, R_in, R_out, ln, ident_bf, pT, wdr, tag):
    E = es.enter_context
    xnT = E(nc.sbuf_tensor(f"f_xnT{tag}", [128, NCH * T], BF16))
    emit_ln_transpose(nc, ctx, blk, R_in, xnT, ln, ident_bf, pT)
    wr = E(nc.sbuf_tensor(f"f_wr{tag}", [128, NCH * D], BF16))
    wk = E(nc.sbuf_tensor(f"f_wk{tag}", [128, NCH * D], BF16))
    wv = E(nc.sbuf_tensor(f"f_wv{tag}", [128, NCH * D], BF16))
    mur = E(nc.sbuf_tensor(f"f_mur{tag}", [128, NCH], F32))
    muk = E(nc.sbuf_tensor(f"f_muk{tag}", [128, NCH], F32))
    omur = E(nc.sbuf_tensor(f"f_omur{tag}", [128, NCH], F32))
    omuk = E(nc.sbuf_tensor(f"f_omuk{tag}", [128, NCH], F32))
    xrg = [E(nc.sbuf_tensor(f"f_xrg{i}{tag}", [128, NCH * 512], BF16)) for i in range(2)]
    xkg = [E(nc.sbuf_tensor(f"f_xkg{i}{tag}", [128, NCH * 512], BF16)) for i in range(2)]
    kag = [E(nc.sbuf_tensor(f"f_kag{i}{tag}", [128, NCH * 512], BF16)) for i in range(2)]
    tmix = E(nc.sbuf_tensor(f"f_tmix{tag}", [128, 512], F32))
    kscr = [E(nc.sbuf_tensor(f"f_kscr{i}{tag}", [128, 512], BF16)) for i in range(2)]
    rsig = [E(nc.sbuf_tensor(f"f_rsig{i}{tag}", [128, D], BF16)) for i in range(2)]
    rvt = [E(nc.sbuf_tensor(f"f_rvt{i}{tag}", [128, 512], F32)) for i in range(2)]
    rin = [E(nc.sbuf_tensor(f"f_rin{i}{tag}", [128, D], F32)) for i in range(2)]
    pr = [E(nc.psum_tensor(f"f_pr{i}{tag}", [128, 512], F32)) for i in range(4)]
    pk = [E(nc.psum_tensor(f"f_pk{i}{tag}", [128, 512], F32)) for i in range(2)]

    def f_gw(g):
        ctx.wall(g, "qw")
        for (wsb, wd) in ((wr, wdr["wr_t"]), (wk, wdr["wk_t"]), (wv, wdr["wv_t"])):
            for c in range(NCH):
                ctx.inc(g.dma_start(out=wsb[:, c * D:(c + 1) * D],
                                    in_=wd[c * 128:(c + 1) * 128, :]), "qw", 16)
        ctx.inc(g.dma_start(out=mur[:, :], in_=wdr["mu_r2"]), "qw", 16)
        ctx.inc(g.dma_start(out=muk[:, :], in_=wdr["mu_k2"]), "qw", 16)
    blk.gpsimd(f_gw)
    w_loaded = ctx.cnt["qw"]

    pe0 = ctx.cnt["pe"]
    qi0 = (ctx.cnt["qi0"], ctx.cnt["qi1"])
    qo0 = (ctx.cnt["qo0"], ctx.cnt["qo1"])

    R_POS = [0, 1, 3, 5]
    V_POS = [2, 4, 6, 7]

    def pe_r_done(tg, jj):
        return pe0 + tg * 16 + 8 + R_POS[jj] + 1

    def pe_v_done(tg, jj):
        return pe0 + tg * 16 + 8 + V_POS[jj] + 1

    mix_done = {}
    comb_done = {}

    def f_vec(v):
        ctx.wall(v, "ve")
        ctx.w(v, "qw", w_loaded)
        ctx.inc(v.tensor_scalar(omur[:, :], mur[:, :], -1.0, 1.0, ALU.mult, ALU.add), "ve")
        ctx.inc(v.tensor_scalar(omuk[:, :], muk[:, :], -1.0, 1.0, ALU.mult, ALU.add), "ve")
        ctx.wall(v, "ve")

        def emit_mix(tg):
            if tg >= 2:
                ctx.w(v, "pe", pe0 + (tg - 2) * 16 + 16)
            for (dst, mu, omu) in ((xkg[tg % 2], muk, omuk), (xrg[tg % 2], mur, omur)):
                for c in range(NCH):
                    src = xnT[:, c * T + tg * 512: c * T + tg * 512 + 512]
                    d = dst[:, c * 512:(c + 1) * 512]
                    if tg % (SEQ // 512) == 0:
                        ctx.inc(v.tensor_scalar(
                            tmix[:, 1:], xnT[:, c * T + tg * 512: c * T + tg * 512 + 511],
                            omu[:, c:c + 1], None, ALU.mult), "ve")
                        ctx.inc(v.memset(tmix[:, 0:1], 0.0), "ve")
                    else:
                        ctx.inc(v.tensor_scalar(
                            tmix[:, :], xnT[:, c * T + tg * 512 - 1: c * T + tg * 512 + 511],
                            omu[:, c:c + 1], None, ALU.mult), "ve")
                    ctx.wall(v, "ve")
                    ctx.inc(v.scalar_tensor_tensor(d, src, mu[:, c:c + 1], tmix[:, :],
                                                   ALU.mult, ALU.add), "ve")
                    ctx.wall(v, "ve")
            mix_done[tg] = ctx.cnt["ve"]

        def emit_comb(tg):
            for jj in range(4):
                j = tg * 4 + jj
                jm = j % 2
                ctx.w(v, "pe", pe_v_done(tg, jj))
                ctx.slot_w(v, "qi", jm, qi0[jm] // 16 + j // 2 + 1)
                for half in range(2):
                    ctx.inc(v.tensor_tensor(rvt[half][:, :],
                                            rsig[jm][:, half * 512:half * 512 + 512],
                                            pr[2 * jm + half][:, :], ALU.mult), "ve")
                ctx.wall(v, "ve")
                for half in range(2):
                    ctx.inc(v.scalar_tensor_tensor(
                        rin[jm][:, half * 512:half * 512 + 512], rvt[half][:, :], 0.5,
                        rin[jm][:, half * 512:half * 512 + 512], ALU.mult, ALU.add), "ve")
                ctx.wall(v, "ve")
                comb_done[j] = ctx.cnt["ve"]

        emit_mix(0)
        emit_mix(1)
        for tg in range(NTG):
            emit_comb(tg)
            if tg + 2 < NTG:
                emit_mix(tg + 2)
    blk.vector(f_vec)

    se0 = ctx.cnt["se"]

    def se_ka(tg, dn):
        return se0 + tg * 24 + (dn + 1) * 2

    def se_sig(tg, jj):
        return se0 + tg * 24 + 16 + (jj + 1) * 2

    def f_pe(t):
        for tg in range(NTG):
            ctx.w(t, "ve", mix_done[tg])
            for dn in range(NCH):
                gi = tg * NCH + dn
                bnk = pk[gi % 2]
                if gi >= 2:
                    tg2, dn2 = divmod(gi - 2, NCH)
                    ctx.w(t, "se", se_ka(tg2, dn2))
                for c in range(NCH):
                    i = t.matmul(bnk[:, :],
                                 wk[:, c * D + dn * 128: c * D + dn * 128 + 128],
                                 xkg[tg % 2][:, c * 512: c * 512 + 512],
                                 start=(c == 0), stop=(c == NCH - 1))
                ctx.inc(i, "pe")
            def emit_rmm(jj):
                j = tg * 4 + jj
                jm = j % 2
                if j >= 2:
                    ctx.w(t, "ve", comb_done[j - 2])
                for half in range(2):
                    for c in range(NCH):
                        i = t.matmul(pr[2 * jm + half][:, :],
                                     xrg[tg % 2][:, c * 512 + jj * 128: c * 512 + jj * 128 + 128],
                                     wr[:, c * D + half * 512: c * D + half * 512 + 512],
                                     start=(c == 0), stop=(c == NCH - 1))
                ctx.inc(i, "pe")

            def emit_vmm(jj):
                j = tg * 4 + jj
                jm = j % 2
                ctx.w(t, "se", se_sig(tg, jj))
                for half in range(2):
                    for dn in range(NCH):
                        i = t.matmul(pr[2 * jm + half][:, :],
                                     kag[tg % 2][:, dn * 512 + jj * 128: dn * 512 + jj * 128 + 128],
                                     wv[:, dn * D + half * 512: dn * D + half * 512 + 512],
                                     start=(dn == 0), stop=(dn == NCH - 1))
                ctx.inc(i, "pe")

            emit_rmm(0); emit_rmm(1); emit_vmm(0); emit_rmm(2)
            emit_vmm(1); emit_rmm(3); emit_vmm(2); emit_vmm(3)
    blk.tensor(f_pe)

    def f_scal(sc):
        for tg in range(NTG):
            for dn in range(NCH):
                gi = tg * NCH + dn
                bnk = pk[gi % 2]
                ctx.w(sc, "pe", pe0 + tg * 16 + dn + 1)
                if gi >= 2:
                    ctx.w(sc, "se", se0 + ((gi - 2) // NCH) * 24 + ((gi - 2) % NCH + 1) * 2)
                ctx.inc(sc.activation(kscr[gi % 2][:, :], bnk[:, :], AF.Relu), "se")
                ctx.wall(sc, "se")
                if tg >= 2 and dn == 0:
                    ctx.w(sc, "pe", pe0 + (tg - 2) * 16 + 16)
                ctx.inc(sc.activation(kag[tg % 2][:, dn * 512:(dn + 1) * 512],
                                      kscr[gi % 2][:, :], AF.Square), "se")
            for jj in range(4):
                j = tg * 4 + jj
                jm = j % 2
                ctx.w(sc, "pe", pe_r_done(tg, jj))
                for half in range(2):
                    ctx.inc(sc.activation(rsig[jm][:, half * 512:half * 512 + 512],
                                          pr[2 * jm + half][:, :], AF.Sigmoid), "se")
    blk.scalar(f_scal)

    def f_rload(sync):
        for j in range(NT):
            jm = j % 2
            if j < 2:
                ctx.slot_w(sync, "qo", jm, qo0[jm] // 16)
            else:
                ctx.slot_w(sync, "qo", jm, qo0[jm] // 16 + (j - 2) // 2 + 1)
            ctx.slot_inc(sync.dma_start(out=rin[jm][:, :],
                                        in_=R_in[j * 128:(j + 1) * 128, :]),
                         "qi", jm)
    blk.sync(f_rload)

    def f_rstore(g):
        for j in range(NT):
            ctx.w(g, "ve", comb_done[j])
            ctx.slot_inc(g.dma_start(out=R_out[j * 128:(j + 1) * 128, :],
                                     in_=rin[j % 2][:, :]), "qo", j % 2)
    blk.gpsimd(f_rstore)


def emit_l0attn(nc, ctx, blk, es, R_in, R_out, ln, ident_bf, onesb, pT,
                wdr, dscr, tag):
    E = es.enter_context
    xnT = E(nc.sbuf_tensor(f"a_xnT{tag}", [128, NCH * T], BF16))
    emit_ln_transpose(nc, ctx, blk, R_in, xnT, ln, ident_bf, pT)
    wqkv = [E(nc.sbuf_tensor(f"a_wqkv{i}{tag}", [128, NCH * 384], BF16))
            for i in range(2)]
    wpj = E(nc.sbuf_tensor(f"a_wpj{tag}", [128, NCH * D], BF16))
    v2 = E(nc.sbuf_tensor(f"a_v2{tag}", [128, NT * 130], BF16))
    q65 = [E(nc.sbuf_tensor(f"a_q65{i}{tag}", [65, SEQ], BF16)) for i in range(2)]
    k65 = [E(nc.sbuf_tensor(f"a_k65{i}{tag}", [65, SEQ], BF16)) for i in range(2)]
    pbuf = [E(nc.sbuf_tensor(f"a_pb{i}{tag}", [128, 512], BF16)) for i in range(3)]
    pmask = E(nc.sbuf_tensor(f"a_pm{tag}", [128, 512], BF16))
    att = [E(nc.sbuf_tensor(f"a_att{i}{tag}", [64, SEQ], BF16)) for i in range(2)]
    negc = E(nc.sbuf_tensor(f"a_negc{tag}", [128, 16], F32))
    negcb = E(nc.sbuf_tensor(f"a_negcb{tag}", [128, 16], BF16))
    negcT = [E(nc.sbuf_tensor(f"a_negcT{i}{tag}", [16, 128], BF16)) for i in range(2)]
    scr65 = E(nc.sbuf_tensor(f"a_scr65{tag}", [65, 1024], F32))
    r65 = E(nc.sbuf_tensor(f"a_r65{tag}", [65, 1216], BF16))
    vs1 = E(nc.sbuf_tensor(f"a_vs1{tag}", [1, 130], F32))
    vscol = E(nc.sbuf_tensor(f"a_vscol{tag}", [64, 1], F32))
    s64 = E(nc.sbuf_tensor(f"a_s64{tag}", [64, 512], F32))
    corr2 = E(nc.sbuf_tensor(f"a_corr2{tag}", [64, 512], F32))
    arep = E(nc.sbuf_tensor(f"a_arep{tag}", [64, 512], F32))
    rin2 = [E(nc.sbuf_tensor(f"a_rin{i}{tag}", [128, D], F32)) for i in range(2)]
    pA = [E(nc.psum_tensor(f"a_pA{i}{tag}", [128, 512], F32)) for i in range(4)]
    ppv = E(nc.psum_tensor(f"a_ppv{tag}", [65, 512], F32))
    pqk = E(nc.psum_tensor(f"a_pqk{tag}", [64, 512], F32))

    attn_park = dscr["attn_park"]
    negc_d = dscr["negc_d"]
    vsum_d = dscr["vsum_d"]

    GAM, ALP, VSU, ONE = 0, 512, 1024, 1089

    pe_pos = {}

    def plan_pe():
        n = ctx.cnt["pe"]
        for hp in range(NCH):
            for tj in range(NT):
                n += 1; pe_pos[("vproj", hp, tj)] = n
            for sh in range(4):
                for tg in range(4):
                    n += 1; pe_pos[("qproj", hp, sh, tg)] = n
                for tg in range(4):
                    n += 1; pe_pos[("kproj", hp, sh, tg)] = n
                if sh % 2 == 0:
                    n += 1; pe_pos[("vsum", hp, sh)] = n
                for j in range(16):
                    n += 1; pe_pos[("sub", hp, sh, j)] = n
                n += 1; pe_pos[("negcT", hp, sh)] = n
                for qg in range(4):
                    for kc in range(4 * qg + 4):
                        n += 1; pe_pos[("S", hp, sh, qg, kc)] = n
                        n += 1; pe_pos[("PV", hp, sh, qg, kc)] = n
                    n += 1; pe_pos[("bcast_a", hp, sh, qg)] = n
                    n += 1; pe_pos[("bcast_g", hp, sh, qg)] = n
        for j in range(NT):
            n += 1; pe_pos[("proj", j)] = n
        return n

    plan_pe()
    last_kproj = pe_pos[("kproj", NCH - 1, 3, 3)]
    last_bcast = pe_pos[("bcast_g", NCH - 1, 3, 3)]

    gp_pos = {}

    def plan_gp():
        n = ctx.cnt["gp"]
        for hp in range(NCH):
            for sh in range(4):
                for qg in range(4):
                    for kc in range(4 * qg, 4 * qg + 4):
                        n += 1; gp_pos[(hp, sh, qg, kc)] = n
    plan_gp()

    def prev_shk(hp, sh, k=1):
        i = 4 * hp + sh - k
        return (i // 4, i % 4) if i >= 0 else None

    ve_m, se_m, dm_m = {}, {}, {}
    qo_prev = (ctx.cnt["qo0"], ctx.cnt["qo1"])
    qi0 = (ctx.cnt["qi0"], ctx.cnt["qi1"])
    qo0 = (ctx.cnt["qo0"], ctx.cnt["qo1"])
    qp0 = (ctx.cnt["qp0"], ctx.cnt["qp1"])
    qw_base = ctx.cnt["qw"]

    def qp_parks_before(hp, sh):
        return 2 * hp + sh // 2

    def f_vector(v):
        ctx.inc(v.memset(r65[:, ONE:ONE + 64], 1.0), "ve")
        ctx.wall(v, "ve")
        for hp in range(NCH):
            for tj in range(NT):
                ctx.w(v, "pe", pe_pos[("vproj", hp, tj)])
                ctx.inc(v.tensor_copy(v2[:, tj * 130: tj * 130 + 64],
                                      pA[2 + tj % 2][:, 0:64]), "ve")
                ctx.inc(v.tensor_copy(v2[:, tj * 130 + 65: tj * 130 + 129],
                                      pA[2 + tj % 2][:, 64:128]), "ve")
                ve_m[("v2", hp, tj)] = ctx.cnt["ve"]
            ctx.inc(v.memset(
                v2.rearrange("p (t c) -> p t c", c=130)[:, :, 64:65], 1.0), "ve")
            ctx.inc(v.memset(
                v2.rearrange("p (t c) -> p t c", c=130)[:, :, 129:130], 1.0), "ve")
            ve_m[("v2done", hp)] = ctx.cnt["ve"]
            for sh in range(4):
                su = sh % 2
                for tg in range(4):
                    ctx.w(v, "pe", pe_pos[("qproj", hp, sh, tg)])
                    ctx.inc(v.tensor_copy(q65[su][0:64, tg * 512:(tg + 1) * 512],
                                          pqk[:, :]), "ve")
                    ve_m[("qev", hp, sh, tg)] = ctx.cnt["ve"]
                for tg in range(4):
                    ctx.w(v, "pe", pe_pos[("kproj", hp, sh, tg)])
                    ctx.inc(v.tensor_copy(k65[su][0:64, tg * 512:(tg + 1) * 512],
                                          pqk[:, :]), "ve")
                    ve_m[("kev", hp, sh, tg)] = ctx.cnt["ve"]
                ctx.inc(v.memset(k65[su][64:65, :], 1.0), "ve")
                ve_m[("qk", hp, sh)] = ctx.cnt["ve"]
                if sh % 2 == 0:
                    ctx.w(v, "pe", pe_pos[("vsum", hp, sh)])
                    ctx.inc(v.tensor_copy(vs1[:, :], pA[2][0:1, 0:130]), "ve")
                    ve_m[("vs1", hp, sh)] = ctx.cnt["ve"]
                for j in range(16):
                    ctx.w(v, "pe", pe_pos[("sub", hp, sh, j)])
                    ctx.inc(v.reduce_max(negc[:, j:j + 1],
                                         pA[3 - j % 2][:, 0:8 * j + 1], axis=AX.X,
                                         negate=True), "ve")
                    ve_m[("sub", hp, sh, j)] = ctx.cnt["ve"]
                ctx.wall(v, "ve")
                p = prev_shk(hp, sh)
                if p is not None:
                    ctx.w(v, "pe", pe_pos[("negcT", p[0], p[1])])
                ctx.inc(v.tensor_copy(negcb[:, :], negc[:, :]), "ve")
                ve_m[("negcb", hp, sh)] = ctx.cnt["ve"]
                ctx.w(v, "pe", pe_pos[("negcT", hp, sh)])
                ctx.inc(v.tensor_copy(negcT[su][:, :], pT[0][0:16, 0:128]), "ve")
                ve_m[("negcT", hp, sh)] = ctx.cnt["ve"]
                for qg in range(4):
                    nk = 4 * qg + 4
                    ctx.w(v, "pe", pe_pos[("PV", hp, sh, qg, nk - 1)])
                    ctx.inc(v.tensor_scalar(scr65[64:65, 0:512], ppv[64:65, :],
                                            1.0 + 2048e-8, 2.048e-11,
                                            ALU.mult, ALU.add), "ve")
                    ctx.wall(v, "ve")
                    ctx.inc(v.reciprocal(scr65[64:65, 0:512],
                                         scr65[64:65, 0:512]), "ve")
                    ctx.wall(v, "ve")
                    ctx.inc(v.tensor_copy(r65[64:65, ALP:ALP + 512],
                                          scr65[64:65, 0:512]), "ve")
                    ctx.inc(v.tensor_scalar(scr65[64:65, 512:1024], ppv[64:65, :],
                                            1e-8, 1e-14, ALU.mult, ALU.add), "ve")
                    ctx.wall(v, "ve")
                    ctx.inc(v.tensor_tensor(r65[64:65, GAM:GAM + 512],
                                            scr65[64:65, 512:1024],
                                            scr65[64:65, 0:512], ALU.mult), "ve")
                    ve_m[("ag", hp, sh, qg)] = ctx.cnt["ve"]
                    ctx.w(v, "pe", pe_pos[("bcast_g", hp, sh, qg)])
                    ctx.inc(v.tensor_copy(arep[:, :], pA[2][0:64, :]), "ve")
                    ctx.wall(v, "ve")
                    ctx.wall(v, "ve")
                    ctx.inc(v.tensor_scalar(corr2[:, :], pA[3][0:64, 0:512],
                                            vscol[:, :], None, ALU.mult), "ve")
                    ctx.wall(v, "ve")
                    ctx.inc(v.tensor_tensor(s64[:, :], ppv[0:64, :],
                                            arep[:, :], ALU.mult), "ve")
                    ctx.wall(v, "ve")
                    if qg == 0:
                        ctx.slot_w(v, "qp", su,
                                   qp0[su] // 16 + qp_parks_before(hp, sh))
                    ctx.inc(v.tensor_tensor(att[su][:, qg * 512:(qg + 1) * 512],
                                            s64[:, :], corr2[:, :], ALU.add),
                            "ve")
                    ve_m[("att", hp, sh, qg)] = ctx.cnt["ve"]
        for j in range(NT):
            jm = j % 2
            ctx.w(v, "pe", pe_pos[("proj", j)])
            ctx.slot_w(v, "qi", jm, qi0[jm] // 16 + j // 2 + 1)
            for half in range(2):
                ctx.inc(v.scalar_tensor_tensor(
                    rin2[jm][:, half * 512:half * 512 + 512],
                    pA[2 * jm + half][:, :], 1.0,
                    rin2[jm][:, half * 512:half * 512 + 512], ALU.mult, ALU.add),
                    "ve")
            ctx.wall(v, "ve")
            ve_m[("comb", j)] = ctx.cnt["ve"]
    blk.vector(f_vector)

    def f_scalar(sc):
        for hp in range(NCH):
            for sh in range(4):
                for qg in range(4):
                    for kc in range(4 * qg + 4):
                        ctx.w(sc, "pe", pe_pos[("S", hp, sh, qg, kc)])
                        if kc >= 3:
                            k3 = kc - 3
                            if k3 >= 4 * qg:
                                ctx.w(sc, "gp", gp_pos[(hp, sh, qg, k3)])
                            else:
                                ctx.w(sc, "pe", pe_pos[("PV", hp, sh, qg, k3)])
                        ctx.inc(sc.activation(pbuf[kc % 3][:, :],
                                              pA[kc % 2][:, :],
                                              AF.Exp, scale=0.125), "se")
                        se_m[("exp", hp, sh, qg, kc)] = ctx.cnt["se"]
    blk.scalar(f_scalar)

    def f_gp(g):
        ctx.w(g, "qw", qw_base)
        for c in range(NCH):
            ctx.inc(g.dma_start(out=wpj[:, c * D:(c + 1) * D],
                                in_=wdr["wproj_t"][c * 128:(c + 1) * 128, :]),
                    "qw", 16)

        def load_hp(hp):
            for c in range(NCH):
                for (slot, off) in ((0, 0), (1, D), (2, 2 * D)):
                    ctx.inc(g.dma_start(
                        out=wqkv[hp % 2][:, c * 384 + slot * 128:
                                         c * 384 + slot * 128 + 128],
                        in_=wdr["wqkv_t"][c * 128:(c + 1) * 128,
                                          off + hp * 128: off + hp * 128 + 128]),
                        "qw", 16)
            dm_m[("w", hp)] = ctx.cnt["qw"]

        load_hp(0)
        load_hp(1)
        last_diag = None
        for hp in range(NCH):
            for sh in range(4):
                su = sh % 2
                for qg in range(4):
                    for kc in range(4 * qg, 4 * qg + 4):
                        ctx.w(g, "se", se_m[("exp", hp, sh, qg, kc)])
                        if last_diag is not None:
                            ctx.w(g, "pe", pe_pos[("PV", *last_diag)])
                        ctx.inc(g.affine_select(
                            pmask[:, :], pbuf[kc % 3][:, :], pattern=[[1, 512]],
                            compare_op=ALU.is_ge, fill=0.0,
                            base=qg * 512 - kc * 128,
                            channel_multiplier=-1), "gp")
                        last_diag = (hp, sh, qg, kc)
                ctx.w(g, "ve", ve_m[("att", hp, sh, 3)])
                ctx.slot_inc(g.dma_start(
                    out=attn_park[hp, (sh % 2) * 64:(sh % 2) * 64 + 64,
                                  (sh // 2) * SEQ:(sh // 2 + 1) * SEQ],
                    in_=att[su][:, :]), "qp", su)
            if hp + 2 < NCH:
                ctx.w(g, "pe", pe_pos[("kproj", hp, 3, 3)])
                ctx.w(g, "qw", dm_m[("w", hp + 1)])
                load_hp(hp + 2)
        ctx.w(g, "pe", last_kproj)
        ctx.slot_w(g, "qp", 0, ctx.cnt["qp0"] // 16)
        ctx.slot_w(g, "qp", 1, ctx.cnt["qp1"] // 16)
        for c in range(NCH):
            ctx.inc(g.dma_start(out=xnT[:, c * T:(c + 1) * T],
                                in_=attn_park[c, :, :]), "qA", 16)
        dm_m[("qA",)] = ctx.cnt["qA"]
        for j in range(NT):
            ctx.w(g, "ve", ve_m[("comb", j)])
            ctx.slot_inc(g.dma_start(out=R_out[j * 128:(j + 1) * 128, :],
                                     in_=rin2[j % 2][:, :]), "qo", j % 2)
    blk.gpsimd(f_gp)

    def f_sync(sync):
        for hp in range(NCH):
            for sh in range(4):
                su = sh % 2
                s_ = sh // 2
                h01 = sh % 2
                if h01 == 0:
                    ctx.w(sync, "ve", ve_m[("vs1", hp, sh)])
                    ctx.slot_inc(sync.dma_start(out=vsum_d[s_], in_=vs1[:, :]),
                                 "qv", s_)
                ctx.slot_w(sync, "qv", s_, ctx.cnt[f"qv{s_}"] // 16)
                p = prev_shk(hp, sh)
                if p is not None:
                    ctx.w(sync, "ve", ve_m[("att", p[0], p[1], 3)])
                ctx.slot_inc(sync.dma_start(
                    out=vscol[:, :],
                    in_=vsum_d[s_][:, h01 * 65: h01 * 65 + 64].rearrange(
                        "o n -> (o n)")[:, None]), "qv", s_)
                dm_m[("vsrow", hp, sh)] = ctx.cnt[f"qv{s_}"]
                ctx.w(sync, "ve", ve_m[("negcT", hp, sh)])
                ctx.slot_inc(sync.dma_start(out=negc_d[su], in_=negcT[su][:, :]),
                             "qn", su)
                ctx.slot_w(sync, "qn", su, ctx.cnt[f"qn{su}"] // 16)
                p2 = prev_shk(hp, sh, 2)
                if p2 is not None:
                    ctx.w(sync, "pe", pe_pos[("S", p2[0], p2[1], 3, 15)])
                ctx.slot_inc(sync.dma_start(
                    out=q65[su][64:65, :],
                    in_=negc_d[su].rearrange("a b -> (a b)")[None, :]), "qn", su)
                dm_m[("negcrow", hp, sh)] = ctx.cnt[f"qn{su}"]
        for j in range(NT):
            jm = j % 2
            if j < 2:
                ctx.slot_w(sync, "qo", jm, qo_prev[jm] // 16)
            else:
                ctx.slot_w(sync, "qo", jm, qo0[jm] // 16 + (j - 2) // 2 + 1)
            ctx.slot_inc(sync.dma_start(out=rin2[jm][:, :],
                                        in_=R_in[j * 128:(j + 1) * 128, :]),
                         "qi", jm)
    blk.sync(f_sync)

    def f_pe(t):
        for hp in range(NCH):
            ctx.w(t, "qw", dm_m[("w", max(hp, 1))])
            if hp > 0:
                ctx.w(t, "ve", ve_m[("att", hp - 1, 3, 3)])
            for tj in range(NT):
                if tj >= 2:
                    ctx.w(t, "ve", ve_m[("v2", hp, tj - 2)])
                for c in range(NCH):
                    i = t.matmul(pA[2 + tj % 2][:, 0:128],
                                 xnT[:, c * T + tj * 128: c * T + tj * 128 + 128],
                                 wqkv[hp % 2][:, c * 384 + 256: c * 384 + 384],
                                 start=(c == 0), stop=(c == NCH - 1))
                ctx.inc(i, "pe")
            for sh in range(4):
                su = sh % 2
                s_ = sh // 2
                h01 = sh % 2
                scol = s_ * SEQ
                p2 = prev_shk(hp, sh, 2)
                if p2 is not None:
                    ctx.w(t, "pe", pe_pos[("S", p2[0], p2[1], 3, 15)])
                for tg in range(4):
                    if tg > 0:
                        ctx.w(t, "ve", ve_m[("qev", hp, sh, tg - 1)])
                    elif (hp, sh) != (0, 0):
                        p = prev_shk(hp, sh)
                        ctx.w(t, "ve", ve_m[("kev", p[0], p[1], 3)])
                    for c in range(NCH):
                        i = t.matmul(pqk[:, :],
                                     wqkv[hp % 2][:, c * 384 + h01 * 64:
                                                  c * 384 + h01 * 64 + 64],
                                     xnT[:, c * T + scol + tg * 512:
                                         c * T + scol + tg * 512 + 512],
                                     start=(c == 0), stop=(c == NCH - 1))
                    ctx.inc(i, "pe")
                for tg in range(4):
                    ctx.w(t, "ve", ve_m[("qev", hp, sh, 3)] if tg == 0 else
                          ve_m[("kev", hp, sh, tg - 1)])
                    for c in range(NCH):
                        i = t.matmul(pqk[:, :],
                                     wqkv[hp % 2][:, c * 384 + 128 + h01 * 64:
                                                  c * 384 + 128 + h01 * 64 + 64],
                                     xnT[:, c * T + scol + tg * 512:
                                         c * T + scol + tg * 512 + 512],
                                     start=(c == 0), stop=(c == NCH - 1))
                    ctx.inc(i, "pe")
                if h01 == 0:
                    ctx.w(t, "ve", ve_m[("v2done", hp)])
                    if (hp, sh) != (0, 0):
                        p = prev_shk(hp, sh)
                        ctx.w(t, "ve", ve_m[("att", p[0], p[1], 3)])
                    for kt in range(16):
                        i = t.matmul(pA[2][0:1, 0:130], onesb[:, :],
                                     v2[:, (s_ * 16 + kt) * 130:
                                        (s_ * 16 + kt) * 130 + 130],
                                     start=(kt == 0), stop=(kt == 15))
                    ctx.inc(i, "pe")
                ctx.w(t, "ve", ve_m[("qk", hp, sh)])
                if h01 == 1 and (hp, sh) != (0, 1):
                    p = prev_shk(hp, sh)
                    ctx.w(t, "ve", ve_m[("att", p[0], p[1], 3)])
                for j in range(16):
                    if j == 1:
                        ctx.w(t, "ve", ve_m.get(("vs1", hp, sh - h01), 0))
                    if j >= 2:
                        ctx.w(t, "ve", ve_m[("sub", hp, sh, j - 2)])
                    rhs = k65[su][0:64, :].rearrange(
                        "p (a b) -> p a b", b=16)[:, 0:8 * j + 1, 0:1]
                    i = t.matmul(pA[3 - j % 2][:, 0:8 * j + 1],
                                 q65[su][0:64, j * 128:(j + 1) * 128],
                                 rhs, start=True, stop=True)
                    ctx.inc(i, "pe")
                ctx.w(t, "ve", ve_m[("negcb", hp, sh)])
                ctx.inc(t.transpose(pT[0][0:16, 0:128], negcb[:, :],
                                    ident_bf[:, :]), "pe")
                ctx.slot_w(t, "qn", su, dm_m[("negcrow", hp, sh)] // 16)
                ctx.slot_w(t, "qv", s_, dm_m[("vsrow", hp, sh)] // 16)
                for qg in range(4):
                    nk = 4 * qg + 4
                    for kc in range(nk):
                        if kc >= 2:
                            ctx.w(t, "se", se_m[("exp", hp, sh, qg, kc - 2)])
                        elif qg == 0 and kc == 0:
                            pass
                        i = t.matmul(pA[kc % 2][:, :],
                                     k65[su][:, kc * 128:(kc + 1) * 128],
                                     q65[su][:, qg * 512:(qg + 1) * 512],
                                     start=True, stop=True)
                        ctx.inc(i, "pe")
                        if kc == 0 and qg > 0:
                            ctx.w(t, "ve", ve_m[("att", hp, sh, qg - 1)])
                        if kc >= 4 * qg:
                            ctx.w(t, "gp", gp_pos[(hp, sh, qg, kc)])
                            pv_rhs = pmask[:, :]
                        else:
                            ctx.w(t, "se", se_m[("exp", hp, sh, qg, kc)])
                            pv_rhs = pbuf[kc % 3][:, :]
                        i = t.matmul(ppv[:, :],
                                     v2[:, (s_ * 16 + kc) * 130 + h01 * 65:
                                        (s_ * 16 + kc) * 130 + h01 * 65 + 65],
                                     pv_rhs, start=(kc == 0), stop=(kc == nk - 1))
                        ctx.inc(i, "pe")
                    ctx.w(t, "ve", ve_m[("ag", hp, sh, qg)])
                    ctx.inc(t.matmul(pA[2][0:64, :], r65[64:65, ONE:ONE + 64],
                                     r65[64:65, ALP:ALP + 512],
                                     start=True, stop=True), "pe")
                    ctx.inc(t.matmul(pA[3][0:64, 0:512],
                                     r65[64:65, ONE:ONE + 64],
                                     r65[64:65, GAM:GAM + 512],
                                     start=True, stop=True), "pe")
        ctx.w(t, "qA", dm_m[("qA",)])
        ctx.w(t, "ve", ve_m[("att", NCH - 1, 3, 3)])
        for j in range(NT):
            jm = j % 2
            if j >= 2:
                ctx.w(t, "ve", ve_m[("comb", j - 2)])
            for half in range(2):
                for c in range(NCH):
                    i = t.matmul(pA[2 * jm + half][:, :],
                                 xnT[:, c * T + j * 128: c * T + j * 128 + 128],
                                 wpj[:, c * D + half * 512: c * D + half * 512 + 512],
                                 start=(c == 0), stop=(c == NCH - 1))
            ctx.inc(i, "pe")
    blk.tensor(f_pe)


def emit_l1ffn(nc, ctx, blk, es, R_in, R_out, ident_bf, ln, pT, wdr, dscr, tag,
               pqo="qo"):
    phv = phase_fence(ctx)
    from contextlib import ExitStack
    E = es.enter_context
    HT = T // 2
    NST = HT // 512
    gallT = E(nc.sbuf_tensor(f"m_gallT{tag}", [8, T], BF16))
    gb = E(nc.sbuf_tensor(f"m_gb{tag}", [128, 8], BF16))
    wrsb = E(nc.sbuf_tensor(f"m_wr{tag}", [128, NCH * 8], BF16))
    brrep = E(nc.sbuf_tensor(f"m_brrep{tag}", [128, 8], F32))
    brsb = E(nc.sbuf_tensor(f"m_brs{tag}", [1, 8], BF16))
    onesr = E(nc.sbuf_tensor(f"m_onesr{tag}", [1, 128], BF16))
    b1sb = E(nc.sbuf_tensor(f"m_b1{tag}", [128, 8 * 16], F32))
    b2sb = E(nc.sbuf_tensor(f"m_b2{tag}", [8, D], BF16))
    yaccT = E(nc.sbuf_tensor(f"m_yacc{tag}", [128, NCH * HT], BF16))
    lg = E(nc.sbuf_tensor(f"m_lg{tag}", [128, 8], F32))
    gs1 = E(nc.sbuf_tensor(f"m_gs1{tag}", [128, 8], F32))
    gs2 = E(nc.sbuf_tensor(f"m_gs2{tag}", [128, 8], F32))
    gm = E(nc.sbuf_tensor(f"m_gm{tag}", [128, 1], F32))
    gm2 = E(nc.sbuf_tensor(f"m_gm2{tag}", [128, 1], F32))
    gnm = E(nc.sbuf_tensor(f"m_gnm{tag}", [128, 1], F32))
    ged = E(nc.sbuf_tensor(f"m_ged{tag}", [128, 1], F32))
    grd = E(nc.sbuf_tensor(f"m_grd{tag}", [128, 1], F32))
    w2b = E(nc.sbuf_tensor(f"m_w2{tag}", [128, 16 * D], BF16))
    hT = E(nc.sbuf_tensor(f"m_hT{tag}", [128, 16 * 512], BF16))
    xst = [E(nc.sbuf_tensor(f"m_xst{i}{tag}", [128, NCH * 512], BF16))
           for i in range(2)]
    gerow = E(nc.sbuf_tensor(f"m_gerow{tag}", [1, HT], BF16))
    grep = E(nc.sbuf_tensor(f"m_grep{tag}", [128, 512], BF16))
    ytmp = E(nc.sbuf_tensor(f"m_ytmp{tag}", [128, 512], F32))
    ytile = E(nc.sbuf_tensor(f"m_ytile{tag}", [128, D], BF16))
    rin3 = [ln["rt0"], ln["rt1"]]
    ph = [E(nc.psum_tensor(f"m_ph{i}{tag}", [128, 512], F32)) for i in range(2)]
    py = [E(nc.psum_tensor(f"m_py{i}{tag}", [128, 512], F32)) for i in range(4)]

    xnpark = dscr["xnpark"]
    g_d = dscr["g_d"]

    xtile = [E(nc.sbuf_tensor(f"m_xt{i}{tag}", [128, NCH * 128], BF16))
             for i in range(2)]
    xg = [E(nc.sbuf_tensor(f"m_xg{i}{tag}", [128, NCH * 128], BF16))
          for i in range(2)]
    w1b = [E(nc.sbuf_tensor(f"m_w1{i}{tag}", [128, NCH * 2048], BF16))
           for i in range(2)]

    def eidx(hf, e):
        return hf * 8 + e

    def sidx(hf, e, st):
        return (hf * 8 + e) * NST + st

    lnve0 = ctx.cnt["ve"]
    qz_base = (ctx.cnt["qz0"], ctx.cnt["qz1"])

    def ln_dest(j):
        return xtile[j % 2].rearrange("p (c t) -> p c t", c=NCH)[:, :, :]

    def ln_war(v, j):
        if j >= 2:
            ctx.slot_w(v, "qz", j % 2, qz_base[j % 2] // 16 + j // 2)

    emit_ln_transpose(nc, ctx, blk, R_in, None, ln, ident_bf, pT, pqo=pqo,
                      dest_of_tile=ln_dest, dest_war=ln_war)

    def ln_evac(j):
        return lnve0 + 5 * j + 5

    pe_pos = {}

    def plan_pe():
        n = ctx.cnt["pe"]
        n += 1; pe_pos[("brbc",)] = n
        for tj in range(NT):
            n += 1; pe_pos[("lg", tj)] = n
            n += 1; pe_pos[("gT", tj)] = n
        for hf in range(2):
            for dn in range(NCH):
                for tg in range(NST):
                    n += 1; pe_pos[("b2g", hf, dn, tg)] = n
            for e in range(8):
                for st in range(NST):
                    n += 1; pe_pos[("grep", hf, e, st)] = n
                    for fn in range(16):
                        n += 1; pe_pos[("h", hf, e, st, fn)] = n
                        n += 1; pe_pos[("yA", hf, e, st, fn)] = n
                    for fn in range(16):
                        n += 1; pe_pos[("yB", hf, e, st, fn)] = n
            for j in range(16):
                n += 1; pe_pos[("yt", hf, j)] = n
        return n
    plan_pe()

    ve_m, se_m, dm_m = {}, {}, {}
    qi0 = (ctx.cnt["qi0"], ctx.cnt["qi1"])
    qo0 = (ctx.cnt["qu0"], ctx.cnt["qu1"])
    qo_prev = (ctx.cnt[pqo + "0"], ctx.cnt[pqo + "1"])
    qw_base = ctx.cnt["qw"]
    se_gate0 = ctx.cnt["se"]
    assert se_gate0 >= 2 * NT

    def f_vector(v):
        ctx.inc(v.memset(onesr[:, :], 1.0), "ve")
        ve_m[("onesr",)] = ctx.cnt["ve"]
        ctx.wall(v, "ve")
        ctx.w(v, "pe", pe_pos[("brbc",)])
        ctx.inc(v.tensor_copy(brrep[:, :], ph[1][:, 0:8]), "ve")
        ve_m[("brrep",)] = ctx.cnt["ve"]
        ctx.wall(v, "ve")
        for tj in range(NT):
            ctx.w(v, "pe", pe_pos[("lg", tj)])
            ctx.inc(v.tensor_tensor(lg[:, :], ph[0][:, 0:8], brrep[:, :],
                                    ALU.add), "ve")
            ve_m[("glg", tj)] = ctx.cnt["ve"]
            ctx.wall(v, "ve")
            ctx.inc(v.reduce_max(gm[:, :], lg[:, :], axis=AX.X), "ve")
            ctx.wall(v, "ve")
            ctx.inc(v.tensor_scalar(gs1[:, :], lg[:, :], gm[:, :], None,
                                    ALU.is_lt), "ve")
            ctx.inc(v.tensor_scalar(gnm[:, :], gm[:, :], -1.0, None,
                                    ALU.mult), "ve")
            ctx.wall(v, "ve")
            ctx.inc(v.tensor_tensor(gs2[:, :], lg[:, :], gs1[:, :],
                                    ALU.mult), "ve")
            ctx.wall(v, "ve")
            ctx.inc(v.tensor_scalar(gs1[:, :], gs1[:, :], 1e30, 1e30,
                                    ALU.mult, ALU.subtract), "ve")
            ctx.wall(v, "ve")
            ctx.inc(v.tensor_tensor(gs2[:, :], gs2[:, :], gs1[:, :],
                                    ALU.add), "ve")
            ctx.wall(v, "ve")
            ctx.inc(v.reduce_max(gm2[:, :], gs2[:, :], axis=AX.X), "ve")
            ve_m[("gm2", tj)] = ctx.cnt["ve"]
            ctx.w(v, "se", 2 * (tj + 1) + se_gate0)
            ctx.inc(v.tensor_scalar(grd[:, :], ged[:, :], 1.0, None,
                                    ALU.add), "ve")
            ctx.wall(v, "ve")
            ctx.inc(v.reciprocal(grd[:, :], grd[:, :]), "ve")
            ctx.wall(v, "ve")
            ctx.inc(v.tensor_scalar(gs2[:, :], lg[:, :], gm2[:, :], None,
                                    ALU.is_ge), "ve")
            ctx.wall(v, "ve")
            ctx.inc(v.tensor_tensor(gs2[:, :], gs2[:, :], gs1[:, :],
                                    ALU.mult), "ve")
            ctx.wall(v, "ve")
            ctx.inc(v.tensor_scalar(gb[:, :], gs2[:, :],
                                    grd[:, :], 0.5, ALU.mult, ALU.mult), "ve")
            ve_m[("gate", tj)] = ctx.cnt["ve"]
            ctx.w(v, "pe", pe_pos[("gT", tj)])
            ctx.inc(v.tensor_copy(gallT[:, tj * 128:(tj + 1) * 128],
                                  pT[0][0:8, 0:128]), "ve")
            ve_m[("gT", tj)] = ctx.cnt["ve"]
        for hf in range(2):
            for dn in range(NCH):
                for tg in range(NST):
                    ctx.w(v, "pe", pe_pos[("b2g", hf, dn, tg)])
                    ctx.inc(v.tensor_copy(
                        yaccT[:, dn * HT + tg * 512: dn * HT + tg * 512 + 512],
                        py[tg % 4][:, :]), "ve")
                    ve_m[("b2g", hf, dn, tg)] = ctx.cnt["ve"]
            for e in range(8):
                for st in range(NST):
                    ctx.w(v, "pe", pe_pos[("grep", hf, e, st)])
                    ctx.inc(v.tensor_copy(grep[:, :], ph[0][:, :]), "ve")
                    ve_m[("grep", hf, e, st)] = ctx.cnt["ve"]
                    for pss, key in ((0, "yA"), (1, "yB")):
                        ctx.w(v, "pe", pe_pos[(key, hf, e, st, 15)])
                        for dn4 in range(4):
                            dn = pss * 4 + dn4
                            ctx.inc(v.tensor_tensor(ytmp[:, :], py[dn4][:, :],
                                                    grep[:, :], ALU.mult), "ve")
                            ctx.wall(v, "ve")
                            sl = yaccT[:, dn * HT + st * 512:
                                       dn * HT + st * 512 + 512]
                            ctx.inc(v.tensor_tensor(sl, sl, ytmp[:, :],
                                                    ALU.add), "ve")
                            ve_m[("yev", hf, e, st, dn)] = ctx.cnt["ve"]
                            ctx.wall(v, "ve")
            for j in range(16):
                jg = hf * 16 + j
                jm = jg % 2
                ctx.w(v, "pe", pe_pos[("yt", hf, j)])
                ctx.inc(v.tensor_copy(
                    ytile.rearrange("p (c n) -> p c n", c=NCH)[:, :, :],
                    pT[j % 2].rearrange("p (c n) -> p c n", c=NCH)[:, :, :]),
                    "ve")
                ctx.wall(v, "ve")
                ctx.slot_w(v, "qi", jm, qi0[jm] // 16 + jg // 2 + 1)
                for half in range(2):
                    ctx.inc(v.tensor_tensor(
                        rin3[jm][:, half * 512:half * 512 + 512],
                        rin3[jm][:, half * 512:half * 512 + 512],
                        ytile[:, half * 512:half * 512 + 512], ALU.add), "ve")
                ctx.wall(v, "ve")
                ve_m[("comb", jg)] = ctx.cnt["ve"]

    def f_scalar(sc):
        for tj in range(NT):
            ctx.w(sc, "ve", ve_m[("gm2", tj)])
            ctx.inc(sc.activation(gs1[:, :], lg[:, :], AF.Exp,
                                  bias=gnm[:, :]), "se")
            ctx.wall(sc, "se")
            ctx.inc(sc.activation(ged[:, :], gm2[:, :], AF.Exp,
                                  bias=gnm[:, :]), "se")
            se_m[("ged", tj)] = ctx.cnt["se"]
        for hf in range(2):
            for e in range(8):
                for st in range(NST):
                    for fn in range(16):
                        ctx.w(sc, "pe", pe_pos[("h", hf, e, st, fn)])
                        if fn == 0 and sidx(hf, e, st) > 0:
                            i2 = sidx(hf, e, st) - 1
                            h2, r2 = divmod(i2, 8 * NST)
                            e2, s2 = divmod(r2, NST)
                            ctx.w(sc, "pe", pe_pos[("yB", h2, e2, s2, 15)])
                        ctx.inc(sc.activation(
                            hT[:, fn * 512:(fn + 1) * 512],
                            ph[fn % 2][:, :], AF.Relu,
                            bias=b1sb[:, e * 16 + fn: e * 16 + fn + 1]), "se")
                        se_m[("relu", hf, e, st, fn)] = ctx.cnt["se"]

    def f_gp(g):
        fence_wait(ctx, g, phv)
        ctx.w(g, "qw", qw_base)
        for c in range(NCH):
            ctx.inc(g.dma_start(out=wrsb[:, c * 8:(c + 1) * 8],
                                in_=wdr["m_wr"][c * 128:(c + 1) * 128, :]),
                    "qw", 16)
        ctx.inc(g.dma_start(out=brsb[:, :], in_=wdr["m_br"][None, :]), "qw", 16)
        ctx.inc(g.dma_start(out=b1sb[:, :], in_=wdr["m_b1"]), "qw", 16)
        ctx.inc(g.dma_start(out=b2sb[:, :], in_=wdr["m_b2"]), "qw", 16)
        dm_m[("wconst",)] = ctx.cnt["qw"]

        def load_w1(hf, e, slot):
            ctx.slot_w(g, "qf", slot, ctx.cnt[f"qf{slot}"] // 16)
            for c in range(NCH):
                ctx.slot_inc(g.dma_start(
                    out=w1b[slot][:, c * 2048:(c + 1) * 2048],
                    in_=wdr["m_w1"][e, c * 128:(c + 1) * 128, :]), "qf", slot)
            dm_m[("w1", hf, e)] = ctx.cnt[f"qf{slot}"]

        def load_w2(hf, e):
            ctx.slot_w(g, "qm", 0, ctx.cnt["qm0"] // 16)
            for fn in range(16):
                ctx.slot_inc(g.dma_start(
                    out=w2b[:, fn * D:(fn + 1) * D],
                    in_=wdr["m_w2"][e, fn * 128:(fn + 1) * 128, :]), "qm", 0)
            dm_m[("w2", hf, e)] = ctx.cnt["qm0"]

        load_w1(0, 0, 0)
        load_w2(0, 0)
        for hf in range(2):
            for e in range(8):
                i = eidx(hf, e)
                if i + 1 < 16:
                    nhf, ne = divmod(i + 1, 8)
                    if i >= 1:
                        p = i - 1
                        ctx.w(g, "pe", pe_pos[("yA", p // 8, p % 8,
                                               NST - 1, 15)])
                    load_w1(nhf, ne, (i + 1) % 2)
                    ctx.w(g, "pe", pe_pos[("yB", hf, e, NST - 1, 15)])
                    load_w2(nhf, ne)
    xpark_cnt = [0]

    def f_park(g):
        for j in range(NT):
            ctx.w(g, "ve", ln_evac(j))
            ctx.slot_inc(g.dma_start(
                out=xnpark[:, :, j * 128:(j + 1) * 128].rearrange(
                    "c p t -> p c t"),
                in_=xtile[j % 2].rearrange("p (c t) -> p c t", c=NCH)),
                "qz", j % 2)
        xpark_cnt[0] = (ctx.cnt["qz0"], ctx.cnt["qz1"])

    def f_sync(sync):
        for j in range(NT):
            jm = j % 2
            if j >= 2:
                ctx.w(sync, "pe", pe_pos[("lg", j - 2)])
            ctx.slot_w(sync, "qz", jm, qz_base[jm] // 16 + j // 2 + 1)
            ctx.slot_inc(sync.dma_start(
                out=xg[jm].rearrange("p (c t) -> p c t", c=NCH),
                in_=xnpark[:, :, j * 128:(j + 1) * 128].rearrange(
                    "c p t -> p c t")), "qy", jm)
            dm_m[("xg", j)] = ctx.cnt[f"qy{jm}"]
        ctx.w(sync, "ve", ve_m[("gT", NT - 1)])
        ctx.inc(sync.dma_start(out=g_d[:, :], in_=gallT[:, :]), "qB", 16)
        dm_m[("gd",)] = ctx.cnt["qB"]
        for hf in range(2):
            for e in range(8):
                i = eidx(hf, e)
                if i >= 1:
                    p = i - 1
                    ctx.w(sync, "pe", pe_pos[("grep", p // 8, p % 8, NST - 1)])
                ctx.slot_w(sync, "qm", 1, ctx.cnt["qm1"] // 16)
                ctx.w(sync, "qB", dm_m[("gd",)])
                ctx.slot_inc(sync.dma_start(
                    out=gerow[:, :],
                    in_=g_d[e:e + 1, hf * HT:(hf + 1) * HT]), "qm", 1)
                dm_m[("ge", hf, e)] = ctx.cnt["qm1"]
                for st in range(NST):
                    sl = st % 2
                    if sidx(hf, e, st) >= 2:
                        i2 = sidx(hf, e, st) - 2
                        h2, r2 = divmod(i2, 8 * NST)
                        e2, s2 = divmod(r2, NST)
                        ctx.w(sync, "pe", pe_pos[("h", h2, e2, s2, 15)])
                    else:
                        ctx.slot_w(sync, "qz", 0, xpark_cnt[0][0] // 16)
                        ctx.slot_w(sync, "qz", 1, xpark_cnt[0][1] // 16)
                    for c in range(NCH):
                        ctx.slot_inc(sync.dma_start(
                            out=xst[sl][:, c * 512:(c + 1) * 512],
                            in_=xnpark[c, :, hf * HT + st * 512:
                                       hf * HT + st * 512 + 512]), "qx", sl)
                    dm_m[("xst", hf, e, st)] = ctx.cnt[f"qx{sl}"]
            for j in range(16):
                jg = hf * 16 + j
                jm = jg % 2
                if jg >= 2:
                    ctx.w(sync, "ve", ve_m[("comb", jg - 2)])
                    ctx.slot_inc(sync.dma_start(
                        out=R_out[(jg - 2) * 128:(jg - 1) * 128, :],
                        in_=rin3[jm][:, :]), "qu", jm)
                    ctx.slot_w(sync, "qu", jm,
                               qo0[jm] // 16 + (jg - 2) // 2 + 1)
                else:
                    ctx.slot_w(sync, pqo, jm, qo_prev[jm] // 16)
                ctx.slot_inc(sync.dma_start(
                    out=rin3[jm][:, :],
                    in_=R_in[jg * 128:(jg + 1) * 128, :]), "qi", jm)
        for jg in (NT - 2, NT - 1):
            jm = jg % 2
            ctx.w(sync, "ve", ve_m[("comb", jg)])
            ctx.slot_inc(sync.dma_start(
                out=R_out[jg * 128:(jg + 1) * 128, :],
                in_=rin3[jm][:, :]), "qu", jm)

    def f_pe(t):
        ctx.w(t, "qw", dm_m[("wconst",)])
        ctx.w(t, "ve", ve_m[("onesr",)])
        ctx.inc(t.matmul(ph[1][:, 0:8], onesr[:, :], brsb[:, :],
                         start=True, stop=True), "pe")
        for tj in range(NT):
            if tj > 0:
                ctx.w(t, "ve", ve_m[("glg", tj - 1)])
            ctx.slot_w(t, "qy", tj % 2, dm_m[("xg", tj)] // 16)
            for c in range(NCH):
                i = t.matmul(ph[0][:, 0:8],
                             xg[tj % 2][:, c * 128:(c + 1) * 128],
                             wrsb[:, c * 8:(c + 1) * 8],
                             start=(c == 0), stop=(c == NCH - 1))
            ctx.inc(i, "pe")
            ctx.w(t, "ve", ve_m[("gate", tj)])
            if tj >= 1:
                ctx.w(t, "ve", ve_m[("gT", tj - 1)])
            ctx.inc(t.transpose(pT[0][0:8, 0:128], gb[:, :], ident_bf[:, :]),
                    "pe")
        for hf in range(2):
            ctx.w(t, "ve", ve_m[("gT", NT - 1)])
            for dn in range(NCH):
                for tg in range(NST):
                    g4 = dn * NST + tg - 4
                    if g4 >= 0:
                        ctx.w(t, "ve", ve_m[("b2g", hf, g4 // NST, g4 % NST)])
                    elif hf == 1:
                        ctx.w(t, "ve", ve_m[("yev", 0, 7, NST - 1,
                                             4 + (dn * NST + tg) % 4)])
                    i = t.matmul(py[(dn * NST + tg) % 4][:, :],
                                 b2sb[:, dn * 128:(dn + 1) * 128],
                                 gallT[:, hf * HT + tg * 512:
                                       hf * HT + tg * 512 + 512],
                                 start=True, stop=True)
                    ctx.inc(i, "pe")
            for e in range(8):
                i_e = eidx(hf, e)
                ctx.slot_w(t, "qf", i_e % 2, dm_m[("w1", hf, e)] // 16)
                ctx.slot_w(t, "qm", 0, dm_m[("w2", hf, e)] // 16)
                for st in range(NST):
                    sl = st % 2
                    ctx.slot_w(t, "qx", sl, dm_m[("xst", hf, e, st)] // 16)
                    ctx.slot_w(t, "qm", 1, dm_m[("ge", hf, e)] // 16)
                    if sidx(hf, e, st) == 0:
                        ctx.w(t, "ve", ve_m[("glg", NT - 1)])
                        ctx.w(t, "ve", ve_m[("brrep",)])
                    else:
                        i2 = sidx(hf, e, st) - 1
                        h2, r2 = divmod(i2, 8 * NST)
                        e2, s2 = divmod(r2, NST)
                        ctx.w(t, "ve", ve_m[("grep", h2, e2, s2)])
                    ctx.inc(t.matmul(ph[0][:, :], onesr[:, :],
                                     gerow[:, st * 512:(st + 1) * 512],
                                     start=True, stop=True), "pe")

                    w1s = w1b[i_e % 2]

                    def hmm(fn):
                        if fn >= 2:
                            ctx.w(t, "se", se_m[("relu", hf, e, st, fn - 2)])
                        if fn == 0:
                            ctx.w(t, "ve", ve_m[("grep", hf, e, st)])
                        if fn == 1 and sidx(hf, e, st) == 0:
                            ctx.w(t, "ve", ve_m[("brrep",)])
                        for c in range(NCH):
                            i = t.matmul(ph[fn % 2][:, :],
                                         w1s[:, c * 2048 + fn * 128:
                                             c * 2048 + fn * 128 + 128],
                                         xst[sl][:, c * 512:(c + 1) * 512],
                                         start=(c == 0), stop=(c == NCH - 1))
                        ctx.inc(i, "pe")

                    def ymm(fn, pss):
                        ctx.w(t, "se", se_m[("relu", hf, e, st, fn)])
                        if fn == 0:
                            if pss == 0:
                                i2 = sidx(hf, e, st) - 1
                                if i2 >= 0:
                                    h2, r2 = divmod(i2, 8 * NST)
                                    e2, s2 = divmod(r2, NST)
                                    ctx.w(t, "ve",
                                          ve_m[("yev", h2, e2, s2, 7)])
                                else:
                                    ctx.w(t, "ve",
                                          ve_m[("b2g", hf, NCH - 1, NST - 1)])
                            else:
                                ctx.w(t, "ve", ve_m[("yev", hf, e, st, 3)])
                        for dn4 in range(4):
                            dn = pss * 4 + dn4
                            i = t.matmul(py[dn4][:, :],
                                         w2b[:, fn * D + dn * 128:
                                             fn * D + dn * 128 + 128],
                                         hT[:, fn * 512:(fn + 1) * 512],
                                         start=(fn == 0), stop=(fn == 15))
                        ctx.inc(i, "pe")

                    hmm(0)
                    for fn in range(1, 16):
                        hmm(fn)
                        ymm(fn - 1, 0)
                    ymm(15, 0)
                    for fn in range(16):
                        ymm(fn, 1)
            for j in range(16):
                ctx.w(t, "ve", ve_m[("yev", hf, 7, NST - 1, 7)])
                if j >= 2:
                    ctx.w(t, "ve", ve_m[("comb", hf * 16 + j - 2)])
                for c in range(NCH):
                    i = t.transpose(pT[j % 2][:, c * 128:(c + 1) * 128],
                                    yaccT[:, c * HT + j * 128:
                                          c * HT + j * 128 + 128],
                                    ident_bf[:, :])
                ctx.inc(i, "pe")

    blk.vector(f_vector)
    blk.scalar(f_scalar)
    blk.gpsimd(f_park)
    blk.sync(f_sync)
    blk.gpsimd(f_gp)
    blk.tensor(f_pe)


def emit_l1attn(nc, ctx, blk, es, R_in, R_out, ln, ident_bf, onesb, pT, pqo,
                wdr, dscr, tag):
    phv = phase_fence(ctx)
    E = es.enter_context
    xnT = E(nc.sbuf_tensor(f"t_xnT{tag}", [128, NCH * T], BF16))
    emit_ln_transpose(nc, ctx, blk, R_in, xnT, ln, ident_bf, pT, pqo=pqo)

    NSH = 32
    INV_CNT = 1.0 / 33554432.0
    SC = 64.0 ** (-0.25)

    wq = E(nc.sbuf_tensor(f"t_wq{tag}", [128, NCH * D], BF16))
    wk = E(nc.sbuf_tensor(f"t_wk{tag}", [128, NCH * D], BF16))
    wv = E(nc.sbuf_tensor(f"t_wv{tag}", [128, NCH * D], BF16))
    wo = wq
    q2s = E(nc.sbuf_tensor(f"t_q2s{tag}", [128, T], BF16))
    k2s = E(nc.sbuf_tensor(f"t_k2s{tag}", [128, T], BF16))
    v2s = E(nc.sbuf_tensor(f"t_v2s{tag}", [128, T], BF16))
    lqT2 = E(nc.sbuf_tensor(f"t_lq{tag}", [128, 64], BF16))
    lkT2 = E(nc.sbuf_tensor(f"t_lk{tag}", [128, 64], BF16))
    lvT2 = E(nc.sbuf_tensor(f"t_lv{tag}", [128, 64], BF16))
    onesr = E(nc.sbuf_tensor(f"t_onesr{tag}", [1, 128], BF16))
    iota2 = E(nc.sbuf_tensor(f"t_iota{tag}", [128, 16], F32))
    lnc64 = E(nc.sbuf_tensor(f"t_lnc64{tag}", [128, 64], F32))
    lnd64 = E(nc.sbuf_tensor(f"t_lnd64{tag}", [128, 64], F32))
    lqb64 = E(nc.sbuf_tensor(f"t_lqb64{tag}", [128, 64], F32))
    lkb64 = E(nc.sbuf_tensor(f"t_lkb64{tag}", [128, 64], F32))
    lvb64 = E(nc.sbuf_tensor(f"t_lvb64{tag}", [128, 64], F32))
    borep = E(nc.sbuf_tensor(f"t_borep{tag}", [128, D], F32))
    lna2 = E(nc.sbuf_tensor(f"t_lna2{tag}", [128, NCH], F32))
    lnb2 = E(nc.sbuf_tensor(f"t_lnb2{tag}", [128, NCH], F32))
    bq2 = E(nc.sbuf_tensor(f"t_bq2{tag}", [128, NCH], F32))
    bv2 = E(nc.sbuf_tensor(f"t_bv2{tag}", [128, NCH], F32))
    scalin = E(nc.sbuf_tensor(f"t_scalin{tag}", [1, 4], F32))
    lncts = E(nc.sbuf_tensor(f"t_lncts{tag}", [128, 64], F32))
    negmn_all = E(nc.sbuf_tensor(f"t_negmn{tag}", [128, NSH], F32))
    diffcols = E(nc.sbuf_tensor(f"t_dcols{tag}", [64, NSH], F32))
    dcol1 = E(nc.sbuf_tensor(f"t_dcol1{tag}", [64, 1], F32))
    drow = E(nc.sbuf_tensor(f"t_drow{tag}", [1, 64], F32))
    sca = E(nc.sbuf_tensor(f"t_sca{tag}", [1, 32], F32))
    scab = E(nc.sbuf_tensor(f"t_scab{tag}", [1, 8], BF16))
    scb_ = E(nc.sbuf_tensor(f"t_scb{tag}", [1, 8], F32))
    dres = E(nc.sbuf_tensor(f"t_dres{tag}", [1, 8], F32))
    kmask = E(nc.sbuf_tensor(f"t_kmask{tag}", [128, 16], F32))
    kmb = E(nc.sbuf_tensor(f"t_kmb{tag}", [128, 16], BF16))
    qmT = E(nc.sbuf_tensor(f"t_qmT{tag}", [16, 128], BF16))
    qmrow65 = E(nc.sbuf_tensor(f"t_qmrow{tag}", [65, SEQ], BF16))
    tsrep = E(nc.sbuf_tensor(f"t_tsrep{tag}", [128, 1], F32))
    cwrep = E(nc.sbuf_tensor(f"t_cwrep{tag}", [128, 1], F32))
    updrep = E(nc.sbuf_tensor(f"t_updrep{tag}", [64, 1], F32))
    c1rep = E(nc.sbuf_tensor(f"t_c1rep{tag}", [64, 1], F32))
    qcb = [E(nc.sbuf_tensor(f"t_qc{i}{tag}", [64, SEQ], BF16)) for i in range(2)]
    skb = [E(nc.sbuf_tensor(f"t_sk{i}{tag}", [65, SEQ], BF16)) for i in range(2)]
    vpb = [E(nc.sbuf_tensor(f"t_vp{i}{tag}", [128, 16 * 65], BF16))
           for i in range(2)]
    iob = [E(nc.sbuf_tensor(f"t_io{i}{tag}", [64, SEQ], BF16)) for i in range(2)]
    ionew = v2s[0:64, 0:SEQ]
    sqT = [k2s[0:65, 0:SEQ], k2s[0:65, SEQ:2 * SEQ]]
    qpal = E(nc.sbuf_tensor(f"t_qpal{tag}", [128, 1024], F32))
    scrap = E(nc.sbuf_tensor(f"t_scrap{tag}", [128, 1024], F32))
    scrb = v2s[0:64, SEQ:2 * SEQ]
    sqb = E(nc.sbuf_tensor(f"t_sqb{tag}", [128, 1024], BF16))
    st16 = E(nc.sbuf_tensor(f"t_st16{tag}", [128, 16], F32))
    st16b = E(nc.sbuf_tensor(f"t_st16c{tag}", [128, 16], F32))
    stm = E(nc.sbuf_tensor(f"t_stm{tag}", [128, 16], F32))
    strd = E(nc.sbuf_tensor(f"t_strd{tag}", [128, 16], F32))
    nsub = E(nc.sbuf_tensor(f"t_nsub{tag}", [128, 16], F32))
    ncb = E(nc.sbuf_tensor(f"t_ncb{tag}", [128, 16], BF16))
    negcT = [E(nc.sbuf_tensor(f"t_ncT{i}{tag}", [16, 128], BF16))
             for i in range(2)]
    pbuf = [E(nc.sbuf_tensor(f"t_pb{i}{tag}", [128, 512], BF16))
            for i in range(3)]
    arep = E(nc.sbuf_tensor(f"t_arep{tag}", [64, 512], F32))
    r65 = E(nc.sbuf_tensor(f"t_r65{tag}", [65, 640], BF16))
    RAL, RON = 0, 512
    scr65 = E(nc.sbuf_tensor(f"t_scr65{tag}", [65, 512], F32))
    rin4 = [ln["rt0"], ln["rt1"]]
    pS = [E(nc.psum_tensor(f"t_pS{i}{tag}", [128, 512], F32)) for i in range(2)]
    pF = [E(nc.psum_tensor(f"t_pF{i}{tag}", [128, 512], F32)) for i in range(2)]
    ppv = E(nc.psum_tensor(f"t_ppv{tag}", [65, 512], F32))
    pbc = E(nc.psum_tensor(f"t_pbc{tag}", [128, 512], F32))

    qcur_d = dscr["qcur_d"]; sk_d = dscr["sk_d"]; vp_d = dscr["vp_d"]
    io_d = dscr["io_d"]; io1_d = dscr["io1_d"]
    attn_park = dscr["attn_park"]; negc_d = dscr["negc_d"]
    din_d = dscr["din_d"]; dout_d = dscr["dout_d"]; qm_d = dscr["qm_d"]
    dcol_d = dscr["dcol_d"]

    def d_idx(hp, sh4):
        return (sh4 // 2) * 16 + hp * 2 + (sh4 % 2)

    def prevk(it, sh, k):
        i = it * NSH + sh - k
        return (i // NSH, i % NSH) if i >= 0 else None

    pe_pos, se_pos = {}, {}
    cc_base = ctx.cnt["cc"]

    def plan_pe_se():
        n = ctx.cnt["pe"]
        for hp in range(NCH):
            for kind in ("q2", "k2", "v2"):
                for tg in range(NTG):
                    n += 1; pe_pos[(kind, hp, tg)] = n
            for sh4 in range(4):
                n += 1; pe_pos[("kp", hp, sh4)] = n
                n += 1; pe_pos[("skT", hp, sh4)] = n
                n += 1; pe_pos[("knm", hp, sh4)] = n
                n += 1; pe_pos[("mnb", hp, sh4)] = n
                n += 1; pe_pos[("vp", hp, sh4)] = n
        for it in range(3):
            n += 1; pe_pos[("tsb", it)] = n
            if it >= 1:
                n += 1; pe_pos[("cwb", it)] = n
                n += 1; pe_pos[("qmT", it)] = n
            if it == 2:
                n += 1; pe_pos[("upb", it)] = n
            for sh in range(NSH):
                n += 1; pe_pos[("qp", it, sh)] = n
                n += 1; pe_pos[("sqT", it, sh)] = n
                for j in range(16):
                    n += 1; pe_pos[("sub", it, sh, j)] = n
                n += 1; pe_pos[("ncT", it, sh)] = n
                for qg in range(4):
                    for kc in range(16):
                        n += 1; pe_pos[("S", it, sh, qg, kc)] = n
                        if kc >= 3:
                            n += 1; pe_pos[("PV", it, sh, qg, kc - 3)] = n
                    for k2 in (13, 14, 15):
                        n += 1; pe_pos[("PV", it, sh, qg, k2)] = n
                    n += 1; pe_pos[("bca", it, sh, qg)] = n
        for j in range(NT):
            n += 1; pe_pos[("proj", j)] = n

        m = ctx.cnt["se"]
        for hp in range(NCH):
            for sh4 in range(4):
                m += 1; se_pos[("lnk", hp, sh4)] = m
                m += 1; se_pos[("mnsq", hp, sh4)] = m
        for it in range(3):
            m += 1; se_pos[("tsq", it)] = m
            for sh in range(NSH):
                m += 1; se_pos[("lnq", it, sh)] = m
                m += 1; se_pos[("nsq", it, sh)] = m
                for qg in range(4):
                    for kc in range(16):
                        m += 1; se_pos[("exp", it, sh, qg, kc)] = m
    plan_pe_se()
    emit_l1attn._pe_pos = pe_pos

    pl = {}
    qa_pos = {}

    def plan_sync():
        c = {k: ctx.cnt[k] for k in ("qd0", "qd1", "qe0", "qe1", "qn0", "qn1",
                                     "qv0", "qv1", "qB")}
        for hp in range(NCH):
            for sh4 in range(4):
                su = sh4 % 2
                c[f"qe{su}"] += 48
                pl[("dwr", hp, sh4)] = c[f"qe{su}"]
        for it in range(3):
            if it >= 1:
                c["qv1"] += 32
                pl[("qm", it)] = c["qv1"]
            for sh in range(NSH):
                su = sh % 2
                c[f"qd{su}"] += 48 if it == 0 else 64
                pl[("ld", it, sh)] = c[f"qd{su}"]
                c[f"qn{su}"] += 32
                pl[("ncrow", it, sh)] = c[f"qn{su}"]
                if it == 0:
                    c[f"qe{su}"] += 32
                elif it == 1:
                    c[f"qe{su}"] += 16
                pl[("iw", it, sh)] = (c["qe0"], c["qe1"])
            if it < 2:
                c["qv0"] += 32
                pl[("drow", it)] = c["qv0"]
                c["qB"] += 16
                qa_pos[("din", it)] = c["qB"]
                c["qB"] += 16
                qa_pos[("dout", it)] = c["qB"]
        c["qB"] += NCH * 16
        qa_pos[("stagef",)] = c["qB"]
    plan_sync()

    qw_pos = ctx.cnt["qw"] + 16 * (3 * NCH + 6 + 12)
    wo_loaded = [0]
    ve_m, dm_m = {}, {}
    qi0 = (ctx.cnt["qi0"], ctx.cnt["qi1"])
    qo0 = (ctx.cnt["qu0"], ctx.cnt["qu1"])
    qo_prev = (ctx.cnt[pqo + "0"], ctx.cnt[pqo + "1"])
    qw_base = ctx.cnt["qw"]
    qp_base = (ctx.cnt["qp0"], ctx.cnt["qp1"])
    last_v2 = pe_pos[("v2", NCH - 1, NTG - 1)]
    prev_sqrt = [None]

    def ln64(v, x16, wrep, outb, key):
        x3 = x16.rearrange("p (a b) -> p a b", b=64)
        if prev_sqrt[0] is not None:
            ctx.w(v, "se", prev_sqrt[0])
        ctx.inc(v.reduce_sum(st16[:, :], x3, axis=AX.X), "ve")
        ctx.wall(v, "ve")
        ctx.inc(v.tensor_scalar(st16[:, :], st16[:, :], 1.0 / 64, None,
                                ALU.mult), "ve")
        ctx.inc(v.tensor_tensor(scrap[:, :], x16, x16, ALU.mult), "ve")
        ctx.wall(v, "ve")
        ctx.inc(v.reduce_sum(st16b[:, :],
                             scrap.rearrange("p (a b) -> p a b", b=64),
                             axis=AX.X), "ve")
        ctx.inc(v.tensor_tensor(stm[:, :], st16[:, :], st16[:, :],
                                ALU.mult), "ve")
        ctx.wall(v, "ve")
        ctx.inc(v.tensor_scalar(st16b[:, :], st16b[:, :], 1.0 / 64, None,
                                ALU.mult), "ve")
        ctx.wall(v, "ve")
        ctx.inc(v.tensor_tensor(st16b[:, :], st16b[:, :], stm[:, :],
                                ALU.subtract), "ve")
        ve_m[("var",) + key] = ctx.cnt["ve"]
        ctx.wall(v, "ve")
        ctx.w(v, "se", se_pos[key])
        ctx.inc(v.reciprocal(strd[:, :], stm[:, :]), "ve")
        ctx.wall(v, "ve")
        m_bc = st16[:, :].unsqueeze(2).broadcast_to([128, 16, 64])
        r_bc = strd[:, :].unsqueeze(2).broadcast_to([128, 16, 64])
        w_bc = wrep[:, :].unsqueeze(1).broadcast_to([128, 16, 64])
        s3 = scrap.rearrange("p (a b) -> p a b", b=64)
        ctx.inc(v.tensor_tensor(s3, x3, m_bc, ALU.subtract), "ve")
        ctx.wall(v, "ve")
        ctx.inc(v.tensor_tensor(s3, s3, r_bc, ALU.mult), "ve")
        ctx.wall(v, "ve")
        ctx.inc(v.tensor_tensor(outb.rearrange("p (a b) -> p a b", b=64),
                                s3, w_bc, ALU.mult), "ve")
        ctx.wall(v, "ve")
        ctx.inc(v.tensor_tensor(scrap[:, :], outb, outb, ALU.mult), "ve")
        ctx.wall(v, "ve")
        ctx.inc(v.reduce_sum(st16[:, :],
                             scrap.rearrange("p (a b) -> p a b", b=64),
                             axis=AX.X), "ve")
        ve_m[("sq2",) + key] = ctx.cnt["ve"]
        ctx.wall(v, "ve")

    def emit_boundary(v, it):
        if it == 0:
            ctx.inc(v.tensor_copy(sca[:, 13:16], scalin[:, 0:3]), "ve")
            ctx.wall(v, "ve")
            ctx.inc(v.tensor_copy(sca[:, 3:4], sca[:, 14:15]), "ve")
            ctx.wall(v, "ve")
        else:
            ctx.w(v, "qB", qa_pos[("dout", it - 1)])
            ctx.inc(v.tensor_scalar(sca[:, 0:1], dres[:, 0:1], INV_CNT, None,
                                    ALU.mult), "ve")
            ctx.wall(v, "ve")
            if it == 1:
                ctx.inc(v.tensor_scalar(sca[:, 1:2], sca[:, 0:1], 0.001, None,
                                        ALU.is_lt), "ve")
                ctx.wall(v, "ve")
                ctx.inc(v.tensor_scalar(sca[:, 2:3], sca[:, 1:2], -205.0,
                                        2048.0, ALU.mult, ALU.add), "ve")
                ctx.inc(v.tensor_scalar(sca[:, 3:4], sca[:, 3:4], 1.0, 0.005,
                                        ALU.mult, ALU.add), "ve")
                ctx.wall(v, "ve")
            else:
                ctx.inc(v.tensor_tensor(sca[:, 7:8], sca[:, 0:1],
                                        sca[:, 15:16], ALU.mult), "ve")
                ctx.wall(v, "ve")
                ctx.inc(v.tensor_tensor(sca[:, 7:8], sca[:, 7:8],
                                        sca[:, 13:14], ALU.add), "ve")
                ctx.wall(v, "ve")
                ctx.inc(v.tensor_tensor(sca[:, 5:6], sca[:, 0:1], sca[:, 7:8],
                                        ALU.is_lt), "ve")
                ctx.wall(v, "ve")
                ctx.inc(v.tensor_scalar(sca[:, 6:7], sca[:, 5:6], -1.0, 1.0,
                                        ALU.mult, ALU.add), "ve")
                ctx.inc(v.tensor_scalar(sca[:, 8:9], sca[:, 0:1], 0.05, None,
                                        ALU.is_gt), "ve")
                ctx.inc(v.tensor_scalar(sca[:, 9:10], sca[:, 0:1], 0.001, None,
                                        ALU.is_lt), "ve")
                ctx.inc(v.tensor_scalar(sca[:, 10:11], sca[:, 1:2], -21.0,
                                        2048.0, ALU.mult, ALU.add), "ve")
                ctx.inc(v.tensor_scalar(sca[:, 11:12], sca[:, 1:2], -185.0,
                                        1843.0, ALU.mult, ALU.add), "ve")
                ctx.wall(v, "ve")
                ctx.inc(v.tensor_tensor(sca[:, 12:13], sca[:, 9:10],
                                        sca[:, 11:12], ALU.mult), "ve")
                ctx.inc(v.tensor_scalar(sca[:, 9:10], sca[:, 9:10], -1.0, 1.0,
                                        ALU.mult, ALU.add), "ve")
                ctx.wall(v, "ve")
                ctx.inc(v.tensor_tensor(sca[:, 9:10], sca[:, 9:10],
                                        sca[:, 2:3], ALU.mult), "ve")
                ctx.wall(v, "ve")
                ctx.inc(v.tensor_tensor(sca[:, 12:13], sca[:, 12:13],
                                        sca[:, 9:10], ALU.add), "ve")
                ctx.wall(v, "ve")
                ctx.inc(v.tensor_tensor(sca[:, 10:11], sca[:, 8:9],
                                        sca[:, 10:11], ALU.mult), "ve")
                ctx.inc(v.tensor_scalar(sca[:, 8:9], sca[:, 8:9], -1.0, 1.0,
                                        ALU.mult, ALU.add), "ve")
                ctx.wall(v, "ve")
                ctx.inc(v.tensor_tensor(sca[:, 12:13], sca[:, 8:9],
                                        sca[:, 12:13], ALU.mult), "ve")
                ctx.wall(v, "ve")
                ctx.inc(v.tensor_tensor(sca[:, 2:3], sca[:, 10:11],
                                        sca[:, 12:13], ALU.add), "ve")
                ctx.inc(v.tensor_scalar(sca[:, 7:8], sca[:, 6:7], 0.005, None,
                                        ALU.mult), "ve")
                ctx.wall(v, "ve")
                ctx.inc(v.tensor_tensor(sca[:, 3:4], sca[:, 3:4], sca[:, 7:8],
                                        ALU.add), "ve")
                ctx.wall(v, "ve")
                ctx.inc(v.tensor_copy(scab[:, 2:3], sca[:, 6:7]), "ve")
                ctx.inc(v.tensor_copy(scab[:, 3:4], sca[:, 5:6]), "ve")
        ctx.inc(v.tensor_scalar(sca[:, 7:8], sca[:, 3:4], 0.0, None,
                                ALU.is_gt), "ve")
        ctx.inc(v.tensor_scalar(sca[:, 8:9], sca[:, 3:4], 1e-30, None,
                                ALU.max), "ve")
        ve_m[("bpre", it)] = ctx.cnt["ve"]
        ctx.wall(v, "ve")
        ctx.w(v, "se", se_pos[("tsq", it)])
        ctx.inc(v.reciprocal(sca[:, 8:9], sca[:, 8:9]), "ve")
        ctx.wall(v, "ve")
        ctx.inc(v.tensor_tensor(sca[:, 8:9], sca[:, 7:8], sca[:, 8:9],
                                ALU.mult), "ve")
        ctx.inc(v.tensor_scalar(sca[:, 7:8], sca[:, 7:8], -1.0, 1.0,
                                ALU.mult, ALU.add), "ve")
        ctx.wall(v, "ve")
        ctx.inc(v.tensor_tensor(sca[:, 4:5], sca[:, 8:9], sca[:, 7:8],
                                ALU.add), "ve")
        ctx.wall(v, "ve")
        ctx.inc(v.tensor_copy(scab[:, 0:1], sca[:, 4:5]), "ve")
        if it >= 1:
            ctx.inc(v.tensor_copy(scab[:, 1:2], sca[:, 2:3]), "ve")
        ve_m[("bmath", it)] = ctx.cnt["ve"]
        ctx.wall(v, "ve")
        ctx.w(v, "pe", pe_pos[("tsb", it)])
        ctx.inc(v.tensor_copy(tsrep[:, :], pbc[:, 0:1]), "ve")
        ctx.wall(v, "ve")
        ctx.inc(v.tensor_scalar(lncts[:, :], lnc64[:, :], tsrep[:, :], None,
                                ALU.mult), "ve")
        ve_m[("lncts", it)] = ctx.cnt["ve"]
        if it >= 1:
            ctx.w(v, "pe", pe_pos[("cwb", it)])
            ctx.inc(v.tensor_copy(cwrep[:, :], pbc[:, 1:2]), "ve")
            ctx.wall(v, "ve")
            ctx.inc(v.tensor_scalar(kmask[:, :], iota2[:, :], cwrep[:, :],
                                    None, ALU.is_lt), "ve")
            ctx.wall(v, "ve")
            ctx.inc(v.tensor_copy(kmb[:, :], kmask[:, :]), "ve")
            ve_m[("kmb", it)] = ctx.cnt["ve"]
            ctx.w(v, "pe", pe_pos[("qmT", it)])
            ctx.inc(v.tensor_copy(qmT[:, :], pT[0][0:16, 0:128]), "ve")
            ve_m[("qmT", it)] = ctx.cnt["ve"]
        if it == 2:
            ctx.w(v, "pe", pe_pos[("upb", it)])
            ctx.inc(v.tensor_copy(updrep[:, :], pbc[0:64, 0:1]), "ve")
            ctx.inc(v.tensor_copy(c1rep[:, :], pbc[0:64, 1:2]), "ve")
            ve_m[("upb", it)] = ctx.cnt["ve"]
            ctx.wall(v, "ve")

    def f_vector(v):
        ctx.inc(v.memset(onesr[:, :], 1.0), "ve")
        ctx.inc(v.memset(r65[:, RON:RON + 64], 1.0), "ve")
        ctx.inc(v.memset(scb_[:, :], 0.0), "ve")
        ctx.inc(v.memset(qmrow65[64:65, :], 1.0), "ve")
        for i in range(2):
            ctx.inc(v.memset(
                vpb[i].rearrange("p (a b) -> p a b", b=65)[:, :, 64:65],
                1.0), "ve")
        ve_m[("init",)] = ctx.cnt["ve"]
        ctx.wall(v, "ve")
        ctx.w(v, "qw", qw_pos)
        for c in range(NCH):
            ctx.inc(v.tensor_scalar(wq[:, c * D:(c + 1) * D],
                                    wq[:, c * D:(c + 1) * D],
                                    lna2[:, c:c + 1], SC,
                                    ALU.mult, ALU.mult), "ve")
            ctx.inc(v.tensor_scalar(wk[:, c * D:(c + 1) * D],
                                    wk[:, c * D:(c + 1) * D],
                                    lnb2[:, c:c + 1], SC,
                                    ALU.mult, ALU.mult), "ve")
            ctx.inc(v.tensor_scalar(wv[:, c * D:(c + 1) * D],
                                    wv[:, c * D:(c + 1) * D],
                                    lnb2[:, c:c + 1], None, ALU.mult), "ve")
        ctx.inc(v.tensor_scalar(bq2[:, :], bq2[:, :], SC, None, ALU.mult), "ve")
        ve_m[("wscale",)] = ctx.cnt["ve"]
        ctx.wall(v, "ve")
        for hp in range(NCH):
            for (kind, dst, bias) in (("q2", q2s, bq2), ("k2", k2s, None),
                                      ("v2", v2s, bv2)):
                for tg in range(NTG):
                    ctx.w(v, "pe", pe_pos[(kind, hp, tg)])
                    if kind == "q2" and tg == 0 and hp >= 1:
                        ctx.slot_w(v, "qe", 0, pl[("dwr", hp - 1, 2)] // 16)
                        ctx.slot_w(v, "qe", 1, pl[("dwr", hp - 1, 3)] // 16)
                    if bias is not None:
                        ctx.inc(v.tensor_scalar(
                            dst[:, tg * 512:(tg + 1) * 512],
                            pS[tg % 2][:, :], bias[:, hp:hp + 1], None,
                            ALU.add), "ve")
                    else:
                        ctx.inc(v.tensor_copy(dst[:, tg * 512:(tg + 1) * 512],
                                              pS[tg % 2][:, :]), "ve")
                    ve_m[(kind + "e", hp, tg)] = ctx.cnt["ve"]
            for sh4 in range(4):
                su = sh4 % 2
                idx = d_idx(hp, sh4)
                ctx.w(v, "pe", pe_pos[("kp", hp, sh4)])
                for hb in range(2):
                    ctx.inc(v.tensor_tensor(
                        qpal.rearrange("p (a b) -> p a b", b=64)[
                            :, hb * 8:(hb + 1) * 8, :],
                        pS[hb].rearrange("p (a b) -> p a b", b=64),
                        lkb64[:, :].unsqueeze(1).broadcast_to([128, 8, 64]),
                        ALU.add), "ve")
                ctx.wall(v, "ve")
                ln64(v, qpal[:, :], lnd64, sqb[:, :], ("lnk", hp, sh4))
                ctx.inc(v.reduce_max(strd[:, 0:1], st16[:, :], axis=AX.X),
                        "ve")
                ctx.wall(v, "ve")
                ctx.inc(v.tensor_copy(ncb[:, 0:1], strd[:, 0:1]), "ve")
                ve_m[("knmb", hp, sh4)] = ctx.cnt["ve"]
                ctx.w(v, "pe", pe_pos[("skT", hp, sh4)])
                if 4 * hp + sh4 >= 2:
                    pd = 4 * hp + sh4 - 2
                    ctx.slot_w(v, "qe", su, pl[("dwr", pd // 4, pd % 4)] // 16)
                ctx.inc(v.tensor_copy(skb[su][0:64, 0:1024],
                                      pT[0][0:64, :]), "ve")
                ctx.inc(v.tensor_copy(skb[su][0:64, 1024:2048],
                                      pT[1][0:64, :]), "ve")
                ctx.inc(v.memset(skb[su][64:65, :], 1.0), "ve")
                ve_m[("skb", hp, sh4)] = ctx.cnt["ve"]
                ctx.w(v, "pe", pe_pos[("knm", hp, sh4)])
                ctx.inc(v.reduce_max(scr65[0:1, 2:3], pT[0][0:1, 0:128],
                                     axis=AX.X), "ve")
                ve_m[("mn2", hp, sh4)] = ctx.cnt["ve"]
                ctx.wall(v, "ve")
                ctx.w(v, "se", se_pos[("mnsq", hp, sh4)])
                ctx.inc(v.tensor_scalar(scr65[0:1, 3:4], scr65[0:1, 4:5],
                                        -0.125, None, ALU.mult), "ve")
                ctx.wall(v, "ve")
                ctx.inc(v.tensor_copy(scab[0:1, 4:5], scr65[0:1, 3:4]), "ve")
                ve_m[("mnbf", hp, sh4)] = ctx.cnt["ve"]
                ctx.w(v, "pe", pe_pos[("mnb", hp, sh4)])
                ctx.inc(v.tensor_copy(negmn_all[:, idx:idx + 1],
                                      pbc[:, 2:3]), "ve")
                ve_m[("mnrep", hp, sh4)] = ctx.cnt["ve"]
                prev_sqrt[0] = se_pos[("mnsq", hp, sh4)]
                ctx.w(v, "pe", pe_pos[("vp", hp, sh4)])
                for hb in range(2):
                    ctx.inc(v.tensor_tensor(
                        vpb[su].rearrange("p (a b) -> p a b", b=65)[
                            :, hb * 8:(hb + 1) * 8, 0:64],
                        pS[hb].rearrange("p (a b) -> p a b", b=64),
                        lvb64[:, :].unsqueeze(1).broadcast_to([128, 8, 64]),
                        ALU.add), "ve")
                ve_m[("vpb", hp, sh4)] = ctx.cnt["ve"]
                ctx.wall(v, "ve")
        for it in range(3):
            emit_boundary(v, it)
            for sh in range(NSH):
                su = sh % 2
                ctx.slot_w(v, "qd", su, pl[("ld", it, sh)] // 16)
                if it == 2:
                    ctx.inc(v.scalar_tensor_tensor(
                        qcb[su][:, :], iob[su][:, :], updrep[:, :],
                        qcb[su][:, :], ALU.mult, ALU.add), "ve")
                    ve_m[("qfix", it, sh)] = ctx.cnt["ve"]
                    ctx.wall(v, "ve")
                ctx.w(v, "pe", pe_pos[("qp", it, sh)])
                for hb in range(2):
                    ctx.inc(v.tensor_tensor(
                        qpal.rearrange("p (a b) -> p a b", b=64)[
                            :, hb * 8:(hb + 1) * 8, :],
                        pS[hb].rearrange("p (a b) -> p a b", b=64),
                        lqb64[:, :].unsqueeze(1).broadcast_to([128, 8, 64]),
                        ALU.add), "ve")
                ctx.wall(v, "ve")
                ln64(v, qpal[:, :], lncts, sqb[:, :], ("lnq", it, sh))
                ctx.w(v, "pe", pe_pos[("sqT", it, sh)])
                ctx.inc(v.tensor_copy(sqT[su][0:64, 0:1024],
                                      pT[0][0:64, :]), "ve")
                ctx.inc(v.tensor_copy(sqT[su][0:64, 1024:2048],
                                      pT[1][0:64, :]), "ve")
                ve_m[("sqTe", it, sh)] = ctx.cnt["ve"]
                for j in range(16):
                    ctx.w(v, "pe", pe_pos[("sub", it, sh, j)])
                    ctx.inc(v.reduce_max(nsub[:, j:j + 1],
                                         pS[j % 2][:, 0:16], axis=AX.X,
                                         negate=True), "ve")
                    ve_m[("nsr", it, sh, j)] = ctx.cnt["ve"]
                ctx.wall(v, "ve")
                ctx.w(v, "se", se_pos[("nsq", it, sh)])
                ctx.inc(v.tensor_scalar(st16b[:, :], stm[:, :],
                                        negmn_all[:, sh:sh + 1],
                                        80.0, ALU.mult, ALU.add), "ve")
                ctx.wall(v, "ve")
                ctx.inc(v.tensor_tensor(st16b[:, :], nsub[:, :], st16b[:, :],
                                        ALU.min), "ve")
                ctx.wall(v, "ve")
                prev_sqrt[0] = se_pos[("nsq", it, sh)]
                p1 = prevk(it, sh, 1)
                if p1 is not None:
                    ctx.w(v, "pe", pe_pos[("ncT",) + p1])
                ctx.inc(v.tensor_copy(ncb[:, :], st16b[:, :]), "ve")
                ve_m[("ncb", it, sh)] = ctx.cnt["ve"]
                ctx.w(v, "pe", pe_pos[("ncT", it, sh)])
                ctx.inc(v.tensor_copy(negcT[su][:, :], pT[0][0:16, 0:128]),
                        "ve")
                ve_m[("ncT", it, sh)] = ctx.cnt["ve"]
                for qg in range(4):
                    if it >= 1:
                        for kc in range(12, 16):
                            ctx.w(v, "se", se_pos[("exp", it, sh, qg, kc)])
                            ctx.inc(v.tensor_scalar(
                                pbuf[kc % 3][:, :], pbuf[kc % 3][:, :],
                                kmask[:, kc:kc + 1], None, ALU.mult), "ve")
                            ve_m[("km", it, sh, qg, kc)] = ctx.cnt["ve"]
                        ctx.wall(v, "ve")
                    ctx.w(v, "pe", pe_pos[("PV", it, sh, qg, 15)])
                    ctx.inc(v.tensor_scalar(r65[64:65, RAL:RAL + 512],
                                            ppv[64:65, :], 1.0, 1e-30,
                                            ALU.mult, ALU.add), "ve")
                    ctx.wall(v, "ve")
                    if it >= 1 and qg == 3:
                        if sh == 0:
                            ctx.slot_w(v, "qv", 1, pl[("qm", it)] // 16)
                            ctx.inc(v.tensor_scalar(
                                qmrow65[64:65, :], qmrow65[64:65, :],
                                -1e6, 1000001.0, ALU.mult, ALU.add), "ve")
                            ctx.wall(v, "ve")
                        ctx.inc(v.tensor_tensor(
                            r65[64:65, RAL:RAL + 512],
                            r65[64:65, RAL:RAL + 512],
                            qmrow65[64:65, qg * 512:(qg + 1) * 512],
                            ALU.mult), "ve")
                        ctx.wall(v, "ve")
                    ve_m[("rq", it, sh, qg)] = ctx.cnt["ve"]
                    ctx.w(v, "pe", pe_pos[("bca", it, sh, qg)])
                    ctx.inc(v.reciprocal(arep[:, :], pbc[0:64, :]), "ve")
                    ctx.wall(v, "ve")
                    if qg == 0:
                        g2 = it * NSH + sh - 2
                        if g2 >= 0 and g2 // NSH <= 1:
                            it2, sh2 = divmod(g2, NSH)
                            ctx.slot_w(v, "qe", su,
                                       pl[("iw", it2, sh2)][su] // 16)
                    ctx.inc(v.tensor_tensor(
                        ionew[:, qg * 512:(qg + 1) * 512],
                        ppv[0:64, :], arep[:, :], ALU.mult), "ve")
                    ve_m[("ion", it, sh, qg)] = ctx.cnt["ve"]
                ctx.wall(v, "ve")
                if it == 0:
                    ctx.inc(v.reduce_sum(diffcols[:, sh:sh + 1], ionew[:, :],
                                         axis=AX.X,
                                         apply_absolute_value=True), "ve")
                    ctx.inc(v.tensor_tensor(qcb[su][:, :], qcb[su][:, :],
                                            ionew[:, :], ALU.add), "ve")
                elif it == 1:
                    ctx.inc(v.tensor_tensor(scrb[:, :], ionew[:, :],
                                            iob[su][:, :], ALU.subtract), "ve")
                    ctx.wall(v, "ve")
                    ctx.inc(v.reduce_sum(diffcols[:, sh:sh + 1], scrb[:, :],
                                         axis=AX.X,
                                         apply_absolute_value=True), "ve")
                else:
                    ctx.inc(v.tensor_tensor(scrb[:, :], iob[su][:, :],
                                            ionew[:, :], ALU.subtract), "ve")
                    ctx.wall(v, "ve")
                    if sh >= 2:
                        ctx.slot_w(v, "qp", su, qp_base[su] // 16 + sh // 2)
                    ctx.inc(v.scalar_tensor_tensor(
                        iob[su][:, :], scrb[:, :], c1rep[:, :],
                        ionew[:, :], ALU.mult, ALU.add), "ve")
                ve_m[("iodone", it, sh)] = ctx.cnt["ve"]
                ctx.wall(v, "ve")
            if it < 2:
                ctx.inc(v.reduce_sum(dcol1[:, :], diffcols[:, :],
                                     axis=AX.X), "ve")
                ve_m[("dcol", it)] = ctx.cnt["ve"]
                ctx.wall(v, "ve")
                ctx.slot_w(v, "qv", 0, pl[("drow", it)] // 16)
                ctx.inc(v.reduce_sum(sca[0:1, 10:11], drow[:, :],
                                     axis=AX.X), "ve")
                ctx.wall(v, "ve")
                ctx.inc(v.tensor_copy(scb_[:, 0:1], sca[0:1, 10:11]), "ve")
                ve_m[("dsum", it)] = ctx.cnt["ve"]
                ctx.wall(v, "ve")
        for j in range(NT):
            jm = j % 2
            pfo = pS if jm == 0 else pF
            ctx.w(v, "pe", pe_pos[("proj", j)])
            ctx.slot_w(v, "qi", jm, qi0[jm] // 16 + j // 2 + 1)
            for half in range(2):
                ctx.inc(v.tensor_tensor(scrap[:, half * 512:half * 512 + 512],
                                        pfo[half][:, :],
                                        borep[:, half * 512:half * 512 + 512],
                                        ALU.add), "ve")
            ctx.wall(v, "ve")
            for half in range(2):
                ctx.inc(v.tensor_tensor(
                    rin4[jm][:, half * 512:half * 512 + 512],
                    rin4[jm][:, half * 512:half * 512 + 512],
                    scrap[:, half * 512:half * 512 + 512], ALU.add), "ve")
            ctx.wall(v, "ve")
            ve_m[("comb", j)] = ctx.cnt["ve"]
    blk.vector(f_vector)

    def f_scalar(sc):
        for hp in range(NCH):
            for sh4 in range(4):
                ctx.w(sc, "ve", ve_m[("var", "lnk", hp, sh4)])
                ctx.inc(sc.activation(stm[:, :], st16b[:, :], AF.Sqrt,
                                      bias=ln["eps"][:, :]), "se")
                ctx.w(sc, "ve", ve_m[("mn2", hp, sh4)])
                ctx.inc(sc.activation(scr65[0:1, 4:5], scr65[0:1, 2:3],
                                      AF.Sqrt), "se")
        for it in range(3):
            ctx.w(sc, "ve", ve_m[("bpre", it)])
            ctx.inc(sc.activation(sca[:, 8:9], sca[:, 8:9], AF.Sqrt), "se")
            for sh in range(NSH):
                ctx.w(sc, "ve", ve_m[("var", "lnq", it, sh)])
                ctx.inc(sc.activation(stm[:, :], st16b[:, :], AF.Sqrt,
                                      bias=ln["eps"][:, :]), "se")
                ctx.w(sc, "ve", ve_m[("sq2", "lnq", it, sh)])
                ctx.inc(sc.activation(stm[:, :], st16[:, :], AF.Sqrt), "se")
                for qg in range(4):
                    for kc in range(16):
                        ctx.w(sc, "pe", pe_pos[("S", it, sh, qg, kc)])
                        if kc >= 3:
                            ctx.w(sc, "pe", pe_pos[("PV", it, sh, qg, kc - 3)])
                        ctx.inc(sc.activation(pbuf[kc % 3][:, :],
                                              (pS[0], pS[1], pF[0],
                                               pF[1])[kc % 4][:, :],
                                              AF.Exp, scale=0.125), "se")
    blk.scalar(f_scalar)


    def f_sync(sync):
        for hp in range(NCH):
            for sh4 in range(4):
                su = sh4 % 2
                s_, h01 = sh4 // 2, sh4 % 2
                idx = d_idx(hp, sh4)
                ctx.w(sync, "ve", ve_m[("q2e", hp, NTG - 1)])
                ctx.slot_inc(sync.dma_start(
                    out=qcur_d[idx], in_=q2s[h01 * 64:h01 * 64 + 64,
                                            s_ * SEQ:(s_ + 1) * SEQ]),
                    "qe", su)
                ctx.w(sync, "ve", ve_m[("skb", hp, sh4)])
                ctx.slot_inc(sync.dma_start(out=sk_d[idx],
                                            in_=skb[su][:, :]), "qe", su)
                ctx.w(sync, "ve", ve_m[("vpb", hp, sh4)])
                ctx.slot_inc(sync.dma_start(out=vp_d[idx],
                                            in_=vpb[su][:, :]), "qe", su)
                assert ctx.cnt[f"qe{su}"] == pl[("dwr", hp, sh4)]
                ctx.slot_w(sync, "qe", su, ctx.cnt[f"qe{su}"] // 16)
        for it in range(3):
            if it >= 1:
                ctx.w(sync, "ve", ve_m[("qmT", it)])
                ctx.slot_inc(sync.dma_start(out=qm_d, in_=qmT[:, :]), "qv", 1)
                ctx.slot_w(sync, "qv", 1, ctx.cnt["qv1"] // 16)
                ctx.slot_inc(sync.dma_start(
                    out=qmrow65[64:65, :],
                    in_=qm_d.rearrange("a b -> (a b)")[None, :]), "qv", 1)
                assert ctx.cnt["qv1"] == pl[("qm", it)]
            for sh in range(NSH):
                su = sh % 2
                p2 = prevk(it, sh, 2)
                if p2 is not None:
                    ctx.w(sync, "pe", pe_pos[("PV",) + p2 + (3, 15)])
                    ctx.w(sync, "ve", ve_m[("iodone",) + p2])
                    if it == 2 and sh >= 2:
                        ctx.slot_w(sync, "qp", su,
                                   qp_base[su] // 16 + sh // 2)
                else:
                    ctx.slot_w(sync, "qe", su,
                               pl[("dwr", NCH - 1, 2 + su)] // 16)
                ctx.slot_inc(sync.dma_start(out=qcb[su][:, :],
                                            in_=qcur_d[sh]), "qd", su)
                ctx.slot_inc(sync.dma_start(out=skb[su][:, :],
                                            in_=sk_d[sh]), "qd", su)
                ctx.slot_inc(sync.dma_start(out=vpb[su][:, :],
                                            in_=vp_d[sh]), "qd", su)
                if it == 1:
                    ctx.slot_inc(sync.dma_start(out=iob[su][:, :],
                                                in_=io_d[sh]), "qd", su)
                elif it == 2:
                    ctx.slot_inc(sync.dma_start(out=iob[su][:, :],
                                                in_=io1_d[sh]), "qd", su)
                assert ctx.cnt[f"qd{su}"] == pl[("ld", it, sh)]
                ctx.w(sync, "ve", ve_m[("ncT", it, sh)])
                ctx.slot_inc(sync.dma_start(out=negc_d[su],
                                            in_=negcT[su][:, :]), "qn", su)
                ctx.slot_w(sync, "qn", su, ctx.cnt[f"qn{su}"] // 16)
                if p2 is not None:
                    ctx.w(sync, "pe", pe_pos[("S",) + p2 + (3, 15)])
                ctx.slot_inc(sync.dma_start(
                    out=sqT[su][64:65, :],
                    in_=negc_d[su].rearrange("a b -> (a b)")[None, :]),
                    "qn", su)
                assert ctx.cnt[f"qn{su}"] == pl[("ncrow", it, sh)]
                if it == 0:
                    ctx.w(sync, "ve", ve_m[("iodone", it, sh)])
                    ctx.slot_inc(sync.dma_start(out=io_d[sh],
                                                in_=ionew[:, :]), "qe", su)
                    ctx.slot_inc(sync.dma_start(out=qcur_d[sh],
                                                in_=qcb[su][:, :]), "qe", su)
                elif it == 1:
                    ctx.w(sync, "ve", ve_m[("iodone", it, sh)])
                    ctx.slot_inc(sync.dma_start(out=io1_d[sh],
                                                in_=ionew[:, :]), "qe", su)
                if it < 2:
                    ctx.slot_w(sync, "qe", su, ctx.cnt[f"qe{su}"] // 16)
                assert (ctx.cnt["qe0"], ctx.cnt["qe1"]) == pl[("iw", it, sh)]
            if it < 2:
                ctx.w(sync, "ve", ve_m[("dcol", it)])
                ctx.slot_inc(sync.dma_start(out=dcol_d, in_=dcol1[:, :]),
                             "qv", 0)
                ctx.slot_w(sync, "qv", 0, ctx.cnt["qv0"] // 16)
                ctx.slot_inc(sync.dma_start(
                    out=drow[:, :],
                    in_=dcol_d.rearrange("a b -> (a b)")[None, :]), "qv", 0)
                assert ctx.cnt["qv0"] == pl[("drow", it)]
                ctx.w(sync, "ve", ve_m[("dsum", it)])
                ctx.inc(sync.dma_start(out=din_d, in_=scb_[:, :]), "qB", 16)
                assert ctx.cnt["qB"] == qa_pos[("din", it)]
                ctx.w(sync, "cc", cc_base + it + 1)
                ctx.inc(sync.dma_start(out=dres[:, :], in_=dout_d), "qB", 16)
                assert ctx.cnt["qB"] == qa_pos[("dout", it)]
        ctx.slot_w(sync, "qp", 0, qp_base[0] // 16 + 16)
        ctx.slot_w(sync, "qp", 1, qp_base[1] // 16 + 16)
        ctx.w(sync, "pe", last_v2)
        for c in range(NCH):
            ctx.inc(sync.dma_start(out=xnT[:, c * T:(c + 1) * T],
                                   in_=attn_park[c, :, :]), "qB", 16)
        assert ctx.cnt["qB"] == qa_pos[("stagef",)]
        for j in range(NT):
            jm = j % 2
            if j >= 2:
                ctx.w(sync, "ve", ve_m[("comb", j - 2)])
                ctx.slot_inc(sync.dma_start(
                    out=R_out[(j - 2) * 128:(j - 1) * 128, :],
                    in_=rin4[jm][:, :]), "qu", jm)
                ctx.slot_w(sync, "qu", jm, qo0[jm] // 16 + (j - 2) // 2 + 1)
            else:
                ctx.slot_w(sync, pqo, jm, qo_prev[jm] // 16)
            ctx.slot_inc(sync.dma_start(out=rin4[jm][:, :],
                                        in_=R_in[j * 128:(j + 1) * 128, :]),
                         "qi", jm)
        for j in (NT - 2, NT - 1):
            jm = j % 2
            ctx.w(sync, "ve", ve_m[("comb", j)])
            ctx.slot_inc(sync.dma_start(out=R_out[j * 128:(j + 1) * 128, :],
                                        in_=rin4[jm][:, :]), "qu", jm)
    blk.sync(f_sync)

    def f_gp(g):
        fence_wait(ctx, g, phv)
        ctx.w(g, "qw", qw_base)
        for (wsb, key) in ((wq, "l1_wq_t"), (wk, "l1_wk_t"), (wv, "l1_wv_t")):
            for c in range(NCH):
                ctx.inc(g.dma_start(out=wsb[:, c * D:(c + 1) * D],
                                    in_=wdr[key][c * 128:(c + 1) * 128, :]),
                        "qw", 16)
        for (dst, key) in ((lqT2, "l1_lq_t"), (lkT2, "l1_lk_t"),
                           (lvT2, "l1_lv_t")):
            ctx.inc(g.dma_start(out=dst[0:64, :], in_=wdr[key]), "qw", 16)
            ctx.inc(g.dma_start(out=dst[64:128, :], in_=wdr[key]), "qw", 16)
        for (dst, key) in ((iota2, "iota2d"), (lnc64, "l1_lnc64"),
                           (lnd64, "l1_lnd64"), (lqb64, "l1_lqb64"),
                           (lkb64, "l1_lkb64"), (lvb64, "l1_lvb64"),
                           (borep, "l1_borep"), (lna2, "l1_lna2"),
                           (lnb2, "l1_lnb2"), (bq2, "l1_bq2"),
                           (bv2, "l1_bv2"), (scalin, "l1_scal")):
            ctx.inc(g.dma_start(out=dst[:, :], in_=wdr[key]), "qw", 16)
        assert ctx.cnt["qw"] == qw_pos
        ctx.w(g, "pe", pe_pos[("q2", NCH - 1, NTG - 1)])
        for c in range(NCH):
            ctx.inc(g.dma_start(out=wo[:, c * D:(c + 1) * D],
                                in_=wdr["l1_wo_t"][c * 128:(c + 1) * 128, :]),
                    "qw", 16)
        wo_loaded[0] = ctx.cnt["qw"]
        for it in range(2):
            ctx.w(g, "qB", qa_pos[("din", it)])
            ctx.inc(g.collective_compute(
                "AllReduce", ALU.add, replica_groups=[list(range(NCORES))],
                ins=[din_d], outs=[dout_d]), "cc")
        for sh in range(NSH):
            su = sh % 2
            ctx.w(g, "ve", ve_m[("iodone", 2, sh)])
            s_, h = sh // 16, sh % 16
            hp, h01 = h // 2, h % 2
            ctx.slot_inc(g.dma_start(
                out=attn_park[hp, h01 * 64:h01 * 64 + 64,
                              s_ * SEQ:(s_ + 1) * SEQ],
                in_=iob[su][:, :]), "qp", su)
    blk.gpsimd(f_gp)

    def f_pe(t):
        ctx.w(t, "ve", ve_m[("wscale",)])
        for hp in range(NCH):
            for (kind, wsb) in (("q2", wq), ("k2", wk), ("v2", wv)):
                for tg in range(NTG):
                    if tg >= 2:
                        ctx.w(t, "ve", ve_m[(kind + "e", hp, tg - 2)])
                    elif kind == "q2" and hp >= 1:
                        ctx.w(t, "ve", ve_m[("vpb", hp - 1, 3)])
                    elif (hp, kind) != (0, "q2"):
                        prevkind = {"q2": "v2", "k2": "q2", "v2": "k2"}[kind]
                        ctx.w(t, "ve", ve_m[(prevkind + "e", hp,
                                             NTG - 2 + tg)])
                    for c in range(NCH):
                        i = t.matmul(pS[tg % 2][:, :],
                                     wsb[:, c * D + hp * 128:
                                         c * D + hp * 128 + 128],
                                     xnT[:, c * T + tg * 512:
                                         c * T + tg * 512 + 512],
                                     start=(c == 0), stop=(c == NCH - 1))
                    ctx.inc(i, "pe")
            for sh4 in range(4):
                su = sh4 % 2
                s_, h01 = sh4 // 2, sh4 % 2
                ctx.w(t, "ve", ve_m[("v2e", hp, NTG - 1)])
                if 4 * hp + sh4 >= 1:
                    pd = 4 * hp + sh4 - 1
                    ctx.w(t, "ve", ve_m[("vpb", pd // 4, pd % 4)])
                for j in range(16):
                    i = t.matmul(pS[j // 8][:, (j % 8) * 64:(j % 8) * 64 + 64],
                                 k2s[h01 * 64:h01 * 64 + 64,
                                     s_ * SEQ + j * 128:
                                     s_ * SEQ + j * 128 + 128],
                                 lkT2[h01 * 64:h01 * 64 + 64, :],
                                 start=True, stop=True)
                ctx.inc(i, "pe")
                ctx.w(t, "ve", ve_m[("knmb", hp, sh4)])
                for j in range(16):
                    i = t.transpose(
                        pT[j // 8][0:64, (j % 8) * 128:(j % 8) * 128 + 128],
                        sqb[:, j * 64:(j + 1) * 64], ident_bf[:, :])
                ctx.inc(i, "pe")
                ctx.w(t, "ve", ve_m[("skb", hp, sh4)])
                ctx.inc(t.transpose(pT[0][0:1, 0:128], ncb[:, 0:1],
                                    ident_bf[:, :]), "pe")
                ctx.w(t, "ve", ve_m[("mnbf", hp, sh4)])
                ctx.inc(t.matmul(pbc[:, 2:3], onesr[:, :], scab[0:1, 4:5],
                                 start=True, stop=True), "pe")
                ctx.w(t, "ve", ve_m[("mnrep", hp, sh4)])
                for j in range(16):
                    i = t.matmul(pS[j // 8][:, (j % 8) * 64:(j % 8) * 64 + 64],
                                 v2s[h01 * 64:h01 * 64 + 64,
                                     s_ * SEQ + j * 128:
                                     s_ * SEQ + j * 128 + 128],
                                 lvT2[h01 * 64:h01 * 64 + 64, :],
                                 start=True, stop=True)
                ctx.inc(i, "pe")
        for it in range(3):
            ctx.w(t, "ve", ve_m[("bmath", it)])
            ctx.inc(t.matmul(pbc[:, 0:1], onesr[:, :], scab[0:1, 0:1],
                             start=True, stop=True), "pe")
            if it >= 1:
                ctx.inc(t.matmul(pbc[:, 1:2], onesr[:, :], scab[0:1, 1:2],
                                 start=True, stop=True), "pe")
                ctx.w(t, "ve", ve_m[("kmb", it)])
                ctx.inc(t.transpose(pT[0][0:16, 0:128], kmb[:, :],
                                    ident_bf[:, :]), "pe")
            if it == 2:
                i = t.matmul(pbc[0:64, 0:1], onesr[:, 0:64], scab[0:1, 2:3],
                             start=True, stop=True)
                i = t.matmul(pbc[0:64, 1:2], onesr[:, 0:64], scab[0:1, 3:4],
                             start=True, stop=True)
                ctx.inc(i, "pe")
            for sh in range(NSH):
                su = sh % 2
                ctx.slot_w(t, "qd", su, pl[("ld", it, sh)] // 16)
                if it == 2:
                    ctx.w(t, "ve", ve_m[("qfix", it, sh)])
                p1 = prevk(it, sh, 1)
                if p1 is not None:
                    ctx.w(t, "ve", ve_m[("ion",) + p1 + (3,)])
                ctx.w(t, "ve", ve_m[("lncts", it)])
                for j in range(16):
                    i = t.matmul(pS[j // 8][:, (j % 8) * 64:(j % 8) * 64 + 64],
                                 qcb[su][:, j * 128:(j + 1) * 128],
                                 lqT2[0:64, :], start=True, stop=True)
                ctx.inc(i, "pe")
                ctx.w(t, "ve", ve_m[("sq2", "lnq", it, sh)])
                for j in range(16):
                    i = t.transpose(
                        pT[j // 8][0:64, (j % 8) * 128:(j % 8) * 128 + 128],
                        sqb[:, j * 64:(j + 1) * 64], ident_bf[:, :])
                ctx.inc(i, "pe")
                ctx.w(t, "ve", ve_m[("sqTe", it, sh)])
                for j in range(16):
                    if j >= 2:
                        ctx.w(t, "ve", ve_m[("nsr", it, sh, j - 2)])
                    rhs = skb[su][0:64, :].rearrange(
                        "p (a b) -> p a b", b=4)[:, 0:16, 0:1]
                    i = t.matmul(pS[j % 2][:, 0:16],
                                 sqT[su][0:64, j * 128:(j + 1) * 128],
                                 rhs, start=True, stop=True)
                    ctx.inc(i, "pe")
                ctx.w(t, "ve", ve_m[("ncb", it, sh)])
                ctx.inc(t.transpose(pT[0][0:16, 0:128], ncb[:, :],
                                    ident_bf[:, :]), "pe")
                ctx.slot_w(t, "qn", su, pl[("ncrow", it, sh)] // 16)
                sb4 = (pS[0], pS[1], pF[0], pF[1])
                for qg in range(4):
                    order = []
                    for kc in range(16):
                        order.append(("S", kc))
                        if kc >= 3:
                            order.append(("PV", kc - 3))
                    order += [("PV", 13), ("PV", 14), ("PV", 15)]
                    for (kind, kc) in order:
                        if kind == "S":
                            if kc >= 4:
                                ctx.w(t, "se",
                                      se_pos[("exp", it, sh, qg, kc - 4)])
                            i = t.matmul(sb4[kc % 4][:, :],
                                         skb[su][:, kc * 128:(kc + 1) * 128],
                                         sqT[su][:, qg * 512:(qg + 1) * 512],
                                         start=True, stop=True)
                            ctx.inc(i, "pe")
                            assert ctx.cnt["pe"] == pe_pos[
                                ("S", it, sh, qg, kc)], (it, sh, qg, kc)
                        else:
                            if kc == 0 and qg > 0:
                                ctx.w(t, "ve", ve_m[("ion", it, sh, qg - 1)])
                            if it >= 1 and kc >= 12:
                                ctx.w(t, "ve", ve_m[("km", it, sh, qg, kc)])
                            else:
                                ctx.w(t, "se", se_pos[("exp", it, sh, qg, kc)])
                            i = t.matmul(ppv[:, :],
                                         vpb[su][:, kc * 65:(kc + 1) * 65],
                                         pbuf[kc % 3][:, :],
                                         start=(kc == 0), stop=(kc == 15))
                            ctx.inc(i, "pe")
                            assert ctx.cnt["pe"] == pe_pos[
                                ("PV", it, sh, qg, kc)], (it, sh, qg, kc)
                    ctx.w(t, "ve", ve_m[("rq", it, sh, qg)])
                    ctx.inc(t.matmul(pbc[0:64, :], r65[64:65, RON:RON + 64],
                                     r65[64:65, RAL:RAL + 512],
                                     start=True, stop=True), "pe")
        ctx.w(t, "qB", qa_pos[("stagef",)])
        ctx.w(t, "qw", wo_loaded[0])
        ctx.w(t, "ve", ve_m[("ion", 2, NSH - 1, 3)])
        ctx.w(t, "se", se_pos[("exp", 2, NSH - 1, 3, 15)])
        for j in range(NT):
            jm = j % 2
            pfo = pS if jm == 0 else pF
            if j >= 2:
                ctx.w(t, "ve", ve_m[("comb", j - 2)])
            for half in range(2):
                for c in range(NCH):
                    i = t.matmul(pfo[half][:, :],
                                 xnT[:, c * T + j * 128: c * T + j * 128 + 128],
                                 wo[:, c * D + half * 512:
                                    c * D + half * 512 + 512],
                                 start=(c == 0), stop=(c == NCH - 1))
            ctx.inc(i, "pe")
    blk.tensor(f_pe)


def build_program(phases):
    from contextlib import ExitStack
    nc = bass.Bass("TRN2", target_bir_lowering=False, debug=False)

    dr = {}
    def din(name, shape):
        dr[name] = nc.dram_tensor(name, shape, F32, kind="ExternalInput").ap()
        return dr[name]

    x_in = din("x_in", [T, D])
    r_out = nc.dram_tensor("r_out", [T, D], F32, kind="ExternalOutput").ap()

    need = set(phases)
    if "l0ffn_a" in need or "l0ffn_b" in need:
        din("wr_t", [D, D]); din("wk_t", [D, D]); din("wv_t", [D, D])
        din("mu_r2", [128, NCH]); din("mu_k2", [128, NCH])
    if "l0attn" in need:
        din("wqkv_t", [D, 3 * D]); din("wproj_t", [D, D])
    if "l1ffn_a" in need or "l1ffn_b" in need:
        din("m_wr", [D, 8]); din("m_br", [8]); din("m_b1", [128, 128])
        din("m_b2", [8, D]); din("m_w1", [8, D, 2048]); din("m_w2", [8, 2048, D])
    if "l1attn" in need:
        for nm in ("l1_wq_t", "l1_wk_t", "l1_wv_t", "l1_wo_t"):
            din(nm, [D, D])
        for nm in ("l1_lq_t", "l1_lk_t", "l1_lv_t"):
            din(nm, [64, 64])
        for nm in ("l1_lnc64", "l1_lnd64", "l1_lqb64", "l1_lkb64", "l1_lvb64"):
            din(nm, [128, 64])
        din("l1_borep", [128, D])
        for nm in ("l1_lna2", "l1_lnb2", "l1_bq2", "l1_bv2"):
            din(nm, [128, NCH])
        din("l1_scal", [1, 4]); din("iota2d", [128, 16])
    din("ident", [128, 128])
    dscr = {}
    if "l0attn" in need:
        dscr["attn_park"] = nc.dram_tensor("attn_park", [NCH, 128, T], BF16,
                                           kind="Internal").ap()
        dscr["negc_d"] = nc.dram_tensor("negc_d", [2, 16, 128], BF16,
                                        kind="Internal").ap()
        dscr["vsum_d"] = nc.dram_tensor("vsum_d", [2, 1, 130], F32,
                                        kind="Internal").ap()
    if "l1ffn_a" in need or "l1ffn_b" in need:
        dscr["xnpark"] = nc.dram_tensor("xnpark", [NCH, 128, T], BF16,
                                        kind="Internal").ap()
        dscr["g_d"] = nc.dram_tensor("g_d", [8, T], BF16,
                                     kind="Internal").ap()
    if "l1attn" in need:
        if "attn_park" not in dscr:
            dscr["attn_park"] = nc.dram_tensor("attn_park", [NCH, 128, T],
                                               BF16, kind="Internal").ap()
            dscr["negc_d"] = nc.dram_tensor("negc_d", [2, 16, 128], BF16,
                                            kind="Internal").ap()
        dscr["qcur_d"] = nc.dram_tensor("qcur_d", [32, 64, SEQ], BF16,
                                        kind="Internal").ap()
        dscr["sk_d"] = nc.dram_tensor("sk_d", [32, 65, SEQ], BF16,
                                      kind="Internal").ap()
        dscr["vp_d"] = nc.dram_tensor("vp_d", [32, 128, 16 * 65], BF16,
                                      kind="Internal").ap()
        dscr["io_d"] = nc.dram_tensor("io_d", [32, 64, SEQ], BF16,
                                      kind="Internal").ap()
        dscr["io1_d"] = nc.dram_tensor("io1_d", [32, 64, SEQ], BF16,
                                       kind="Internal").ap()
        dscr["din_d"] = nc.dram_tensor("din_d", [1, 8], F32,
                                       kind="Internal").ap()
        dscr["dout_d"] = nc.dram_tensor("dout_d", [1, 8], F32,
                                        kind="Internal").ap()
        dscr["qm_d"] = nc.dram_tensor("qm_d", [16, 128], BF16,
                                      kind="Internal").ap()
        dscr["dcol_d"] = nc.dram_tensor("dcol_d", [64, 1], F32,
                                        kind="Internal").ap()

    es_outer = ExitStack()
    with es_outer as es:
        ctx = Ctx(nc)
        for s_ in ("qw", "qc", "qA", "ve", "se", "pe", "gp", "cc",
                   "qr0", "qr1", "qi0", "qi1", "qo0", "qo1",
                   "qp0", "qp1", "qn0", "qn1", "qv0", "qv1",
                   "qf0", "qf1", "qm0", "qm1", "qx0", "qx1", "qy0", "qy1",
                   "qz0", "qz1", "qd0", "qd1", "qe0", "qe1", "qB",
                   "qu0", "qu1"):
            ctx.add_sem(es, s_)

        ident_bf = es.enter_context(nc.sbuf_tensor("identb", [128, 128], BF16))
        onesb = es.enter_context(nc.sbuf_tensor("onesb", [128, 1], BF16))
        ln = {}
        for nm, shape, dt in (
            ("rt0", [128, D], F32), ("rt1", [128, D], F32),
            ("xn0", [128, D], BF16), ("xn1", [128, D], BF16),
            ("s", [128, 1], F32), ("negm", [128, 1], F32), ("ssq", [128, 1], F32),
            ("sd", [128, 1], F32), ("rstd", [128, 1], F32),
            ("eps", [128, 1], F32),
        ):
            ln[nm] = es.enter_context(nc.sbuf_tensor("ln_" + nm, shape, dt))
        pT = [es.enter_context(nc.psum_tensor(f"pT{i}", [128, 1024], BF16))
              for i in range(2)]

        with nc.Block() as blk:
            def f_const(g):
                ctx.inc(g.dma_start(out=ident_bf[:, :], in_=dr["ident"]), "qc", 16)
                ctx.inc(g.memset(ln["eps"][:, :], LN_EPS), "gp")
                ctx.inc(g.memset(onesb[:, :], 1.0), "gp")
            blk.gpsimd(f_const)
            ctx.setup = {"qc": ctx.cnt["qc"], "gp": ctx.cnt["gp"]}

            def f_cw(t):
                ctx.w(t, "qc", ctx.setup["qc"])
                ctx.w(t, "gp", ctx.setup["gp"])
            blk.tensor(f_cw)

            src = x_in
            psem = "qo"
            for ph in phases:
                nc.push_named_scope(ph)
                if ph.startswith("l0ffn"):
                    tag = ph[-1]
                    with ExitStack() as es_ph:
                        emit_l0ffn(nc, ctx, blk, es_ph, src, r_out,
                                   ln, ident_bf, pT,
                                   {k: dr[k] for k in
                                    ("wr_t", "wk_t", "wv_t", "mu_r2", "mu_k2")}, tag)
                    src = r_out
                elif ph.startswith("l1ffn"):
                    tag = ph[-1]
                    with ExitStack() as es_ph:
                        emit_l1ffn(nc, ctx, blk, es_ph, src, r_out,
                                   ident_bf, ln, pT,
                                   {k: dr[k] for k in
                                    ("m_wr", "m_br", "m_b1", "m_b2",
                                     "m_w1", "m_w2")}, dscr, tag, pqo=psem)
                    src = r_out
                    psem = "qu"
                elif ph == "l1attn":
                    with ExitStack() as es_ph:
                        emit_l1attn(nc, ctx, blk, es_ph, src, r_out,
                                    ln, ident_bf, onesb, pT, psem,
                                    {kk: dr[kk] for kk in dr
                                     if kk.startswith(("l1_", "iota"))},
                                    dscr, "y")
                    src = r_out
                    psem = "qu"
                elif ph == "l0attn":
                    with ExitStack() as es_ph:
                        emit_l0attn(nc, ctx, blk, es_ph, src, r_out,
                                    ln, ident_bf, onesb, pT,
                                    {k: dr[k] for k in ("wqkv_t", "wproj_t")},
                                    dscr, "x")
                    src = r_out
                else:
                    raise ValueError(ph)
                nc.pop_named_scope(ph)

            def f_tail(sync):
                ctx.wall(sync, "qo0")
                ctx.wall(sync, "qo1")
                ctx.wall(sync, "qu0")
                ctx.wall(sync, "qu1")
            blk.sync(f_tail)

    names = [t for t in dr]
    return nc, names


def _prep_shared(inputs):
    f32 = np.float32
    t = lambda a: np.ascontiguousarray(np.asarray(a).T).astype(f32, copy=False)
    out = {
        "wr_t": t(inputs["L0_ffn_W_r"]),
        "wk_t": t(inputs["L0_ffn_W_k"]),
        "wv_t": t(inputs["L0_ffn_W_v"]),
        "mu_r2": np.ascontiguousarray(
            np.asarray(inputs["L0_ffn_mu_r"], dtype=f32).reshape(NCH, 128).T),
        "mu_k2": np.ascontiguousarray(
            np.asarray(inputs["L0_ffn_mu_k"], dtype=f32).reshape(NCH, 128).T),
        "wqkv_t": t(inputs["L0_attn_w_qkv"]),
        "wproj_t": t(inputs["L0_attn_w_proj"]),
        "m_wr": np.ascontiguousarray(np.asarray(inputs["L1_ffn_Wr"], dtype=f32)),
        "m_br": np.ascontiguousarray(np.asarray(inputs["L1_ffn_br"], dtype=f32)),
        "m_b1": np.ascontiguousarray(
            np.asarray(inputs["L1_ffn_b1"], dtype=f32).reshape(8, 16, 128)
            .transpose(2, 0, 1).reshape(128, 128)),
        "m_b2": np.ascontiguousarray(np.asarray(inputs["L1_ffn_b2"], dtype=f32)),
        "m_w1": np.ascontiguousarray(np.asarray(inputs["L1_ffn_W1"], dtype=f32)),
        "m_w2": np.ascontiguousarray(np.asarray(inputs["L1_ffn_W2"], dtype=f32)),
        "l1_wq_t": t(inputs["L1_attn_Wq"]),
        "l1_wk_t": t(inputs["L1_attn_Wk"]),
        "l1_wv_t": t(inputs["L1_attn_Wv"]),
        "l1_wo_t": t(inputs["L1_attn_Wo"]),
        "l1_lq_t": t(inputs["L1_attn_lq_w"]),
        "l1_lk_t": t(inputs["L1_attn_lk_w"]),
        "l1_lv_t": t(inputs["L1_attn_lv_w"]),
        "l1_lnc64": np.ascontiguousarray(np.broadcast_to(
            np.asarray(inputs["L1_attn_lnc_w"], f32)[None, :], (128, 64))),
        "l1_lnd64": np.ascontiguousarray(np.broadcast_to(
            np.asarray(inputs["L1_attn_lnd_w"], f32)[None, :], (128, 64))),
        "l1_lqb64": np.ascontiguousarray(np.broadcast_to(
            np.asarray(inputs["L1_attn_lq_b"], f32)[None, :], (128, 64))),
        "l1_lkb64": np.ascontiguousarray(np.broadcast_to(
            np.asarray(inputs["L1_attn_lk_b"], f32)[None, :], (128, 64))),
        "l1_lvb64": np.ascontiguousarray(np.broadcast_to(
            np.asarray(inputs["L1_attn_lv_b"], f32)[None, :], (128, 64))),
        "l1_borep": np.ascontiguousarray(np.broadcast_to(
            np.asarray(inputs["L1_attn_bo"], f32)[None, :], (128, D))),
        "l1_lna2": np.ascontiguousarray(
            np.asarray(inputs["L1_attn_lna_w"], f32).reshape(NCH, 128).T),
        "l1_lnb2": np.ascontiguousarray(
            np.asarray(inputs["L1_attn_lnb_w"], f32).reshape(NCH, 128).T),
        "l1_bq2": np.ascontiguousarray(
            np.asarray(inputs["L1_attn_bq"], f32).reshape(NCH, 128).T),
        "l1_bv2": np.ascontiguousarray(
            np.asarray(inputs["L1_attn_bv"], f32).reshape(NCH, 128).T),
        "l1_scal": np.array([[np.float32(inputs["L1_attn_threshold"]),
                              np.float32(inputs["L1_attn_temp_base"]),
                              np.float32(inputs["L1_attn_factor"]), 0.0]],
                            dtype=f32),
        "iota2d": np.ascontiguousarray(
            (np.arange(128, dtype=f32)[:, None]
             + 128.0 * np.arange(16, dtype=f32)[None, :])),
        "ident": np.eye(128, dtype=f32),
    }
    return out


def kernel(**inputs):
    x = np.asarray(inputs["L0_attn_x"], dtype=np.float32)
    shared = _prep_shared(inputs)

    nc, names = build_program(["l0ffn_a", "l0attn", "l0ffn_b",
                               "l1ffn_a", "l1attn", "l1ffn_b"])
    in_maps = []
    for c in range(NCORES):
        m = {k: shared[k] for k in names if k in shared}
        m["x_in"] = np.ascontiguousarray(
            x[c * NSEQ:(c + 1) * NSEQ].reshape(T, D))
        in_maps.append(m)
    res = run_bass_kernel_spmd(nc, in_maps, core_ids=list(range(NCORES)))
    out = np.concatenate(
        [res.results[c]["r_out"].reshape(NSEQ, SEQ, D) for c in range(NCORES)],
        axis=0)
    return out
